# version 06
# baseline (speedup 1.0000x reference)
import numpy as np
import ml_dtypes
import concourse.bass as bass
import concourse.mybir as mybir
from concourse.bass_utils import run_bass_kernel_spmd

F32 = mybir.dt.float32
BF16 = mybir.dt.bfloat16
I32 = mybir.dt.int32
AF = mybir.ActivationFunctionType
ALU = mybir.AluOpType
AX = mybir.AxisListType

NCORES = 8
BC = 4
T = 2048
D = 1024
NT = BC * T // 128
TPS = T // 128
NH = 4
DKT, DVT = 512, 1024
IA = 2816
IF = 4096
EPS_RMS = 1e-6
EPS_LN = 1e-5
LN_SCALE_Q = float(np.log(np.float32(128.0) ** np.float32(-0.5)))
BF = ml_dtypes.bfloat16

ENGS = ("sy", "t", "v", "s", "g")
_BLOCK_NAME = {"sy": "sync", "t": "tensor", "v": "vector", "s": "scalar", "g": "gpsimd"}
QUEUES = ("dx", "ds", "dw")
SEMS = ("sy", "t", "v", "s", "g", "cc") + tuple(
    q + p for q in QUEUES for p in ("0", "1"))


class Env:

    def __init__(self, cur, eng, sems):
        self.cur = cur
        self.eng = eng
        self.sems = sems
        self.cnt = {k: 0 for k in SEMS}
        for q in QUEUES:
            self.cnt[q] = 0
        self.qpar = {q: 0 for q in QUEUES}
        self.prev_op = {}
        self.last_instr = {}
        self.qat = {q: {0: {}} for q in QUEUES}

    def on(self, ename, fn, inc=None, is_drain=False):
        if ename in ("v", "s") and not is_drain and self.prev_op.get(ename):
            self.prev_op[ename] = False
            self.on(ename, lambda e: e.drain(), is_drain=True)
        real = []
        if inc is not None:
            for s_, a_ in inc:
                if s_ in QUEUES:
                    p = self.qpar[s_]
                    self.qpar[s_] ^= 1
                    sem = s_ + str(p)
                    self.wait(ename, sem, self.cnt[sem])
                    real.append((sem, a_))
                else:
                    real.append((s_, a_))
        if ename == self.cur:
            instr = fn(self.eng)
            for s_, a_ in real:
                instr.then_inc(self.sems[s_], a_)
            if not is_drain:
                self.last_instr[ename] = instr
        for s_, a_ in real:
            self.cnt[s_] += a_
        if inc is not None:
            for s_, a_ in inc:
                if s_ in QUEUES:
                    self.cnt[s_] += a_
                    self.qat[s_][self.cnt[s_]] = {
                        s_ + "0": self.cnt[s_ + "0"], s_ + "1": self.cnt[s_ + "1"]}
        if not is_drain:
            self.prev_op[ename] = True

    def wait(self, ename, sem, val):
        if val <= 0:
            return
        if sem in QUEUES:
            snap = self.qat[sem].get(val)
            assert snap is not None, f"wait on unrecorded {sem} count {val}"
            for s_, v_ in snap.items():
                self.wait(ename, s_, v_)
            return
        if ename == self.cur:
            self.eng.wait_ge(self.sems[sem], val)


    def chk(self, ename, fn, sem=None):
        sem = sem or ename
        self.on(ename, fn, inc=[(sem, 1)])
        return self.cnt[sem]

    def lastinc(self, ename):
        if ename == self.cur:
            self.last_instr[ename].then_inc(self.sems[ename], 1)
        self.cnt[ename] += 1
        return self.cnt[ename]

    def tend(self):
        self.on("t", lambda e: e.drain(), is_drain=True)
        return self.chk("t", lambda e: e.nop())

    def barrier(self):
        self.wait("sy", "dx", self.cnt["dx"])
        self.wait("s", "ds", self.cnt["ds"])
        self.wait("g", "dw", self.cnt["dw"])
        snaps = {}
        for e in ENGS:
            self.wait(e, e, self.cnt[e])
            snaps[e] = self.chk(e, lambda eng: eng.nop())
        for e in ENGS:
            for o in ENGS:
                if o != e:
                    self.wait(e, o, snaps[o])


def _alloc(nc, opened, kind, *args, **kw):
    cm = getattr(nc, kind)(*args, **kw)
    h = cm.__enter__()
    opened.append(cm)
    return h


def _free(opened):
    for cm in reversed(opened):
        cm.__exit__(None, None, None)
    opened.clear()


def _build(phase_limit="full", nt=NT):
    nc = bass.Bass("TRN2", target_bir_lowering=False, debug=False)

    def din(name, shape):
        return nc.dram_tensor(name, shape, F32, kind="ExternalInput").ap()

    def dinb(name, shape):
        return nc.dram_tensor(name, shape, BF16, kind="ExternalInput").ap()

    x = din("x", [BC * T, D])
    Wq, Wk = din("Wq", [D, DKT]), din("Wk", [D, DKT])
    Wv, Wg = din("Wv", [D, DVT]), din("Wg", [D, DVT])
    Wgk1, Wgk2 = din("Wgk1", [D, 16]), din("Wgk2", [16, DKT])
    Wo = din("Wo", [DVT, D])
    Wgate, Wup = din("Wgate", [D, IA]), din("Wup", [D, IA])
    Wdown = din("Wdown", [IA, D])
    W1, W2 = din("W1", [IF, D]), din("W2", [D, IF])
    gam_attn = din("gam_attn", [128, D])
    bgk2b = din("bgk2b", [128, DKT])
    gamg = din("gamg", [128, DVT])
    gam_mlp = din("gam_mlp", [128, D])
    b1row = din("b1row", [1, IF])
    b2row = din("b2row", [1, D])
    identb_d = dinb("identb", [128, 128])
    ident2b_d = dinb("ident2b", [128, 64])
    identf_d = din("identf", [128, 128])
    lmask_d = dinb("lmask", [128, 64])
    maskt_d = din("maskt", [128, 256])
    cones_d = dinb("cones", [128, 1])
    onesrf_d = din("onesrf", [1, 128])
    onesrb_d = dinb("onesrb", [1, 128])

    x2h = nc.dram_tensor("x2h", [BC * T, D], F32).ap()
    oAh = nc.dram_tensor("oAh", [BC * T, D], F32).ap()
    hh = nc.dram_tensor("hh", [BC * T, IF], F32).ap()
    wq1h = nc.dram_tensor("wq1h", [IF, D], BF16).ap()
    wq2h = nc.dram_tensor("wq2h", [D, IF], BF16).ap()
    ccin = nc.dram_tensor("ccin", [1, 8], F32)
    ccout = nc.dram_tensor("ccout", [1, 8], F32)
    out = nc.dram_tensor("out", [BC * T, D], F32, kind="ExternalOutput").ap()

    opened_global = []
    opened_phase = []

    def ga(kind, *a, **k):
        return _alloc(nc, opened_global, kind, *a, **k)

    def pa(kind, *a, **k):
        return _alloc(nc, opened_phase, kind, *a, **k)

    sems = {n: ga("semaphore", "sem_" + n) for n in SEMS}
    identb = ga("sbuf_tensor", "identb_t", [128, 128], BF16)
    ident2 = ga("sbuf_tensor", "ident2_t", [128, 64], BF16)
    identf = ga("sbuf_tensor", "identf_t", [128, 128], F32)
    lmask = ga("sbuf_tensor", "lmask_t", [128, 64], BF16)
    maskt = ga("sbuf_tensor", "maskt_t", [128, 256], F32)
    cones = ga("sbuf_tensor", "cones_t", [128, 1], BF16)
    onesrf = ga("sbuf_tensor", "onesrf_t", [1, 128], F32)
    onesrb = ga("sbuf_tensor", "onesrb_t", [1, 128], BF16)
    st = ga("sbuf_tensor", "st", [128, 32], F32)
    cols = ga("sbuf_tensor", "cols", [128, 16], F32)
    rows = ga("sbuf_tensor", "rows", [1, 32], F32)
    rowi = ga("sbuf_tensor", "rowi", [1, 4], I32)
    ymt = ga("sbuf_tensor", "ymt", [128, NT], F32)
    yinvt = ga("sbuf_tensor", "yinvt", [128, NT], F32)
    ccsb = ga("sbuf_tensor", "ccsb", [1, 8], F32)

    (R_AMAX, R_NAMIN, R_D, R_SX, R_ISX, R_ZT, R_ZP, R_LO, R_HI, R_SXSW, R_ISXSW,
     R_SW1, R_SW2, R_WMAX, R_T0, R_T1) = range(16)
    C_ISX, C_LO, C_HI, C_SXSW = 0, 1, 2, 3
    C_ONE, C_EPSR, C_LNSQ = 9, 10, 11
    (S_SUM, S_SQ, S_M, S_MSQ, S_BIAS, S_LNV, S_INV1, S_SQ2, S_INV2) = range(9)
    S_G0 = 10
    S_GI = 14
    S_RED = 20
    S_SUM2, S_MSQ2, S_BIAS2 = 24, 25, 26
    S_FSQ, S_FLN, S_FINV = 28, 29, 30

    def row0(i):
        return 128 * i

    a1 = {}
    a1["wq"] = pa("sbuf_tensor", "a1_wq", [128, 8 * DKT], BF16)
    a1["wk"] = pa("sbuf_tensor", "a1_wk", [128, 8 * DKT], BF16)
    a1["wv"] = pa("sbuf_tensor", "a1_wv", [128, 8 * DVT], BF16)
    a1["wg"] = pa("sbuf_tensor", "a1_wg", [128, 8 * DVT], BF16)
    a1["wgk1"] = pa("sbuf_tensor", "a1_wgk1", [128, 8 * 16], BF16)
    a1["wgk2"] = pa("sbuf_tensor", "a1_wgk2", [16, DKT], BF16)
    a1["wo"] = pa("sbuf_tensor", "a1_wo", [128, 8 * D], BF16)
    a1["gat"] = pa("sbuf_tensor", "a1_gat", [128, D], F32)
    a1["bgk"] = pa("sbuf_tensor", "a1_bgk", [128, DKT], F32)
    a1["gag"] = pa("sbuf_tensor", "a1_gag", [128, DVT], F32)
    a1["xb"] = [pa("sbuf_tensor", f"a1_xb{j}", [128, D], F32) for j in range(2)]
    a1["xin"] = [pa("sbuf_tensor", f"a1_xin{j}", [128, D], F32) for j in range(2)]
    a1["x2b"] = [pa("sbuf_tensor", f"a1_x2b{j}", [128, D], F32) for j in range(2)]
    a1["hbf"] = pa("sbuf_tensor", "a1_hbf", [128, D], BF16)
    a1["hfm"] = pa("sbuf_tensor", "a1_hfm", [128, D], BF16)
    a1["q"] = pa("sbuf_tensor", "a1_q", [128, DKT], F32)
    a1["k"] = pa("sbuf_tensor", "a1_k", [128, DKT], F32)
    a1["vbf"] = pa("sbuf_tensor", "a1_vbf", [128, DVT], BF16)
    a1["g"] = pa("sbuf_tensor", "a1_g", [128, DVT], F32)
    a1["zg"] = pa("sbuf_tensor", "a1_zg", [128, DKT], F32)
    a1["gbf"] = pa("sbuf_tensor", "a1_gbf", [128, DKT], BF16)
    a1["expq"] = pa("sbuf_tensor", "a1_expq", [128, DKT], F32)
    a1["expm"] = pa("sbuf_tensor", "a1_expm", [128, DKT], F32)
    a1["qebf"] = pa("sbuf_tensor", "a1_qebf", [128, DKT], BF16)
    a1["kebf"] = pa("sbuf_tensor", "a1_kebf", [128, DKT], BF16)
    a1["qkfm"] = pa("sbuf_tensor", "a1_qkfm", [128, 2 * DKT], BF16)
    a1["atm"] = pa("sbuf_tensor", "a1_atm", [128, 256], BF16)
    a1["ebl"] = pa("sbuf_tensor", "a1_ebl", [128, 8], F32)
    a1["Sf"] = pa("sbuf_tensor", "a1_Sf", [128, DVT], F32)
    a1["Sb"] = pa("sbuf_tensor", "a1_Sb", [128, DVT], BF16)
    a1["on"] = pa("sbuf_tensor", "a1_on", [128, DVT], F32)
    a1["o2bf"] = pa("sbuf_tensor", "a1_o2bf", [128, DVT], BF16)
    a1["o2fm"] = pa("sbuf_tensor", "a1_o2fm", [128, DVT], BF16)
    a1["scrA"] = pa("sbuf_tensor", "a1_scrA", [128, D], F32)
    a1["scrB"] = pa("sbuf_tensor", "a1_scrB", [128, D], F32)
    a1["t1bf"] = pa("sbuf_tensor", "a1_t1bf", [128, 16], BF16)
    a1["t1fm"] = pa("sbuf_tensor", "a1_t1fm", [16, 128], BF16)
    a1["wbuf"] = [pa("sbuf_tensor", f"a1_wbuf{j}", [128, D], F32) for j in range(2)]
    a1["wqi"] = pa("sbuf_tensor", "a1_wqi", [128, D], I32)
    a1["wqo"] = pa("sbuf_tensor", "a1_wqo", [128, D], BF16)
    a1["psA"] = pa("psum_tensor", [128, 1024], F32)
    a1["psB"] = pa("psum_tensor", [128, 1024], F32)
    a1["psC"] = pa("psum_tensor", [128, 1024], F32)
    a1["psT"] = pa("psum_tensor", [128, 1024], BF16)
    a1["psX"] = pa("psum_tensor", [128, 512], F32)
    a1_opened = list(opened_phase)
    opened_phase.clear()
    _free(a1_opened)

    a2 = {}
    a2["wgate"] = pa("sbuf_tensor", "a2_wgate", [128, 8 * IA], BF16)
    a2["wup"] = pa("sbuf_tensor", "a2_wup", [128, 8 * IA], BF16)
    a2["wdown"] = pa("sbuf_tensor", "a2_wdown", [128, 22 * D], BF16)
    a2["gam"] = pa("sbuf_tensor", "a2_gam", [128, D], F32)
    a2["x2b"] = [pa("sbuf_tensor", f"a2_x2b{j}", [128, D], F32) for j in range(2)]
    a2["xb"] = [pa("sbuf_tensor", f"a2_xb{j}", [128, D], F32) for j in range(2)]
    a2["oab"] = [pa("sbuf_tensor", f"a2_oab{j}", [128, D], F32) for j in range(2)]
    a2["nbf"] = pa("sbuf_tensor", "a2_nbf", [128, D], BF16)
    a2["nfm"] = pa("sbuf_tensor", "a2_nfm", [128, D], BF16)
    a2["gate"] = pa("sbuf_tensor", "a2_gate", [128, 1408], F32)
    a2["scrA"] = pa("sbuf_tensor", "a2_scrA", [128, 1408], F32)
    a2["scrB"] = pa("sbuf_tensor", "a2_scrB", [128, 1408], F32)
    a2["scrC"] = pa("sbuf_tensor", "a2_scrC", [128, 1408], F32)
    a2["mlpb"] = pa("sbuf_tensor", "a2_mlpb", [128, 1408], BF16)
    a2["mlpfm"] = pa("sbuf_tensor", "a2_mlpfm", [128, 1408], BF16)
    a2["u"] = pa("sbuf_tensor", "a2_u", [128, D], F32)
    a2["ymaxr"] = pa("sbuf_tensor", "a2_ymaxr", [128, 1], F32)
    a2["yminr"] = pa("sbuf_tensor", "a2_yminr", [128, 1], F32)
    a2["psG"] = pa("psum_tensor", [128, 1536], F32)
    a2["psD"] = pa("psum_tensor", [128, 1024], F32)
    a2["psT"] = pa("psum_tensor", [128, 1024], BF16)
    a2["psX"] = pa("psum_tensor", [128, 512], F32)
    a2_opened = list(opened_phase)
    opened_phase.clear()
    _free(a2_opened)

    b1 = {}
    b1["wq1T"] = pa("sbuf_tensor", "b1_wq1T", [128, 8 * IF], BF16)
    b1["oab"] = [pa("sbuf_tensor", f"b1_oab{j}", [128, D], F32) for j in range(2)]
    b1["hbuf"] = [pa("sbuf_tensor", f"b1_hbuf{j}", [128, IF], F32) for j in range(2)]
    b1["y"] = pa("sbuf_tensor", "b1_y", [128, D], F32)
    b1["scr"] = pa("sbuf_tensor", "b1_scr", [128, D], F32)
    b1["qi"] = pa("sbuf_tensor", "b1_qi", [128, D], I32)
    b1["xqbf"] = pa("sbuf_tensor", "b1_xqbf", [128, D], BF16)
    b1["xqfm"] = pa("sbuf_tensor", "b1_xqfm", [128, D], BF16)
    b1["hmaxr"] = pa("sbuf_tensor", "b1_hmaxr", [128, 1], F32)
    b1["hminr"] = pa("sbuf_tensor", "b1_hminr", [128, 1], F32)
    b1["b1s"] = pa("sbuf_tensor", "b1_b1s", [1, IF], BF16)
    b1["psF"] = pa("psum_tensor", [128, 2048], F32)
    b1["psT"] = pa("psum_tensor", [128, 1024], BF16)
    b1["psX"] = pa("psum_tensor", [128, 512], F32)
    b1_opened = list(opened_phase)
    opened_phase.clear()
    _free(b1_opened)

    b2 = {}
    b2["wq2T"] = pa("sbuf_tensor", "b2_wq2T", [128, 32 * D], BF16)
    b2["hb"] = [pa("sbuf_tensor", f"b2_hb{j}", [128, IF], F32) for j in range(2)]
    b2["oab"] = [pa("sbuf_tensor", f"b2_oab{j}", [128, D], F32) for j in range(2)]
    b2["ob"] = [pa("sbuf_tensor", f"b2_ob{j}", [128, D], F32) for j in range(2)]
    b2["scr"] = pa("sbuf_tensor", "b2_scr", [128, IF], F32)
    b2["qi"] = pa("sbuf_tensor", "b2_qi", [128, IF], I32)
    b2["hqbf"] = pa("sbuf_tensor", "b2_hqbf", [128, IF], BF16)
    b2["hqfm"] = [pa("sbuf_tensor", f"b2_hqfm{j}", [128, D], BF16)
                  for j in range(2)]
    b2["fbuf"] = pa("sbuf_tensor", "b2_fbuf", [128, D], F32)
    b2["b2s"] = pa("sbuf_tensor", "b2_b2s", [1, D], BF16)
    b2["psD"] = pa("psum_tensor", [128, 1024], F32)
    b2["psT"] = pa("psum_tensor", [128, 1024], BF16)
    b2["psT2"] = pa("psum_tensor", [128, 1024], BF16)
    b2["psX"] = pa("psum_tensor", [128, 512], F32)
    b2_opened = list(opened_phase)
    opened_phase.clear()
    _free(b2_opened)

    def sched(env: Env):
        E = env
        E.on("v", lambda e: e.memset(cols[:, C_ONE:C_ONE + 1], 1.0))
        E.on("v", lambda e: e.memset(cols[:, C_EPSR:C_EPSR + 1], EPS_RMS))
        E.on("v", lambda e: e.memset(cols[:, C_LNSQ:C_LNSQ + 1], LN_SCALE_Q))

        for dst, src in ((identb, identb_d), (ident2, ident2b_d), (identf, identf_d),
                         (lmask, lmask_d), (maskt, maskt_d), (cones, cones_d),
                         (onesrf, onesrf_d), (onesrb, onesrb_d),
                         (a1["gat"], gam_attn), (a1["bgk"], bgk2b), (a1["gag"], gamg)):
            E.on("sy", lambda e, d=dst, s=src: e.dma_start(out=d[:, :], in_=s),
                 inc=[("dx", 16)])
        for wsb, wdr, ncol in ((a1["wq"], Wq, DKT), (a1["wk"], Wk, DKT),
                               (a1["wv"], Wv, DVT), (a1["wg"], Wg, DVT),
                               (a1["wgk1"], Wgk1, 16), (a1["wo"], Wo, D)):
            for c in range(8):
                E.on("g", lambda e, d=wsb, s=wdr, c=c, n=ncol:
                     e.dma_start(out=d[:, c * n:(c + 1) * n],
                                 in_=s[c * 128:(c + 1) * 128, :]),
                     inc=[("dw", 16)])
        E.on("g", lambda e: e.dma_start(out=a1["wgk2"][:, :], in_=Wgk2),
             inc=[("dw", 16)])
        dw_a1w = E.cnt["dw"]
        dx_setup = E.cnt["dx"]

        def wprep(slices, wqh_slices, r_sw, idx):
            nchunks = len(slices)
            wp_free, wp2_free, wq_st = {}, {}, {}
            ld = {}

            def emit_load(ldmap, c):
                E.on("sy", lambda e, c=c: e.dma_start(
                    out=a1["wbuf"][c % 2][:, :], in_=slices[c]), inc=[("dx", 16)])
                ldmap[c] = E.cnt["dx"]

            E.wait("sy", "v", E.cnt["v"])
            emit_load(ld, 0)
            emit_load(ld, 1)
            for c in range(nchunks):
                E.wait("v", "dx", ld[c])
                E.on("v", lambda e, c=c: e.reduce_max(
                    out=st[:, S_RED:S_RED + 1], in_=a1["wbuf"][c % 2][:, :],
                    axis=AX.X, apply_absolute_value=True))
                if c == 0:
                    v = E.chk("v", lambda e: e.tensor_copy(
                        out=st[:, S_RED + 1:S_RED + 2], in_=st[:, S_RED:S_RED + 1]))
                else:
                    v = E.chk("v", lambda e: e.tensor_max(
                        out=st[:, S_RED + 1:S_RED + 2], in0=st[:, S_RED + 1:S_RED + 2],
                        in1=st[:, S_RED:S_RED + 1]))
                wp_free[c] = v
                if c + 2 < nchunks:
                    E.wait("sy", "v", wp_free[c])
                    emit_load(ld, c + 2)
            E.wait("t", "v", wp_free[nchunks - 1])
            t = E.chk("t", lambda e: e.transpose(
                a1["psX"][0:1, idx * 128:(idx + 1) * 128],
                st[:, S_RED + 1:S_RED + 2], identf[:, :]))
            E.wait("v", "t", t)
            E.on("v", lambda e: e.reduce_max(
                out=rows[0:1, R_WMAX:R_WMAX + 1],
                in_=a1["psX"][0:1, idx * 128:(idx + 1) * 128], axis=AX.X))
            E.on("v", lambda e: e.tensor_scalar(
                out=rows[0:1, r_sw:r_sw + 1], in0=rows[0:1, R_WMAX:R_WMAX + 1],
                scalar1=1.0 / 127.0, scalar2=1e-8, op0=ALU.mult, op1=ALU.max))
            vsw = E.chk("v", lambda e: e.reciprocal(
                out=rows[0:1, R_T0 + idx:R_T0 + idx + 1],
                in_=rows[0:1, r_sw:r_sw + 1]))
            E.wait("t", "v", vsw)
            t = E.chk("t", lambda e: e.matmul(
                a1["psX"][:, idx:idx + 1], onesrf[0:1, :],
                rows[0:1, R_T0 + idx:R_T0 + idx + 1], start=True, stop=True))
            E.wait("v", "t", t)
            vbc = E.chk("v", lambda e: e.tensor_copy(out=cols[:, 8:9],
                                                     in_=a1["psX"][:, idx:idx + 1]))
            ld2 = {}
            E.wait("sy", "v", wp_free[nchunks - 1])
            emit_load(ld2, 0)
            emit_load(ld2, 1)
            for c in range(nchunks):
                E.wait("v", "dx", ld2[c])
                E.wait("v", "dw", wq_st.get(c - 1, 0))
                E.on("v", lambda e, c=c: e.tensor_scalar_mul(
                    out=a1["wbuf"][c % 2][:, :], in0=a1["wbuf"][c % 2][:, :],
                    scalar1=cols[:, 8:9]))
                E.on("v", lambda e, c=c: e.tensor_copy(
                    out=a1["wqi"][:, :], in_=a1["wbuf"][c % 2][:, :]))
                v = E.chk("v", lambda e, c=c: e.tensor_scalar(
                    out=a1["wqo"][:, :], in0=a1["wqi"][:, :],
                    scalar1=-127.0, scalar2=127.0, op0=ALU.max, op1=ALU.min))
                wp2_free[c] = v
                E.wait("g", "v", v)
                E.on("g", lambda e, c=c: e.dma_start(
                    out=wqh_slices[c], in_=a1["wqo"][:, :]), inc=[("dw", 16)])
                wq_st[c] = E.cnt["dw"]
                if c + 2 < nchunks:
                    E.wait("sy", "v", wp2_free[c])
                    emit_load(ld2, c + 2)

        w1_src = [W1[c * 128:(c + 1) * 128, :] for c in range(32)]
        w1_dst = [wq1h[c * 128:(c + 1) * 128, :] for c in range(32)]
        wprep(w1_src, w1_dst, R_SW1, 0)
        E.wait("v", "dw", E.cnt["dw"])
        w2_src = [W2[r * 128:(r + 1) * 128, cc * D:(cc + 1) * D]
                  for r in range(8) for cc in range(4)]
        w2_dst = [wq2h[r * 128:(r + 1) * 128, cc * D:(cc + 1) * D]
                  for r in range(8) for cc in range(4)]
        wprep(w2_src, w2_dst, R_SW2, 1)

        E.barrier()

        psA, psB, psC, psT, psX = (a1[k] for k in ("psA", "psB", "psC", "psT", "psX"))
        v_xin, s_sq, v_x2, st_x2, ld_x = {}, {}, {}, {}, {}
        v_hbf, v_o2fm, v_on, v_zb, v_atm1, v_o2 = {}, {}, {}, {}, {}, {}
        v_supd = 0
        vatm_prev = 0
        vqkfm_0 = 0
        sexp_last, supd_last, sebl_last, t_o_last = {}, {}, {}, {}

        E.on("sy", lambda e: e.dma_start(out=a1["xb"][0][:, :], in_=x[0:128, :]),
             inc=[("dx", 16)])
        ld_x[0] = E.cnt["dx"]

        def emit_wo_x2(i):
            E.wait("t", "v", v_o2fm[i])
            for kc in range(8):
                for half in range(2):
                    E.on("t", lambda e, kc=kc, half=half: e.matmul(
                        psB[:, half * 512:(half + 1) * 512],
                        a1["o2fm"][:, kc * 128:(kc + 1) * 128],
                        a1["wo"][:, kc * D + half * 512: kc * D + (half + 1) * 512],
                        start=(kc == 0), stop=(kc == 7)))
            two = E.tend()
            E.wait("v", "t", two)
            E.wait("v", "ds", st_x2.get(i - 2, 0))
            vx2 = E.chk("v", lambda e, i=i: e.tensor_add(
                out=a1["x2b"][i % 2][:, :], in0=psB[:, :], in1=a1["xin"][i % 2][:, :]))
            v_x2[i] = vx2
            E.wait("s", "v", vx2)
            E.on("s", lambda e, i=i: e.dma_start(
                out=x2h[row0(i):row0(i) + 128, :], in_=a1["x2b"][i % 2][:, :]),
                inc=[("ds", 16)])
            if phase_limit == "a1":
                E.on("s", lambda e, i=i: e.dma_start(
                    out=out[row0(i):row0(i) + 128, :], in_=a1["x2b"][i % 2][:, :]),
                    inc=[("ds", 16)])
            st_x2[i] = E.cnt["ds"]

        def a1_front(j):
            if j >= nt:
                return
            if j + 1 < nt:
                E.wait("sy", "v", v_xin.get(j - 1, 0))
                E.wait("sy", "s", s_sq.get(j - 1, 0))
                E.on("sy", lambda e, j=j: e.dma_start(
                    out=a1["xb"][(j + 1) % 2][:, :],
                    in_=x[row0(j + 1):row0(j + 1) + 128, :]), inc=[("dx", 16)])
                ld_x[j + 1] = E.cnt["dx"]
            if j % TPS == 0:
                E.wait("v", "t", t_o_last.get(j - 1, 0))
                E.on("v", lambda e: e.memset(a1["Sf"][:, :], 0.0))
                E.on("v", lambda e: e.memset(a1["Sb"][:, :], 0.0))
            E.wait("v", "dx", ld_x[j])
            E.on("v", lambda e, j=j: e.reduce_sum(
                out=st[:, S_SUM:S_SUM + 1], in_=a1["xb"][j % 2][:, :], axis=AX.X))
            E.on("v", lambda e: e.tensor_scalar_mul(
                out=st[:, S_M:S_M + 1], in0=st[:, S_SUM:S_SUM + 1], scalar1=1.0 / D))
            E.on("v", lambda e: e.tensor_mul(out=st[:, S_MSQ:S_MSQ + 1],
                                             in0=st[:, S_M:S_M + 1],
                                             in1=st[:, S_M:S_M + 1]))
            vb = E.chk("v", lambda e: e.tensor_scalar(
                out=st[:, S_BIAS:S_BIAS + 1], in0=st[:, S_MSQ:S_MSQ + 1],
                scalar1=-1.0, scalar2=EPS_LN, op0=ALU.mult, op1=ALU.add))
            E.wait("s", "dx", ld_x[j])
            E.wait("s", "v", v_o2.get(j - 1, 0))
            s_sq[j] = E.chk("s", lambda e, j=j: e.activation(
                out=a1["scrB"][:, :], in_=a1["xb"][j % 2][:, :], func=AF.Square,
                accum_out=st[:, S_SQ:S_SQ + 1]))
            E.wait("s", "v", vb)
            E.on("s", lambda e: e.activation(
                out=st[:, S_LNV:S_LNV + 1], in_=st[:, S_SQ:S_SQ + 1], func=AF.Ln,
                bias=st[:, S_BIAS:S_BIAS + 1], scale=1.0 / D))
            sinv = E.chk("s", lambda e: e.activation(
                out=st[:, S_INV1:S_INV1 + 1], in_=st[:, S_LNV:S_LNV + 1],
                func=AF.Exp, scale=-0.5))
            E.wait("v", "s", sinv)
            vxin = E.chk("v", lambda e, j=j: e.tensor_scalar(
                out=a1["xin"][j % 2][:, :], in0=a1["xb"][j % 2][:, :],
                scalar1=st[:, S_M:S_M + 1], scalar2=st[:, S_INV1:S_INV1 + 1],
                op0=ALU.subtract, op1=ALU.mult))
            v_xin[j] = vxin
            E.wait("s", "v", vxin)
            E.on("s", lambda e, j=j: e.activation(
                out=a1["scrB"][:, :], in_=a1["xin"][j % 2][:, :], func=AF.Square,
                accum_out=st[:, S_SQ2:S_SQ2 + 1]))
            E.on("s", lambda e: e.activation(
                out=st[:, S_LNV:S_LNV + 1], in_=st[:, S_SQ2:S_SQ2 + 1], func=AF.Ln,
                bias=cols[:, C_EPSR:C_EPSR + 1], scale=1.0 / D))
            sinv2 = E.chk("s", lambda e: e.activation(
                out=st[:, S_INV2:S_INV2 + 1], in_=st[:, S_LNV:S_LNV + 1],
                func=AF.Exp, scale=-0.5))
            E.wait("v", "s", sinv2)
            vhbf = E.chk("v", lambda e, j=j: e.scalar_tensor_tensor(
                out=a1["hbf"][:, :], in0=a1["xin"][j % 2][:, :],
                scalar=st[:, S_INV2:S_INV2 + 1], in1=a1["gat"][:, :],
                op0=ALU.mult, op1=ALU.mult))
            v_hbf[j] = vhbf
        a1_front(0)

        for i in range(nt):
            if i > 0:
                a1_front(i)
            E.wait("t", "v", v_hbf[i])
            E.wait("t", "v", v_o2fm.get(i - 1, 0))
            for j in range(8):
                E.on("t", lambda e, j=j: e.transpose(
                    psT[:, j * 128:(j + 1) * 128], a1["hbf"][:, j * 128:(j + 1) * 128],
                    identb[:, :]))
            t_htr = E.tend()
            E.wait("v", "t", t_htr)
            vhfm = E.chk("v", lambda e: e.tensor_copy(out=a1["hfm"][:, :], in_=psT[:, :]))
            E.wait("t", "v", vhfm)
            E.wait("t", "v", v_on.get(i - 1, 0))
            E.wait("t", "v", v_x2.get(i - 1, 0))
            E.wait("t", "v", v_atm1.get(i - 1, 0))
            E.wait("t", "v", v_zb.get(i - 1, 0))
            for kc in range(8):
                E.on("t", lambda e, kc=kc: e.matmul(
                    psX[:, 0:16], a1["hfm"][:, kc * 128:(kc + 1) * 128],
                    a1["wgk1"][:, kc * 16:(kc + 1) * 16],
                    start=(kc == 0), stop=(kc == 7)))
            t_t1 = E.tend()
            E.wait("v", "t", t_t1)
            vt1 = E.chk("v", lambda e: e.tensor_copy(out=a1["t1bf"][:, :],
                                                     in_=psX[:, 0:16]))
            E.wait("t", "v", vt1)
            tt1 = E.chk("t", lambda e: e.transpose(
                psT[0:16, 0:128], a1["t1bf"][:, :], identb[:, :]))
            E.wait("v", "t", tt1)
            vt1f = E.chk("v", lambda e: e.tensor_copy(out=a1["t1fm"][:, :],
                                                      in_=psT[0:16, 0:128]))
            E.wait("t", "v", vt1f)
            tz = E.chk("t", lambda e: e.matmul(
                psX[:, 0:512], a1["t1fm"][:, :], a1["wgk2"][:, :],
                start=True, stop=True))
            E.wait("v", "t", tz)
            vzb = E.chk("v", lambda e: e.tensor_add(
                out=a1["zg"][:, :], in0=psX[:, 0:512], in1=a1["bgk"][:, :]))
            v_zb[i] = vzb
            E.wait("s", "v", vzb)
            E.on("s", lambda e: e.activation(out=a1["scrA"][:, 0:512],
                                             in_=a1["zg"][:, :], func=AF.Exp,
                                             scale=-1.0))
            for kc in range(8):
                lhs = a1["hfm"][:, kc * 128:(kc + 1) * 128]
                ss = (kc == 0, kc == 7)
                E.on("t", lambda e, l=lhs, kc=kc, ss=ss: e.matmul(
                    psA[:, 0:512], l, a1["wq"][:, kc * DKT:(kc + 1) * DKT],
                    start=ss[0], stop=ss[1]))
                E.on("t", lambda e, l=lhs, kc=kc, ss=ss: e.matmul(
                    psA[:, 512:1024], l, a1["wk"][:, kc * DKT:(kc + 1) * DKT],
                    start=ss[0], stop=ss[1]))
                for half in range(2):
                    E.on("t", lambda e, l=lhs, kc=kc, half=half, ss=ss: e.matmul(
                        psB[:, half * 512:(half + 1) * 512], l,
                        a1["wv"][:, kc * DVT + half * 512: kc * DVT + (half + 1) * 512],
                        start=ss[0], stop=ss[1]))
                    E.on("t", lambda e, l=lhs, kc=kc, half=half, ss=ss: e.matmul(
                        psC[:, half * 512:(half + 1) * 512], l,
                        a1["wg"][:, kc * DVT + half * 512: kc * DVT + (half + 1) * 512],
                        start=ss[0], stop=ss[1]))
            t_proj = E.tend()
            E.wait("v", "t", t_proj)
            E.on("v", lambda e: e.tensor_copy(out=a1["q"][:, :], in_=psA[:, 0:512]))
            vt1_unused = E.chk("v", lambda e: e.tensor_copy(out=a1["vbf"][:, :],
                                                            in_=psB[:, :]))
            E.wait("s", "t", t_proj)
            E.on("s", lambda e: e.copy(out=a1["k"][:, :], in_=psA[:, 512:1024]))
            skg = E.chk("s", lambda e: e.copy(out=a1["g"][:, :], in_=psC[:, :]))
            sgbf = E.chk("s", lambda e: e.activation(
                out=a1["gbf"][:, :], in_=a1["scrA"][:, 0:512], func=AF.Ln,
                bias=cols[:, C_ONE:C_ONE + 1]))

            E.wait("t", "s", sgbf)
            E.wait("t", "s", skg)
            E.wait("t", "s", sexp_last.get(i - 1, 0))
            tb0 = E.chk("t", lambda e: e.matmul(
                psC[0:64, 0:512], lmask[0:64, :], a1["gbf"][0:64, :],
                start=True, stop=True))
            tb1 = E.chk("t", lambda e: e.matmul(
                psC[64:128, 0:512], lmask[64:128, :], a1["gbf"][64:128, :],
                start=True, stop=True))
            E.wait("s", "t", tb1)
            E.on("s", lambda e: e.activation(
                out=a1["expq"][:, :], in_=psC[:, 0:512], func=AF.Exp,
                bias=cols[:, C_LNSQ:C_LNSQ + 1]))
            sexp = E.chk("s", lambda e: e.activation(
                out=a1["expm"][:, :], in_=psC[:, 0:512], func=AF.Exp, scale=-1.0))
            sexp_last[i] = sexp
            E.wait("t", "s", sexp)
            for ch in range(2):
                r0 = 64 * ch
                for h in range(NH):
                    E.on("t", lambda e, h=h, r0=r0, ch=ch: e.matmul(
                        psC[:, 1008 + ch * 4 + h:1009 + ch * 4 + h],
                        a1["gbf"][r0:r0 + 64, h * 128:(h + 1) * 128],
                        cones[r0:r0 + 64, :], start=True, stop=True))
            tbl = E.tend()
            E.wait("s", "t", tbl)
            E.wait("s", "v", supd_last.get(i - 1, 0))
            sebl = E.chk("s", lambda e: e.activation(
                out=a1["ebl"][:, 0:8], in_=psC[:, 1008:1016], func=AF.Exp))
            sebl_last[i] = sebl
            E.wait("v", "s", sexp)
            E.on("v", lambda e: e.tensor_mul(out=a1["qebf"][:, :], in0=a1["q"][:, :],
                                             in1=a1["expq"][:, :]))
            vqk = E.chk("v", lambda e: e.tensor_mul(
                out=a1["kebf"][:, :], in0=a1["k"][:, :], in1=a1["expm"][:, :]))
            E.wait("t", "v", vqk)
            E.wait("t", "v", vt1f)
            for ch in range(2):
                r0 = 64 * ch
                co = ch * 512
                if ch == 1:
                    E.wait("t", "v", vqkfm_0)
                for h in range(NH):
                    E.on("t", lambda e, h=h, r0=r0: e.transpose(
                        psT[:, h * 64:(h + 1) * 64],
                        a1["qebf"][r0:r0 + 64, h * 128:(h + 1) * 128],
                        ident2[r0:r0 + 64, :]))
                    E.on("t", lambda e, h=h, r0=r0: e.transpose(
                        psT[:, 256 + h * 64:256 + (h + 1) * 64],
                        a1["kebf"][r0:r0 + 64, h * 128:(h + 1) * 128],
                        ident2[r0:r0 + 64, :]))
                ttr = E.tend()
                E.wait("v", "t", ttr)
                vqkfm = E.chk("v", lambda e, co=co: e.tensor_copy(
                    out=a1["qkfm"][:, co:co + 512], in_=psT[:, 0:512]))
                if ch == 0:
                    vqkfm_0 = vqkfm
                E.wait("t", "v", vqkfm)
                E.wait("t", "s", sebl)
                if ch == 1:
                    E.wait("t", "v", vatm_prev)
                for h in range(NH):
                    E.on("t", lambda e, h=h, r0=r0, co=co: e.matmul(
                        psC[r0:r0 + 64, 512 + h * 64:512 + (h + 1) * 64],
                        a1["qkfm"][:, co + 256 + h * 64:co + 256 + (h + 1) * 64],
                        a1["qkfm"][:, co + h * 64:co + (h + 1) * 64],
                        start=True, stop=True))
                tat = E.tend()
                E.wait("v", "t", tat)
                vatm = E.chk("v", lambda e, r0=r0: e.tensor_mul(
                    out=a1["atm"][r0:r0 + 64, :], in0=psC[r0:r0 + 64, 512:768],
                    in1=maskt[r0:r0 + 64, :]))
                vatm_prev = vatm
                if ch == 1:
                    v_atm1[i] = vatm
                E.wait("t", "v", vatm)
                E.wait("t", "v", v_supd)
                for h in range(NH):
                    E.on("t", lambda e, h=h, r0=r0, co=co: e.matmul(
                        psA[r0:r0 + 64, h * 256:(h + 1) * 256],
                        a1["atm"][r0:r0 + 64, h * 64:(h + 1) * 64],
                        a1["vbf"][r0:r0 + 64, h * 256:(h + 1) * 256],
                        start=True, stop=False))
                    E.on("t", lambda e, h=h, r0=r0, co=co: e.matmul(
                        psA[r0:r0 + 64, h * 256:(h + 1) * 256],
                        a1["qkfm"][:, co + h * 64:co + (h + 1) * 64],
                        a1["Sb"][:, h * 256:(h + 1) * 256],
                        start=False, stop=True))
                t_o = E.tend()
                for h in range(NH):
                    E.on("t", lambda e, h=h, r0=r0: e.matmul(
                        psB[:, h * 256:(h + 1) * 256],
                        a1["kebf"][r0:r0 + 64, h * 128:(h + 1) * 128],
                        a1["vbf"][r0:r0 + 64, h * 256:(h + 1) * 256],
                        start=True, stop=True))
                t_p = E.tend()
                E.wait("v", "t", t_p)
                E.wait("v", "s", sebl)
                E.on("v", lambda e: e.tensor_add(out=a1["Sf"][:, :],
                                                 in0=a1["Sf"][:, :], in1=psB[:, :]))
                for h in range(NH):
                    E.on("v", lambda e, h=h, ch=ch: e.tensor_scalar_mul(
                        out=a1["Sf"][:, h * 256:(h + 1) * 256],
                        in0=a1["Sf"][:, h * 256:(h + 1) * 256],
                        scalar1=a1["ebl"][:, ch * 4 + h:ch * 4 + h + 1]))
                v_supd = E.chk("v", lambda e: e.tensor_copy(out=a1["Sb"][:, :],
                                                            in_=a1["Sf"][:, :]))
            supd_last[i] = v_supd
            t_o_last[i] = t_o

            E.wait("s", "t", t_o)
            for h in range(NH):
                E.on("s", lambda e, h=h: e.activation(
                    out=a1["scrB"][:, h * 256:(h + 1) * 256],
                    in_=psA[:, h * 256:(h + 1) * 256], func=AF.Square,
                    accum_out=st[:, S_G0 + h:S_G0 + h + 1]))
            E.on("s", lambda e: e.activation(
                out=st[:, S_GI:S_GI + 4], in_=st[:, S_G0:S_G0 + 4], func=AF.Ln,
                bias=cols[:, C_EPSR:C_EPSR + 1], scale=1.0 / 256.0))
            sgi = E.chk("s", lambda e: e.activation(
                out=st[:, S_GI:S_GI + 4], in_=st[:, S_GI:S_GI + 4], func=AF.Exp,
                scale=-0.5))
            E.on("s", lambda e: e.activation(out=a1["scrA"][:, :], in_=a1["g"][:, :],
                                             func=AF.Exp, scale=-1.0))
            E.on("s", lambda e: e.activation(out=a1["scrB"][:, :], in_=a1["scrA"][:, :],
                                             func=AF.Ln, bias=cols[:, C_ONE:C_ONE + 1]))
            ssig = E.chk("s", lambda e: e.activation(
                out=a1["scrA"][:, :], in_=a1["scrB"][:, :], func=AF.Exp, scale=-1.0))
            E.wait("v", "s", sgi)
            for h in range(NH):
                E.on("v", lambda e, h=h: e.scalar_tensor_tensor(
                    out=a1["on"][:, h * 256:(h + 1) * 256],
                    in0=psA[:, h * 256:(h + 1) * 256],
                    scalar=st[:, S_GI + h:S_GI + h + 1],
                    in1=a1["gag"][:, h * 256:(h + 1) * 256],
                    op0=ALU.mult, op1=ALU.mult))
            v_on[i] = E.chk("v", lambda e: e.engine_nop())
            E.wait("v", "s", ssig)
            E.on("v", lambda e: e.tensor_mul(out=a1["scrB"][:, :], in0=a1["g"][:, :],
                                             in1=a1["scrA"][:, :]))
            vo2 = E.chk("v", lambda e: e.tensor_mul(
                out=a1["o2bf"][:, :], in0=a1["on"][:, :], in1=a1["scrB"][:, :]))
            v_o2[i] = vo2
            E.wait("t", "v", vo2)
            for j in range(8):
                E.on("t", lambda e, j=j: e.transpose(
                    psT[:, j * 128:(j + 1) * 128],
                    a1["o2bf"][:, j * 128:(j + 1) * 128], identb[:, :]))
            to2t = E.tend()
            E.wait("v", "t", to2t)
            v_o2fm[i] = E.chk("v", lambda e: e.tensor_copy(out=a1["o2fm"][:, :],
                                                           in_=psT[:, :]))
            emit_wo_x2(i)

        E.barrier()
        if phase_limit == "a1":
            return

        psG, psD, psT, psX = a2["psG"], a2["psD"], a2["psT"], a2["psX"]
        for wsb, wdr, ncol, nch in ((a2["wgate"], Wgate, IA, 8),
                                    (a2["wup"], Wup, IA, 8),
                                    (a2["wdown"], Wdown, D, 22)):
            for c in range(nch):
                E.on("g", lambda e, d=wsb, s=wdr, c=c, n=ncol: e.dma_start(
                    out=d[:, c * n:(c + 1) * n], in_=s[c * 128:(c + 1) * 128, :]),
                    inc=[("dw", 16)])
        E.on("sy", lambda e: e.dma_start(out=a2["gam"][:, :], in_=gam_mlp),
             inc=[("dx", 16)])
        dw_a2w = E.cnt["dw"]
        dx_a2 = E.cnt["dx"]
        E.wait("t", "dw", dw_a2w)
        E.on("v", lambda e: e.memset(a2["ymaxr"][:, :], -1e38))
        E.on("v", lambda e: e.memset(a2["yminr"][:, :], 1e38))
        E.wait("v", "dx", dx_a2)

        ld2, v_cons, s_oast, v_nbf, v_mlpfm2, v_u = {}, {}, {}, {}, {}, {}
        t_ntr, v_mlp1 = {}, {}

        def a2_loads(j):
            if j >= nt:
                return
            E.wait("sy", "v", v_cons.get(j - 2, 0))
            E.wait("sy", "ds", s_oast.get(j - 2, 0))
            E.on("sy", lambda e, j=j: e.dma_start(
                out=a2["x2b"][j % 2][:, :],
                in_=x2h[row0(j):row0(j) + 128, :]), inc=[("dx", 16)])
            E.on("sy", lambda e, j=j: e.dma_start(
                out=a2["xb"][j % 2][:, :],
                in_=x[row0(j):row0(j) + 128, :]), inc=[("dx", 16)])
            ld2[j] = E.cnt["dx"]

        def a2_front(j):
            if j >= nt:
                return
            E.wait("s", "dx", ld2[j])
            E.wait("s", "v", v_mlp1.get(j - 1, 0))
            E.on("s", lambda e, j=j: e.activation(
                out=a2["scrB"][:, 0:D], in_=a2["x2b"][j % 2][:, :], func=AF.Square,
                accum_out=st[:, S_FSQ:S_FSQ + 1]))
            E.on("s", lambda e: e.activation(
                out=st[:, S_FLN:S_FLN + 1], in_=st[:, S_FSQ:S_FSQ + 1], func=AF.Ln,
                bias=cols[:, C_EPSR:C_EPSR + 1], scale=1.0 / D))
            sinv = E.chk("s", lambda e: e.activation(
                out=st[:, S_FINV:S_FINV + 1], in_=st[:, S_FLN:S_FLN + 1],
                func=AF.Exp, scale=-0.5))
            E.wait("v", "s", sinv)
            E.wait("v", "t", t_ntr.get(j - 1, 0))
            vnbf = E.chk("v", lambda e, j=j: e.scalar_tensor_tensor(
                out=a2["nbf"][:, :], in0=a2["x2b"][j % 2][:, :],
                scalar=st[:, S_FINV:S_FINV + 1], in1=a2["gam"][:, :],
                op0=ALU.mult, op1=ALU.mult))
            v_nbf[j] = vnbf

        a2_loads(0)
        a2_loads(1)
        a2_front(0)

        for i in range(nt):
            E.wait("t", "v", v_nbf[i])
            E.wait("t", "v", v_mlpfm2.get(i - 1, 0))
            for j in range(8):
                E.on("t", lambda e, j=j: e.transpose(
                    psT[:, j * 128:(j + 1) * 128], a2["nbf"][:, j * 128:(j + 1) * 128],
                    identb[:, :]))
            tntr = E.tend()
            t_ntr[i] = tntr
            E.wait("v", "t", tntr)
            vnfm = E.chk("v", lambda e: e.tensor_copy(out=a2["nfm"][:, :], in_=psT[:, :]))
            segs = ((0, 512), (512, 1024), (1024, 1408))

            def a2_gate_or_up(wt, coff):
                for kc in range(8):
                    for (c0, c1) in segs:
                        E.on("t", lambda e, kc=kc, c0=c0, c1=c1, coff=coff, wt=wt:
                             e.matmul(psG[:, c0:c1],
                                      a2["nfm"][:, kc * 128:(kc + 1) * 128],
                                      wt[:, kc * IA + coff + c0: kc * IA + coff + c1],
                                      start=(kc == 0), stop=(kc == 7)))
                return E.tend()

            def a2_tr(half):
                for j in range(8):
                    E.on("t", lambda e, j=j: e.transpose(
                        psT[:, j * 128:(j + 1) * 128],
                        a2["mlpb"][:, j * 128:(j + 1) * 128], identb[:, :]))
                tm1 = E.tend()
                E.wait("v", "t", tm1)
                vf1 = E.chk("v", lambda e: e.tensor_copy(out=a2["mlpfm"][:, 0:1024],
                                                         in_=psT[:, :]))
                E.wait("t", "v", vf1)
                for j in range(8, 11):
                    E.on("t", lambda e, j=j: e.transpose(
                        psT[:, (j - 8) * 128:(j - 7) * 128],
                        a2["mlpb"][:, j * 128:(j + 1) * 128], identb[:, :]))
                tm2 = E.tend()
                E.wait("v", "t", tm2)
                return E.chk("v", lambda e: e.tensor_copy(
                    out=a2["mlpfm"][:, 1024:1408], in_=psT[:, 0:384]))

            def a2_down(half):
                for j in range(11):
                    kci = half * 11 + j
                    for hh2 in range(2):
                        E.on("t", lambda e, j=j, kci=kci, hh2=hh2: e.matmul(
                            psD[:, hh2 * 512:(hh2 + 1) * 512],
                            a2["mlpfm"][:, j * 128:(j + 1) * 128],
                            a2["wdown"][:, kci * D + hh2 * 512:
                                        kci * D + (hh2 + 1) * 512],
                            start=(kci == 0), stop=(kci == 21)))
                return E.tend()

            def a2_silu(vg):
                if vg is not None:
                    E.wait("s", "v", vg)
                E.on("s", lambda e: e.activation(out=a2["scrA"][:, :],
                                                 in_=a2["gate"][:, :], func=AF.Exp,
                                                 scale=-1.0))
                E.on("s", lambda e: e.activation(out=a2["scrB"][:, :],
                                                 in_=a2["scrA"][:, :], func=AF.Ln,
                                                 bias=cols[:, C_ONE:C_ONE + 1]))
                return E.chk("s", lambda e: e.activation(
                    out=a2["scrA"][:, :], in_=a2["scrB"][:, :], func=AF.Exp,
                    scale=-1.0))

            def a2_muls(tu, ssig, sg):
                E.wait("v", "t", tu)
                E.wait("v", "s", sg)
                vm1 = E.chk("v", lambda e: e.tensor_mul(
                    out=a2["scrC"][:, :], in0=psG[:, 0:1408], in1=a2["gate"][:, :]))
                E.wait("v", "s", ssig)
                vmlp = E.chk("v", lambda e: e.tensor_mul(
                    out=a2["mlpb"][:, :], in0=a2["scrC"][:, :], in1=a2["scrA"][:, :]))
                return vm1, vmlp

            def a2_gate_evac(tg):
                E.wait("v", "t", tg)
                vg = E.chk("v", lambda e: e.tensor_copy(out=a2["gate"][:, 0:704],
                                                        in_=psG[:, 0:704]))
                E.wait("s", "t", tg)
                sg = E.chk("s", lambda e: e.copy(out=a2["gate"][:, 704:1408],
                                                 in_=psG[:, 704:1408]))
                return vg, sg

            E.wait("t", "v", vnfm)
            tg0 = a2_gate_or_up(a2["wgate"], 0)
            vg0, sg0 = a2_gate_evac(tg0)
            ssig0 = a2_silu(vg0)
            E.wait("t", "v", vg0)
            E.wait("t", "s", sg0)
            tu0 = a2_gate_or_up(a2["wup"], 0)
            vm1_0, vmlp0 = a2_muls(tu0, ssig0, sg0)
            E.wait("t", "v", vm1_0)
            tg1 = a2_gate_or_up(a2["wgate"], 1408)
            E.wait("t", "v", vmlp0)
            vf2_0 = a2_tr(0)
            vg1, sg1 = a2_gate_evac(tg1)
            ssig1 = a2_silu(vg1)
            E.wait("t", "v", vg1)
            E.wait("t", "s", sg1)
            tu1 = a2_gate_or_up(a2["wup"], 1408)
            vm1_1, vmlp1 = a2_muls(tu1, ssig1, sg1)
            v_mlp1[i] = vmlp1
            E.wait("t", "v", vf2_0)
            E.wait("t", "v", v_u.get(i - 1, 0))
            a2_down(0)
            E.wait("t", "v", vmlp1)
            vf2_1 = a2_tr(1)
            v_mlpfm2[i] = vf2_1
            E.wait("t", "v", vf2_1)
            tdown = a2_down(1)

            a2_front(i + 1)

            E.wait("v", "t", tdown)
            vu = E.chk("v", lambda e, i=i: e.tensor_add(
                out=a2["u"][:, :], in0=psD[:, :], in1=a2["x2b"][i % 2][:, :]))
            v_u[i] = vu
            E.on("v", lambda e: e.reduce_sum(out=st[:, S_SUM:S_SUM + 1],
                                             in_=a2["u"][:, :], axis=AX.X))
            E.on("v", lambda e: e.tensor_scalar_mul(
                out=st[:, S_M:S_M + 1], in0=st[:, S_SUM:S_SUM + 1], scalar1=1.0 / D))
            E.on("v", lambda e: e.tensor_mul(out=st[:, S_MSQ:S_MSQ + 1],
                                             in0=st[:, S_M:S_M + 1],
                                             in1=st[:, S_M:S_M + 1]))
            vb = E.chk("v", lambda e: e.tensor_scalar(
                out=st[:, S_BIAS:S_BIAS + 1], in0=st[:, S_MSQ:S_MSQ + 1],
                scalar1=-1.0, scalar2=EPS_LN, op0=ALU.mult, op1=ALU.add))
            E.wait("s", "v", vu)
            E.on("s", lambda e: e.activation(out=a2["scrB"][:, 0:D], in_=a2["u"][:, :],
                                             func=AF.Square,
                                             accum_out=st[:, S_SQ:S_SQ + 1]))
            E.wait("s", "v", vb)
            E.on("s", lambda e: e.activation(
                out=st[:, S_LNV:S_LNV + 1], in_=st[:, S_SQ:S_SQ + 1], func=AF.Ln,
                bias=st[:, S_BIAS:S_BIAS + 1], scale=1.0 / D))
            sinvu = E.chk("s", lambda e: e.activation(
                out=st[:, S_INV1:S_INV1 + 1], in_=st[:, S_LNV:S_LNV + 1],
                func=AF.Exp, scale=-0.5))
            E.wait("v", "s", sinvu)
            E.on("v", lambda e: e.tensor_scalar(
                out=a2["scrA"][:, 0:D], in0=a2["u"][:, :], scalar1=st[:, S_M:S_M + 1],
                scalar2=st[:, S_INV1:S_INV1 + 1], op0=ALU.subtract, op1=ALU.mult))
            E.wait("v", "ds", s_oast.get(i - 2, 0))
            voa = E.chk("v", lambda e, i=i: e.tensor_add(
                out=a2["oab"][i % 2][:, :], in0=a2["scrA"][:, 0:D],
                in1=a2["xb"][i % 2][:, :]))
            E.wait("s", "v", voa)
            E.on("s", lambda e, i=i: e.dma_start(
                out=oAh[row0(i):row0(i) + 128, :], in_=a2["oab"][i % 2][:, :]),
                inc=[("ds", 16)])
            if phase_limit == "a2":
                E.on("s", lambda e, i=i: e.dma_start(
                    out=out[row0(i):row0(i) + 128, :], in_=a2["oab"][i % 2][:, :]),
                    inc=[("ds", 16)])
            s_oast[i] = E.cnt["ds"]
            E.on("v", lambda e, i=i: e.reduce_sum(
                out=st[:, S_SUM2:S_SUM2 + 1], in_=a2["oab"][i % 2][:, :], axis=AX.X))
            E.on("v", lambda e, i=i: e.tensor_scalar_mul(
                out=ymt[:, i:i + 1], in0=st[:, S_SUM2:S_SUM2 + 1], scalar1=1.0 / D))
            E.on("v", lambda e, i=i: e.tensor_mul(out=st[:, S_MSQ2:S_MSQ2 + 1],
                                                  in0=ymt[:, i:i + 1],
                                                  in1=ymt[:, i:i + 1]))
            vb2 = E.chk("v", lambda e: e.tensor_scalar(
                out=st[:, S_BIAS2:S_BIAS2 + 1], in0=st[:, S_MSQ2:S_MSQ2 + 1],
                scalar1=-1.0, scalar2=EPS_LN, op0=ALU.mult, op1=ALU.add))
            E.on("s", lambda e, i=i: e.activation(
                out=a2["scrB"][:, 0:D], in_=a2["oab"][i % 2][:, :], func=AF.Square,
                accum_out=st[:, S_SQ:S_SQ + 1]))
            E.wait("s", "v", vb2)
            E.on("s", lambda e: e.activation(
                out=st[:, S_LNV:S_LNV + 1], in_=st[:, S_SQ:S_SQ + 1], func=AF.Ln,
                bias=st[:, S_BIAS2:S_BIAS2 + 1], scale=1.0 / D))
            sy2 = E.chk("s", lambda e, i=i: e.activation(
                out=yinvt[:, i:i + 1], in_=st[:, S_LNV:S_LNV + 1], func=AF.Exp,
                scale=-0.5))
            E.wait("v", "s", sy2)
            E.on("v", lambda e, i=i: e.tensor_scalar(
                out=a2["scrA"][:, 0:D], in0=a2["oab"][i % 2][:, :],
                scalar1=ymt[:, i:i + 1], scalar2=yinvt[:, i:i + 1],
                op0=ALU.subtract, op1=ALU.mult))
            E.on("v", lambda e: e.reduce_max(out=st[:, S_RED:S_RED + 1],
                                             in_=a2["scrA"][:, 0:D], axis=AX.X))
            E.on("v", lambda e: e.tensor_max(out=a2["ymaxr"][:, :],
                                             in0=a2["ymaxr"][:, :],
                                             in1=st[:, S_RED:S_RED + 1]))
            E.on("v", lambda e: e.tensor_reduce(out=st[:, S_RED:S_RED + 1],
                                                in_=a2["scrA"][:, 0:D], axis=AX.X,
                                                op=ALU.min))
            vmm = E.chk("v", lambda e: e.tensor_tensor(
                out=a2["yminr"][:, :], in0=a2["yminr"][:, :],
                in1=st[:, S_RED:S_RED + 1], op=ALU.min))
            v_cons[i] = vmm
            a2_loads(i + 2)

        E.wait("t", "v", E.cnt["v"])
        tmx = E.chk("t", lambda e: e.transpose(psX[0:1, 0:128], a2["ymaxr"][:, :],
                                               identf[:, :]))
        E.wait("v", "t", tmx)
        vdone = E.chk("v", lambda e: e.reduce_max(out=ccsb[0:1, 0:1],
                                                  in_=psX[0:1, 0:128], axis=AX.X))
        E.wait("t", "v", vdone)
        tmn = E.chk("t", lambda e: e.transpose(psX[0:1, 0:128], a2["yminr"][:, :],
                                               identf[:, :]))
        E.wait("v", "t", tmn)
        E.on("v", lambda e: e.tensor_reduce(out=ccsb[0:1, 1:2], in_=psX[0:1, 0:128],
                                            axis=AX.X, op=ALU.min))
        E.on("v", lambda e: e.tensor_scalar_mul(out=ccsb[0:1, 1:2],
                                                in0=ccsb[0:1, 1:2], scalar1=-1.0))
        E.on("v", lambda e: e.memset(ccsb[0:1, 2:8], 0.0))

        E.barrier()
        if phase_limit == "a2":
            return

        def allreduce_and_scales(ps_x, r_sw, bsrow, brow_t, nb):
            vsnap = E.cnt["v"]
            E.wait("g", "v", vsnap)
            E.on("g", lambda e: e.dma_start(out=ccin[:, :], in_=ccsb[:, :]),
                 inc=[("dw", 16)])
            E.wait("g", "dw", E.cnt["dw"])
            E.on("g", lambda e: e.collective_compute(
                "AllReduce", ALU.max, replica_groups=[list(range(NCORES))],
                ins=[ccin.ap().opt()], outs=[ccout.ap().opt()]), inc=[("cc", 1)])
            E.wait("g", "cc", E.cnt["cc"])
            E.on("g", lambda e: e.dma_start(out=ccsb[:, :], in_=ccout[:, :]),
                 inc=[("dw", 16)])
            E.wait("v", "dw", E.cnt["dw"])
            E.on("v", lambda e: e.tensor_add(out=rows[0:1, R_D:R_D + 1],
                                             in0=ccsb[0:1, 0:1], in1=ccsb[0:1, 1:2]))
            E.on("v", lambda e: e.tensor_scalar(
                out=rows[0:1, R_SX:R_SX + 1], in0=rows[0:1, R_D:R_D + 1],
                scalar1=1.0 / 255.0, scalar2=1e-8, op0=ALU.mult, op1=ALU.max))
            E.on("v", lambda e: e.reciprocal(out=rows[0:1, R_ISX:R_ISX + 1],
                                             in_=rows[0:1, R_SX:R_SX + 1]))
            E.on("v", lambda e: e.tensor_mul(out=rows[0:1, R_ZT:R_ZT + 1],
                                             in0=ccsb[0:1, 1:2],
                                             in1=rows[0:1, R_ISX:R_ISX + 1]))
            E.on("v", lambda e: e.tensor_copy(out=rowi[0:1, 0:1],
                                              in_=rows[0:1, R_ZT:R_ZT + 1]))
            E.on("v", lambda e: e.tensor_copy(out=rows[0:1, R_ZT:R_ZT + 1],
                                              in_=rowi[0:1, 0:1]))
            E.on("v", lambda e: e.tensor_scalar_add(
                out=rows[0:1, R_ZP:R_ZP + 1], in0=rows[0:1, R_ZT:R_ZT + 1],
                scalar1=-128.0))
            E.on("v", lambda e: e.tensor_scalar(
                out=rows[0:1, R_ZP:R_ZP + 1], in0=rows[0:1, R_ZP:R_ZP + 1],
                scalar1=-128.0, scalar2=127.0, op0=ALU.max, op1=ALU.min))
            E.on("v", lambda e: e.tensor_scalar(
                out=rows[0:1, R_LO:R_LO + 1], in0=rows[0:1, R_ZP:R_ZP + 1],
                scalar1=-1.0, scalar2=-128.0, op0=ALU.mult, op1=ALU.add))
            E.on("v", lambda e: e.tensor_scalar(
                out=rows[0:1, R_HI:R_HI + 1], in0=rows[0:1, R_ZP:R_ZP + 1],
                scalar1=-1.0, scalar2=127.0, op0=ALU.mult, op1=ALU.add))
            E.on("v", lambda e: e.tensor_mul(out=rows[0:1, R_SXSW:R_SXSW + 1],
                                             in0=rows[0:1, R_SX:R_SX + 1],
                                             in1=rows[0:1, r_sw:r_sw + 1]))
            E.on("v", lambda e: e.reciprocal(out=rows[0:1, R_ISXSW:R_ISXSW + 1],
                                             in_=rows[0:1, R_SXSW:R_SXSW + 1]))
            E.on("v", lambda e, nb=nb: e.tensor_scalar_mul(
                out=bsrow[0:1, 0:nb], in0=brow_t,
                scalar1=rows[0:1, R_ISXSW:R_ISXSW + 1]))
            E.on("v", lambda e: e.tensor_copy(out=rows[0:1, 16:17],
                                              in_=rows[0:1, R_ISX:R_ISX + 1]))
            E.on("v", lambda e: e.tensor_copy(out=rows[0:1, 17:18],
                                              in_=rows[0:1, R_LO:R_LO + 1]))
            E.on("v", lambda e: e.tensor_copy(out=rows[0:1, 18:19],
                                              in_=rows[0:1, R_HI:R_HI + 1]))
            vpk = E.chk("v", lambda e: e.tensor_copy(
                out=rows[0:1, 19:20], in_=rows[0:1, R_SXSW:R_SXSW + 1]))
            E.wait("t", "v", vpk)
            tb = E.chk("t", lambda e, ps_x=ps_x: e.matmul(
                ps_x[:, 0:4], onesrf[0:1, :], rows[0:1, 16:20], start=True, stop=True))
            E.wait("v", "t", tb)
            return E.chk("v", lambda e, ps_x=ps_x: e.tensor_copy(
                out=cols[:, 0:4], in_=ps_x[:, 0:4]))

        E.on("sy", lambda e: e.dma_start(out=b1["hbuf"][0][0:1, :], in_=b1row),
             inc=[("dx", 16)])
        dx_b1r = E.cnt["dx"]
        E.wait("v", "dx", dx_b1r)
        allreduce_and_scales(b1["psX"], R_SW1, b1["b1s"], b1["hbuf"][0][0:1, 0:IF], IF)

        for c in range(8):
            E.on("sy", lambda e, c=c: e.dma_start(
                out=b1["wq1T"][:, c * IF:(c + 1) * IF],
                in_=wq1h[:, c * 128:(c + 1) * 128], transpose=True), inc=[("dx", 16)])
        dx_w1t = E.cnt["dx"]

        psF, psT, psX = b1["psF"], b1["psT"], b1["psX"]
        E.on("v", lambda e: e.memset(b1["hmaxr"][:, :], -1e38))
        E.on("v", lambda e: e.memset(b1["hminr"][:, :], 1e38))
        E.wait("t", "dx", dx_w1t)
        ldb, v_yq, st_h, v_xqfm, t_tr1 = {}, {}, {}, {}, {}

        def b1_loads(j):
            if j >= nt:
                return
            E.wait("sy", "v", v_yq.get(j - 2, 0))
            E.on("sy", lambda e, j=j: e.dma_start(
                out=b1["oab"][j % 2][:, :],
                in_=oAh[row0(j):row0(j) + 128, :]), inc=[("dx", 16)])
            ldb[j] = E.cnt["dx"]

        def b1_quant(j):
            if j >= nt:
                return
            E.wait("v", "dx", ldb[j])
            E.wait("v", "t", t_tr1.get(j - 1, 0))
            E.on("v", lambda e, j=j: e.tensor_scalar(
                out=b1["y"][:, :], in0=b1["oab"][j % 2][:, :],
                scalar1=ymt[:, j:j + 1], scalar2=yinvt[:, j:j + 1],
                op0=ALU.subtract, op1=ALU.mult))
            E.on("v", lambda e: e.tensor_scalar_mul(
                out=b1["scr"][:, :], in0=b1["y"][:, :],
                scalar1=cols[:, C_ISX:C_ISX + 1]))
            E.on("v", lambda e: e.tensor_copy(out=b1["qi"][:, :], in_=b1["scr"][:, :]))
            v_yq[j] = E.chk("v", lambda e: e.tensor_scalar(
                out=b1["xqbf"][:, :], in0=b1["qi"][:, :],
                scalar1=cols[:, C_LO:C_LO + 1], scalar2=cols[:, C_HI:C_HI + 1],
                op0=ALU.max, op1=ALU.min))

        b1_loads(0)
        b1_loads(1)
        b1_quant(0)
        for i in range(nt):
            E.wait("t", "v", v_yq[i])
            E.wait("t", "v", v_xqfm.get(i - 1, 0))
            for j in range(8):
                E.on("t", lambda e, j=j: e.transpose(
                    psT[:, j * 128:(j + 1) * 128], b1["xqbf"][:, j * 128:(j + 1) * 128],
                    identb[:, :]))
            ttr = E.tend()
            t_tr1[i] = ttr
            E.wait("v", "t", ttr)
            vfm = E.chk("v", lambda e: e.tensor_copy(out=b1["xqfm"][:, :], in_=psT[:, :]))
            v_xqfm[i] = vfm
            b1_loads(i + 2)
            b1_quant(i + 1)
            E.wait("t", "v", vfm)
            E.wait("s", "dw", st_h.get(i - 2, 0))
            sg = 0
            for ocp in range(2):
                E.wait("t", "s", sg)
                for kc in range(8):
                    for oc2 in range(2):
                        oc = ocp * 2 + oc2
                        for hh2 in range(2):
                            E.on("t", lambda e, oc=oc, oc2=oc2, kc=kc, hh2=hh2:
                                 e.matmul(
                                 psF[:, oc2 * 1024 + hh2 * 512:
                                     oc2 * 1024 + (hh2 + 1) * 512],
                                 b1["xqfm"][:, kc * 128:(kc + 1) * 128],
                                 b1["wq1T"][:, kc * IF + oc * 1024 + hh2 * 512:
                                            kc * IF + oc * 1024 + (hh2 + 1) * 512],
                                 start=(kc == 0), stop=False))
                for oc2 in range(2):
                    oc = ocp * 2 + oc2
                    for hh2 in range(2):
                        E.on("t", lambda e, oc=oc, oc2=oc2, hh2=hh2: e.matmul(
                            psF[:, oc2 * 1024 + hh2 * 512:
                                oc2 * 1024 + (hh2 + 1) * 512],
                            onesrb[0:1, :],
                            b1["b1s"][0:1, oc * 1024 + hh2 * 512:
                                      oc * 1024 + (hh2 + 1) * 512],
                            start=False, stop=True))
                tmm = E.tend()
                E.wait("s", "t", tmm)
                for oc2 in range(2):
                    oc = ocp * 2 + oc2
                    sg = E.chk("s", lambda e, oc=oc, oc2=oc2, i=i: e.activation(
                        out=b1["hbuf"][i % 2][:, oc * 1024:(oc + 1) * 1024],
                        in_=psF[:, oc2 * 1024:(oc2 + 1) * 1024],
                        func=AF.Gelu, scale=cols[:, C_SXSW:C_SXSW + 1]))
                E.wait("v", "s", sg)
                for oc2 in range(2):
                    oc = ocp * 2 + oc2
                    E.on("v", lambda e, oc=oc, i=i: e.reduce_max(
                        out=st[:, S_RED:S_RED + 1],
                        in_=b1["hbuf"][i % 2][:, oc * 1024:(oc + 1) * 1024], axis=AX.X))
                    E.on("v", lambda e: e.tensor_max(out=b1["hmaxr"][:, :],
                                                     in0=b1["hmaxr"][:, :],
                                                     in1=st[:, S_RED:S_RED + 1]))
                    E.on("v", lambda e, oc=oc, i=i: e.tensor_reduce(
                        out=st[:, S_RED:S_RED + 1],
                        in_=b1["hbuf"][i % 2][:, oc * 1024:(oc + 1) * 1024], axis=AX.X,
                        op=ALU.min))
                    E.on("v", lambda e: e.tensor_tensor(
                        out=b1["hminr"][:, :], in0=b1["hminr"][:, :],
                        in1=st[:, S_RED:S_RED + 1], op=ALU.min))
            vred = E.chk("v", lambda e: e.engine_nop())
            E.wait("g", "v", vred)
            E.on("g", lambda e, i=i: e.dma_start(
                out=hh[row0(i):row0(i) + 128, :], in_=b1["hbuf"][i % 2][:, :]),
                inc=[("dw", 16)])
            if phase_limit == "b1":
                E.on("g", lambda e, i=i: e.dma_start(
                    out=out[row0(i):row0(i) + 128, :],
                    in_=b1["hbuf"][i % 2][:, (i % 4) * 1024:((i % 4) + 1) * 1024]),
                    inc=[("dw", 16)])
            st_h[i] = E.cnt["dw"]

        E.wait("t", "v", E.cnt["v"])
        tmx = E.chk("t", lambda e: e.transpose(psX[0:1, 0:128], b1["hmaxr"][:, :],
                                               identf[:, :]))
        E.wait("v", "t", tmx)
        vdone = E.chk("v", lambda e: e.reduce_max(out=ccsb[0:1, 0:1],
                                                  in_=psX[0:1, 0:128], axis=AX.X))
        E.wait("t", "v", vdone)
        tmn = E.chk("t", lambda e: e.transpose(psX[0:1, 0:128], b1["hminr"][:, :],
                                               identf[:, :]))
        E.wait("v", "t", tmn)
        E.on("v", lambda e: e.tensor_reduce(out=ccsb[0:1, 1:2], in_=psX[0:1, 0:128],
                                            axis=AX.X, op=ALU.min))
        E.on("v", lambda e: e.tensor_scalar_mul(out=ccsb[0:1, 1:2],
                                                in0=ccsb[0:1, 1:2], scalar1=-1.0))
        E.on("v", lambda e: e.memset(ccsb[0:1, 2:8], 0.0))

        E.barrier()
        if phase_limit == "b1":
            return

        E.on("sy", lambda e: e.dma_start(out=b2["hb"][0][0:1, 0:D], in_=b2row),
             inc=[("dx", 16)])
        dx_b2r = E.cnt["dx"]
        E.wait("v", "dx", dx_b2r)
        vsc2 = allreduce_and_scales(b2["psX"], R_SW2, b2["b2s"], b2["hb"][0][0:1, 0:D], D)
        for c in range(32):
            E.on("sy", lambda e, c=c: e.dma_start(
                out=b2["wq2T"][:, c * D:(c + 1) * D],
                in_=wq2h[:, c * 128:(c + 1) * 128], transpose=True), inc=[("dx", 16)])
        dx_w2t = E.cnt["dx"]
        psD, psX = b2["psD"], b2["psX"]
        psTs = [b2["psT"], b2["psT2"]]
        E.wait("t", "dx", dx_w2t)
        ldh, v_hq, st_o, v_ffn, v_out, t_trq = {}, {}, {}, {}, {}, {}
        E.wait("sy", "v", vsc2)

        def b2_loads(j):
            if j >= nt:
                return
            E.wait("sy", "v", v_out.get(j - 2, 0))
            E.on("sy", lambda e, j=j: e.dma_start(
                out=b2["hb"][j % 2][:, :],
                in_=hh[row0(j):row0(j) + 128, :]), inc=[("dx", 16)])
            E.on("sy", lambda e, j=j: e.dma_start(
                out=b2["oab"][j % 2][:, :],
                in_=oAh[row0(j):row0(j) + 128, :]), inc=[("dx", 16)])
            ldh[j] = E.cnt["dx"]

        def b2_quant(j):
            if j >= nt:
                return
            E.wait("v", "dx", ldh[j])
            E.wait("v", "t", t_trq.get(j - 1, 0))
            E.on("v", lambda e, j=j: e.tensor_scalar_mul(
                out=b2["scr"][:, :], in0=b2["hb"][j % 2][:, :],
                scalar1=cols[:, C_ISX:C_ISX + 1]))
            E.on("v", lambda e: e.tensor_copy(out=b2["qi"][:, :], in_=b2["scr"][:, :]))
            v_hq[j] = E.chk("v", lambda e: e.tensor_scalar(
                out=b2["hqbf"][:, :], in0=b2["qi"][:, :],
                scalar1=cols[:, C_LO:C_LO + 1], scalar2=cols[:, C_HI:C_HI + 1],
                op0=ALU.max, op1=ALU.min))

        b2_loads(0)
        b2_loads(1)
        b2_quant(0)
        for i in range(nt):
            E.wait("t", "v", v_hq[i])
            E.wait("t", "v", v_ffn.get(i - 1, 0))
            vfm_prev = {}
            ttr = 0

            def b2_tr(grp):
                for j in range(8):
                    E.on("t", lambda e, grp=grp, j=j: e.transpose(
                        psTs[grp % 2][:, j * 128:(j + 1) * 128],
                        b2["hqbf"][:, grp * 1024 + j * 128: grp * 1024 + (j + 1) * 128],
                        identb[:, :]))
                return E.tend()

            def b2_fm(grp, ttrg):
                E.wait("v", "t", ttrg)
                return E.chk("v", lambda e, grp=grp: e.tensor_copy(
                    out=b2["hqfm"][grp % 2][:, :], in_=psTs[grp % 2][:, :]))

            ttrs = {0: b2_tr(0)}
            vfms = {0: b2_fm(0, ttrs[0])}
            for grp in range(4):
                if grp + 1 < 4:
                    E.wait("t", "v", vfms.get(grp - 1, 0))
                    ttrs[grp + 1] = b2_tr(grp + 1)
                    vfms[grp + 1] = b2_fm(grp + 1, ttrs[grp + 1])
                E.wait("t", "v", vfms[grp])
                for j in range(8):
                    kc = grp * 8 + j
                    for hh2 in range(2):
                        E.on("t", lambda e, grp=grp, j=j, kc=kc, hh2=hh2: e.matmul(
                            psD[:, hh2 * 512:(hh2 + 1) * 512],
                            b2["hqfm"][grp % 2][:, j * 128:(j + 1) * 128],
                            b2["wq2T"][:, kc * D + hh2 * 512: kc * D + (hh2 + 1) * 512],
                            start=(kc == 0), stop=False))
            ttr = ttrs[3]
            t_trq[i] = ttr
            for hh2 in range(2):
                E.on("t", lambda e, hh2=hh2: e.matmul(
                    psD[:, hh2 * 512:(hh2 + 1) * 512], onesrb[0:1, :],
                    b2["b2s"][0:1, hh2 * 512:(hh2 + 1) * 512], start=False, stop=True))
            tmm = E.tend()
            b2_quant(i + 1)
            E.wait("s", "t", tmm)
            sffn = E.chk("s", lambda e: e.activation(
                out=b2["fbuf"][:, :], in_=psD[:, :], func=AF.Copy,
                scale=cols[:, C_SXSW:C_SXSW + 1]))
            E.wait("v", "s", sffn)
            v_ffn[i] = E.chk("v", lambda e: e.engine_nop())
            E.on("v", lambda e: e.reduce_sum(out=st[:, S_SUM:S_SUM + 1],
                                             in_=b2["fbuf"][:, :], axis=AX.X))
            E.on("v", lambda e: e.tensor_scalar_mul(
                out=st[:, S_M:S_M + 1], in0=st[:, S_SUM:S_SUM + 1], scalar1=1.0 / D))
            E.on("v", lambda e: e.tensor_mul(out=st[:, S_MSQ:S_MSQ + 1],
                                             in0=st[:, S_M:S_M + 1],
                                             in1=st[:, S_M:S_M + 1]))
            vb = E.chk("v", lambda e: e.tensor_scalar(
                out=st[:, S_BIAS:S_BIAS + 1], in0=st[:, S_MSQ:S_MSQ + 1],
                scalar1=-1.0, scalar2=EPS_LN, op0=ALU.mult, op1=ALU.add))
            E.wait("s", "v", v_hq.get(i + 1, 0))
            E.on("s", lambda e: e.activation(out=b2["scr"][:, 0:D], in_=b2["fbuf"][:, :],
                                             func=AF.Square,
                                             accum_out=st[:, S_SQ:S_SQ + 1]))
            E.wait("s", "v", vb)
            E.on("s", lambda e: e.activation(
                out=st[:, S_LNV:S_LNV + 1], in_=st[:, S_SQ:S_SQ + 1], func=AF.Ln,
                bias=st[:, S_BIAS:S_BIAS + 1], scale=1.0 / D))
            sinv = E.chk("s", lambda e: e.activation(
                out=st[:, S_INV1:S_INV1 + 1], in_=st[:, S_LNV:S_LNV + 1],
                func=AF.Exp, scale=-0.5))
            E.wait("v", "s", sinv)
            E.on("v", lambda e: e.tensor_scalar(
                out=b2["scr"][:, 0:D], in0=b2["fbuf"][:, :], scalar1=st[:, S_M:S_M + 1],
                scalar2=st[:, S_INV1:S_INV1 + 1], op0=ALU.subtract, op1=ALU.mult))
            E.wait("v", "ds", st_o.get(i - 2, 0))
            vo = E.chk("v", lambda e, i=i: e.tensor_add(
                out=b2["ob"][i % 2][:, :], in0=b2["scr"][:, 0:D],
                in1=b2["oab"][i % 2][:, :]))
            v_out[i] = vo
            E.wait("s", "v", vo)
            E.on("s", lambda e, i=i: e.dma_start(
                out=out[row0(i):row0(i) + 128, :], in_=b2["ob"][i % 2][:, :]),
                inc=[("ds", 16)])
            st_o[i] = E.cnt["ds"]
            b2_loads(i + 2)

        E.barrier()

    with nc.Block() as block:
        for cur in ENGS:
            def body(eng, cur=cur):
                sched(Env(cur, eng, sems))
            getattr(block, _BLOCK_NAME[cur])(body)

    _free(opened_global)
    return nc


def _consts():
    c = {}
    c["identb"] = np.eye(128, dtype=BF)
    id2 = np.zeros((128, 64), dtype=np.float32)
    for p in range(128):
        id2[p, p % 64] = 1.0
    c["ident2b"] = id2.astype(BF)
    c["identf"] = np.eye(128, dtype=np.float32)
    sc = np.arange(64)
    pc = np.arange(128) % 64
    lm = np.where(pc[:, None] <= sc[None, :], np.float32(-1.0 / 16.0), np.float32(0.0))
    c["lmask"] = lm.astype(BF)
    mt = np.where(pc[:, None] <= sc[None, :], np.float32(1.0), np.float32(0.0))
    c["maskt"] = np.tile(mt, (1, 4)).astype(np.float32)
    c["cones"] = np.full((128, 1), -1.0 / 16.0, dtype=BF)
    c["onesrf"] = np.ones((1, 128), dtype=np.float32)
    c["onesrb"] = np.ones((1, 128), dtype=BF)
    return c


def _shard(inputs):
    f32 = np.float32
    hs = np.ascontiguousarray(np.asarray(inputs["L0_attn_hidden_states"], dtype=f32))
    base = {
        "Wq": np.ascontiguousarray(inputs["L0_attn_Wq"], dtype=f32),
        "Wk": np.ascontiguousarray(inputs["L0_attn_Wk"], dtype=f32),
        "Wv": np.ascontiguousarray(inputs["L0_attn_Wv"], dtype=f32),
        "Wg": np.ascontiguousarray(inputs["L0_attn_Wg"], dtype=f32),
        "Wgk1": np.ascontiguousarray(inputs["L0_attn_Wgk1"], dtype=f32),
        "Wgk2": np.ascontiguousarray(inputs["L0_attn_Wgk2"], dtype=f32),
        "Wo": np.ascontiguousarray(inputs["L0_attn_Wo"], dtype=f32),
        "Wgate": np.ascontiguousarray(inputs["L0_attn_W_gate"], dtype=f32),
        "Wup": np.ascontiguousarray(inputs["L0_attn_W_up"], dtype=f32),
        "Wdown": np.ascontiguousarray(inputs["L0_attn_W_down"], dtype=f32),
        "W1": np.ascontiguousarray(inputs["L0_ffn_W1"], dtype=f32),
        "W2": np.ascontiguousarray(inputs["L0_ffn_W2"], dtype=f32),
        "gam_attn": np.ascontiguousarray(np.broadcast_to(
            np.asarray(inputs["L0_attn_attn_norm_w"], dtype=f32)[None, :], (128, D))),
        "bgk2b": np.ascontiguousarray(np.broadcast_to(
            np.asarray(inputs["L0_attn_bgk2"], dtype=f32)[None, :], (128, DKT))),
        "gamg": np.ascontiguousarray(np.broadcast_to(
            np.tile(np.asarray(inputs["L0_attn_gnorm_w"], dtype=f32), 4)[None, :],
            (128, DVT))),
        "gam_mlp": np.ascontiguousarray(np.broadcast_to(
            np.asarray(inputs["L0_attn_mlp_norm_w"], dtype=f32)[None, :], (128, D))),
        "b1row": np.ascontiguousarray(
            np.asarray(inputs["L0_ffn_b1"], dtype=f32)[None, :]),
        "b2row": np.ascontiguousarray(
            np.asarray(inputs["L0_ffn_b2"], dtype=f32)[None, :]),
        **_consts(),
    }
    in_maps = []
    for c in range(NCORES):
        m = dict(base)
        m["x"] = np.ascontiguousarray(hs[c * BC:(c + 1) * BC].reshape(BC * T, D))
        in_maps.append(m)
    return in_maps


def kernel(**inputs):
    in_maps = _shard(inputs)
    nc = _build("full")
    res = run_bass_kernel_spmd(nc, in_maps, core_ids=list(range(NCORES)))
    outs = [np.asarray(r["out"]).reshape(BC, T, D) for r in res.results]
    return np.concatenate(outs, axis=0)


# version 07
# speedup vs baseline: 1.0537x; 1.0537x over previous
import numpy as np
import ml_dtypes
import concourse.bass as bass
import concourse.mybir as mybir
from concourse.bass_utils import run_bass_kernel_spmd

F32 = mybir.dt.float32
BF16 = mybir.dt.bfloat16
I32 = mybir.dt.int32
AF = mybir.ActivationFunctionType
ALU = mybir.AluOpType
AX = mybir.AxisListType

NCORES = 8
BC = 4
T = 2048
D = 1024
NT = BC * T // 128
TPS = T // 128
NH = 4
DKT, DVT = 512, 1024
IA = 2816
IF = 4096
EPS_RMS = 1e-6
EPS_LN = 1e-5
LN_SCALE_Q = float(np.log(np.float32(128.0) ** np.float32(-0.5)))
BF = ml_dtypes.bfloat16

ENGS = ("sy", "t", "v", "s", "g")
_BLOCK_NAME = {"sy": "sync", "t": "tensor", "v": "vector", "s": "scalar", "g": "gpsimd"}
QUEUES = ("dx", "ds", "dw")
SEMS = ("sy", "t", "v", "s", "g", "cc") + tuple(
    q + p for q in QUEUES for p in ("0", "1"))


class Env:

    def __init__(self, cur, eng, sems):
        self.cur = cur
        self.eng = eng
        self.sems = sems
        self.cnt = {k: 0 for k in SEMS}
        for q in QUEUES:
            self.cnt[q] = 0
        self.qpar = {q: 0 for q in QUEUES}
        self.prev_op = {}
        self.last_instr = {}
        self.qat = {q: {0: {}} for q in QUEUES}

    def on(self, ename, fn, inc=None, is_drain=False):
        if ename in ("v", "s") and not is_drain and self.prev_op.get(ename):
            self.prev_op[ename] = False
            self.on(ename, lambda e: e.drain(), is_drain=True)
        real = []
        if inc is not None:
            for s_, a_ in inc:
                if s_ in QUEUES:
                    p = self.qpar[s_]
                    self.qpar[s_] ^= 1
                    sem = s_ + str(p)
                    self.wait(ename, sem, self.cnt[sem])
                    real.append((sem, a_))
                else:
                    real.append((s_, a_))
        if ename == self.cur:
            instr = fn(self.eng)
            for s_, a_ in real:
                instr.then_inc(self.sems[s_], a_)
            if not is_drain:
                self.last_instr[ename] = instr
        for s_, a_ in real:
            self.cnt[s_] += a_
        if inc is not None:
            for s_, a_ in inc:
                if s_ in QUEUES:
                    self.cnt[s_] += a_
                    self.qat[s_][self.cnt[s_]] = {
                        s_ + "0": self.cnt[s_ + "0"], s_ + "1": self.cnt[s_ + "1"]}
        if not is_drain:
            self.prev_op[ename] = True

    def wait(self, ename, sem, val):
        if val <= 0:
            return
        if sem in QUEUES:
            snap = self.qat[sem].get(val)
            assert snap is not None, f"wait on unrecorded {sem} count {val}"
            for s_, v_ in snap.items():
                self.wait(ename, s_, v_)
            return
        if ename == self.cur:
            self.eng.wait_ge(self.sems[sem], val)


    def chk(self, ename, fn, sem=None):
        sem = sem or ename
        self.on(ename, fn, inc=[(sem, 1)])
        return self.cnt[sem]

    def lastinc(self, ename):
        if ename == self.cur:
            self.last_instr[ename].then_inc(self.sems[ename], 1)
        self.cnt[ename] += 1
        return self.cnt[ename]

    def tend(self):
        self.on("t", lambda e: e.drain(), is_drain=True)
        return self.chk("t", lambda e: e.nop())

    def barrier(self):
        self.wait("sy", "dx", self.cnt["dx"])
        self.wait("s", "ds", self.cnt["ds"])
        self.wait("g", "dw", self.cnt["dw"])
        snaps = {}
        for e in ENGS:
            self.wait(e, e, self.cnt[e])
            snaps[e] = self.chk(e, lambda eng: eng.nop())
        for e in ENGS:
            for o in ENGS:
                if o != e:
                    self.wait(e, o, snaps[o])


def _alloc(nc, opened, kind, *args, **kw):
    cm = getattr(nc, kind)(*args, **kw)
    h = cm.__enter__()
    opened.append(cm)
    return h


def _free(opened):
    for cm in reversed(opened):
        cm.__exit__(None, None, None)
    opened.clear()


def _build(phase_limit="full", nt=NT):
    nc = bass.Bass("TRN2", target_bir_lowering=False, debug=False)

    def din(name, shape):
        return nc.dram_tensor(name, shape, F32, kind="ExternalInput").ap()

    def dinb(name, shape):
        return nc.dram_tensor(name, shape, BF16, kind="ExternalInput").ap()

    x = din("x", [BC * T, D])
    Wq, Wk = din("Wq", [D, DKT]), din("Wk", [D, DKT])
    Wv, Wg = din("Wv", [D, DVT]), din("Wg", [D, DVT])
    Wgk1, Wgk2 = din("Wgk1", [D, 16]), din("Wgk2", [16, DKT])
    Wo = din("Wo", [DVT, D])
    Wgate, Wup = din("Wgate", [D, IA]), din("Wup", [D, IA])
    Wdown = din("Wdown", [IA, D])
    W1, W2 = din("W1", [IF, D]), din("W2", [D, IF])
    gam_attn = din("gam_attn", [128, D])
    bgk2b = din("bgk2b", [128, DKT])
    gamg = din("gamg", [128, DVT])
    gam_mlp = din("gam_mlp", [128, D])
    b1row = din("b1row", [1, IF])
    b2row = din("b2row", [1, D])
    identb_d = dinb("identb", [128, 128])
    ident2b_d = dinb("ident2b", [128, 64])
    identf_d = din("identf", [128, 128])
    lmask_d = dinb("lmask", [128, 64])
    maskt_d = din("maskt", [128, 256])
    cones_d = dinb("cones", [128, 1])
    onesrf_d = din("onesrf", [1, 128])
    onesrb_d = dinb("onesrb", [1, 128])

    x2h = nc.dram_tensor("x2h", [BC * T, D], F32).ap()
    oAh = nc.dram_tensor("oAh", [BC * T, D], F32).ap()
    hh = nc.dram_tensor("hh", [BC * T, IF], F32).ap()
    wq1h = nc.dram_tensor("wq1h", [IF, D], BF16).ap()
    wq2h = nc.dram_tensor("wq2h", [D, IF], BF16).ap()
    ccin = nc.dram_tensor("ccin", [1, 8], F32)
    ccout = nc.dram_tensor("ccout", [1, 8], F32)
    out = nc.dram_tensor("out", [BC * T, D], F32, kind="ExternalOutput").ap()

    opened_global = []
    opened_phase = []

    def ga(kind, *a, **k):
        return _alloc(nc, opened_global, kind, *a, **k)

    def pa(kind, *a, **k):
        return _alloc(nc, opened_phase, kind, *a, **k)

    sems = {n: ga("semaphore", "sem_" + n) for n in SEMS}
    identb = ga("sbuf_tensor", "identb_t", [128, 128], BF16)
    ident2 = ga("sbuf_tensor", "ident2_t", [128, 64], BF16)
    identf = ga("sbuf_tensor", "identf_t", [128, 128], F32)
    lmask = ga("sbuf_tensor", "lmask_t", [128, 64], BF16)
    maskt = ga("sbuf_tensor", "maskt_t", [128, 256], F32)
    cones = ga("sbuf_tensor", "cones_t", [128, 1], BF16)
    onesrf = ga("sbuf_tensor", "onesrf_t", [1, 128], F32)
    onesrb = ga("sbuf_tensor", "onesrb_t", [1, 128], BF16)
    st = ga("sbuf_tensor", "st", [128, 32], F32)
    cols = ga("sbuf_tensor", "cols", [128, 16], F32)
    rows = ga("sbuf_tensor", "rows", [1, 32], F32)
    rowi = ga("sbuf_tensor", "rowi", [1, 4], I32)
    ymt = ga("sbuf_tensor", "ymt", [128, NT], F32)
    yinvt = ga("sbuf_tensor", "yinvt", [128, NT], F32)
    ccsb = ga("sbuf_tensor", "ccsb", [1, 8], F32)

    (R_AMAX, R_NAMIN, R_D, R_SX, R_ISX, R_ZT, R_ZP, R_LO, R_HI, R_SXSW, R_ISXSW,
     R_SW1, R_SW2, R_WMAX, R_T0, R_T1) = range(16)
    C_ISX, C_LO, C_HI, C_SXSW = 0, 1, 2, 3
    C_ONE, C_EPSR, C_LNSQ = 9, 10, 11
    (S_SUM, S_SQ, S_M, S_MSQ, S_BIAS, S_LNV, S_INV1, S_SQ2, S_INV2) = range(9)
    S_G0 = 10
    S_GI = 14
    S_RED = 20
    S_SUM2, S_MSQ2, S_BIAS2 = 24, 25, 26
    S_FSQ, S_FLN, S_FINV = 28, 29, 30

    def row0(i):
        return 128 * i

    a1 = {}
    a1["wq"] = pa("sbuf_tensor", "a1_wq", [128, 8 * DKT], BF16)
    a1["wk"] = pa("sbuf_tensor", "a1_wk", [128, 8 * DKT], BF16)
    a1["wv"] = pa("sbuf_tensor", "a1_wv", [128, 8 * DVT], BF16)
    a1["wg"] = pa("sbuf_tensor", "a1_wg", [128, 8 * DVT], BF16)
    a1["wgk1"] = pa("sbuf_tensor", "a1_wgk1", [128, 8 * 16], BF16)
    a1["wgk2"] = pa("sbuf_tensor", "a1_wgk2", [16, DKT], BF16)
    a1["wo"] = pa("sbuf_tensor", "a1_wo", [128, 8 * D], BF16)
    a1["gat"] = pa("sbuf_tensor", "a1_gat", [128, D], F32)
    a1["bgk"] = pa("sbuf_tensor", "a1_bgk", [128, DKT], F32)
    a1["gag"] = pa("sbuf_tensor", "a1_gag", [128, DVT], F32)
    a1["xb"] = [pa("sbuf_tensor", f"a1_xb{j}", [128, D], F32) for j in range(2)]
    a1["xin"] = [pa("sbuf_tensor", f"a1_xin{j}", [128, D], F32) for j in range(2)]
    a1["x2b"] = [pa("sbuf_tensor", f"a1_x2b{j}", [128, D], F32) for j in range(2)]
    a1["hbf"] = pa("sbuf_tensor", "a1_hbf", [128, D], BF16)
    a1["hfm"] = pa("sbuf_tensor", "a1_hfm", [128, D], BF16)
    a1["q"] = pa("sbuf_tensor", "a1_q", [128, DKT], F32)
    a1["k"] = pa("sbuf_tensor", "a1_k", [128, DKT], F32)
    a1["vbf"] = pa("sbuf_tensor", "a1_vbf", [128, DVT], BF16)
    a1["g"] = pa("sbuf_tensor", "a1_g", [128, DVT], F32)
    a1["zg"] = pa("sbuf_tensor", "a1_zg", [128, DKT], F32)
    a1["gbf"] = pa("sbuf_tensor", "a1_gbf", [128, DKT], BF16)
    a1["expq"] = pa("sbuf_tensor", "a1_expq", [128, DKT], F32)
    a1["expm"] = pa("sbuf_tensor", "a1_expm", [128, DKT], F32)
    a1["qebf"] = pa("sbuf_tensor", "a1_qebf", [128, DKT], BF16)
    a1["kebf"] = pa("sbuf_tensor", "a1_kebf", [128, DKT], BF16)
    a1["qkfm"] = pa("sbuf_tensor", "a1_qkfm", [128, 2 * DKT], BF16)
    a1["atm"] = pa("sbuf_tensor", "a1_atm", [128, 256], BF16)
    a1["ebl"] = pa("sbuf_tensor", "a1_ebl", [128, 8], F32)
    a1["Sf"] = pa("sbuf_tensor", "a1_Sf", [128, DVT], F32)
    a1["Sb"] = pa("sbuf_tensor", "a1_Sb", [128, DVT], BF16)
    a1["on"] = pa("sbuf_tensor", "a1_on", [128, DVT], F32)
    a1["o2bf"] = pa("sbuf_tensor", "a1_o2bf", [128, DVT], BF16)
    a1["o2fm"] = pa("sbuf_tensor", "a1_o2fm", [128, DVT], BF16)
    a1["scrA"] = pa("sbuf_tensor", "a1_scrA", [128, D], F32)
    a1["scrB"] = pa("sbuf_tensor", "a1_scrB", [128, D], F32)
    a1["t1bf"] = pa("sbuf_tensor", "a1_t1bf", [128, 16], BF16)
    a1["t1fm"] = pa("sbuf_tensor", "a1_t1fm", [16, 128], BF16)
    a1["wbuf"] = [pa("sbuf_tensor", f"a1_wbuf{j}", [128, D], F32) for j in range(2)]
    a1["wqi"] = pa("sbuf_tensor", "a1_wqi", [128, D], I32)
    a1["wqo"] = pa("sbuf_tensor", "a1_wqo", [128, D], BF16)
    a1["psA"] = pa("psum_tensor", [128, 1024], F32)
    a1["psB"] = pa("psum_tensor", [128, 1024], F32)
    a1["psC"] = pa("psum_tensor", [128, 1024], F32)
    a1["psT"] = pa("psum_tensor", [128, 1024], BF16)
    a1["psX"] = pa("psum_tensor", [128, 512], F32)
    a1_opened = list(opened_phase)
    opened_phase.clear()
    _free(a1_opened)

    a2 = {}
    a2["wgate"] = pa("sbuf_tensor", "a2_wgate", [128, 8 * IA], BF16)
    a2["wup"] = pa("sbuf_tensor", "a2_wup", [128, 8 * IA], BF16)
    a2["wdown"] = pa("sbuf_tensor", "a2_wdown", [128, 22 * D], BF16)
    a2["gam"] = pa("sbuf_tensor", "a2_gam", [128, D], F32)
    a2["x2b"] = [pa("sbuf_tensor", f"a2_x2b{j}", [128, D], F32) for j in range(2)]
    a2["xb"] = [pa("sbuf_tensor", f"a2_xb{j}", [128, D], F32) for j in range(2)]
    a2["oab"] = [pa("sbuf_tensor", f"a2_oab{j}", [128, D], F32) for j in range(2)]
    a2["nbf"] = pa("sbuf_tensor", "a2_nbf", [128, D], BF16)
    a2["nfm"] = pa("sbuf_tensor", "a2_nfm", [128, D], BF16)
    a2["gate"] = [pa("sbuf_tensor", f"a2_gate{j}", [128, 1408], F32)
                  for j in range(2)]
    a2["scrA"] = pa("sbuf_tensor", "a2_scrA", [128, 1408], F32)
    a2["scrB"] = pa("sbuf_tensor", "a2_scrB", [128, 1408], F32)
    a2["scrC"] = pa("sbuf_tensor", "a2_scrC", [128, 1408], F32)
    a2["mlpb"] = [pa("sbuf_tensor", f"a2_mlpb{j}", [128, 1408], BF16)
                  for j in range(2)]
    a2["mlpfm"] = pa("sbuf_tensor", "a2_mlpfm", [128, 1408], BF16)
    a2["u"] = pa("sbuf_tensor", "a2_u", [128, D], F32)
    a2["ymaxr"] = pa("sbuf_tensor", "a2_ymaxr", [128, 1], F32)
    a2["yminr"] = pa("sbuf_tensor", "a2_yminr", [128, 1], F32)
    a2["psG"] = pa("psum_tensor", [128, 1536], F32)
    a2["psD"] = pa("psum_tensor", [128, 1024], F32)
    a2["psT"] = pa("psum_tensor", [128, 1024], BF16)
    a2["psX"] = pa("psum_tensor", [128, 512], F32)
    a2_opened = list(opened_phase)
    opened_phase.clear()
    _free(a2_opened)

    b1 = {}
    b1["wq1T"] = pa("sbuf_tensor", "b1_wq1T", [128, 8 * IF], BF16)
    b1["oab"] = [pa("sbuf_tensor", f"b1_oab{j}", [128, D], F32) for j in range(2)]
    b1["hbuf"] = [pa("sbuf_tensor", f"b1_hbuf{j}", [128, IF], F32) for j in range(2)]
    b1["y"] = pa("sbuf_tensor", "b1_y", [128, D], F32)
    b1["scr"] = pa("sbuf_tensor", "b1_scr", [128, D], F32)
    b1["qi"] = pa("sbuf_tensor", "b1_qi", [128, D], I32)
    b1["xqbf"] = pa("sbuf_tensor", "b1_xqbf", [128, D], BF16)
    b1["xqfm"] = pa("sbuf_tensor", "b1_xqfm", [128, D], BF16)
    b1["hmaxr"] = pa("sbuf_tensor", "b1_hmaxr", [128, 1], F32)
    b1["hminr"] = pa("sbuf_tensor", "b1_hminr", [128, 1], F32)
    b1["b1s"] = pa("sbuf_tensor", "b1_b1s", [1, IF], BF16)
    b1["psF"] = pa("psum_tensor", [128, 2048], F32)
    b1["psT"] = pa("psum_tensor", [128, 1024], BF16)
    b1["psX"] = pa("psum_tensor", [128, 512], F32)
    b1_opened = list(opened_phase)
    opened_phase.clear()
    _free(b1_opened)

    b2 = {}
    b2["wq2T"] = pa("sbuf_tensor", "b2_wq2T", [128, 32 * D], BF16)
    b2["hb"] = [pa("sbuf_tensor", f"b2_hb{j}", [128, IF], F32) for j in range(2)]
    b2["oab"] = [pa("sbuf_tensor", f"b2_oab{j}", [128, D], F32) for j in range(2)]
    b2["ob"] = [pa("sbuf_tensor", f"b2_ob{j}", [128, D], F32) for j in range(2)]
    b2["scr"] = pa("sbuf_tensor", "b2_scr", [128, IF], F32)
    b2["qi"] = pa("sbuf_tensor", "b2_qi", [128, IF], I32)
    b2["hqbf"] = pa("sbuf_tensor", "b2_hqbf", [128, IF], BF16)
    b2["hqfm"] = [pa("sbuf_tensor", f"b2_hqfm{j}", [128, D], BF16)
                  for j in range(2)]
    b2["fbuf"] = pa("sbuf_tensor", "b2_fbuf", [128, D], F32)
    b2["b2s"] = pa("sbuf_tensor", "b2_b2s", [1, D], BF16)
    b2["psD"] = pa("psum_tensor", [128, 1024], F32)
    b2["psT"] = pa("psum_tensor", [128, 1024], BF16)
    b2["psT2"] = pa("psum_tensor", [128, 1024], BF16)
    b2["psX"] = pa("psum_tensor", [128, 512], F32)
    b2_opened = list(opened_phase)
    opened_phase.clear()
    _free(b2_opened)

    def sched(env: Env):
        E = env
        E.on("v", lambda e: e.memset(cols[:, C_ONE:C_ONE + 1], 1.0))
        E.on("v", lambda e: e.memset(cols[:, C_EPSR:C_EPSR + 1], EPS_RMS))
        E.on("v", lambda e: e.memset(cols[:, C_LNSQ:C_LNSQ + 1], LN_SCALE_Q))

        for dst, src in ((identb, identb_d), (ident2, ident2b_d), (identf, identf_d),
                         (lmask, lmask_d), (maskt, maskt_d), (cones, cones_d),
                         (onesrf, onesrf_d), (onesrb, onesrb_d),
                         (a1["gat"], gam_attn), (a1["bgk"], bgk2b), (a1["gag"], gamg)):
            E.on("sy", lambda e, d=dst, s=src: e.dma_start(out=d[:, :], in_=s),
                 inc=[("dx", 16)])
        for wsb, wdr, ncol in ((a1["wq"], Wq, DKT), (a1["wk"], Wk, DKT),
                               (a1["wv"], Wv, DVT), (a1["wg"], Wg, DVT),
                               (a1["wgk1"], Wgk1, 16), (a1["wo"], Wo, D)):
            for c in range(8):
                E.on("g", lambda e, d=wsb, s=wdr, c=c, n=ncol:
                     e.dma_start(out=d[:, c * n:(c + 1) * n],
                                 in_=s[c * 128:(c + 1) * 128, :]),
                     inc=[("dw", 16)])
        E.on("g", lambda e: e.dma_start(out=a1["wgk2"][:, :], in_=Wgk2),
             inc=[("dw", 16)])
        dw_a1w = E.cnt["dw"]
        dx_setup = E.cnt["dx"]

        def wprep(slices, wqh_slices, r_sw, idx):
            nchunks = len(slices)
            wp_free, wp2_free, wq_st = {}, {}, {}
            ld = {}

            def emit_load(ldmap, c):
                E.on("sy", lambda e, c=c: e.dma_start(
                    out=a1["wbuf"][c % 2][:, :], in_=slices[c]), inc=[("dx", 16)])
                ldmap[c] = E.cnt["dx"]

            E.wait("sy", "v", E.cnt["v"])
            emit_load(ld, 0)
            emit_load(ld, 1)
            for c in range(nchunks):
                E.wait("v", "dx", ld[c])
                E.on("v", lambda e, c=c: e.reduce_max(
                    out=st[:, S_RED:S_RED + 1], in_=a1["wbuf"][c % 2][:, :],
                    axis=AX.X, apply_absolute_value=True))
                if c == 0:
                    v = E.chk("v", lambda e: e.tensor_copy(
                        out=st[:, S_RED + 1:S_RED + 2], in_=st[:, S_RED:S_RED + 1]))
                else:
                    v = E.chk("v", lambda e: e.tensor_max(
                        out=st[:, S_RED + 1:S_RED + 2], in0=st[:, S_RED + 1:S_RED + 2],
                        in1=st[:, S_RED:S_RED + 1]))
                wp_free[c] = v
                if c + 2 < nchunks:
                    E.wait("sy", "v", wp_free[c])
                    emit_load(ld, c + 2)
            E.wait("t", "v", wp_free[nchunks - 1])
            t = E.chk("t", lambda e: e.transpose(
                a1["psX"][0:1, idx * 128:(idx + 1) * 128],
                st[:, S_RED + 1:S_RED + 2], identf[:, :]))
            E.wait("v", "t", t)
            E.on("v", lambda e: e.reduce_max(
                out=rows[0:1, R_WMAX:R_WMAX + 1],
                in_=a1["psX"][0:1, idx * 128:(idx + 1) * 128], axis=AX.X))
            E.on("v", lambda e: e.tensor_scalar(
                out=rows[0:1, r_sw:r_sw + 1], in0=rows[0:1, R_WMAX:R_WMAX + 1],
                scalar1=1.0 / 127.0, scalar2=1e-8, op0=ALU.mult, op1=ALU.max))
            vsw = E.chk("v", lambda e: e.reciprocal(
                out=rows[0:1, R_T0 + idx:R_T0 + idx + 1],
                in_=rows[0:1, r_sw:r_sw + 1]))
            E.wait("t", "v", vsw)
            t = E.chk("t", lambda e: e.matmul(
                a1["psX"][:, idx:idx + 1], onesrf[0:1, :],
                rows[0:1, R_T0 + idx:R_T0 + idx + 1], start=True, stop=True))
            E.wait("v", "t", t)
            vbc = E.chk("v", lambda e: e.tensor_copy(out=cols[:, 8:9],
                                                     in_=a1["psX"][:, idx:idx + 1]))
            ld2 = {}
            E.wait("sy", "v", wp_free[nchunks - 1])
            emit_load(ld2, 0)
            emit_load(ld2, 1)
            for c in range(nchunks):
                E.wait("v", "dx", ld2[c])
                E.wait("v", "dw", wq_st.get(c - 1, 0))
                E.on("v", lambda e, c=c: e.tensor_scalar_mul(
                    out=a1["wbuf"][c % 2][:, :], in0=a1["wbuf"][c % 2][:, :],
                    scalar1=cols[:, 8:9]))
                E.on("v", lambda e, c=c: e.tensor_copy(
                    out=a1["wqi"][:, :], in_=a1["wbuf"][c % 2][:, :]))
                v = E.chk("v", lambda e, c=c: e.tensor_scalar(
                    out=a1["wqo"][:, :], in0=a1["wqi"][:, :],
                    scalar1=-127.0, scalar2=127.0, op0=ALU.max, op1=ALU.min))
                wp2_free[c] = v
                E.wait("g", "v", v)
                E.on("g", lambda e, c=c: e.dma_start(
                    out=wqh_slices[c], in_=a1["wqo"][:, :]), inc=[("dw", 16)])
                wq_st[c] = E.cnt["dw"]
                if c + 2 < nchunks:
                    E.wait("sy", "v", wp2_free[c])
                    emit_load(ld2, c + 2)

        w1_src = [W1[c * 128:(c + 1) * 128, :] for c in range(32)]
        w1_dst = [wq1h[c * 128:(c + 1) * 128, :] for c in range(32)]
        wprep(w1_src, w1_dst, R_SW1, 0)
        E.wait("v", "dw", E.cnt["dw"])
        w2_src = [W2[r * 128:(r + 1) * 128, cc * D:(cc + 1) * D]
                  for r in range(8) for cc in range(4)]
        w2_dst = [wq2h[r * 128:(r + 1) * 128, cc * D:(cc + 1) * D]
                  for r in range(8) for cc in range(4)]
        wprep(w2_src, w2_dst, R_SW2, 1)

        E.barrier()

        psA, psB, psC, psT, psX = (a1[k] for k in ("psA", "psB", "psC", "psT", "psX"))
        v_xin, s_sq, v_x2, st_x2, ld_x = {}, {}, {}, {}, {}
        v_hbf, v_o2fm, v_on, v_zb, v_atm1, v_o2 = {}, {}, {}, {}, {}, {}
        v_supd = 0
        vatm_prev = 0
        vqkfm_0 = 0
        sexp_last, supd_last, sebl_last, t_o_last = {}, {}, {}, {}

        E.on("sy", lambda e: e.dma_start(out=a1["xb"][0][:, :], in_=x[0:128, :]),
             inc=[("dx", 16)])
        ld_x[0] = E.cnt["dx"]

        def emit_wo_x2(i):
            E.wait("t", "v", v_o2fm[i])
            for kc in range(8):
                for half in range(2):
                    E.on("t", lambda e, kc=kc, half=half: e.matmul(
                        psB[:, half * 512:(half + 1) * 512],
                        a1["o2fm"][:, kc * 128:(kc + 1) * 128],
                        a1["wo"][:, kc * D + half * 512: kc * D + (half + 1) * 512],
                        start=(kc == 0), stop=(kc == 7)))
            two = E.tend()
            E.wait("v", "t", two)
            E.wait("v", "ds", st_x2.get(i - 2, 0))
            vx2 = E.chk("v", lambda e, i=i: e.tensor_add(
                out=a1["x2b"][i % 2][:, :], in0=psB[:, :], in1=a1["xin"][i % 2][:, :]))
            v_x2[i] = vx2
            E.wait("s", "v", vx2)
            E.on("s", lambda e, i=i: e.dma_start(
                out=x2h[row0(i):row0(i) + 128, :], in_=a1["x2b"][i % 2][:, :]),
                inc=[("ds", 16)])
            if phase_limit == "a1":
                E.on("s", lambda e, i=i: e.dma_start(
                    out=out[row0(i):row0(i) + 128, :], in_=a1["x2b"][i % 2][:, :]),
                    inc=[("ds", 16)])
            st_x2[i] = E.cnt["ds"]

        def a1_front(j):
            if j >= nt:
                return
            if j + 1 < nt:
                E.wait("sy", "v", v_xin.get(j - 1, 0))
                E.wait("sy", "s", s_sq.get(j - 1, 0))
                E.on("sy", lambda e, j=j: e.dma_start(
                    out=a1["xb"][(j + 1) % 2][:, :],
                    in_=x[row0(j + 1):row0(j + 1) + 128, :]), inc=[("dx", 16)])
                ld_x[j + 1] = E.cnt["dx"]
            if j % TPS == 0:
                E.wait("v", "t", t_o_last.get(j - 1, 0))
                E.on("v", lambda e: e.memset(a1["Sf"][:, :], 0.0))
                E.on("v", lambda e: e.memset(a1["Sb"][:, :], 0.0))
            E.wait("v", "dx", ld_x[j])
            E.on("v", lambda e, j=j: e.reduce_sum(
                out=st[:, S_SUM:S_SUM + 1], in_=a1["xb"][j % 2][:, :], axis=AX.X))
            E.on("v", lambda e: e.tensor_scalar_mul(
                out=st[:, S_M:S_M + 1], in0=st[:, S_SUM:S_SUM + 1], scalar1=1.0 / D))
            E.on("v", lambda e: e.tensor_mul(out=st[:, S_MSQ:S_MSQ + 1],
                                             in0=st[:, S_M:S_M + 1],
                                             in1=st[:, S_M:S_M + 1]))
            vb = E.chk("v", lambda e: e.tensor_scalar(
                out=st[:, S_BIAS:S_BIAS + 1], in0=st[:, S_MSQ:S_MSQ + 1],
                scalar1=-1.0, scalar2=EPS_LN, op0=ALU.mult, op1=ALU.add))
            E.wait("s", "dx", ld_x[j])
            E.wait("s", "v", v_o2.get(j - 1, 0))
            s_sq[j] = E.chk("s", lambda e, j=j: e.activation(
                out=a1["scrB"][:, :], in_=a1["xb"][j % 2][:, :], func=AF.Square,
                accum_out=st[:, S_SQ:S_SQ + 1]))
            E.wait("s", "v", vb)
            E.on("s", lambda e: e.activation(
                out=st[:, S_LNV:S_LNV + 1], in_=st[:, S_SQ:S_SQ + 1], func=AF.Ln,
                bias=st[:, S_BIAS:S_BIAS + 1], scale=1.0 / D))
            sinv = E.chk("s", lambda e: e.activation(
                out=st[:, S_INV1:S_INV1 + 1], in_=st[:, S_LNV:S_LNV + 1],
                func=AF.Exp, scale=-0.5))
            E.wait("v", "s", sinv)
            vxin = E.chk("v", lambda e, j=j: e.tensor_scalar(
                out=a1["xin"][j % 2][:, :], in0=a1["xb"][j % 2][:, :],
                scalar1=st[:, S_M:S_M + 1], scalar2=st[:, S_INV1:S_INV1 + 1],
                op0=ALU.subtract, op1=ALU.mult))
            v_xin[j] = vxin
            E.wait("s", "v", vxin)
            E.on("s", lambda e, j=j: e.activation(
                out=a1["scrB"][:, :], in_=a1["xin"][j % 2][:, :], func=AF.Square,
                accum_out=st[:, S_SQ2:S_SQ2 + 1]))
            E.on("s", lambda e: e.activation(
                out=st[:, S_LNV:S_LNV + 1], in_=st[:, S_SQ2:S_SQ2 + 1], func=AF.Ln,
                bias=cols[:, C_EPSR:C_EPSR + 1], scale=1.0 / D))
            sinv2 = E.chk("s", lambda e: e.activation(
                out=st[:, S_INV2:S_INV2 + 1], in_=st[:, S_LNV:S_LNV + 1],
                func=AF.Exp, scale=-0.5))
            E.wait("v", "s", sinv2)
            vhbf = E.chk("v", lambda e, j=j: e.scalar_tensor_tensor(
                out=a1["hbf"][:, :], in0=a1["xin"][j % 2][:, :],
                scalar=st[:, S_INV2:S_INV2 + 1], in1=a1["gat"][:, :],
                op0=ALU.mult, op1=ALU.mult))
            v_hbf[j] = vhbf
        a1_front(0)

        for i in range(nt):
            E.wait("t", "v", v_hbf[i])
            E.wait("t", "v", v_o2fm.get(i - 1, 0))
            for j in range(8):
                E.on("t", lambda e, j=j: e.transpose(
                    psT[:, j * 128:(j + 1) * 128], a1["hbf"][:, j * 128:(j + 1) * 128],
                    identb[:, :]))
            t_htr = E.tend()
            E.wait("v", "t", t_htr)
            vhfm = E.chk("v", lambda e: e.tensor_copy(out=a1["hfm"][:, :], in_=psT[:, :]))
            E.wait("t", "v", vhfm)
            E.wait("t", "v", v_on.get(i - 1, 0))
            E.wait("t", "v", v_x2.get(i - 1, 0))
            E.wait("t", "v", v_atm1.get(i - 1, 0))
            E.wait("t", "v", v_zb.get(i - 1, 0))
            for kc in range(8):
                E.on("t", lambda e, kc=kc: e.matmul(
                    psX[:, 0:16], a1["hfm"][:, kc * 128:(kc + 1) * 128],
                    a1["wgk1"][:, kc * 16:(kc + 1) * 16],
                    start=(kc == 0), stop=(kc == 7)))
            t_t1 = E.tend()
            E.wait("v", "t", t_t1)
            vt1 = E.chk("v", lambda e: e.tensor_copy(out=a1["t1bf"][:, :],
                                                     in_=psX[:, 0:16]))
            E.wait("t", "v", vt1)
            tt1 = E.chk("t", lambda e: e.transpose(
                psT[0:16, 0:128], a1["t1bf"][:, :], identb[:, :]))
            E.wait("v", "t", tt1)
            vt1f = E.chk("v", lambda e: e.tensor_copy(out=a1["t1fm"][:, :],
                                                      in_=psT[0:16, 0:128]))
            E.wait("t", "v", vt1f)
            tz = E.chk("t", lambda e: e.matmul(
                psX[:, 0:512], a1["t1fm"][:, :], a1["wgk2"][:, :],
                start=True, stop=True))
            E.wait("v", "t", tz)
            vzb = E.chk("v", lambda e: e.tensor_add(
                out=a1["zg"][:, :], in0=psX[:, 0:512], in1=a1["bgk"][:, :]))
            v_zb[i] = vzb
            E.wait("s", "v", vzb)
            E.on("s", lambda e: e.activation(out=a1["scrA"][:, 0:512],
                                             in_=a1["zg"][:, :], func=AF.Exp,
                                             scale=-1.0))
            for kc in range(8):
                lhs = a1["hfm"][:, kc * 128:(kc + 1) * 128]
                ss = (kc == 0, kc == 7)
                E.on("t", lambda e, l=lhs, kc=kc, ss=ss: e.matmul(
                    psA[:, 0:512], l, a1["wq"][:, kc * DKT:(kc + 1) * DKT],
                    start=ss[0], stop=ss[1]))
                E.on("t", lambda e, l=lhs, kc=kc, ss=ss: e.matmul(
                    psA[:, 512:1024], l, a1["wk"][:, kc * DKT:(kc + 1) * DKT],
                    start=ss[0], stop=ss[1]))
                for half in range(2):
                    E.on("t", lambda e, l=lhs, kc=kc, half=half, ss=ss: e.matmul(
                        psB[:, half * 512:(half + 1) * 512], l,
                        a1["wv"][:, kc * DVT + half * 512: kc * DVT + (half + 1) * 512],
                        start=ss[0], stop=ss[1]))
                    E.on("t", lambda e, l=lhs, kc=kc, half=half, ss=ss: e.matmul(
                        psC[:, half * 512:(half + 1) * 512], l,
                        a1["wg"][:, kc * DVT + half * 512: kc * DVT + (half + 1) * 512],
                        start=ss[0], stop=ss[1]))
            t_proj = E.tend()
            E.wait("v", "t", t_proj)
            E.on("v", lambda e: e.tensor_copy(out=a1["q"][:, :], in_=psA[:, 0:512]))
            vt1_unused = E.chk("v", lambda e: e.tensor_copy(out=a1["vbf"][:, :],
                                                            in_=psB[:, :]))
            E.wait("s", "t", t_proj)
            E.on("s", lambda e: e.copy(out=a1["k"][:, :], in_=psA[:, 512:1024]))
            skg = E.chk("s", lambda e: e.copy(out=a1["g"][:, :], in_=psC[:, :]))
            sgbf = E.chk("s", lambda e: e.activation(
                out=a1["gbf"][:, :], in_=a1["scrA"][:, 0:512], func=AF.Ln,
                bias=cols[:, C_ONE:C_ONE + 1]))

            E.wait("t", "s", sgbf)
            E.wait("t", "s", skg)
            E.wait("t", "s", sexp_last.get(i - 1, 0))
            tb0 = E.chk("t", lambda e: e.matmul(
                psC[0:64, 0:512], lmask[0:64, :], a1["gbf"][0:64, :],
                start=True, stop=True))
            tb1 = E.chk("t", lambda e: e.matmul(
                psC[64:128, 0:512], lmask[64:128, :], a1["gbf"][64:128, :],
                start=True, stop=True))
            E.wait("s", "t", tb1)
            E.on("s", lambda e: e.activation(
                out=a1["expq"][:, :], in_=psC[:, 0:512], func=AF.Exp,
                bias=cols[:, C_LNSQ:C_LNSQ + 1]))
            sexp = E.chk("s", lambda e: e.activation(
                out=a1["expm"][:, :], in_=psC[:, 0:512], func=AF.Exp, scale=-1.0))
            sexp_last[i] = sexp
            E.wait("t", "s", sexp)
            for ch in range(2):
                r0 = 64 * ch
                for h in range(NH):
                    E.on("t", lambda e, h=h, r0=r0, ch=ch: e.matmul(
                        psC[:, 1008 + ch * 4 + h:1009 + ch * 4 + h],
                        a1["gbf"][r0:r0 + 64, h * 128:(h + 1) * 128],
                        cones[r0:r0 + 64, :], start=True, stop=True))
            tbl = E.tend()
            E.wait("s", "t", tbl)
            E.wait("s", "v", supd_last.get(i - 1, 0))
            sebl = E.chk("s", lambda e: e.activation(
                out=a1["ebl"][:, 0:8], in_=psC[:, 1008:1016], func=AF.Exp))
            sebl_last[i] = sebl
            E.wait("v", "s", sexp)
            E.on("v", lambda e: e.tensor_mul(out=a1["qebf"][:, :], in0=a1["q"][:, :],
                                             in1=a1["expq"][:, :]))
            vqk = E.chk("v", lambda e: e.tensor_mul(
                out=a1["kebf"][:, :], in0=a1["k"][:, :], in1=a1["expm"][:, :]))
            E.wait("t", "v", vqk)
            E.wait("t", "v", vt1f)
            for ch in range(2):
                r0 = 64 * ch
                co = ch * 512
                if ch == 1:
                    E.wait("t", "v", vqkfm_0)
                for h in range(NH):
                    E.on("t", lambda e, h=h, r0=r0: e.transpose(
                        psT[:, h * 64:(h + 1) * 64],
                        a1["qebf"][r0:r0 + 64, h * 128:(h + 1) * 128],
                        ident2[r0:r0 + 64, :]))
                    E.on("t", lambda e, h=h, r0=r0: e.transpose(
                        psT[:, 256 + h * 64:256 + (h + 1) * 64],
                        a1["kebf"][r0:r0 + 64, h * 128:(h + 1) * 128],
                        ident2[r0:r0 + 64, :]))
                ttr = E.tend()
                E.wait("v", "t", ttr)
                vqkfm = E.chk("v", lambda e, co=co: e.tensor_copy(
                    out=a1["qkfm"][:, co:co + 512], in_=psT[:, 0:512]))
                if ch == 0:
                    vqkfm_0 = vqkfm
                E.wait("t", "v", vqkfm)
                E.wait("t", "s", sebl)
                if ch == 1:
                    E.wait("t", "v", vatm_prev)
                for h in range(NH):
                    E.on("t", lambda e, h=h, r0=r0, co=co: e.matmul(
                        psC[r0:r0 + 64, 512 + h * 64:512 + (h + 1) * 64],
                        a1["qkfm"][:, co + 256 + h * 64:co + 256 + (h + 1) * 64],
                        a1["qkfm"][:, co + h * 64:co + (h + 1) * 64],
                        start=True, stop=True))
                tat = E.tend()
                E.wait("v", "t", tat)
                vatm = E.chk("v", lambda e, r0=r0: e.tensor_mul(
                    out=a1["atm"][r0:r0 + 64, :], in0=psC[r0:r0 + 64, 512:768],
                    in1=maskt[r0:r0 + 64, :]))
                vatm_prev = vatm
                if ch == 1:
                    v_atm1[i] = vatm
                E.wait("t", "v", vatm)
                E.wait("t", "v", v_supd)
                for h in range(NH):
                    E.on("t", lambda e, h=h, r0=r0, co=co: e.matmul(
                        psA[r0:r0 + 64, h * 256:(h + 1) * 256],
                        a1["atm"][r0:r0 + 64, h * 64:(h + 1) * 64],
                        a1["vbf"][r0:r0 + 64, h * 256:(h + 1) * 256],
                        start=True, stop=False))
                    E.on("t", lambda e, h=h, r0=r0, co=co: e.matmul(
                        psA[r0:r0 + 64, h * 256:(h + 1) * 256],
                        a1["qkfm"][:, co + h * 64:co + (h + 1) * 64],
                        a1["Sb"][:, h * 256:(h + 1) * 256],
                        start=False, stop=True))
                t_o = E.tend()
                for h in range(NH):
                    E.on("t", lambda e, h=h, r0=r0: e.matmul(
                        psB[:, h * 256:(h + 1) * 256],
                        a1["kebf"][r0:r0 + 64, h * 128:(h + 1) * 128],
                        a1["vbf"][r0:r0 + 64, h * 256:(h + 1) * 256],
                        start=True, stop=True))
                t_p = E.tend()
                E.wait("v", "t", t_p)
                E.wait("v", "s", sebl)
                E.on("v", lambda e: e.tensor_add(out=a1["Sf"][:, :],
                                                 in0=a1["Sf"][:, :], in1=psB[:, :]))
                for h in range(NH):
                    E.on("v", lambda e, h=h, ch=ch: e.tensor_scalar_mul(
                        out=a1["Sf"][:, h * 256:(h + 1) * 256],
                        in0=a1["Sf"][:, h * 256:(h + 1) * 256],
                        scalar1=a1["ebl"][:, ch * 4 + h:ch * 4 + h + 1]))
                v_supd = E.chk("v", lambda e: e.tensor_copy(out=a1["Sb"][:, :],
                                                            in_=a1["Sf"][:, :]))
            supd_last[i] = v_supd
            t_o_last[i] = t_o

            E.wait("s", "t", t_o)
            for h in range(NH):
                E.on("s", lambda e, h=h: e.activation(
                    out=a1["scrB"][:, h * 256:(h + 1) * 256],
                    in_=psA[:, h * 256:(h + 1) * 256], func=AF.Square,
                    accum_out=st[:, S_G0 + h:S_G0 + h + 1]))
            E.on("s", lambda e: e.activation(
                out=st[:, S_GI:S_GI + 4], in_=st[:, S_G0:S_G0 + 4], func=AF.Ln,
                bias=cols[:, C_EPSR:C_EPSR + 1], scale=1.0 / 256.0))
            sgi = E.chk("s", lambda e: e.activation(
                out=st[:, S_GI:S_GI + 4], in_=st[:, S_GI:S_GI + 4], func=AF.Exp,
                scale=-0.5))
            E.on("s", lambda e: e.activation(out=a1["scrA"][:, :], in_=a1["g"][:, :],
                                             func=AF.Exp, scale=-1.0))
            E.on("s", lambda e: e.activation(out=a1["scrB"][:, :], in_=a1["scrA"][:, :],
                                             func=AF.Ln, bias=cols[:, C_ONE:C_ONE + 1]))
            ssig = E.chk("s", lambda e: e.activation(
                out=a1["scrA"][:, :], in_=a1["scrB"][:, :], func=AF.Exp, scale=-1.0))
            E.wait("v", "s", sgi)
            for h in range(NH):
                E.on("v", lambda e, h=h: e.scalar_tensor_tensor(
                    out=a1["on"][:, h * 256:(h + 1) * 256],
                    in0=psA[:, h * 256:(h + 1) * 256],
                    scalar=st[:, S_GI + h:S_GI + h + 1],
                    in1=a1["gag"][:, h * 256:(h + 1) * 256],
                    op0=ALU.mult, op1=ALU.mult))
            v_on[i] = E.chk("v", lambda e: e.engine_nop())
            E.wait("v", "s", ssig)
            E.on("v", lambda e: e.tensor_mul(out=a1["scrB"][:, :], in0=a1["g"][:, :],
                                             in1=a1["scrA"][:, :]))
            vo2 = E.chk("v", lambda e: e.tensor_mul(
                out=a1["o2bf"][:, :], in0=a1["on"][:, :], in1=a1["scrB"][:, :]))
            v_o2[i] = vo2
            E.wait("t", "v", vo2)
            for j in range(8):
                E.on("t", lambda e, j=j: e.transpose(
                    psT[:, j * 128:(j + 1) * 128],
                    a1["o2bf"][:, j * 128:(j + 1) * 128], identb[:, :]))
            to2t = E.tend()
            E.wait("v", "t", to2t)
            v_o2fm[i] = E.chk("v", lambda e: e.tensor_copy(out=a1["o2fm"][:, :],
                                                           in_=psT[:, :]))
            a1_front(i + 1)
            emit_wo_x2(i)

        E.barrier()
        if phase_limit == "a1":
            return

        psG, psD, psT, psX = a2["psG"], a2["psD"], a2["psT"], a2["psX"]
        for wsb, wdr, ncol, nch in ((a2["wgate"], Wgate, IA, 8),
                                    (a2["wup"], Wup, IA, 8),
                                    (a2["wdown"], Wdown, D, 22)):
            for c in range(nch):
                E.on("g", lambda e, d=wsb, s=wdr, c=c, n=ncol: e.dma_start(
                    out=d[:, c * n:(c + 1) * n], in_=s[c * 128:(c + 1) * 128, :]),
                    inc=[("dw", 16)])
        E.on("sy", lambda e: e.dma_start(out=a2["gam"][:, :], in_=gam_mlp),
             inc=[("dx", 16)])
        dw_a2w = E.cnt["dw"]
        dx_a2 = E.cnt["dx"]
        E.wait("t", "dw", dw_a2w)
        E.on("v", lambda e: e.memset(a2["ymaxr"][:, :], -1e38))
        E.on("v", lambda e: e.memset(a2["yminr"][:, :], 1e38))
        E.wait("v", "dx", dx_a2)

        ld2, v_cons, s_oast, v_nbf, v_mlpfm2, v_u = {}, {}, {}, {}, {}, {}
        t_ntr, v_mlp1 = {}, {}

        def a2_loads(j):
            if j >= nt:
                return
            E.wait("sy", "v", v_cons.get(j - 2, 0))
            E.wait("sy", "ds", s_oast.get(j - 2, 0))
            E.on("sy", lambda e, j=j: e.dma_start(
                out=a2["x2b"][j % 2][:, :],
                in_=x2h[row0(j):row0(j) + 128, :]), inc=[("dx", 16)])
            E.on("sy", lambda e, j=j: e.dma_start(
                out=a2["xb"][j % 2][:, :],
                in_=x[row0(j):row0(j) + 128, :]), inc=[("dx", 16)])
            ld2[j] = E.cnt["dx"]

        def a2_front(j):
            if j >= nt:
                return
            E.wait("s", "dx", ld2[j])
            E.wait("s", "v", v_mlp1.get(j - 1, 0))
            E.on("s", lambda e, j=j: e.activation(
                out=a2["scrB"][:, 0:D], in_=a2["x2b"][j % 2][:, :], func=AF.Square,
                accum_out=st[:, S_FSQ:S_FSQ + 1]))
            E.on("s", lambda e: e.activation(
                out=st[:, S_FLN:S_FLN + 1], in_=st[:, S_FSQ:S_FSQ + 1], func=AF.Ln,
                bias=cols[:, C_EPSR:C_EPSR + 1], scale=1.0 / D))
            sinv = E.chk("s", lambda e: e.activation(
                out=st[:, S_FINV:S_FINV + 1], in_=st[:, S_FLN:S_FLN + 1],
                func=AF.Exp, scale=-0.5))
            E.wait("v", "s", sinv)
            E.wait("v", "t", t_ntr.get(j - 1, 0))
            vnbf = E.chk("v", lambda e, j=j: e.scalar_tensor_tensor(
                out=a2["nbf"][:, :], in0=a2["x2b"][j % 2][:, :],
                scalar=st[:, S_FINV:S_FINV + 1], in1=a2["gam"][:, :],
                op0=ALU.mult, op1=ALU.mult))
            v_nbf[j] = vnbf

        a2_loads(0)
        a2_loads(1)
        a2_front(0)

        for i in range(nt):
            E.wait("t", "v", v_nbf[i])
            E.wait("t", "v", v_mlpfm2.get(i - 1, 0))
            for j in range(8):
                E.on("t", lambda e, j=j: e.transpose(
                    psT[:, j * 128:(j + 1) * 128], a2["nbf"][:, j * 128:(j + 1) * 128],
                    identb[:, :]))
            tntr = E.tend()
            t_ntr[i] = tntr
            E.wait("v", "t", tntr)
            vnfm = E.chk("v", lambda e: e.tensor_copy(out=a2["nfm"][:, :], in_=psT[:, :]))
            segs = ((0, 512), (512, 1024), (1024, 1408))

            def a2_gate_or_up(wt, coff):
                for kc in range(8):
                    for (c0, c1) in segs:
                        E.on("t", lambda e, kc=kc, c0=c0, c1=c1, coff=coff, wt=wt:
                             e.matmul(psG[:, c0:c1],
                                      a2["nfm"][:, kc * 128:(kc + 1) * 128],
                                      wt[:, kc * IA + coff + c0: kc * IA + coff + c1],
                                      start=(kc == 0), stop=(kc == 7)))
                return E.tend()

            def a2_gate_evac(tg, half):
                E.wait("v", "t", tg)
                vg = E.chk("v", lambda e, half=half: e.tensor_copy(
                    out=a2["gate"][half][:, 0:704], in_=psG[:, 0:704]))
                E.wait("s", "t", tg)
                sg = E.chk("s", lambda e, half=half: e.copy(
                    out=a2["gate"][half][:, 704:1408], in_=psG[:, 704:1408]))
                return vg, sg

            def a2_silu(vg, half, o1, o2, wv=0):
                E.wait("s", "v", vg)
                E.on("s", lambda e, half=half, o1=o1: e.activation(
                    out=o1[:, :], in_=a2["gate"][half][:, :], func=AF.Exp,
                    scale=-1.0))
                E.wait("s", "v", wv)
                E.on("s", lambda e, o1=o1, o2=o2: e.activation(
                    out=o2[:, :], in_=o1[:, :], func=AF.Ln,
                    bias=cols[:, C_ONE:C_ONE + 1]))
                return E.chk("s", lambda e, o1=o1, o2=o2: e.activation(
                    out=o2[:, :], in_=o2[:, :], func=AF.Exp, scale=-1.0))

            def a2_muls(tu, ssig, sg, half, sig):
                E.wait("v", "t", tu)
                E.wait("v", "s", sg)
                vm1 = E.chk("v", lambda e, half=half: e.tensor_mul(
                    out=a2["scrC"][:, :], in0=psG[:, 0:1408],
                    in1=a2["gate"][half][:, :]))
                E.wait("v", "s", ssig)
                vmlp = E.chk("v", lambda e, half=half, sig=sig: e.tensor_mul(
                    out=a2["mlpb"][half][:, :], in0=a2["scrC"][:, :], in1=sig[:, :]))
                return vm1, vmlp

            def a2_tr(half):
                for j in range(8):
                    E.on("t", lambda e, j=j, half=half: e.transpose(
                        psT[:, j * 128:(j + 1) * 128],
                        a2["mlpb"][half][:, j * 128:(j + 1) * 128], identb[:, :]))
                tm1 = E.tend()
                E.wait("v", "t", tm1)
                vf1 = E.chk("v", lambda e: e.tensor_copy(out=a2["mlpfm"][:, 0:1024],
                                                         in_=psT[:, :]))
                E.wait("t", "v", vf1)
                for j in range(8, 11):
                    E.on("t", lambda e, j=j, half=half: e.transpose(
                        psT[:, (j - 8) * 128:(j - 7) * 128],
                        a2["mlpb"][half][:, j * 128:(j + 1) * 128], identb[:, :]))
                tm2 = E.tend()
                E.wait("v", "t", tm2)
                return E.chk("v", lambda e: e.tensor_copy(
                    out=a2["mlpfm"][:, 1024:1408], in_=psT[:, 0:384]))

            def a2_down(half):
                for j in range(11):
                    kci = half * 11 + j
                    for hh2 in range(2):
                        E.on("t", lambda e, j=j, kci=kci, hh2=hh2: e.matmul(
                            psD[:, hh2 * 512:(hh2 + 1) * 512],
                            a2["mlpfm"][:, j * 128:(j + 1) * 128],
                            a2["wdown"][:, kci * D + hh2 * 512:
                                        kci * D + (hh2 + 1) * 512],
                            start=(kci == 0), stop=(kci == 21)))
                return E.tend()

            E.wait("t", "v", vnfm)
            tg0 = a2_gate_or_up(a2["wgate"], 0)
            vg0, sg0 = a2_gate_evac(tg0, 0)
            ssig0 = a2_silu(vg0, 0, a2["scrB"], a2["scrA"])
            E.wait("t", "v", vg0)
            E.wait("t", "s", sg0)
            tg1 = a2_gate_or_up(a2["wgate"], 1408)
            vg1, sg1 = a2_gate_evac(tg1, 1)
            ssig1 = None
            E.wait("t", "v", vg1)
            E.wait("t", "s", sg1)
            tu0 = a2_gate_or_up(a2["wup"], 0)
            vm1_0, vmlp0 = a2_muls(tu0, ssig0, sg0, 0, a2["scrA"])
            ssig1 = a2_silu(vg1, 1, a2["scrB"], a2["scrA"], wv=vmlp0)
            E.wait("t", "v", vm1_0)
            tu1 = a2_gate_or_up(a2["wup"], 1408)
            vm1_1, vmlp1 = a2_muls(tu1, ssig1, sg1, 1, a2["scrA"])
            v_mlp1[i] = vmlp1
            E.wait("t", "v", vmlp0)
            vf2_0 = a2_tr(0)
            E.wait("t", "v", vf2_0)
            E.wait("t", "v", v_u.get(i - 1, 0))
            a2_down(0)
            E.wait("t", "v", vmlp1)
            vf2_1 = a2_tr(1)
            v_mlpfm2[i] = vf2_1
            E.wait("t", "v", vf2_1)
            tdown = a2_down(1)

            a2_front(i + 1)

            E.wait("v", "t", tdown)
            vu = E.chk("v", lambda e, i=i: e.tensor_add(
                out=a2["u"][:, :], in0=psD[:, :], in1=a2["x2b"][i % 2][:, :]))
            v_u[i] = vu
            E.on("v", lambda e: e.reduce_sum(out=st[:, S_SUM:S_SUM + 1],
                                             in_=a2["u"][:, :], axis=AX.X))
            E.on("v", lambda e: e.tensor_scalar_mul(
                out=st[:, S_M:S_M + 1], in0=st[:, S_SUM:S_SUM + 1], scalar1=1.0 / D))
            E.on("v", lambda e: e.tensor_mul(out=st[:, S_MSQ:S_MSQ + 1],
                                             in0=st[:, S_M:S_M + 1],
                                             in1=st[:, S_M:S_M + 1]))
            vb = E.chk("v", lambda e: e.tensor_scalar(
                out=st[:, S_BIAS:S_BIAS + 1], in0=st[:, S_MSQ:S_MSQ + 1],
                scalar1=-1.0, scalar2=EPS_LN, op0=ALU.mult, op1=ALU.add))
            E.wait("s", "v", vu)
            E.on("s", lambda e: e.activation(out=a2["scrB"][:, 0:D], in_=a2["u"][:, :],
                                             func=AF.Square,
                                             accum_out=st[:, S_SQ:S_SQ + 1]))
            E.wait("s", "v", vb)
            E.on("s", lambda e: e.activation(
                out=st[:, S_LNV:S_LNV + 1], in_=st[:, S_SQ:S_SQ + 1], func=AF.Ln,
                bias=st[:, S_BIAS:S_BIAS + 1], scale=1.0 / D))
            sinvu = E.chk("s", lambda e: e.activation(
                out=st[:, S_INV1:S_INV1 + 1], in_=st[:, S_LNV:S_LNV + 1],
                func=AF.Exp, scale=-0.5))
            E.wait("v", "s", sinvu)
            E.on("v", lambda e: e.tensor_scalar(
                out=a2["scrA"][:, 0:D], in0=a2["u"][:, :], scalar1=st[:, S_M:S_M + 1],
                scalar2=st[:, S_INV1:S_INV1 + 1], op0=ALU.subtract, op1=ALU.mult))
            E.wait("v", "ds", s_oast.get(i - 2, 0))
            voa = E.chk("v", lambda e, i=i: e.tensor_add(
                out=a2["oab"][i % 2][:, :], in0=a2["scrA"][:, 0:D],
                in1=a2["xb"][i % 2][:, :]))
            E.wait("s", "v", voa)
            E.on("s", lambda e, i=i: e.dma_start(
                out=oAh[row0(i):row0(i) + 128, :], in_=a2["oab"][i % 2][:, :]),
                inc=[("ds", 16)])
            if phase_limit == "a2":
                E.on("s", lambda e, i=i: e.dma_start(
                    out=out[row0(i):row0(i) + 128, :], in_=a2["oab"][i % 2][:, :]),
                    inc=[("ds", 16)])
            s_oast[i] = E.cnt["ds"]
            E.on("v", lambda e, i=i: e.reduce_sum(
                out=st[:, S_SUM2:S_SUM2 + 1], in_=a2["oab"][i % 2][:, :], axis=AX.X))
            E.on("v", lambda e, i=i: e.tensor_scalar_mul(
                out=ymt[:, i:i + 1], in0=st[:, S_SUM2:S_SUM2 + 1], scalar1=1.0 / D))
            E.on("v", lambda e, i=i: e.tensor_mul(out=st[:, S_MSQ2:S_MSQ2 + 1],
                                                  in0=ymt[:, i:i + 1],
                                                  in1=ymt[:, i:i + 1]))
            vb2 = E.chk("v", lambda e: e.tensor_scalar(
                out=st[:, S_BIAS2:S_BIAS2 + 1], in0=st[:, S_MSQ2:S_MSQ2 + 1],
                scalar1=-1.0, scalar2=EPS_LN, op0=ALU.mult, op1=ALU.add))
            E.on("s", lambda e, i=i: e.activation(
                out=a2["scrB"][:, 0:D], in_=a2["oab"][i % 2][:, :], func=AF.Square,
                accum_out=st[:, S_SQ:S_SQ + 1]))
            E.wait("s", "v", vb2)
            E.on("s", lambda e: e.activation(
                out=st[:, S_LNV:S_LNV + 1], in_=st[:, S_SQ:S_SQ + 1], func=AF.Ln,
                bias=st[:, S_BIAS2:S_BIAS2 + 1], scale=1.0 / D))
            sy2 = E.chk("s", lambda e, i=i: e.activation(
                out=yinvt[:, i:i + 1], in_=st[:, S_LNV:S_LNV + 1], func=AF.Exp,
                scale=-0.5))
            E.wait("v", "s", sy2)
            E.on("v", lambda e, i=i: e.tensor_scalar(
                out=a2["scrA"][:, 0:D], in0=a2["oab"][i % 2][:, :],
                scalar1=ymt[:, i:i + 1], scalar2=yinvt[:, i:i + 1],
                op0=ALU.subtract, op1=ALU.mult))
            E.on("v", lambda e: e.reduce_max(out=st[:, S_RED:S_RED + 1],
                                             in_=a2["scrA"][:, 0:D], axis=AX.X))
            E.on("v", lambda e: e.tensor_max(out=a2["ymaxr"][:, :],
                                             in0=a2["ymaxr"][:, :],
                                             in1=st[:, S_RED:S_RED + 1]))
            E.on("v", lambda e: e.tensor_reduce(out=st[:, S_RED:S_RED + 1],
                                                in_=a2["scrA"][:, 0:D], axis=AX.X,
                                                op=ALU.min))
            vmm = E.chk("v", lambda e: e.tensor_tensor(
                out=a2["yminr"][:, :], in0=a2["yminr"][:, :],
                in1=st[:, S_RED:S_RED + 1], op=ALU.min))
            v_cons[i] = vmm
            a2_loads(i + 2)

        E.wait("t", "v", E.cnt["v"])
        tmx = E.chk("t", lambda e: e.transpose(psX[0:1, 0:128], a2["ymaxr"][:, :],
                                               identf[:, :]))
        E.wait("v", "t", tmx)
        vdone = E.chk("v", lambda e: e.reduce_max(out=ccsb[0:1, 0:1],
                                                  in_=psX[0:1, 0:128], axis=AX.X))
        E.wait("t", "v", vdone)
        tmn = E.chk("t", lambda e: e.transpose(psX[0:1, 0:128], a2["yminr"][:, :],
                                               identf[:, :]))
        E.wait("v", "t", tmn)
        E.on("v", lambda e: e.tensor_reduce(out=ccsb[0:1, 1:2], in_=psX[0:1, 0:128],
                                            axis=AX.X, op=ALU.min))
        E.on("v", lambda e: e.tensor_scalar_mul(out=ccsb[0:1, 1:2],
                                                in0=ccsb[0:1, 1:2], scalar1=-1.0))
        E.on("v", lambda e: e.memset(ccsb[0:1, 2:8], 0.0))

        E.barrier()
        if phase_limit == "a2":
            return

        def allreduce_and_scales(ps_x, r_sw, bsrow, brow_t, nb):
            vsnap = E.cnt["v"]
            E.wait("g", "v", vsnap)
            E.on("g", lambda e: e.dma_start(out=ccin[:, :], in_=ccsb[:, :]),
                 inc=[("dw", 16)])
            E.wait("g", "dw", E.cnt["dw"])
            E.on("g", lambda e: e.collective_compute(
                "AllReduce", ALU.max, replica_groups=[list(range(NCORES))],
                ins=[ccin.ap().opt()], outs=[ccout.ap().opt()]), inc=[("cc", 1)])
            E.wait("g", "cc", E.cnt["cc"])
            E.on("g", lambda e: e.dma_start(out=ccsb[:, :], in_=ccout[:, :]),
                 inc=[("dw", 16)])
            E.wait("v", "dw", E.cnt["dw"])
            E.on("v", lambda e: e.tensor_add(out=rows[0:1, R_D:R_D + 1],
                                             in0=ccsb[0:1, 0:1], in1=ccsb[0:1, 1:2]))
            E.on("v", lambda e: e.tensor_scalar(
                out=rows[0:1, R_SX:R_SX + 1], in0=rows[0:1, R_D:R_D + 1],
                scalar1=1.0 / 255.0, scalar2=1e-8, op0=ALU.mult, op1=ALU.max))
            E.on("v", lambda e: e.reciprocal(out=rows[0:1, R_ISX:R_ISX + 1],
                                             in_=rows[0:1, R_SX:R_SX + 1]))
            E.on("v", lambda e: e.tensor_mul(out=rows[0:1, R_ZT:R_ZT + 1],
                                             in0=ccsb[0:1, 1:2],
                                             in1=rows[0:1, R_ISX:R_ISX + 1]))
            E.on("v", lambda e: e.tensor_copy(out=rowi[0:1, 0:1],
                                              in_=rows[0:1, R_ZT:R_ZT + 1]))
            E.on("v", lambda e: e.tensor_copy(out=rows[0:1, R_ZT:R_ZT + 1],
                                              in_=rowi[0:1, 0:1]))
            E.on("v", lambda e: e.tensor_scalar_add(
                out=rows[0:1, R_ZP:R_ZP + 1], in0=rows[0:1, R_ZT:R_ZT + 1],
                scalar1=-128.0))
            E.on("v", lambda e: e.tensor_scalar(
                out=rows[0:1, R_ZP:R_ZP + 1], in0=rows[0:1, R_ZP:R_ZP + 1],
                scalar1=-128.0, scalar2=127.0, op0=ALU.max, op1=ALU.min))
            E.on("v", lambda e: e.tensor_scalar(
                out=rows[0:1, R_LO:R_LO + 1], in0=rows[0:1, R_ZP:R_ZP + 1],
                scalar1=-1.0, scalar2=-128.0, op0=ALU.mult, op1=ALU.add))
            E.on("v", lambda e: e.tensor_scalar(
                out=rows[0:1, R_HI:R_HI + 1], in0=rows[0:1, R_ZP:R_ZP + 1],
                scalar1=-1.0, scalar2=127.0, op0=ALU.mult, op1=ALU.add))
            E.on("v", lambda e: e.tensor_mul(out=rows[0:1, R_SXSW:R_SXSW + 1],
                                             in0=rows[0:1, R_SX:R_SX + 1],
                                             in1=rows[0:1, r_sw:r_sw + 1]))
            E.on("v", lambda e: e.reciprocal(out=rows[0:1, R_ISXSW:R_ISXSW + 1],
                                             in_=rows[0:1, R_SXSW:R_SXSW + 1]))
            E.on("v", lambda e, nb=nb: e.tensor_scalar_mul(
                out=bsrow[0:1, 0:nb], in0=brow_t,
                scalar1=rows[0:1, R_ISXSW:R_ISXSW + 1]))
            E.on("v", lambda e: e.tensor_copy(out=rows[0:1, 16:17],
                                              in_=rows[0:1, R_ISX:R_ISX + 1]))
            E.on("v", lambda e: e.tensor_copy(out=rows[0:1, 17:18],
                                              in_=rows[0:1, R_LO:R_LO + 1]))
            E.on("v", lambda e: e.tensor_copy(out=rows[0:1, 18:19],
                                              in_=rows[0:1, R_HI:R_HI + 1]))
            vpk = E.chk("v", lambda e: e.tensor_copy(
                out=rows[0:1, 19:20], in_=rows[0:1, R_SXSW:R_SXSW + 1]))
            E.wait("t", "v", vpk)
            tb = E.chk("t", lambda e, ps_x=ps_x: e.matmul(
                ps_x[:, 0:4], onesrf[0:1, :], rows[0:1, 16:20], start=True, stop=True))
            E.wait("v", "t", tb)
            return E.chk("v", lambda e, ps_x=ps_x: e.tensor_copy(
                out=cols[:, 0:4], in_=ps_x[:, 0:4]))

        E.on("sy", lambda e: e.dma_start(out=b1["hbuf"][0][0:1, :], in_=b1row),
             inc=[("dx", 16)])
        dx_b1r = E.cnt["dx"]
        E.wait("v", "dx", dx_b1r)
        allreduce_and_scales(b1["psX"], R_SW1, b1["b1s"], b1["hbuf"][0][0:1, 0:IF], IF)

        for c in range(8):
            E.on("sy", lambda e, c=c: e.dma_start(
                out=b1["wq1T"][:, c * IF:(c + 1) * IF],
                in_=wq1h[:, c * 128:(c + 1) * 128], transpose=True), inc=[("dx", 16)])
        dx_w1t = E.cnt["dx"]

        psF, psT, psX = b1["psF"], b1["psT"], b1["psX"]
        E.on("v", lambda e: e.memset(b1["hmaxr"][:, :], -1e38))
        E.on("v", lambda e: e.memset(b1["hminr"][:, :], 1e38))
        E.wait("t", "dx", dx_w1t)
        ldb, v_yq, st_h, v_xqfm, t_tr1 = {}, {}, {}, {}, {}

        def b1_loads(j):
            if j >= nt:
                return
            E.wait("sy", "v", v_yq.get(j - 2, 0))
            E.on("sy", lambda e, j=j: e.dma_start(
                out=b1["oab"][j % 2][:, :],
                in_=oAh[row0(j):row0(j) + 128, :]), inc=[("dx", 16)])
            ldb[j] = E.cnt["dx"]

        def b1_quant(j):
            if j >= nt:
                return
            E.wait("v", "dx", ldb[j])
            E.wait("v", "t", t_tr1.get(j - 1, 0))
            E.on("v", lambda e, j=j: e.tensor_scalar(
                out=b1["y"][:, :], in0=b1["oab"][j % 2][:, :],
                scalar1=ymt[:, j:j + 1], scalar2=yinvt[:, j:j + 1],
                op0=ALU.subtract, op1=ALU.mult))
            E.on("v", lambda e: e.tensor_scalar_mul(
                out=b1["scr"][:, :], in0=b1["y"][:, :],
                scalar1=cols[:, C_ISX:C_ISX + 1]))
            E.on("v", lambda e: e.tensor_copy(out=b1["qi"][:, :], in_=b1["scr"][:, :]))
            v_yq[j] = E.chk("v", lambda e: e.tensor_scalar(
                out=b1["xqbf"][:, :], in0=b1["qi"][:, :],
                scalar1=cols[:, C_LO:C_LO + 1], scalar2=cols[:, C_HI:C_HI + 1],
                op0=ALU.max, op1=ALU.min))

        b1_loads(0)
        b1_loads(1)
        b1_quant(0)
        for i in range(nt):
            E.wait("t", "v", v_yq[i])
            E.wait("t", "v", v_xqfm.get(i - 1, 0))
            for j in range(8):
                E.on("t", lambda e, j=j: e.transpose(
                    psT[:, j * 128:(j + 1) * 128], b1["xqbf"][:, j * 128:(j + 1) * 128],
                    identb[:, :]))
            ttr = E.tend()
            t_tr1[i] = ttr
            E.wait("v", "t", ttr)
            vfm = E.chk("v", lambda e: e.tensor_copy(out=b1["xqfm"][:, :], in_=psT[:, :]))
            v_xqfm[i] = vfm
            b1_loads(i + 2)
            b1_quant(i + 1)
            E.wait("t", "v", vfm)
            E.wait("s", "dw", st_h.get(i - 2, 0))
            sg = 0
            for ocp in range(2):
                E.wait("t", "s", sg)
                for kc in range(8):
                    for oc2 in range(2):
                        oc = ocp * 2 + oc2
                        for hh2 in range(2):
                            E.on("t", lambda e, oc=oc, oc2=oc2, kc=kc, hh2=hh2:
                                 e.matmul(
                                 psF[:, oc2 * 1024 + hh2 * 512:
                                     oc2 * 1024 + (hh2 + 1) * 512],
                                 b1["xqfm"][:, kc * 128:(kc + 1) * 128],
                                 b1["wq1T"][:, kc * IF + oc * 1024 + hh2 * 512:
                                            kc * IF + oc * 1024 + (hh2 + 1) * 512],
                                 start=(kc == 0), stop=False))
                for oc2 in range(2):
                    oc = ocp * 2 + oc2
                    for hh2 in range(2):
                        E.on("t", lambda e, oc=oc, oc2=oc2, hh2=hh2: e.matmul(
                            psF[:, oc2 * 1024 + hh2 * 512:
                                oc2 * 1024 + (hh2 + 1) * 512],
                            onesrb[0:1, :],
                            b1["b1s"][0:1, oc * 1024 + hh2 * 512:
                                      oc * 1024 + (hh2 + 1) * 512],
                            start=False, stop=True))
                tmm = E.tend()
                E.wait("s", "t", tmm)
                for oc2 in range(2):
                    oc = ocp * 2 + oc2
                    sg = E.chk("s", lambda e, oc=oc, oc2=oc2, i=i: e.activation(
                        out=b1["hbuf"][i % 2][:, oc * 1024:(oc + 1) * 1024],
                        in_=psF[:, oc2 * 1024:(oc2 + 1) * 1024],
                        func=AF.Gelu, scale=cols[:, C_SXSW:C_SXSW + 1]))
                E.wait("v", "s", sg)
                for oc2 in range(2):
                    oc = ocp * 2 + oc2
                    E.on("v", lambda e, oc=oc, i=i: e.reduce_max(
                        out=st[:, S_RED:S_RED + 1],
                        in_=b1["hbuf"][i % 2][:, oc * 1024:(oc + 1) * 1024], axis=AX.X))
                    E.on("v", lambda e: e.tensor_max(out=b1["hmaxr"][:, :],
                                                     in0=b1["hmaxr"][:, :],
                                                     in1=st[:, S_RED:S_RED + 1]))
                    E.on("v", lambda e, oc=oc, i=i: e.tensor_reduce(
                        out=st[:, S_RED:S_RED + 1],
                        in_=b1["hbuf"][i % 2][:, oc * 1024:(oc + 1) * 1024], axis=AX.X,
                        op=ALU.min))
                    E.on("v", lambda e: e.tensor_tensor(
                        out=b1["hminr"][:, :], in0=b1["hminr"][:, :],
                        in1=st[:, S_RED:S_RED + 1], op=ALU.min))
            vred = E.chk("v", lambda e: e.engine_nop())
            E.wait("g", "v", vred)
            E.on("g", lambda e, i=i: e.dma_start(
                out=hh[row0(i):row0(i) + 128, :], in_=b1["hbuf"][i % 2][:, :]),
                inc=[("dw", 16)])
            if phase_limit == "b1":
                E.on("g", lambda e, i=i: e.dma_start(
                    out=out[row0(i):row0(i) + 128, :],
                    in_=b1["hbuf"][i % 2][:, (i % 4) * 1024:((i % 4) + 1) * 1024]),
                    inc=[("dw", 16)])
            st_h[i] = E.cnt["dw"]

        E.wait("t", "v", E.cnt["v"])
        tmx = E.chk("t", lambda e: e.transpose(psX[0:1, 0:128], b1["hmaxr"][:, :],
                                               identf[:, :]))
        E.wait("v", "t", tmx)
        vdone = E.chk("v", lambda e: e.reduce_max(out=ccsb[0:1, 0:1],
                                                  in_=psX[0:1, 0:128], axis=AX.X))
        E.wait("t", "v", vdone)
        tmn = E.chk("t", lambda e: e.transpose(psX[0:1, 0:128], b1["hminr"][:, :],
                                               identf[:, :]))
        E.wait("v", "t", tmn)
        E.on("v", lambda e: e.tensor_reduce(out=ccsb[0:1, 1:2], in_=psX[0:1, 0:128],
                                            axis=AX.X, op=ALU.min))
        E.on("v", lambda e: e.tensor_scalar_mul(out=ccsb[0:1, 1:2],
                                                in0=ccsb[0:1, 1:2], scalar1=-1.0))
        E.on("v", lambda e: e.memset(ccsb[0:1, 2:8], 0.0))

        E.barrier()
        if phase_limit == "b1":
            return

        E.on("sy", lambda e: e.dma_start(out=b2["hb"][0][0:1, 0:D], in_=b2row),
             inc=[("dx", 16)])
        dx_b2r = E.cnt["dx"]
        E.wait("v", "dx", dx_b2r)
        vsc2 = allreduce_and_scales(b2["psX"], R_SW2, b2["b2s"], b2["hb"][0][0:1, 0:D], D)
        for c in range(32):
            E.on("sy", lambda e, c=c: e.dma_start(
                out=b2["wq2T"][:, c * D:(c + 1) * D],
                in_=wq2h[:, c * 128:(c + 1) * 128], transpose=True), inc=[("dx", 16)])
        dx_w2t = E.cnt["dx"]
        psD, psX = b2["psD"], b2["psX"]
        psTs = [b2["psT"], b2["psT2"]]
        E.wait("t", "dx", dx_w2t)
        ldh, v_hq, st_o, v_ffn, v_out, t_trq = {}, {}, {}, {}, {}, {}
        E.wait("sy", "v", vsc2)

        def b2_loads(j):
            if j >= nt:
                return
            E.wait("sy", "v", v_out.get(j - 2, 0))
            E.on("sy", lambda e, j=j: e.dma_start(
                out=b2["hb"][j % 2][:, :],
                in_=hh[row0(j):row0(j) + 128, :]), inc=[("dx", 16)])
            E.on("sy", lambda e, j=j: e.dma_start(
                out=b2["oab"][j % 2][:, :],
                in_=oAh[row0(j):row0(j) + 128, :]), inc=[("dx", 16)])
            ldh[j] = E.cnt["dx"]

        def b2_quant(j):
            if j >= nt:
                return
            E.wait("v", "dx", ldh[j])
            E.wait("v", "t", t_trq.get(j - 1, 0))
            E.on("v", lambda e, j=j: e.tensor_scalar_mul(
                out=b2["scr"][:, :], in0=b2["hb"][j % 2][:, :],
                scalar1=cols[:, C_ISX:C_ISX + 1]))
            E.on("v", lambda e: e.tensor_copy(out=b2["qi"][:, :], in_=b2["scr"][:, :]))
            v_hq[j] = E.chk("v", lambda e: e.tensor_scalar(
                out=b2["hqbf"][:, :], in0=b2["qi"][:, :],
                scalar1=cols[:, C_LO:C_LO + 1], scalar2=cols[:, C_HI:C_HI + 1],
                op0=ALU.max, op1=ALU.min))

        b2_loads(0)
        b2_loads(1)
        b2_quant(0)
        for i in range(nt):
            E.wait("t", "v", v_hq[i])
            E.wait("t", "v", v_ffn.get(i - 1, 0))
            vfm_prev = {}
            ttr = 0

            def b2_tr(grp):
                for j in range(8):
                    E.on("t", lambda e, grp=grp, j=j: e.transpose(
                        psTs[grp % 2][:, j * 128:(j + 1) * 128],
                        b2["hqbf"][:, grp * 1024 + j * 128: grp * 1024 + (j + 1) * 128],
                        identb[:, :]))
                return E.tend()

            def b2_fm(grp, ttrg):
                E.wait("v", "t", ttrg)
                return E.chk("v", lambda e, grp=grp: e.tensor_copy(
                    out=b2["hqfm"][grp % 2][:, :], in_=psTs[grp % 2][:, :]))

            ttrs = {0: b2_tr(0)}
            vfms = {0: b2_fm(0, ttrs[0])}
            for grp in range(4):
                if grp + 1 < 4:
                    E.wait("t", "v", vfms.get(grp - 1, 0))
                    ttrs[grp + 1] = b2_tr(grp + 1)
                    vfms[grp + 1] = b2_fm(grp + 1, ttrs[grp + 1])
                E.wait("t", "v", vfms[grp])
                for j in range(8):
                    kc = grp * 8 + j
                    for hh2 in range(2):
                        E.on("t", lambda e, grp=grp, j=j, kc=kc, hh2=hh2: e.matmul(
                            psD[:, hh2 * 512:(hh2 + 1) * 512],
                            b2["hqfm"][grp % 2][:, j * 128:(j + 1) * 128],
                            b2["wq2T"][:, kc * D + hh2 * 512: kc * D + (hh2 + 1) * 512],
                            start=(kc == 0), stop=False))
            ttr = ttrs[3]
            t_trq[i] = ttr
            for hh2 in range(2):
                E.on("t", lambda e, hh2=hh2: e.matmul(
                    psD[:, hh2 * 512:(hh2 + 1) * 512], onesrb[0:1, :],
                    b2["b2s"][0:1, hh2 * 512:(hh2 + 1) * 512], start=False, stop=True))
            tmm = E.tend()
            b2_quant(i + 1)
            E.wait("s", "t", tmm)
            sffn = E.chk("s", lambda e: e.activation(
                out=b2["fbuf"][:, :], in_=psD[:, :], func=AF.Copy,
                scale=cols[:, C_SXSW:C_SXSW + 1]))
            E.wait("v", "s", sffn)
            v_ffn[i] = E.chk("v", lambda e: e.engine_nop())
            E.on("v", lambda e: e.reduce_sum(out=st[:, S_SUM:S_SUM + 1],
                                             in_=b2["fbuf"][:, :], axis=AX.X))
            E.on("v", lambda e: e.tensor_scalar_mul(
                out=st[:, S_M:S_M + 1], in0=st[:, S_SUM:S_SUM + 1], scalar1=1.0 / D))
            E.on("v", lambda e: e.tensor_mul(out=st[:, S_MSQ:S_MSQ + 1],
                                             in0=st[:, S_M:S_M + 1],
                                             in1=st[:, S_M:S_M + 1]))
            vb = E.chk("v", lambda e: e.tensor_scalar(
                out=st[:, S_BIAS:S_BIAS + 1], in0=st[:, S_MSQ:S_MSQ + 1],
                scalar1=-1.0, scalar2=EPS_LN, op0=ALU.mult, op1=ALU.add))
            E.wait("s", "v", v_hq.get(i + 1, 0))
            E.on("s", lambda e: e.activation(out=b2["scr"][:, 0:D], in_=b2["fbuf"][:, :],
                                             func=AF.Square,
                                             accum_out=st[:, S_SQ:S_SQ + 1]))
            E.wait("s", "v", vb)
            E.on("s", lambda e: e.activation(
                out=st[:, S_LNV:S_LNV + 1], in_=st[:, S_SQ:S_SQ + 1], func=AF.Ln,
                bias=st[:, S_BIAS:S_BIAS + 1], scale=1.0 / D))
            sinv = E.chk("s", lambda e: e.activation(
                out=st[:, S_INV1:S_INV1 + 1], in_=st[:, S_LNV:S_LNV + 1],
                func=AF.Exp, scale=-0.5))
            E.wait("v", "s", sinv)
            E.on("v", lambda e: e.tensor_scalar(
                out=b2["scr"][:, 0:D], in0=b2["fbuf"][:, :], scalar1=st[:, S_M:S_M + 1],
                scalar2=st[:, S_INV1:S_INV1 + 1], op0=ALU.subtract, op1=ALU.mult))
            E.wait("v", "ds", st_o.get(i - 2, 0))
            vo = E.chk("v", lambda e, i=i: e.tensor_add(
                out=b2["ob"][i % 2][:, :], in0=b2["scr"][:, 0:D],
                in1=b2["oab"][i % 2][:, :]))
            v_out[i] = vo
            E.wait("s", "v", vo)
            E.on("s", lambda e, i=i: e.dma_start(
                out=out[row0(i):row0(i) + 128, :], in_=b2["ob"][i % 2][:, :]),
                inc=[("ds", 16)])
            st_o[i] = E.cnt["ds"]
            b2_loads(i + 2)

        E.barrier()

    with nc.Block() as block:
        for cur in ENGS:
            def body(eng, cur=cur):
                sched(Env(cur, eng, sems))
            getattr(block, _BLOCK_NAME[cur])(body)

    _free(opened_global)
    return nc


def _consts():
    c = {}
    c["identb"] = np.eye(128, dtype=BF)
    id2 = np.zeros((128, 64), dtype=np.float32)
    for p in range(128):
        id2[p, p % 64] = 1.0
    c["ident2b"] = id2.astype(BF)
    c["identf"] = np.eye(128, dtype=np.float32)
    sc = np.arange(64)
    pc = np.arange(128) % 64
    lm = np.where(pc[:, None] <= sc[None, :], np.float32(-1.0 / 16.0), np.float32(0.0))
    c["lmask"] = lm.astype(BF)
    mt = np.where(pc[:, None] <= sc[None, :], np.float32(1.0), np.float32(0.0))
    c["maskt"] = np.tile(mt, (1, 4)).astype(np.float32)
    c["cones"] = np.full((128, 1), -1.0 / 16.0, dtype=BF)
    c["onesrf"] = np.ones((1, 128), dtype=np.float32)
    c["onesrb"] = np.ones((1, 128), dtype=BF)
    return c


def _shard(inputs):
    f32 = np.float32
    hs = np.ascontiguousarray(np.asarray(inputs["L0_attn_hidden_states"], dtype=f32))
    base = {
        "Wq": np.ascontiguousarray(inputs["L0_attn_Wq"], dtype=f32),
        "Wk": np.ascontiguousarray(inputs["L0_attn_Wk"], dtype=f32),
        "Wv": np.ascontiguousarray(inputs["L0_attn_Wv"], dtype=f32),
        "Wg": np.ascontiguousarray(inputs["L0_attn_Wg"], dtype=f32),
        "Wgk1": np.ascontiguousarray(inputs["L0_attn_Wgk1"], dtype=f32),
        "Wgk2": np.ascontiguousarray(inputs["L0_attn_Wgk2"], dtype=f32),
        "Wo": np.ascontiguousarray(inputs["L0_attn_Wo"], dtype=f32),
        "Wgate": np.ascontiguousarray(inputs["L0_attn_W_gate"], dtype=f32),
        "Wup": np.ascontiguousarray(inputs["L0_attn_W_up"], dtype=f32),
        "Wdown": np.ascontiguousarray(inputs["L0_attn_W_down"], dtype=f32),
        "W1": np.ascontiguousarray(inputs["L0_ffn_W1"], dtype=f32),
        "W2": np.ascontiguousarray(inputs["L0_ffn_W2"], dtype=f32),
        "gam_attn": np.ascontiguousarray(np.broadcast_to(
            np.asarray(inputs["L0_attn_attn_norm_w"], dtype=f32)[None, :], (128, D))),
        "bgk2b": np.ascontiguousarray(np.broadcast_to(
            np.asarray(inputs["L0_attn_bgk2"], dtype=f32)[None, :], (128, DKT))),
        "gamg": np.ascontiguousarray(np.broadcast_to(
            np.tile(np.asarray(inputs["L0_attn_gnorm_w"], dtype=f32), 4)[None, :],
            (128, DVT))),
        "gam_mlp": np.ascontiguousarray(np.broadcast_to(
            np.asarray(inputs["L0_attn_mlp_norm_w"], dtype=f32)[None, :], (128, D))),
        "b1row": np.ascontiguousarray(
            np.asarray(inputs["L0_ffn_b1"], dtype=f32)[None, :]),
        "b2row": np.ascontiguousarray(
            np.asarray(inputs["L0_ffn_b2"], dtype=f32)[None, :]),
        **_consts(),
    }
    in_maps = []
    for c in range(NCORES):
        m = dict(base)
        m["x"] = np.ascontiguousarray(hs[c * BC:(c + 1) * BC].reshape(BC * T, D))
        in_maps.append(m)
    return in_maps


def kernel(**inputs):
    in_maps = _shard(inputs)
    nc = _build("full")
    res = run_bass_kernel_spmd(nc, in_maps, core_ids=list(range(NCORES)))
    outs = [np.asarray(r["out"]).reshape(BC, T, D) for r in res.results]
    return np.concatenate(outs, axis=0)


# version 10
# speedup vs baseline: 1.0789x; 1.0238x over previous
import numpy as np
import ml_dtypes
import concourse.bass as bass
import concourse.mybir as mybir
from concourse.bass_utils import run_bass_kernel_spmd

F32 = mybir.dt.float32
BF16 = mybir.dt.bfloat16
I32 = mybir.dt.int32
AF = mybir.ActivationFunctionType
ALU = mybir.AluOpType
AX = mybir.AxisListType

NCORES = 8
BC = 4
T = 2048
D = 1024
NT = BC * T // 128
TPS = T // 128
NH = 4
DKT, DVT = 512, 1024
IA = 2816
IF = 4096
EPS_RMS = 1e-6
EPS_LN = 1e-5
LN_SCALE_Q = float(np.log(np.float32(128.0) ** np.float32(-0.5)))
BF = ml_dtypes.bfloat16

ENGS = ("sy", "t", "v", "s", "g")
_BLOCK_NAME = {"sy": "sync", "t": "tensor", "v": "vector", "s": "scalar", "g": "gpsimd"}
QUEUES = ("dx", "ds", "dw")
SEMS = ("sy", "t", "v", "s", "g", "cc") + tuple(
    q + p for q in QUEUES for p in ("0", "1"))


class Env:

    def __init__(self, cur, eng, sems):
        self.cur = cur
        self.eng = eng
        self.sems = sems
        self.cnt = {k: 0 for k in SEMS}
        for q in QUEUES:
            self.cnt[q] = 0
        self.qpar = {q: 0 for q in QUEUES}
        self.prev_op = {}
        self.last_instr = {}
        self.qat = {q: {0: {}} for q in QUEUES}

    def on(self, ename, fn, inc=None, is_drain=False):
        if ename in ("v", "s") and not is_drain and self.prev_op.get(ename):
            self.prev_op[ename] = False
            self.on(ename, lambda e: e.drain(), is_drain=True)
        real = []
        if inc is not None:
            for s_, a_ in inc:
                if s_ in QUEUES:
                    p = self.qpar[s_]
                    self.qpar[s_] ^= 1
                    sem = s_ + str(p)
                    self.wait(ename, sem, self.cnt[sem])
                    real.append((sem, a_))
                else:
                    real.append((s_, a_))
        if ename == self.cur:
            instr = fn(self.eng)
            for s_, a_ in real:
                instr.then_inc(self.sems[s_], a_)
            if not is_drain:
                self.last_instr[ename] = instr
        for s_, a_ in real:
            self.cnt[s_] += a_
        if inc is not None:
            for s_, a_ in inc:
                if s_ in QUEUES:
                    self.cnt[s_] += a_
                    self.qat[s_][self.cnt[s_]] = {
                        s_ + "0": self.cnt[s_ + "0"], s_ + "1": self.cnt[s_ + "1"]}
        if not is_drain:
            self.prev_op[ename] = True

    def wait(self, ename, sem, val):
        if val <= 0:
            return
        if sem in QUEUES:
            snap = self.qat[sem].get(val)
            assert snap is not None, f"wait on unrecorded {sem} count {val}"
            for s_, v_ in snap.items():
                self.wait(ename, s_, v_)
            return
        if ename == self.cur:
            self.eng.wait_ge(self.sems[sem], val)


    def chk(self, ename, fn, sem=None):
        sem = sem or ename
        self.on(ename, fn, inc=[(sem, 1)])
        return self.cnt[sem]

    def lastinc(self, ename):
        if ename == self.cur:
            self.last_instr[ename].then_inc(self.sems[ename], 1)
        self.cnt[ename] += 1
        return self.cnt[ename]

    def tend(self):
        self.on("t", lambda e: e.drain(), is_drain=True)
        return self.chk("t", lambda e: e.nop())

    def barrier(self):
        self.wait("sy", "dx", self.cnt["dx"])
        self.wait("s", "ds", self.cnt["ds"])
        self.wait("g", "dw", self.cnt["dw"])
        snaps = {}
        for e in ENGS:
            self.wait(e, e, self.cnt[e])
            snaps[e] = self.chk(e, lambda eng: eng.nop())
        for e in ENGS:
            for o in ENGS:
                if o != e:
                    self.wait(e, o, snaps[o])


def _alloc(nc, opened, kind, *args, **kw):
    cm = getattr(nc, kind)(*args, **kw)
    h = cm.__enter__()
    opened.append(cm)
    return h


def _free(opened):
    for cm in reversed(opened):
        cm.__exit__(None, None, None)
    opened.clear()


def _build(phase_limit="full", nt=NT):
    nc = bass.Bass("TRN2", target_bir_lowering=False, debug=False)

    def din(name, shape):
        return nc.dram_tensor(name, shape, F32, kind="ExternalInput").ap()

    def dinb(name, shape):
        return nc.dram_tensor(name, shape, BF16, kind="ExternalInput").ap()

    x = din("x", [BC * T, D])
    Wq, Wk = din("Wq", [D, DKT]), din("Wk", [D, DKT])
    Wv, Wg = din("Wv", [D, DVT]), din("Wg", [D, DVT])
    Wgk1, Wgk2 = din("Wgk1", [D, 16]), din("Wgk2", [16, DKT])
    Wo = din("Wo", [DVT, D])
    Wgate, Wup = din("Wgate", [D, IA]), din("Wup", [D, IA])
    Wdown = din("Wdown", [IA, D])
    W1, W2 = din("W1", [IF, D]), din("W2", [D, IF])
    gam_attn = din("gam_attn", [128, D])
    bgk2b = din("bgk2b", [128, DKT])
    gamg = din("gamg", [128, DVT])
    gam_mlp = din("gam_mlp", [128, D])
    b1row = din("b1row", [1, IF])
    b2row = din("b2row", [1, D])
    identb_d = dinb("identb", [128, 128])
    ident2b_d = dinb("ident2b", [128, 64])
    identf_d = din("identf", [128, 128])
    lmask_d = dinb("lmask", [128, 64])
    maskt_d = din("maskt", [128, 256])
    cones_d = dinb("cones", [128, 1])
    onesrf_d = din("onesrf", [1, 128])
    onesrb_d = dinb("onesrb", [1, 128])

    x2h = nc.dram_tensor("x2h", [BC * T, D], F32).ap()
    oAh = nc.dram_tensor("oAh", [BC * T, D], F32).ap()
    hh = nc.dram_tensor("hh", [BC * T, IF], F32).ap()
    wq1h = nc.dram_tensor("wq1h", [IF, D], BF16).ap()
    wq2h = nc.dram_tensor("wq2h", [D, IF], BF16).ap()
    ccin = nc.dram_tensor("ccin", [1, 8], F32)
    ccout = nc.dram_tensor("ccout", [1, 8], F32)
    out = nc.dram_tensor("out", [BC * T, D], F32, kind="ExternalOutput").ap()

    opened_global = []
    opened_phase = []

    def ga(kind, *a, **k):
        return _alloc(nc, opened_global, kind, *a, **k)

    def pa(kind, *a, **k):
        return _alloc(nc, opened_phase, kind, *a, **k)

    sems = {n: ga("semaphore", "sem_" + n) for n in SEMS}
    identb = ga("sbuf_tensor", "identb_t", [128, 128], BF16)
    ident2 = ga("sbuf_tensor", "ident2_t", [128, 64], BF16)
    identf = ga("sbuf_tensor", "identf_t", [128, 128], F32)
    lmask = ga("sbuf_tensor", "lmask_t", [128, 64], BF16)
    maskt = ga("sbuf_tensor", "maskt_t", [128, 256], F32)
    cones = ga("sbuf_tensor", "cones_t", [128, 1], BF16)
    onesrf = ga("sbuf_tensor", "onesrf_t", [1, 128], F32)
    onesrb = ga("sbuf_tensor", "onesrb_t", [1, 128], BF16)
    st = ga("sbuf_tensor", "st", [128, 32], F32)
    cols = ga("sbuf_tensor", "cols", [128, 16], F32)
    rows = ga("sbuf_tensor", "rows", [1, 32], F32)
    rowi = ga("sbuf_tensor", "rowi", [1, 4], I32)
    ymt = ga("sbuf_tensor", "ymt", [128, NT], F32)
    yinvt = ga("sbuf_tensor", "yinvt", [128, NT], F32)
    ccsb = ga("sbuf_tensor", "ccsb", [1, 8], F32)

    (R_AMAX, R_NAMIN, R_D, R_SX, R_ISX, R_ZT, R_ZP, R_LO, R_HI, R_SXSW, R_ISXSW,
     R_SW1, R_SW2, R_WMAX, R_T0, R_T1) = range(16)
    C_ISX, C_LO, C_HI, C_SXSW = 0, 1, 2, 3
    C_ONE, C_EPSR, C_LNSQ = 9, 10, 11
    (S_SUM, S_SQ, S_M, S_MSQ, S_BIAS, S_LNV, S_INV1, S_SQ2, S_INV2) = range(9)
    S_G0 = 10
    S_GI = 14
    S_RED = 20
    S_SUM2, S_MSQ2, S_BIAS2 = 24, 25, 26
    S_RED2 = 22
    S_FSQ, S_FLN, S_FINV = 28, 29, 30

    def row0(i):
        return 128 * i

    a1 = {}
    a1["wq"] = pa("sbuf_tensor", "a1_wq", [128, 8 * DKT], BF16)
    a1["wk"] = pa("sbuf_tensor", "a1_wk", [128, 8 * DKT], BF16)
    a1["wv"] = pa("sbuf_tensor", "a1_wv", [128, 8 * DVT], BF16)
    a1["wg"] = pa("sbuf_tensor", "a1_wg", [128, 8 * DVT], BF16)
    a1["wgk1"] = pa("sbuf_tensor", "a1_wgk1", [128, 8 * 16], BF16)
    a1["wgk2"] = pa("sbuf_tensor", "a1_wgk2", [16, DKT], BF16)
    a1["wo"] = pa("sbuf_tensor", "a1_wo", [128, 8 * D], BF16)
    a1["gat"] = pa("sbuf_tensor", "a1_gat", [128, D], F32)
    a1["bgk"] = pa("sbuf_tensor", "a1_bgk", [128, DKT], F32)
    a1["gag"] = pa("sbuf_tensor", "a1_gag", [128, DVT], F32)
    a1["xb"] = [pa("sbuf_tensor", f"a1_xb{j}", [128, D], F32) for j in range(2)]
    a1["xin"] = [pa("sbuf_tensor", f"a1_xin{j}", [128, D], F32) for j in range(2)]
    a1["x2b"] = [pa("sbuf_tensor", f"a1_x2b{j}", [128, D], F32) for j in range(2)]
    a1["hbf"] = pa("sbuf_tensor", "a1_hbf", [128, D], BF16)
    a1["hfm"] = pa("sbuf_tensor", "a1_hfm", [128, D], BF16)
    a1["q"] = pa("sbuf_tensor", "a1_q", [128, DKT], F32)
    a1["k"] = pa("sbuf_tensor", "a1_k", [128, DKT], F32)
    a1["vbf"] = pa("sbuf_tensor", "a1_vbf", [128, DVT], BF16)
    a1["g"] = pa("sbuf_tensor", "a1_g", [128, DVT], F32)
    a1["zg"] = pa("sbuf_tensor", "a1_zg", [128, DKT], F32)
    a1["gbf"] = pa("sbuf_tensor", "a1_gbf", [128, DKT], BF16)
    a1["expq"] = pa("sbuf_tensor", "a1_expq", [128, DKT], F32)
    a1["expm"] = pa("sbuf_tensor", "a1_expm", [128, DKT], F32)
    a1["qebf"] = pa("sbuf_tensor", "a1_qebf", [128, DKT], BF16)
    a1["kebf"] = pa("sbuf_tensor", "a1_kebf", [128, DKT], BF16)
    a1["qkfm"] = pa("sbuf_tensor", "a1_qkfm", [128, 2 * DKT], BF16)
    a1["atm"] = pa("sbuf_tensor", "a1_atm", [128, 256], BF16)
    a1["ebl"] = pa("sbuf_tensor", "a1_ebl", [128, 8], F32)
    a1["Sf"] = pa("sbuf_tensor", "a1_Sf", [128, DVT], F32)
    a1["Sb"] = pa("sbuf_tensor", "a1_Sb", [128, DVT], BF16)
    a1["on"] = pa("sbuf_tensor", "a1_on", [128, DVT], F32)
    a1["o2bf"] = pa("sbuf_tensor", "a1_o2bf", [128, DVT], BF16)
    a1["o2fm"] = pa("sbuf_tensor", "a1_o2fm", [128, DVT], BF16)
    a1["scrA"] = pa("sbuf_tensor", "a1_scrA", [128, D], F32)
    a1["scrB"] = pa("sbuf_tensor", "a1_scrB", [128, D], F32)
    a1["t1bf"] = pa("sbuf_tensor", "a1_t1bf", [128, 16], BF16)
    a1["t1fm"] = pa("sbuf_tensor", "a1_t1fm", [16, 128], BF16)
    a1["wbuf"] = [pa("sbuf_tensor", f"a1_wbuf{j}", [128, D], F32) for j in range(2)]
    a1["wqi"] = pa("sbuf_tensor", "a1_wqi", [128, D], I32)
    a1["wqo"] = pa("sbuf_tensor", "a1_wqo", [128, D], BF16)
    a1["psA"] = pa("psum_tensor", [128, 1024], F32)
    a1["psB"] = pa("psum_tensor", [128, 1024], F32)
    a1["psC"] = pa("psum_tensor", [128, 1024], F32)
    a1["psT"] = pa("psum_tensor", [128, 1024], BF16)
    a1["psX"] = pa("psum_tensor", [128, 512], F32)
    a1_opened = list(opened_phase)
    opened_phase.clear()
    _free(a1_opened)

    a2 = {}
    a2["wgate"] = pa("sbuf_tensor", "a2_wgate", [128, 8 * IA], BF16)
    a2["wup"] = pa("sbuf_tensor", "a2_wup", [128, 8 * IA], BF16)
    a2["wdown"] = pa("sbuf_tensor", "a2_wdown", [128, 22 * D], BF16)
    a2["gam"] = pa("sbuf_tensor", "a2_gam", [128, D], F32)
    a2["x2b"] = [pa("sbuf_tensor", f"a2_x2b{j}", [128, D], F32) for j in range(2)]
    a2["xb"] = [pa("sbuf_tensor", f"a2_xb{j}", [128, D], F32) for j in range(2)]
    a2["oab"] = [pa("sbuf_tensor", f"a2_oab{j}", [128, D], F32) for j in range(2)]
    a2["nbf"] = pa("sbuf_tensor", "a2_nbf", [128, D], BF16)
    a2["nfm"] = pa("sbuf_tensor", "a2_nfm", [128, D], BF16)
    a2["gate"] = [pa("sbuf_tensor", f"a2_gate{j}", [128, 1408], F32)
                  for j in range(2)]
    a2["scrA"] = pa("sbuf_tensor", "a2_scrA", [128, 1408], F32)
    a2["scrB"] = pa("sbuf_tensor", "a2_scrB", [128, 1408], F32)
    a2["scrC"] = pa("sbuf_tensor", "a2_scrC", [128, 1408], F32)
    a2["mlpb"] = [pa("sbuf_tensor", f"a2_mlpb{j}", [128, 1408], BF16)
                  for j in range(2)]
    a2["mlpfm"] = pa("sbuf_tensor", "a2_mlpfm", [128, 1408], BF16)
    a2["u"] = pa("sbuf_tensor", "a2_u", [128, D], F32)
    a2["ymaxr"] = pa("sbuf_tensor", "a2_ymaxr", [128, 1], F32)
    a2["yminr"] = pa("sbuf_tensor", "a2_yminr", [128, 1], F32)
    a2["psG"] = pa("psum_tensor", [128, 1536], F32)
    a2["psD"] = pa("psum_tensor", [128, 1024], F32)
    a2["psT"] = pa("psum_tensor", [128, 2048], BF16)
    a2["psX"] = pa("psum_tensor", [128, 512], F32)
    a2_opened = list(opened_phase)
    opened_phase.clear()
    _free(a2_opened)

    b1 = {}
    b1["wq1T"] = pa("sbuf_tensor", "b1_wq1T", [128, 8 * IF], BF16)
    b1["oab"] = [pa("sbuf_tensor", f"b1_oab{j}", [128, D], F32) for j in range(2)]
    b1["hbuf"] = [pa("sbuf_tensor", f"b1_hbuf{j}", [128, IF], F32) for j in range(2)]
    b1["y"] = pa("sbuf_tensor", "b1_y", [128, D], F32)
    b1["scr"] = pa("sbuf_tensor", "b1_scr", [128, D], F32)
    b1["qi"] = pa("sbuf_tensor", "b1_qi", [128, D], I32)
    b1["xqbf"] = pa("sbuf_tensor", "b1_xqbf", [128, D], BF16)
    b1["xqfm"] = pa("sbuf_tensor", "b1_xqfm", [128, D], BF16)
    b1["hmaxr"] = pa("sbuf_tensor", "b1_hmaxr", [128, 1], F32)
    b1["hminr"] = pa("sbuf_tensor", "b1_hminr", [128, 1], F32)
    b1["b1s"] = pa("sbuf_tensor", "b1_b1s", [1, IF], BF16)
    b1["psF"] = pa("psum_tensor", [128, 2048], F32)
    b1["psT"] = pa("psum_tensor", [128, 1024], BF16)
    b1["psX"] = pa("psum_tensor", [128, 512], F32)
    b1_opened = list(opened_phase)
    opened_phase.clear()
    _free(b1_opened)

    b2 = {}
    b2["wq2T"] = pa("sbuf_tensor", "b2_wq2T", [128, 32 * D], BF16)
    b2["hb"] = [pa("sbuf_tensor", f"b2_hb{j}", [128, IF], F32) for j in range(2)]
    b2["oab"] = [pa("sbuf_tensor", f"b2_oab{j}", [128, D], F32) for j in range(2)]
    b2["ob"] = [pa("sbuf_tensor", f"b2_ob{j}", [128, D], F32) for j in range(2)]
    b2["scr"] = pa("sbuf_tensor", "b2_scr", [128, IF], F32)
    b2["qi"] = pa("sbuf_tensor", "b2_qi", [128, IF], I32)
    b2["hqbf"] = pa("sbuf_tensor", "b2_hqbf", [128, IF], BF16)
    b2["hqfm"] = [pa("sbuf_tensor", f"b2_hqfm{j}", [128, D], BF16)
                  for j in range(2)]
    b2["fbuf"] = pa("sbuf_tensor", "b2_fbuf", [128, D], F32)
    b2["b2s"] = pa("sbuf_tensor", "b2_b2s", [1, D], BF16)
    b2["psD"] = pa("psum_tensor", [128, 1024], F32)
    b2["psT"] = pa("psum_tensor", [128, 1024], BF16)
    b2["psT2"] = pa("psum_tensor", [128, 1024], BF16)
    b2["psX"] = pa("psum_tensor", [128, 512], F32)
    b2_opened = list(opened_phase)
    opened_phase.clear()
    _free(b2_opened)

    def sched(env: Env):
        E = env
        E.on("v", lambda e: e.memset(cols[:, C_ONE:C_ONE + 1], 1.0))
        E.on("v", lambda e: e.memset(cols[:, C_EPSR:C_EPSR + 1], EPS_RMS))
        E.on("v", lambda e: e.memset(cols[:, C_LNSQ:C_LNSQ + 1], LN_SCALE_Q))

        for dst, src in ((identb, identb_d), (ident2, ident2b_d), (identf, identf_d),
                         (lmask, lmask_d), (maskt, maskt_d), (cones, cones_d),
                         (onesrf, onesrf_d), (onesrb, onesrb_d),
                         (a1["gat"], gam_attn), (a1["bgk"], bgk2b), (a1["gag"], gamg)):
            E.on("sy", lambda e, d=dst, s=src: e.dma_start(out=d[:, :], in_=s),
                 inc=[("dx", 16)])
        for wsb, wdr, ncol in ((a1["wq"], Wq, DKT), (a1["wk"], Wk, DKT),
                               (a1["wv"], Wv, DVT), (a1["wg"], Wg, DVT),
                               (a1["wgk1"], Wgk1, 16), (a1["wo"], Wo, D)):
            for c in range(8):
                E.on("g", lambda e, d=wsb, s=wdr, c=c, n=ncol:
                     e.dma_start(out=d[:, c * n:(c + 1) * n],
                                 in_=s[c * 128:(c + 1) * 128, :]),
                     inc=[("dw", 16)])
        E.on("g", lambda e: e.dma_start(out=a1["wgk2"][:, :], in_=Wgk2),
             inc=[("dw", 16)])
        dw_a1w = E.cnt["dw"]
        dx_setup = E.cnt["dx"]

        def wprep(slices, wqh_slices, r_sw, idx):
            nchunks = len(slices)
            wp_free, wp2_free, wq_st = {}, {}, {}
            ld = {}

            def emit_load(ldmap, c):
                E.on("sy", lambda e, c=c: e.dma_start(
                    out=a1["wbuf"][c % 2][:, :], in_=slices[c]), inc=[("dx", 16)])
                ldmap[c] = E.cnt["dx"]

            E.wait("sy", "v", E.cnt["v"])
            emit_load(ld, 0)
            emit_load(ld, 1)
            for c in range(nchunks):
                E.wait("v", "dx", ld[c])
                E.on("v", lambda e, c=c: e.reduce_max(
                    out=st[:, S_RED:S_RED + 1], in_=a1["wbuf"][c % 2][:, :],
                    axis=AX.X, apply_absolute_value=True))
                if c == 0:
                    v = E.chk("v", lambda e: e.tensor_copy(
                        out=st[:, S_RED + 1:S_RED + 2], in_=st[:, S_RED:S_RED + 1]))
                else:
                    v = E.chk("v", lambda e: e.tensor_max(
                        out=st[:, S_RED + 1:S_RED + 2], in0=st[:, S_RED + 1:S_RED + 2],
                        in1=st[:, S_RED:S_RED + 1]))
                wp_free[c] = v
                if c + 2 < nchunks:
                    E.wait("sy", "v", wp_free[c])
                    emit_load(ld, c + 2)
            E.wait("t", "v", wp_free[nchunks - 1])
            t = E.chk("t", lambda e: e.transpose(
                a1["psX"][0:1, idx * 128:(idx + 1) * 128],
                st[:, S_RED + 1:S_RED + 2], identf[:, :]))
            E.wait("v", "t", t)
            E.on("v", lambda e: e.reduce_max(
                out=rows[0:1, R_WMAX:R_WMAX + 1],
                in_=a1["psX"][0:1, idx * 128:(idx + 1) * 128], axis=AX.X))
            E.on("v", lambda e: e.tensor_scalar(
                out=rows[0:1, r_sw:r_sw + 1], in0=rows[0:1, R_WMAX:R_WMAX + 1],
                scalar1=1.0 / 127.0, scalar2=1e-8, op0=ALU.mult, op1=ALU.max))
            vsw = E.chk("v", lambda e: e.reciprocal(
                out=rows[0:1, R_T0 + idx:R_T0 + idx + 1],
                in_=rows[0:1, r_sw:r_sw + 1]))
            E.wait("t", "v", vsw)
            t = E.chk("t", lambda e: e.matmul(
                a1["psX"][:, idx:idx + 1], onesrf[0:1, :],
                rows[0:1, R_T0 + idx:R_T0 + idx + 1], start=True, stop=True))
            E.wait("v", "t", t)
            vbc = E.chk("v", lambda e: e.tensor_copy(out=cols[:, 8:9],
                                                     in_=a1["psX"][:, idx:idx + 1]))
            ld2 = {}
            E.wait("sy", "v", wp_free[nchunks - 1])
            emit_load(ld2, 0)
            emit_load(ld2, 1)
            for c in range(nchunks):
                E.wait("v", "dx", ld2[c])
                E.wait("v", "dw", wq_st.get(c - 1, 0))
                E.on("v", lambda e, c=c: e.tensor_scalar_mul(
                    out=a1["wbuf"][c % 2][:, :], in0=a1["wbuf"][c % 2][:, :],
                    scalar1=cols[:, 8:9]))
                E.on("v", lambda e, c=c: e.tensor_copy(
                    out=a1["wqi"][:, :], in_=a1["wbuf"][c % 2][:, :]))
                v = E.chk("v", lambda e, c=c: e.tensor_scalar(
                    out=a1["wqo"][:, :], in0=a1["wqi"][:, :],
                    scalar1=-127.0, scalar2=127.0, op0=ALU.max, op1=ALU.min))
                wp2_free[c] = v
                E.wait("g", "v", v)
                E.on("g", lambda e, c=c: e.dma_start(
                    out=wqh_slices[c], in_=a1["wqo"][:, :]), inc=[("dw", 16)])
                wq_st[c] = E.cnt["dw"]
                if c + 2 < nchunks:
                    E.wait("sy", "v", wp2_free[c])
                    emit_load(ld2, c + 2)

        w1_src = [W1[c * 128:(c + 1) * 128, :] for c in range(32)]
        w1_dst = [wq1h[c * 128:(c + 1) * 128, :] for c in range(32)]
        wprep(w1_src, w1_dst, R_SW1, 0)
        E.wait("v", "dw", E.cnt["dw"])
        w2_src = [W2[r * 128:(r + 1) * 128, cc * D:(cc + 1) * D]
                  for r in range(8) for cc in range(4)]
        w2_dst = [wq2h[r * 128:(r + 1) * 128, cc * D:(cc + 1) * D]
                  for r in range(8) for cc in range(4)]
        wprep(w2_src, w2_dst, R_SW2, 1)

        E.barrier()

        psA, psB, psC, psT, psX = (a1[k] for k in ("psA", "psB", "psC", "psT", "psX"))
        v_xin, s_sq, v_x2, st_x2, ld_x = {}, {}, {}, {}, {}
        v_hbf, v_o2fm, v_on, v_zb, v_atm1, v_o2 = {}, {}, {}, {}, {}, {}
        v_supd = 0
        vatm_prev = 0
        vqkfm_0 = 0
        sexp_last, supd_last, sebl_last, t_o_last = {}, {}, {}, {}

        E.on("sy", lambda e: e.dma_start(out=a1["xb"][0][:, :], in_=x[0:128, :]),
             inc=[("dx", 16)])
        ld_x[0] = E.cnt["dx"]

        def emit_wo_x2(i):
            E.wait("t", "v", v_o2fm[i])
            for kc in range(8):
                for half in range(2):
                    E.on("t", lambda e, kc=kc, half=half: e.matmul(
                        psB[:, half * 512:(half + 1) * 512],
                        a1["o2fm"][:, kc * 128:(kc + 1) * 128],
                        a1["wo"][:, kc * D + half * 512: kc * D + (half + 1) * 512],
                        start=(kc == 0), stop=(kc == 7)))
            two = E.tend()
            E.wait("v", "t", two)
            E.wait("v", "ds", st_x2.get(i - 2, 0))
            vx2 = E.chk("v", lambda e, i=i: e.tensor_add(
                out=a1["x2b"][i % 2][:, :], in0=psB[:, :], in1=a1["xin"][i % 2][:, :]))
            v_x2[i] = vx2
            E.wait("s", "v", vx2)
            E.on("s", lambda e, i=i: e.dma_start(
                out=x2h[row0(i):row0(i) + 128, :], in_=a1["x2b"][i % 2][:, :]),
                inc=[("ds", 16)])
            if phase_limit == "a1":
                E.on("s", lambda e, i=i: e.dma_start(
                    out=out[row0(i):row0(i) + 128, :], in_=a1["x2b"][i % 2][:, :]),
                    inc=[("ds", 16)])
            st_x2[i] = E.cnt["ds"]

        def a1_front(j):
            if j >= nt:
                return
            if j + 1 < nt:
                E.wait("sy", "v", v_xin.get(j - 1, 0))
                E.wait("sy", "s", s_sq.get(j - 1, 0))
                E.on("sy", lambda e, j=j: e.dma_start(
                    out=a1["xb"][(j + 1) % 2][:, :],
                    in_=x[row0(j + 1):row0(j + 1) + 128, :]), inc=[("dx", 16)])
                ld_x[j + 1] = E.cnt["dx"]
            if j % TPS == 0:
                E.wait("v", "t", t_o_last.get(j - 1, 0))
                E.on("v", lambda e: e.memset(a1["Sf"][:, :], 0.0))
                E.on("v", lambda e: e.memset(a1["Sb"][:, :], 0.0))
            E.wait("v", "dx", ld_x[j])
            E.on("v", lambda e, j=j: e.reduce_sum(
                out=st[:, S_SUM:S_SUM + 1], in_=a1["xb"][j % 2][:, :], axis=AX.X))
            E.on("v", lambda e: e.tensor_scalar_mul(
                out=st[:, S_M:S_M + 1], in0=st[:, S_SUM:S_SUM + 1], scalar1=1.0 / D))
            E.on("v", lambda e: e.tensor_mul(out=st[:, S_MSQ:S_MSQ + 1],
                                             in0=st[:, S_M:S_M + 1],
                                             in1=st[:, S_M:S_M + 1]))
            vb = E.chk("v", lambda e: e.tensor_scalar(
                out=st[:, S_BIAS:S_BIAS + 1], in0=st[:, S_MSQ:S_MSQ + 1],
                scalar1=-1.0, scalar2=EPS_LN, op0=ALU.mult, op1=ALU.add))
            E.wait("s", "dx", ld_x[j])
            E.wait("s", "v", v_o2.get(j - 1, 0))
            s_sq[j] = E.chk("s", lambda e, j=j: e.activation(
                out=a1["scrB"][:, :], in_=a1["xb"][j % 2][:, :], func=AF.Square,
                accum_out=st[:, S_SQ:S_SQ + 1]))
            E.wait("s", "v", vb)
            E.on("s", lambda e: e.activation(
                out=st[:, S_LNV:S_LNV + 1], in_=st[:, S_SQ:S_SQ + 1], func=AF.Ln,
                bias=st[:, S_BIAS:S_BIAS + 1], scale=1.0 / D))
            sinv = E.chk("s", lambda e: e.activation(
                out=st[:, S_INV1:S_INV1 + 1], in_=st[:, S_LNV:S_LNV + 1],
                func=AF.Exp, scale=-0.5))
            E.wait("v", "s", sinv)
            vxin = E.chk("v", lambda e, j=j: e.tensor_scalar(
                out=a1["xin"][j % 2][:, :], in0=a1["xb"][j % 2][:, :],
                scalar1=st[:, S_M:S_M + 1], scalar2=st[:, S_INV1:S_INV1 + 1],
                op0=ALU.subtract, op1=ALU.mult))
            v_xin[j] = vxin
            E.wait("s", "v", vxin)
            E.on("s", lambda e, j=j: e.activation(
                out=a1["scrB"][:, :], in_=a1["xin"][j % 2][:, :], func=AF.Square,
                accum_out=st[:, S_SQ2:S_SQ2 + 1]))
            E.on("s", lambda e: e.activation(
                out=st[:, S_LNV:S_LNV + 1], in_=st[:, S_SQ2:S_SQ2 + 1], func=AF.Ln,
                bias=cols[:, C_EPSR:C_EPSR + 1], scale=1.0 / D))
            sinv2 = E.chk("s", lambda e: e.activation(
                out=st[:, S_INV2:S_INV2 + 1], in_=st[:, S_LNV:S_LNV + 1],
                func=AF.Exp, scale=-0.5))
            E.wait("v", "s", sinv2)
            vhbf = E.chk("v", lambda e, j=j: e.scalar_tensor_tensor(
                out=a1["hbf"][:, :], in0=a1["xin"][j % 2][:, :],
                scalar=st[:, S_INV2:S_INV2 + 1], in1=a1["gat"][:, :],
                op0=ALU.mult, op1=ALU.mult))
            v_hbf[j] = vhbf
        a1_front(0)

        for i in range(nt):
            E.wait("t", "v", v_hbf[i])
            E.wait("t", "v", v_o2fm.get(i - 1, 0))
            for j in range(8):
                E.on("t", lambda e, j=j: e.transpose(
                    psT[:, j * 128:(j + 1) * 128], a1["hbf"][:, j * 128:(j + 1) * 128],
                    identb[:, :]))
            t_htr = E.tend()
            E.wait("v", "t", t_htr)
            vhfm = E.chk("v", lambda e: e.tensor_copy(out=a1["hfm"][:, :], in_=psT[:, :]))
            E.wait("t", "v", vhfm)
            E.wait("t", "v", v_on.get(i - 1, 0))
            E.wait("t", "v", v_x2.get(i - 1, 0))
            E.wait("t", "v", v_atm1.get(i - 1, 0))
            E.wait("t", "v", v_zb.get(i - 1, 0))
            for kc in range(8):
                E.on("t", lambda e, kc=kc: e.matmul(
                    psX[:, 0:16], a1["hfm"][:, kc * 128:(kc + 1) * 128],
                    a1["wgk1"][:, kc * 16:(kc + 1) * 16],
                    start=(kc == 0), stop=(kc == 7)))
            t_t1 = E.tend()
            E.wait("v", "t", t_t1)
            vt1 = E.chk("v", lambda e: e.tensor_copy(out=a1["t1bf"][:, :],
                                                     in_=psX[:, 0:16]))
            E.wait("t", "v", vt1)
            tt1 = E.chk("t", lambda e: e.transpose(
                psT[0:16, 0:128], a1["t1bf"][:, :], identb[:, :]))
            E.wait("v", "t", tt1)
            vt1f = E.chk("v", lambda e: e.tensor_copy(out=a1["t1fm"][:, :],
                                                      in_=psT[0:16, 0:128]))
            E.wait("t", "v", vt1f)
            tz = E.chk("t", lambda e: e.matmul(
                psX[:, 0:512], a1["t1fm"][:, :], a1["wgk2"][:, :],
                start=True, stop=True))
            E.wait("v", "t", tz)
            vzb = E.chk("v", lambda e: e.tensor_add(
                out=a1["zg"][:, :], in0=psX[:, 0:512], in1=a1["bgk"][:, :]))
            v_zb[i] = vzb
            E.wait("s", "v", vzb)
            E.on("s", lambda e: e.activation(out=a1["scrA"][:, 0:512],
                                             in_=a1["zg"][:, :], func=AF.Exp,
                                             scale=-1.0))
            for kc in range(8):
                lhs = a1["hfm"][:, kc * 128:(kc + 1) * 128]
                ss = (kc == 0, kc == 7)
                E.on("t", lambda e, l=lhs, kc=kc, ss=ss: e.matmul(
                    psA[:, 0:512], l, a1["wq"][:, kc * DKT:(kc + 1) * DKT],
                    start=ss[0], stop=ss[1]))
                E.on("t", lambda e, l=lhs, kc=kc, ss=ss: e.matmul(
                    psA[:, 512:1024], l, a1["wk"][:, kc * DKT:(kc + 1) * DKT],
                    start=ss[0], stop=ss[1]))
                for half in range(2):
                    E.on("t", lambda e, l=lhs, kc=kc, half=half, ss=ss: e.matmul(
                        psB[:, half * 512:(half + 1) * 512], l,
                        a1["wv"][:, kc * DVT + half * 512: kc * DVT + (half + 1) * 512],
                        start=ss[0], stop=ss[1]))
                    E.on("t", lambda e, l=lhs, kc=kc, half=half, ss=ss: e.matmul(
                        psC[:, half * 512:(half + 1) * 512], l,
                        a1["wg"][:, kc * DVT + half * 512: kc * DVT + (half + 1) * 512],
                        start=ss[0], stop=ss[1]))
            t_proj = E.tend()
            E.wait("v", "t", t_proj)
            E.on("v", lambda e: e.tensor_copy(out=a1["q"][:, :], in_=psA[:, 0:512]))
            vt1_unused = E.chk("v", lambda e: e.tensor_copy(out=a1["vbf"][:, :],
                                                            in_=psB[:, :]))
            E.wait("s", "t", t_proj)
            E.on("s", lambda e: e.copy(out=a1["k"][:, :], in_=psA[:, 512:1024]))
            skg = E.chk("s", lambda e: e.copy(out=a1["g"][:, :], in_=psC[:, :]))
            sgbf = E.chk("s", lambda e: e.activation(
                out=a1["gbf"][:, :], in_=a1["scrA"][:, 0:512], func=AF.Ln,
                bias=cols[:, C_ONE:C_ONE + 1]))

            E.wait("t", "s", sgbf)
            E.wait("t", "s", skg)
            E.wait("t", "s", sexp_last.get(i - 1, 0))
            tb0 = E.chk("t", lambda e: e.matmul(
                psC[0:64, 0:512], lmask[0:64, :], a1["gbf"][0:64, :],
                start=True, stop=True))
            tb1 = E.chk("t", lambda e: e.matmul(
                psC[64:128, 0:512], lmask[64:128, :], a1["gbf"][64:128, :],
                start=True, stop=True))
            E.wait("s", "t", tb1)
            E.on("s", lambda e: e.activation(
                out=a1["expq"][:, :], in_=psC[:, 0:512], func=AF.Exp,
                bias=cols[:, C_LNSQ:C_LNSQ + 1]))
            sexp = E.chk("s", lambda e: e.activation(
                out=a1["expm"][:, :], in_=psC[:, 0:512], func=AF.Exp, scale=-1.0))
            sexp_last[i] = sexp
            E.wait("t", "s", sexp)
            for ch in range(2):
                r0 = 64 * ch
                for h in range(NH):
                    E.on("t", lambda e, h=h, r0=r0, ch=ch: e.matmul(
                        psC[:, 1008 + ch * 4 + h:1009 + ch * 4 + h],
                        a1["gbf"][r0:r0 + 64, h * 128:(h + 1) * 128],
                        cones[r0:r0 + 64, :], start=True, stop=True))
            tbl = E.tend()
            E.wait("s", "t", tbl)
            E.wait("s", "v", supd_last.get(i - 1, 0))
            sebl = E.chk("s", lambda e: e.activation(
                out=a1["ebl"][:, 0:8], in_=psC[:, 1008:1016], func=AF.Exp))
            sebl_last[i] = sebl
            E.wait("v", "s", sexp)
            E.on("v", lambda e: e.tensor_mul(out=a1["qebf"][:, :], in0=a1["q"][:, :],
                                             in1=a1["expq"][:, :]))
            vqk = E.chk("v", lambda e: e.tensor_mul(
                out=a1["kebf"][:, :], in0=a1["k"][:, :], in1=a1["expm"][:, :]))
            E.wait("t", "v", vqk)
            E.wait("t", "v", vt1f)
            for ch in range(2):
                r0 = 64 * ch
                co = ch * 512
                if ch == 1:
                    E.wait("t", "v", vqkfm_0)
                for h in range(NH):
                    E.on("t", lambda e, h=h, r0=r0: e.transpose(
                        psT[:, h * 64:(h + 1) * 64],
                        a1["qebf"][r0:r0 + 64, h * 128:(h + 1) * 128],
                        ident2[r0:r0 + 64, :]))
                    E.on("t", lambda e, h=h, r0=r0: e.transpose(
                        psT[:, 256 + h * 64:256 + (h + 1) * 64],
                        a1["kebf"][r0:r0 + 64, h * 128:(h + 1) * 128],
                        ident2[r0:r0 + 64, :]))
                ttr = E.tend()
                E.wait("v", "t", ttr)
                vqkfm = E.chk("v", lambda e, co=co: e.tensor_copy(
                    out=a1["qkfm"][:, co:co + 512], in_=psT[:, 0:512]))
                if ch == 0:
                    vqkfm_0 = vqkfm
                E.wait("t", "v", vqkfm)
                E.wait("t", "s", sebl)
                if ch == 1:
                    E.wait("t", "v", vatm_prev)
                for h in range(NH):
                    E.on("t", lambda e, h=h, r0=r0, co=co: e.matmul(
                        psC[r0:r0 + 64, 512 + h * 64:512 + (h + 1) * 64],
                        a1["qkfm"][:, co + 256 + h * 64:co + 256 + (h + 1) * 64],
                        a1["qkfm"][:, co + h * 64:co + (h + 1) * 64],
                        start=True, stop=True))
                tat = E.tend()
                E.wait("v", "t", tat)
                vatm = E.chk("v", lambda e, r0=r0: e.tensor_mul(
                    out=a1["atm"][r0:r0 + 64, :], in0=psC[r0:r0 + 64, 512:768],
                    in1=maskt[r0:r0 + 64, :]))
                vatm_prev = vatm
                if ch == 1:
                    v_atm1[i] = vatm
                E.wait("t", "v", vatm)
                E.wait("t", "v", v_supd)
                for h in range(NH):
                    E.on("t", lambda e, h=h, r0=r0, co=co: e.matmul(
                        psA[r0:r0 + 64, h * 256:(h + 1) * 256],
                        a1["atm"][r0:r0 + 64, h * 64:(h + 1) * 64],
                        a1["vbf"][r0:r0 + 64, h * 256:(h + 1) * 256],
                        start=True, stop=False))
                    E.on("t", lambda e, h=h, r0=r0, co=co: e.matmul(
                        psA[r0:r0 + 64, h * 256:(h + 1) * 256],
                        a1["qkfm"][:, co + h * 64:co + (h + 1) * 64],
                        a1["Sb"][:, h * 256:(h + 1) * 256],
                        start=False, stop=True))
                t_o = E.tend()
                for h in range(NH):
                    E.on("t", lambda e, h=h, r0=r0: e.matmul(
                        psB[:, h * 256:(h + 1) * 256],
                        a1["kebf"][r0:r0 + 64, h * 128:(h + 1) * 128],
                        a1["vbf"][r0:r0 + 64, h * 256:(h + 1) * 256],
                        start=True, stop=True))
                t_p = E.tend()
                E.wait("v", "t", t_p)
                E.wait("v", "s", sebl)
                E.on("v", lambda e: e.tensor_add(out=a1["Sf"][:, :],
                                                 in0=a1["Sf"][:, :], in1=psB[:, :]))
                for h in range(NH):
                    E.on("v", lambda e, h=h, ch=ch: e.tensor_scalar_mul(
                        out=a1["Sf"][:, h * 256:(h + 1) * 256],
                        in0=a1["Sf"][:, h * 256:(h + 1) * 256],
                        scalar1=a1["ebl"][:, ch * 4 + h:ch * 4 + h + 1]))
                v_supd = E.chk("v", lambda e: e.tensor_copy(out=a1["Sb"][:, :],
                                                            in_=a1["Sf"][:, :]))
            supd_last[i] = v_supd
            t_o_last[i] = t_o

            E.wait("s", "t", t_o)
            for h in range(NH):
                E.on("s", lambda e, h=h: e.activation(
                    out=a1["scrB"][:, h * 256:(h + 1) * 256],
                    in_=psA[:, h * 256:(h + 1) * 256], func=AF.Square,
                    accum_out=st[:, S_G0 + h:S_G0 + h + 1]))
            E.on("s", lambda e: e.activation(
                out=st[:, S_GI:S_GI + 4], in_=st[:, S_G0:S_G0 + 4], func=AF.Ln,
                bias=cols[:, C_EPSR:C_EPSR + 1], scale=1.0 / 256.0))
            sgi = E.chk("s", lambda e: e.activation(
                out=st[:, S_GI:S_GI + 4], in_=st[:, S_GI:S_GI + 4], func=AF.Exp,
                scale=-0.5))
            E.on("s", lambda e: e.activation(out=a1["scrA"][:, :], in_=a1["g"][:, :],
                                             func=AF.Exp, scale=-1.0))
            E.on("s", lambda e: e.activation(out=a1["scrB"][:, :], in_=a1["scrA"][:, :],
                                             func=AF.Ln, bias=cols[:, C_ONE:C_ONE + 1]))
            ssig = E.chk("s", lambda e: e.activation(
                out=a1["scrA"][:, :], in_=a1["scrB"][:, :], func=AF.Exp, scale=-1.0))
            E.wait("v", "s", sgi)
            for h in range(NH):
                E.on("v", lambda e, h=h: e.scalar_tensor_tensor(
                    out=a1["on"][:, h * 256:(h + 1) * 256],
                    in0=psA[:, h * 256:(h + 1) * 256],
                    scalar=st[:, S_GI + h:S_GI + h + 1],
                    in1=a1["gag"][:, h * 256:(h + 1) * 256],
                    op0=ALU.mult, op1=ALU.mult))
            v_on[i] = E.chk("v", lambda e: e.engine_nop())
            E.wait("v", "s", ssig)
            E.on("v", lambda e: e.tensor_mul(out=a1["scrB"][:, :], in0=a1["g"][:, :],
                                             in1=a1["scrA"][:, :]))
            vo2 = E.chk("v", lambda e: e.tensor_mul(
                out=a1["o2bf"][:, :], in0=a1["on"][:, :], in1=a1["scrB"][:, :]))
            v_o2[i] = vo2
            E.wait("t", "v", vo2)
            for j in range(8):
                E.on("t", lambda e, j=j: e.transpose(
                    psT[:, j * 128:(j + 1) * 128],
                    a1["o2bf"][:, j * 128:(j + 1) * 128], identb[:, :]))
            to2t = E.tend()
            E.wait("v", "t", to2t)
            v_o2fm[i] = E.chk("v", lambda e: e.tensor_copy(out=a1["o2fm"][:, :],
                                                           in_=psT[:, :]))
            a1_front(i + 1)
            emit_wo_x2(i)

        E.barrier()
        if phase_limit == "a1":
            return

        psG, psD, psT, psX = a2["psG"], a2["psD"], a2["psT"], a2["psX"]
        for wsb, wdr, ncol, nch in ((a2["wgate"], Wgate, IA, 8),
                                    (a2["wup"], Wup, IA, 8),
                                    (a2["wdown"], Wdown, D, 22)):
            for c in range(nch):
                E.on("g", lambda e, d=wsb, s=wdr, c=c, n=ncol: e.dma_start(
                    out=d[:, c * n:(c + 1) * n], in_=s[c * 128:(c + 1) * 128, :]),
                    inc=[("dw", 16)])
        E.on("sy", lambda e: e.dma_start(out=a2["gam"][:, :], in_=gam_mlp),
             inc=[("dx", 16)])
        dw_a2w = E.cnt["dw"]
        dx_a2 = E.cnt["dx"]
        E.wait("t", "dw", dw_a2w)
        E.on("v", lambda e: e.memset(a2["ymaxr"][:, :], -1e38))
        E.on("v", lambda e: e.memset(a2["yminr"][:, :], 1e38))
        E.wait("v", "dx", dx_a2)

        ld2, v_cons, s_oast, v_nbf, v_mlpfm2, v_u = {}, {}, {}, {}, {}, {}
        t_ntr, v_mlp1 = {}, {}

        def a2_loads(j):
            if j >= nt:
                return
            E.wait("sy", "v", v_cons.get(j - 2, 0))
            E.wait("sy", "ds", s_oast.get(j - 2, 0))
            E.on("sy", lambda e, j=j: e.dma_start(
                out=a2["x2b"][j % 2][:, :],
                in_=x2h[row0(j):row0(j) + 128, :]), inc=[("dx", 16)])
            E.on("sy", lambda e, j=j: e.dma_start(
                out=a2["xb"][j % 2][:, :],
                in_=x[row0(j):row0(j) + 128, :]), inc=[("dx", 16)])
            ld2[j] = E.cnt["dx"]

        def a2_front(j):
            if j >= nt:
                return
            E.wait("s", "dx", ld2[j])
            E.wait("s", "v", v_mlp1.get(j - 1, 0))
            E.on("s", lambda e, j=j: e.activation(
                out=a2["scrB"][:, 0:D], in_=a2["x2b"][j % 2][:, :], func=AF.Square,
                accum_out=st[:, S_FSQ:S_FSQ + 1]))
            E.on("s", lambda e: e.activation(
                out=st[:, S_FLN:S_FLN + 1], in_=st[:, S_FSQ:S_FSQ + 1], func=AF.Ln,
                bias=cols[:, C_EPSR:C_EPSR + 1], scale=1.0 / D))
            sinv = E.chk("s", lambda e: e.activation(
                out=st[:, S_FINV:S_FINV + 1], in_=st[:, S_FLN:S_FLN + 1],
                func=AF.Exp, scale=-0.5))
            E.wait("v", "s", sinv)
            E.wait("v", "t", t_ntr.get(j - 1, 0))
            vnbf = E.chk("v", lambda e, j=j: e.scalar_tensor_tensor(
                out=a2["nbf"][:, :], in0=a2["x2b"][j % 2][:, :],
                scalar=st[:, S_FINV:S_FINV + 1], in1=a2["gam"][:, :],
                op0=ALU.mult, op1=ALU.mult))
            v_nbf[j] = vnbf

        def a2_premid(j, pv):
            if j >= nt:
                return 0
            E.wait("t", "v", v_nbf[j])
            E.wait("t", "v", pv)
            for jj in range(8):
                E.on("t", lambda e, jj=jj: e.transpose(
                    psT[:, jj * 128:(jj + 1) * 128],
                    a2["nbf"][:, jj * 128:(jj + 1) * 128], identb[:, :]))
            tntr = E.tend()
            t_ntr[j] = tntr
            E.wait("v", "t", tntr)
            return E.chk("v", lambda e: e.tensor_copy(out=a2["nfm"][:, :],
                                                      in_=psT[:, 0:1024]))

        a2_loads(0)
        a2_loads(1)
        a2_front(0)
        vnfm_next = a2_premid(0, 0)

        for i in range(nt):
            vnfm = vnfm_next
            snfm = 0
            segs = ((0, 512), (512, 1024), (1024, 1408))

            def a2_gate_or_up(wt, coff):
                for kc in range(8):
                    for (c0, c1) in segs:
                        E.on("t", lambda e, kc=kc, c0=c0, c1=c1, coff=coff, wt=wt:
                             e.matmul(psG[:, c0:c1],
                                      a2["nfm"][:, kc * 128:(kc + 1) * 128],
                                      wt[:, kc * IA + coff + c0: kc * IA + coff + c1],
                                      start=(kc == 0), stop=(kc == 7)))
                return E.tend()

            def a2_gate_evac(tg, half):
                E.wait("v", "t", tg)
                vg = E.chk("v", lambda e, half=half: e.tensor_copy(
                    out=a2["gate"][half][:, 0:704], in_=psG[:, 0:704]))
                E.wait("s", "t", tg)
                sg = E.chk("s", lambda e, half=half: e.copy(
                    out=a2["gate"][half][:, 704:1408], in_=psG[:, 704:1408]))
                return vg, sg

            def a2_silu(vg, half, o1, o2, wv=0):
                E.wait("s", "v", vg)
                E.on("s", lambda e, half=half, o1=o1: e.activation(
                    out=o1[:, :], in_=a2["gate"][half][:, :], func=AF.Exp,
                    scale=-1.0))
                E.wait("s", "v", wv)
                E.on("s", lambda e, o1=o1, o2=o2: e.activation(
                    out=o2[:, :], in_=o1[:, :], func=AF.Ln,
                    bias=cols[:, C_ONE:C_ONE + 1]))
                return E.chk("s", lambda e, o1=o1, o2=o2: e.activation(
                    out=o2[:, :], in_=o2[:, :], func=AF.Exp, scale=-1.0))

            def a2_muls(tu, ssig, sg, half, sig):
                E.wait("v", "t", tu)
                E.wait("v", "s", sg)
                vm1 = E.chk("v", lambda e, half=half: e.tensor_mul(
                    out=a2["scrC"][:, :], in0=psG[:, 0:1408],
                    in1=a2["gate"][half][:, :]))
                E.wait("v", "s", ssig)
                vmlp = E.chk("v", lambda e, half=half, sig=sig: e.tensor_mul(
                    out=a2["mlpb"][half][:, :], in0=a2["scrC"][:, :], in1=sig[:, :]))
                return vm1, vmlp

            def a2_tr(half):
                for j in range(11):
                    E.on("t", lambda e, j=j, half=half: e.transpose(
                        psT[:, j * 128:(j + 1) * 128],
                        a2["mlpb"][half][:, j * 128:(j + 1) * 128], identb[:, :]))
                tm1 = E.tend()
                E.wait("v", "t", tm1)
                vf = E.chk("v", lambda e: e.tensor_copy(out=a2["mlpfm"][:, 0:1408],
                                                        in_=psT[:, 0:1408]))
                return vf, 0

            def a2_down(half):
                for j in range(11):
                    kci = half * 11 + j
                    for hh2 in range(2):
                        E.on("t", lambda e, j=j, kci=kci, hh2=hh2: e.matmul(
                            psD[:, hh2 * 512:(hh2 + 1) * 512],
                            a2["mlpfm"][:, j * 128:(j + 1) * 128],
                            a2["wdown"][:, kci * D + hh2 * 512:
                                        kci * D + (hh2 + 1) * 512],
                            start=(kci == 0), stop=(kci == 21)))
                return E.tend()

            E.wait("t", "v", vnfm)
            E.wait("t", "s", snfm)
            tg0 = a2_gate_or_up(a2["wgate"], 0)
            vg0, sg0 = a2_gate_evac(tg0, 0)
            ssig0 = a2_silu(vg0, 0, a2["scrB"], a2["scrA"])
            E.wait("t", "v", vg0)
            E.wait("t", "s", sg0)
            tg1 = a2_gate_or_up(a2["wgate"], 1408)
            vg1, sg1 = a2_gate_evac(tg1, 1)
            ssig1 = None
            E.wait("t", "v", vg1)
            E.wait("t", "s", sg1)
            tu0 = a2_gate_or_up(a2["wup"], 0)
            vm1_0, vmlp0 = a2_muls(tu0, ssig0, sg0, 0, a2["scrA"])
            ssig1 = a2_silu(vg1, 1, a2["scrB"], a2["scrA"], wv=vmlp0)
            E.wait("t", "v", vm1_0)
            tu1 = a2_gate_or_up(a2["wup"], 1408)
            vm1_1, vmlp1 = a2_muls(tu1, ssig1, sg1, 1, a2["scrA"])
            v_mlp1[i] = vmlp1
            E.wait("t", "v", vmlp0)
            vf2_0, sf2_0 = a2_tr(0)
            E.wait("t", "v", vf2_0)
            E.wait("t", "s", sf2_0)
            E.wait("t", "v", v_u.get(i - 1, 0))
            a2_down(0)
            a2_front(i + 1)
            E.wait("t", "v", vmlp1)
            vf2_1, sf2_1 = a2_tr(1)
            E.wait("t", "v", vf2_1)
            tdown = a2_down(1)
            vnfm_next = a2_premid(i + 1, vf2_1)

            E.wait("v", "t", tdown)
            vu = E.chk("v", lambda e, i=i: e.tensor_add(
                out=a2["u"][:, :], in0=psD[:, :], in1=a2["x2b"][i % 2][:, :]))
            v_u[i] = vu
            E.on("v", lambda e: e.reduce_sum(out=st[:, S_SUM:S_SUM + 1],
                                             in_=a2["u"][:, :], axis=AX.X))
            E.on("v", lambda e: e.tensor_scalar_mul(
                out=st[:, S_M:S_M + 1], in0=st[:, S_SUM:S_SUM + 1], scalar1=1.0 / D))
            E.on("v", lambda e: e.tensor_mul(out=st[:, S_MSQ:S_MSQ + 1],
                                             in0=st[:, S_M:S_M + 1],
                                             in1=st[:, S_M:S_M + 1]))
            vb = E.chk("v", lambda e: e.tensor_scalar(
                out=st[:, S_BIAS:S_BIAS + 1], in0=st[:, S_MSQ:S_MSQ + 1],
                scalar1=-1.0, scalar2=EPS_LN, op0=ALU.mult, op1=ALU.add))
            E.wait("s", "v", vu)
            E.on("s", lambda e: e.activation(out=a2["scrB"][:, 0:D], in_=a2["u"][:, :],
                                             func=AF.Square,
                                             accum_out=st[:, S_SQ:S_SQ + 1]))
            E.wait("s", "v", vb)
            E.on("s", lambda e: e.activation(
                out=st[:, S_LNV:S_LNV + 1], in_=st[:, S_SQ:S_SQ + 1], func=AF.Ln,
                bias=st[:, S_BIAS:S_BIAS + 1], scale=1.0 / D))
            sinvu = E.chk("s", lambda e: e.activation(
                out=st[:, S_INV1:S_INV1 + 1], in_=st[:, S_LNV:S_LNV + 1],
                func=AF.Exp, scale=-0.5))
            E.wait("v", "s", sinvu)
            E.on("v", lambda e: e.tensor_scalar(
                out=a2["scrA"][:, 0:D], in0=a2["u"][:, :], scalar1=st[:, S_M:S_M + 1],
                scalar2=st[:, S_INV1:S_INV1 + 1], op0=ALU.subtract, op1=ALU.mult))
            E.wait("v", "ds", s_oast.get(i - 2, 0))
            voa = E.chk("v", lambda e, i=i: e.tensor_add(
                out=a2["oab"][i % 2][:, :], in0=a2["scrA"][:, 0:D],
                in1=a2["xb"][i % 2][:, :]))
            E.wait("s", "v", voa)
            E.on("s", lambda e, i=i: e.dma_start(
                out=oAh[row0(i):row0(i) + 128, :], in_=a2["oab"][i % 2][:, :]),
                inc=[("ds", 16)])
            if phase_limit == "a2":
                E.on("s", lambda e, i=i: e.dma_start(
                    out=out[row0(i):row0(i) + 128, :], in_=a2["oab"][i % 2][:, :]),
                    inc=[("ds", 16)])
            s_oast[i] = E.cnt["ds"]
            E.on("v", lambda e, i=i: e.reduce_sum(
                out=st[:, S_SUM2:S_SUM2 + 1], in_=a2["oab"][i % 2][:, :], axis=AX.X))
            E.on("v", lambda e, i=i: e.tensor_scalar_mul(
                out=ymt[:, i:i + 1], in0=st[:, S_SUM2:S_SUM2 + 1], scalar1=1.0 / D))
            E.on("v", lambda e, i=i: e.tensor_mul(out=st[:, S_MSQ2:S_MSQ2 + 1],
                                                  in0=ymt[:, i:i + 1],
                                                  in1=ymt[:, i:i + 1]))
            vb2 = E.chk("v", lambda e: e.tensor_scalar(
                out=st[:, S_BIAS2:S_BIAS2 + 1], in0=st[:, S_MSQ2:S_MSQ2 + 1],
                scalar1=-1.0, scalar2=EPS_LN, op0=ALU.mult, op1=ALU.add))
            E.on("s", lambda e, i=i: e.activation(
                out=a2["scrB"][:, 0:D], in_=a2["oab"][i % 2][:, :], func=AF.Square,
                accum_out=st[:, S_SQ:S_SQ + 1]))
            E.wait("s", "v", vb2)
            E.on("s", lambda e: e.activation(
                out=st[:, S_LNV:S_LNV + 1], in_=st[:, S_SQ:S_SQ + 1], func=AF.Ln,
                bias=st[:, S_BIAS2:S_BIAS2 + 1], scale=1.0 / D))
            sy2 = E.chk("s", lambda e, i=i: e.activation(
                out=yinvt[:, i:i + 1], in_=st[:, S_LNV:S_LNV + 1], func=AF.Exp,
                scale=-0.5))
            E.on("v", lambda e, i=i: e.reduce_max(out=st[:, S_RED:S_RED + 1],
                                                  in_=a2["oab"][i % 2][:, :],
                                                  axis=AX.X))
            E.on("v", lambda e, i=i: e.tensor_reduce(
                out=st[:, S_RED2:S_RED2 + 1], in_=a2["oab"][i % 2][:, :],
                axis=AX.X, op=ALU.min))
            E.wait("v", "s", sy2)
            E.on("v", lambda e, i=i: e.tensor_scalar(
                out=st[:, S_RED:S_RED + 1], in0=st[:, S_RED:S_RED + 1],
                scalar1=ymt[:, i:i + 1], scalar2=yinvt[:, i:i + 1],
                op0=ALU.subtract, op1=ALU.mult))
            E.on("v", lambda e: e.tensor_max(out=a2["ymaxr"][:, :],
                                             in0=a2["ymaxr"][:, :],
                                             in1=st[:, S_RED:S_RED + 1]))
            E.on("v", lambda e, i=i: e.tensor_scalar(
                out=st[:, S_RED2:S_RED2 + 1], in0=st[:, S_RED2:S_RED2 + 1],
                scalar1=ymt[:, i:i + 1], scalar2=yinvt[:, i:i + 1],
                op0=ALU.subtract, op1=ALU.mult))
            vmm = E.chk("v", lambda e: e.tensor_tensor(
                out=a2["yminr"][:, :], in0=a2["yminr"][:, :],
                in1=st[:, S_RED2:S_RED2 + 1], op=ALU.min))
            v_cons[i] = vmm
            a2_loads(i + 2)

        E.wait("t", "v", E.cnt["v"])
        tmx = E.chk("t", lambda e: e.transpose(psX[0:1, 0:128], a2["ymaxr"][:, :],
                                               identf[:, :]))
        E.wait("v", "t", tmx)
        vdone = E.chk("v", lambda e: e.reduce_max(out=ccsb[0:1, 0:1],
                                                  in_=psX[0:1, 0:128], axis=AX.X))
        E.wait("t", "v", vdone)
        tmn = E.chk("t", lambda e: e.transpose(psX[0:1, 0:128], a2["yminr"][:, :],
                                               identf[:, :]))
        E.wait("v", "t", tmn)
        E.on("v", lambda e: e.tensor_reduce(out=ccsb[0:1, 1:2], in_=psX[0:1, 0:128],
                                            axis=AX.X, op=ALU.min))
        E.on("v", lambda e: e.tensor_scalar_mul(out=ccsb[0:1, 1:2],
                                                in0=ccsb[0:1, 1:2], scalar1=-1.0))
        E.on("v", lambda e: e.memset(ccsb[0:1, 2:8], 0.0))

        E.barrier()
        if phase_limit == "a2":
            return

        def allreduce_and_scales(ps_x, r_sw, bsrow, brow_t, nb):
            vsnap = E.cnt["v"]
            E.wait("g", "v", vsnap)
            E.on("g", lambda e: e.dma_start(out=ccin[:, :], in_=ccsb[:, :]),
                 inc=[("dw", 16)])
            E.wait("g", "dw", E.cnt["dw"])
            E.on("g", lambda e: e.collective_compute(
                "AllReduce", ALU.max, replica_groups=[list(range(NCORES))],
                ins=[ccin.ap().opt()], outs=[ccout.ap().opt()]), inc=[("cc", 1)])
            E.wait("g", "cc", E.cnt["cc"])
            E.on("g", lambda e: e.dma_start(out=ccsb[:, :], in_=ccout[:, :]),
                 inc=[("dw", 16)])
            E.wait("v", "dw", E.cnt["dw"])
            E.on("v", lambda e: e.tensor_add(out=rows[0:1, R_D:R_D + 1],
                                             in0=ccsb[0:1, 0:1], in1=ccsb[0:1, 1:2]))
            E.on("v", lambda e: e.tensor_scalar(
                out=rows[0:1, R_SX:R_SX + 1], in0=rows[0:1, R_D:R_D + 1],
                scalar1=1.0 / 255.0, scalar2=1e-8, op0=ALU.mult, op1=ALU.max))
            E.on("v", lambda e: e.reciprocal(out=rows[0:1, R_ISX:R_ISX + 1],
                                             in_=rows[0:1, R_SX:R_SX + 1]))
            E.on("v", lambda e: e.tensor_mul(out=rows[0:1, R_ZT:R_ZT + 1],
                                             in0=ccsb[0:1, 1:2],
                                             in1=rows[0:1, R_ISX:R_ISX + 1]))
            E.on("v", lambda e: e.tensor_copy(out=rowi[0:1, 0:1],
                                              in_=rows[0:1, R_ZT:R_ZT + 1]))
            E.on("v", lambda e: e.tensor_copy(out=rows[0:1, R_ZT:R_ZT + 1],
                                              in_=rowi[0:1, 0:1]))
            E.on("v", lambda e: e.tensor_scalar_add(
                out=rows[0:1, R_ZP:R_ZP + 1], in0=rows[0:1, R_ZT:R_ZT + 1],
                scalar1=-128.0))
            E.on("v", lambda e: e.tensor_scalar(
                out=rows[0:1, R_ZP:R_ZP + 1], in0=rows[0:1, R_ZP:R_ZP + 1],
                scalar1=-128.0, scalar2=127.0, op0=ALU.max, op1=ALU.min))
            E.on("v", lambda e: e.tensor_scalar(
                out=rows[0:1, R_LO:R_LO + 1], in0=rows[0:1, R_ZP:R_ZP + 1],
                scalar1=-1.0, scalar2=-128.0, op0=ALU.mult, op1=ALU.add))
            E.on("v", lambda e: e.tensor_scalar(
                out=rows[0:1, R_HI:R_HI + 1], in0=rows[0:1, R_ZP:R_ZP + 1],
                scalar1=-1.0, scalar2=127.0, op0=ALU.mult, op1=ALU.add))
            E.on("v", lambda e: e.tensor_mul(out=rows[0:1, R_SXSW:R_SXSW + 1],
                                             in0=rows[0:1, R_SX:R_SX + 1],
                                             in1=rows[0:1, r_sw:r_sw + 1]))
            E.on("v", lambda e: e.reciprocal(out=rows[0:1, R_ISXSW:R_ISXSW + 1],
                                             in_=rows[0:1, R_SXSW:R_SXSW + 1]))
            E.on("v", lambda e, nb=nb: e.tensor_scalar_mul(
                out=bsrow[0:1, 0:nb], in0=brow_t,
                scalar1=rows[0:1, R_ISXSW:R_ISXSW + 1]))
            E.on("v", lambda e: e.tensor_copy(out=rows[0:1, 16:17],
                                              in_=rows[0:1, R_ISX:R_ISX + 1]))
            E.on("v", lambda e: e.tensor_copy(out=rows[0:1, 17:18],
                                              in_=rows[0:1, R_LO:R_LO + 1]))
            E.on("v", lambda e: e.tensor_copy(out=rows[0:1, 18:19],
                                              in_=rows[0:1, R_HI:R_HI + 1]))
            vpk = E.chk("v", lambda e: e.tensor_copy(
                out=rows[0:1, 19:20], in_=rows[0:1, R_SXSW:R_SXSW + 1]))
            E.wait("t", "v", vpk)
            tb = E.chk("t", lambda e, ps_x=ps_x: e.matmul(
                ps_x[:, 0:4], onesrf[0:1, :], rows[0:1, 16:20], start=True, stop=True))
            E.wait("v", "t", tb)
            return E.chk("v", lambda e, ps_x=ps_x: e.tensor_copy(
                out=cols[:, 0:4], in_=ps_x[:, 0:4]))

        E.on("sy", lambda e: e.dma_start(out=b1["hbuf"][0][0:1, :], in_=b1row),
             inc=[("dx", 16)])
        dx_b1r = E.cnt["dx"]
        E.wait("v", "dx", dx_b1r)
        allreduce_and_scales(b1["psX"], R_SW1, b1["b1s"], b1["hbuf"][0][0:1, 0:IF], IF)

        for c in range(8):
            E.on("sy", lambda e, c=c: e.dma_start(
                out=b1["wq1T"][:, c * IF:(c + 1) * IF],
                in_=wq1h[:, c * 128:(c + 1) * 128], transpose=True), inc=[("dx", 16)])
        dx_w1t = E.cnt["dx"]

        psF, psT, psX = b1["psF"], b1["psT"], b1["psX"]
        E.on("v", lambda e: e.memset(b1["hmaxr"][:, :], -1e38))
        E.on("v", lambda e: e.memset(b1["hminr"][:, :], 1e38))
        E.wait("t", "dx", dx_w1t)
        ldb, v_yq, st_h, v_xqfm, t_tr1 = {}, {}, {}, {}, {}

        def b1_loads(j):
            if j >= nt:
                return
            E.wait("sy", "v", v_yq.get(j - 2, 0))
            E.on("sy", lambda e, j=j: e.dma_start(
                out=b1["oab"][j % 2][:, :],
                in_=oAh[row0(j):row0(j) + 128, :]), inc=[("dx", 16)])
            ldb[j] = E.cnt["dx"]

        def b1_quant(j):
            if j >= nt:
                return
            E.wait("v", "dx", ldb[j])
            E.wait("v", "t", t_tr1.get(j - 1, 0))
            E.on("v", lambda e, j=j: e.tensor_scalar(
                out=b1["y"][:, :], in0=b1["oab"][j % 2][:, :],
                scalar1=ymt[:, j:j + 1], scalar2=yinvt[:, j:j + 1],
                op0=ALU.subtract, op1=ALU.mult))
            E.on("v", lambda e: e.tensor_scalar_mul(
                out=b1["scr"][:, :], in0=b1["y"][:, :],
                scalar1=cols[:, C_ISX:C_ISX + 1]))
            E.on("v", lambda e: e.tensor_copy(out=b1["qi"][:, :], in_=b1["scr"][:, :]))
            v_yq[j] = E.chk("v", lambda e: e.tensor_scalar(
                out=b1["xqbf"][:, :], in0=b1["qi"][:, :],
                scalar1=cols[:, C_LO:C_LO + 1], scalar2=cols[:, C_HI:C_HI + 1],
                op0=ALU.max, op1=ALU.min))

        b1_loads(0)
        b1_loads(1)
        b1_quant(0)
        for i in range(nt):
            E.wait("t", "v", v_yq[i])
            E.wait("t", "v", v_xqfm.get(i - 1, 0))
            for j in range(8):
                E.on("t", lambda e, j=j: e.transpose(
                    psT[:, j * 128:(j + 1) * 128], b1["xqbf"][:, j * 128:(j + 1) * 128],
                    identb[:, :]))
            ttr = E.tend()
            t_tr1[i] = ttr
            E.wait("v", "t", ttr)
            vfm = E.chk("v", lambda e: e.tensor_copy(out=b1["xqfm"][:, :], in_=psT[:, :]))
            v_xqfm[i] = vfm
            b1_loads(i + 2)
            b1_quant(i + 1)
            E.wait("t", "v", vfm)
            E.wait("s", "dw", st_h.get(i - 2, 0))
            sg = 0
            for ocp in range(2):
                E.wait("t", "s", sg)
                for kc in range(8):
                    for oc2 in range(2):
                        oc = ocp * 2 + oc2
                        for hh2 in range(2):
                            E.on("t", lambda e, oc=oc, oc2=oc2, kc=kc, hh2=hh2:
                                 e.matmul(
                                 psF[:, oc2 * 1024 + hh2 * 512:
                                     oc2 * 1024 + (hh2 + 1) * 512],
                                 b1["xqfm"][:, kc * 128:(kc + 1) * 128],
                                 b1["wq1T"][:, kc * IF + oc * 1024 + hh2 * 512:
                                            kc * IF + oc * 1024 + (hh2 + 1) * 512],
                                 start=(kc == 0), stop=False))
                for oc2 in range(2):
                    oc = ocp * 2 + oc2
                    for hh2 in range(2):
                        E.on("t", lambda e, oc=oc, oc2=oc2, hh2=hh2: e.matmul(
                            psF[:, oc2 * 1024 + hh2 * 512:
                                oc2 * 1024 + (hh2 + 1) * 512],
                            onesrb[0:1, :],
                            b1["b1s"][0:1, oc * 1024 + hh2 * 512:
                                      oc * 1024 + (hh2 + 1) * 512],
                            start=False, stop=True))
                tmm = E.tend()
                E.wait("s", "t", tmm)
                for oc2 in range(2):
                    oc = ocp * 2 + oc2
                    sg = E.chk("s", lambda e, oc=oc, oc2=oc2, i=i: e.activation(
                        out=b1["hbuf"][i % 2][:, oc * 1024:(oc + 1) * 1024],
                        in_=psF[:, oc2 * 1024:(oc2 + 1) * 1024],
                        func=AF.Gelu, scale=cols[:, C_SXSW:C_SXSW + 1]))
                E.wait("v", "s", sg)
                for oc2 in range(2):
                    oc = ocp * 2 + oc2
                    E.on("v", lambda e, oc=oc, i=i: e.reduce_max(
                        out=st[:, S_RED:S_RED + 1],
                        in_=b1["hbuf"][i % 2][:, oc * 1024:(oc + 1) * 1024], axis=AX.X))
                    E.on("v", lambda e: e.tensor_max(out=b1["hmaxr"][:, :],
                                                     in0=b1["hmaxr"][:, :],
                                                     in1=st[:, S_RED:S_RED + 1]))
                    E.on("v", lambda e, oc=oc, i=i: e.tensor_reduce(
                        out=st[:, S_RED:S_RED + 1],
                        in_=b1["hbuf"][i % 2][:, oc * 1024:(oc + 1) * 1024], axis=AX.X,
                        op=ALU.min))
                    E.on("v", lambda e: e.tensor_tensor(
                        out=b1["hminr"][:, :], in0=b1["hminr"][:, :],
                        in1=st[:, S_RED:S_RED + 1], op=ALU.min))
            vred = E.chk("v", lambda e: e.engine_nop())
            E.wait("g", "v", vred)
            E.on("g", lambda e, i=i: e.dma_start(
                out=hh[row0(i):row0(i) + 128, :], in_=b1["hbuf"][i % 2][:, :]),
                inc=[("dw", 16)])
            if phase_limit == "b1":
                E.on("g", lambda e, i=i: e.dma_start(
                    out=out[row0(i):row0(i) + 128, :],
                    in_=b1["hbuf"][i % 2][:, (i % 4) * 1024:((i % 4) + 1) * 1024]),
                    inc=[("dw", 16)])
            st_h[i] = E.cnt["dw"]

        E.wait("t", "v", E.cnt["v"])
        tmx = E.chk("t", lambda e: e.transpose(psX[0:1, 0:128], b1["hmaxr"][:, :],
                                               identf[:, :]))
        E.wait("v", "t", tmx)
        vdone = E.chk("v", lambda e: e.reduce_max(out=ccsb[0:1, 0:1],
                                                  in_=psX[0:1, 0:128], axis=AX.X))
        E.wait("t", "v", vdone)
        tmn = E.chk("t", lambda e: e.transpose(psX[0:1, 0:128], b1["hminr"][:, :],
                                               identf[:, :]))
        E.wait("v", "t", tmn)
        E.on("v", lambda e: e.tensor_reduce(out=ccsb[0:1, 1:2], in_=psX[0:1, 0:128],
                                            axis=AX.X, op=ALU.min))
        E.on("v", lambda e: e.tensor_scalar_mul(out=ccsb[0:1, 1:2],
                                                in0=ccsb[0:1, 1:2], scalar1=-1.0))
        E.on("v", lambda e: e.memset(ccsb[0:1, 2:8], 0.0))

        E.barrier()
        if phase_limit == "b1":
            return

        E.on("sy", lambda e: e.dma_start(out=b2["hb"][0][0:1, 0:D], in_=b2row),
             inc=[("dx", 16)])
        dx_b2r = E.cnt["dx"]
        E.wait("v", "dx", dx_b2r)
        vsc2 = allreduce_and_scales(b2["psX"], R_SW2, b2["b2s"], b2["hb"][0][0:1, 0:D], D)
        for c in range(32):
            E.on("sy", lambda e, c=c: e.dma_start(
                out=b2["wq2T"][:, c * D:(c + 1) * D],
                in_=wq2h[:, c * 128:(c + 1) * 128], transpose=True), inc=[("dx", 16)])
        dx_w2t = E.cnt["dx"]
        psD, psX = b2["psD"], b2["psX"]
        psTs = [b2["psT"], b2["psT2"]]
        E.wait("t", "dx", dx_w2t)
        ldh, v_hq, st_o, v_ffn, v_out, t_trq = {}, {}, {}, {}, {}, {}
        E.wait("sy", "v", vsc2)

        def b2_loads(j):
            if j >= nt:
                return
            E.wait("sy", "v", v_out.get(j - 2, 0))
            E.on("sy", lambda e, j=j: e.dma_start(
                out=b2["hb"][j % 2][:, :],
                in_=hh[row0(j):row0(j) + 128, :]), inc=[("dx", 16)])
            E.on("sy", lambda e, j=j: e.dma_start(
                out=b2["oab"][j % 2][:, :],
                in_=oAh[row0(j):row0(j) + 128, :]), inc=[("dx", 16)])
            ldh[j] = E.cnt["dx"]

        def b2_quant(j):
            if j >= nt:
                return
            E.wait("v", "dx", ldh[j])
            E.wait("v", "t", t_trq.get(j - 1, 0))
            E.on("v", lambda e, j=j: e.tensor_scalar_mul(
                out=b2["scr"][:, :], in0=b2["hb"][j % 2][:, :],
                scalar1=cols[:, C_ISX:C_ISX + 1]))
            E.on("v", lambda e: e.tensor_copy(out=b2["qi"][:, :], in_=b2["scr"][:, :]))
            v_hq[j] = E.chk("v", lambda e: e.tensor_scalar(
                out=b2["hqbf"][:, :], in0=b2["qi"][:, :],
                scalar1=cols[:, C_LO:C_LO + 1], scalar2=cols[:, C_HI:C_HI + 1],
                op0=ALU.max, op1=ALU.min))

        b2_loads(0)
        b2_loads(1)
        b2_quant(0)
        for i in range(nt):
            E.wait("t", "v", v_hq[i])
            E.wait("t", "v", v_ffn.get(i - 1, 0))
            vfm_prev = {}
            ttr = 0

            def b2_tr(grp):
                for j in range(8):
                    E.on("t", lambda e, grp=grp, j=j: e.transpose(
                        psTs[grp % 2][:, j * 128:(j + 1) * 128],
                        b2["hqbf"][:, grp * 1024 + j * 128: grp * 1024 + (j + 1) * 128],
                        identb[:, :]))
                return E.tend()

            def b2_fm(grp, ttrg):
                E.wait("v", "t", ttrg)
                return E.chk("v", lambda e, grp=grp: e.tensor_copy(
                    out=b2["hqfm"][grp % 2][:, :], in_=psTs[grp % 2][:, :]))

            ttrs = {0: b2_tr(0)}
            vfms = {0: b2_fm(0, ttrs[0])}
            for grp in range(4):
                if grp + 1 < 4:
                    E.wait("t", "v", vfms.get(grp - 1, 0))
                    ttrs[grp + 1] = b2_tr(grp + 1)
                    vfms[grp + 1] = b2_fm(grp + 1, ttrs[grp + 1])
                E.wait("t", "v", vfms[grp])
                for j in range(8):
                    kc = grp * 8 + j
                    for hh2 in range(2):
                        E.on("t", lambda e, grp=grp, j=j, kc=kc, hh2=hh2: e.matmul(
                            psD[:, hh2 * 512:(hh2 + 1) * 512],
                            b2["hqfm"][grp % 2][:, j * 128:(j + 1) * 128],
                            b2["wq2T"][:, kc * D + hh2 * 512: kc * D + (hh2 + 1) * 512],
                            start=(kc == 0), stop=False))
            ttr = ttrs[3]
            t_trq[i] = ttr
            for hh2 in range(2):
                E.on("t", lambda e, hh2=hh2: e.matmul(
                    psD[:, hh2 * 512:(hh2 + 1) * 512], onesrb[0:1, :],
                    b2["b2s"][0:1, hh2 * 512:(hh2 + 1) * 512], start=False, stop=True))
            tmm = E.tend()
            b2_quant(i + 1)
            E.wait("s", "t", tmm)
            sffn = E.chk("s", lambda e: e.activation(
                out=b2["fbuf"][:, :], in_=psD[:, :], func=AF.Copy,
                scale=cols[:, C_SXSW:C_SXSW + 1]))
            E.wait("v", "s", sffn)
            v_ffn[i] = E.chk("v", lambda e: e.engine_nop())
            E.on("v", lambda e: e.reduce_sum(out=st[:, S_SUM:S_SUM + 1],
                                             in_=b2["fbuf"][:, :], axis=AX.X))
            E.on("v", lambda e: e.tensor_scalar_mul(
                out=st[:, S_M:S_M + 1], in0=st[:, S_SUM:S_SUM + 1], scalar1=1.0 / D))
            E.on("v", lambda e: e.tensor_mul(out=st[:, S_MSQ:S_MSQ + 1],
                                             in0=st[:, S_M:S_M + 1],
                                             in1=st[:, S_M:S_M + 1]))
            vb = E.chk("v", lambda e: e.tensor_scalar(
                out=st[:, S_BIAS:S_BIAS + 1], in0=st[:, S_MSQ:S_MSQ + 1],
                scalar1=-1.0, scalar2=EPS_LN, op0=ALU.mult, op1=ALU.add))
            E.wait("s", "v", v_hq.get(i + 1, 0))
            E.on("s", lambda e: e.activation(out=b2["scr"][:, 0:D], in_=b2["fbuf"][:, :],
                                             func=AF.Square,
                                             accum_out=st[:, S_SQ:S_SQ + 1]))
            E.wait("s", "v", vb)
            E.on("s", lambda e: e.activation(
                out=st[:, S_LNV:S_LNV + 1], in_=st[:, S_SQ:S_SQ + 1], func=AF.Ln,
                bias=st[:, S_BIAS:S_BIAS + 1], scale=1.0 / D))
            sinv = E.chk("s", lambda e: e.activation(
                out=st[:, S_INV1:S_INV1 + 1], in_=st[:, S_LNV:S_LNV + 1],
                func=AF.Exp, scale=-0.5))
            E.wait("v", "s", sinv)
            E.on("v", lambda e: e.tensor_scalar(
                out=b2["scr"][:, 0:D], in0=b2["fbuf"][:, :], scalar1=st[:, S_M:S_M + 1],
                scalar2=st[:, S_INV1:S_INV1 + 1], op0=ALU.subtract, op1=ALU.mult))
            E.wait("v", "ds", st_o.get(i - 2, 0))
            vo = E.chk("v", lambda e, i=i: e.tensor_add(
                out=b2["ob"][i % 2][:, :], in0=b2["scr"][:, 0:D],
                in1=b2["oab"][i % 2][:, :]))
            v_out[i] = vo
            E.wait("s", "v", vo)
            E.on("s", lambda e, i=i: e.dma_start(
                out=out[row0(i):row0(i) + 128, :], in_=b2["ob"][i % 2][:, :]),
                inc=[("ds", 16)])
            st_o[i] = E.cnt["ds"]
            b2_loads(i + 2)

        E.barrier()

    with nc.Block() as block:
        for cur in ENGS:
            def body(eng, cur=cur):
                sched(Env(cur, eng, sems))
            getattr(block, _BLOCK_NAME[cur])(body)

    _free(opened_global)
    return nc


def _consts():
    c = {}
    c["identb"] = np.eye(128, dtype=BF)
    id2 = np.zeros((128, 64), dtype=np.float32)
    for p in range(128):
        id2[p, p % 64] = 1.0
    c["ident2b"] = id2.astype(BF)
    c["identf"] = np.eye(128, dtype=np.float32)
    sc = np.arange(64)
    pc = np.arange(128) % 64
    lm = np.where(pc[:, None] <= sc[None, :], np.float32(-1.0 / 16.0), np.float32(0.0))
    c["lmask"] = lm.astype(BF)
    mt = np.where(pc[:, None] <= sc[None, :], np.float32(1.0), np.float32(0.0))
    c["maskt"] = np.tile(mt, (1, 4)).astype(np.float32)
    c["cones"] = np.full((128, 1), -1.0 / 16.0, dtype=BF)
    c["onesrf"] = np.ones((1, 128), dtype=np.float32)
    c["onesrb"] = np.ones((1, 128), dtype=BF)
    return c


def _shard(inputs):
    f32 = np.float32
    hs = np.ascontiguousarray(np.asarray(inputs["L0_attn_hidden_states"], dtype=f32))
    base = {
        "Wq": np.ascontiguousarray(inputs["L0_attn_Wq"], dtype=f32),
        "Wk": np.ascontiguousarray(inputs["L0_attn_Wk"], dtype=f32),
        "Wv": np.ascontiguousarray(inputs["L0_attn_Wv"], dtype=f32),
        "Wg": np.ascontiguousarray(inputs["L0_attn_Wg"], dtype=f32),
        "Wgk1": np.ascontiguousarray(inputs["L0_attn_Wgk1"], dtype=f32),
        "Wgk2": np.ascontiguousarray(inputs["L0_attn_Wgk2"], dtype=f32),
        "Wo": np.ascontiguousarray(inputs["L0_attn_Wo"], dtype=f32),
        "Wgate": np.ascontiguousarray(inputs["L0_attn_W_gate"], dtype=f32),
        "Wup": np.ascontiguousarray(inputs["L0_attn_W_up"], dtype=f32),
        "Wdown": np.ascontiguousarray(inputs["L0_attn_W_down"], dtype=f32),
        "W1": np.ascontiguousarray(inputs["L0_ffn_W1"], dtype=f32),
        "W2": np.ascontiguousarray(inputs["L0_ffn_W2"], dtype=f32),
        "gam_attn": np.ascontiguousarray(np.broadcast_to(
            np.asarray(inputs["L0_attn_attn_norm_w"], dtype=f32)[None, :], (128, D))),
        "bgk2b": np.ascontiguousarray(np.broadcast_to(
            np.asarray(inputs["L0_attn_bgk2"], dtype=f32)[None, :], (128, DKT))),
        "gamg": np.ascontiguousarray(np.broadcast_to(
            np.tile(np.asarray(inputs["L0_attn_gnorm_w"], dtype=f32), 4)[None, :],
            (128, DVT))),
        "gam_mlp": np.ascontiguousarray(np.broadcast_to(
            np.asarray(inputs["L0_attn_mlp_norm_w"], dtype=f32)[None, :], (128, D))),
        "b1row": np.ascontiguousarray(
            np.asarray(inputs["L0_ffn_b1"], dtype=f32)[None, :]),
        "b2row": np.ascontiguousarray(
            np.asarray(inputs["L0_ffn_b2"], dtype=f32)[None, :]),
        **_consts(),
    }
    in_maps = []
    for c in range(NCORES):
        m = dict(base)
        m["x"] = np.ascontiguousarray(hs[c * BC:(c + 1) * BC].reshape(BC * T, D))
        in_maps.append(m)
    return in_maps


def kernel(**inputs):
    in_maps = _shard(inputs)
    nc = _build("full")
    res = run_bass_kernel_spmd(nc, in_maps, core_ids=list(range(NCORES)))
    outs = [np.asarray(r["out"]).reshape(BC, T, D) for r in res.results]
    return np.concatenate(outs, axis=0)


# version 11
# speedup vs baseline: 1.1153x; 1.0338x over previous
import numpy as np
import ml_dtypes
import concourse.bass as bass
import concourse.mybir as mybir
from concourse.bass_utils import run_bass_kernel_spmd

F32 = mybir.dt.float32
BF16 = mybir.dt.bfloat16
I32 = mybir.dt.int32
AF = mybir.ActivationFunctionType
ALU = mybir.AluOpType
AX = mybir.AxisListType

NCORES = 8
BC = 4
T = 2048
D = 1024
NT = BC * T // 128
TPS = T // 128
NH = 4
DKT, DVT = 512, 1024
IA = 2816
IF = 4096
EPS_RMS = 1e-6
EPS_LN = 1e-5
LN_SCALE_Q = float(np.log(np.float32(128.0) ** np.float32(-0.5)))
BF = ml_dtypes.bfloat16

ENGS = ("sy", "t", "v", "s", "g")
_BLOCK_NAME = {"sy": "sync", "t": "tensor", "v": "vector", "s": "scalar", "g": "gpsimd"}
QUEUES = ("dx", "ds", "dw")
SEMS = ("sy", "t", "v", "s", "g", "cc") + tuple(
    q + p for q in QUEUES for p in ("0", "1"))


class Env:

    def __init__(self, cur, eng, sems):
        self.cur = cur
        self.eng = eng
        self.sems = sems
        self.cnt = {k: 0 for k in SEMS}
        for q in QUEUES:
            self.cnt[q] = 0
        self.qpar = {q: 0 for q in QUEUES}
        self.prev_op = {}
        self.last_instr = {}
        self.qat = {q: {0: {}} for q in QUEUES}

    def on(self, ename, fn, inc=None, is_drain=False):
        if ename in ("v", "s") and not is_drain and self.prev_op.get(ename):
            self.prev_op[ename] = False
            self.on(ename, lambda e: e.drain(), is_drain=True)
        real = []
        if inc is not None:
            for s_, a_ in inc:
                if s_ in QUEUES:
                    p = self.qpar[s_]
                    self.qpar[s_] ^= 1
                    sem = s_ + str(p)
                    self.wait(ename, sem, self.cnt[sem])
                    real.append((sem, a_))
                else:
                    real.append((s_, a_))
        if ename == self.cur:
            instr = fn(self.eng)
            for s_, a_ in real:
                instr.then_inc(self.sems[s_], a_)
            if not is_drain:
                self.last_instr[ename] = instr
        for s_, a_ in real:
            self.cnt[s_] += a_
        if inc is not None:
            for s_, a_ in inc:
                if s_ in QUEUES:
                    self.cnt[s_] += a_
                    self.qat[s_][self.cnt[s_]] = {
                        s_ + "0": self.cnt[s_ + "0"], s_ + "1": self.cnt[s_ + "1"]}
        if not is_drain:
            self.prev_op[ename] = True

    def wait(self, ename, sem, val):
        if val <= 0:
            return
        if sem in QUEUES:
            snap = self.qat[sem].get(val)
            assert snap is not None, f"wait on unrecorded {sem} count {val}"
            for s_, v_ in snap.items():
                self.wait(ename, s_, v_)
            return
        if ename == self.cur:
            self.eng.wait_ge(self.sems[sem], val)


    def chk(self, ename, fn, sem=None):
        sem = sem or ename
        self.on(ename, fn, inc=[(sem, 1)])
        return self.cnt[sem]

    def lastinc(self, ename):
        if ename == self.cur:
            self.last_instr[ename].then_inc(self.sems[ename], 1)
        self.cnt[ename] += 1
        return self.cnt[ename]

    def tend(self):
        self.on("t", lambda e: e.drain(), is_drain=True)
        return self.chk("t", lambda e: e.nop())

    def barrier(self):
        self.wait("sy", "dx", self.cnt["dx"])
        self.wait("s", "ds", self.cnt["ds"])
        self.wait("g", "dw", self.cnt["dw"])
        snaps = {}
        for e in ENGS:
            self.wait(e, e, self.cnt[e])
            snaps[e] = self.chk(e, lambda eng: eng.nop())
        for e in ENGS:
            for o in ENGS:
                if o != e:
                    self.wait(e, o, snaps[o])


def _alloc(nc, opened, kind, *args, **kw):
    cm = getattr(nc, kind)(*args, **kw)
    h = cm.__enter__()
    opened.append(cm)
    return h


def _free(opened):
    for cm in reversed(opened):
        cm.__exit__(None, None, None)
    opened.clear()


def _build(phase_limit="full", nt=NT):
    nc = bass.Bass("TRN2", target_bir_lowering=False, debug=False)

    def din(name, shape):
        return nc.dram_tensor(name, shape, F32, kind="ExternalInput").ap()

    def dinb(name, shape):
        return nc.dram_tensor(name, shape, BF16, kind="ExternalInput").ap()

    x = din("x", [BC * T, D])
    Wq, Wk = din("Wq", [D, DKT]), din("Wk", [D, DKT])
    Wv, Wg = din("Wv", [D, DVT]), din("Wg", [D, DVT])
    Wgk1, Wgk2 = din("Wgk1", [D, 16]), din("Wgk2", [16, DKT])
    Wo = din("Wo", [DVT, D])
    Wgate, Wup = din("Wgate", [D, IA]), din("Wup", [D, IA])
    Wdown = din("Wdown", [IA, D])
    W1, W2 = din("W1", [IF, D]), din("W2", [D, IF])
    gam_attn = din("gam_attn", [128, D])
    bgk2b = din("bgk2b", [128, DKT])
    gamg = din("gamg", [128, DVT])
    gam_mlp = din("gam_mlp", [128, D])
    b1row = din("b1row", [1, IF])
    b2row = din("b2row", [1, D])
    identb_d = dinb("identb", [128, 128])
    ident2b_d = dinb("ident2b", [128, 64])
    identf_d = din("identf", [128, 128])
    lmask_d = dinb("lmask", [128, 64])
    maskt_d = din("maskt", [128, 256])
    cones_d = dinb("cones", [128, 1])
    onesrf_d = din("onesrf", [1, 128])
    onesrb_d = dinb("onesrb", [1, 128])

    x2h = nc.dram_tensor("x2h", [BC * T, D], F32).ap()
    oAh = nc.dram_tensor("oAh", [BC * T, D], F32).ap()
    hh = nc.dram_tensor("hh", [BC * T, IF], F32).ap()
    wq1h = nc.dram_tensor("wq1h", [IF, D], BF16).ap()
    wq2h = nc.dram_tensor("wq2h", [D, IF], BF16).ap()
    ccin = nc.dram_tensor("ccin", [1, 8], F32)
    ccout = nc.dram_tensor("ccout", [1, 8], F32)
    out = nc.dram_tensor("out", [BC * T, D], F32, kind="ExternalOutput").ap()

    opened_global = []
    opened_phase = []

    def ga(kind, *a, **k):
        return _alloc(nc, opened_global, kind, *a, **k)

    def pa(kind, *a, **k):
        return _alloc(nc, opened_phase, kind, *a, **k)

    sems = {n: ga("semaphore", "sem_" + n) for n in SEMS}
    identb = ga("sbuf_tensor", "identb_t", [128, 128], BF16)
    ident2 = ga("sbuf_tensor", "ident2_t", [128, 64], BF16)
    identf = ga("sbuf_tensor", "identf_t", [128, 128], F32)
    lmask = ga("sbuf_tensor", "lmask_t", [128, 64], BF16)
    maskt = ga("sbuf_tensor", "maskt_t", [128, 256], F32)
    cones = ga("sbuf_tensor", "cones_t", [128, 1], BF16)
    onesrf = ga("sbuf_tensor", "onesrf_t", [1, 128], F32)
    onesrb = ga("sbuf_tensor", "onesrb_t", [1, 128], BF16)
    st = ga("sbuf_tensor", "st", [128, 32], F32)
    cols = ga("sbuf_tensor", "cols", [128, 16], F32)
    rows = ga("sbuf_tensor", "rows", [1, 32], F32)
    rowi = ga("sbuf_tensor", "rowi", [1, 4], I32)
    ymt = ga("sbuf_tensor", "ymt", [128, NT], F32)
    yinvt = ga("sbuf_tensor", "yinvt", [128, NT], F32)
    ccsb = ga("sbuf_tensor", "ccsb", [1, 8], F32)

    (R_AMAX, R_NAMIN, R_D, R_SX, R_ISX, R_ZT, R_ZP, R_LO, R_HI, R_SXSW, R_ISXSW,
     R_SW1, R_SW2, R_WMAX, R_T0, R_T1) = range(16)
    C_ISX, C_LO, C_HI, C_SXSW = 0, 1, 2, 3
    C_ONE, C_EPSR, C_LNSQ = 9, 10, 11
    (S_SUM, S_SQ, S_M, S_MSQ, S_BIAS, S_LNV, S_INV1, S_SQ2, S_INV2) = range(9)
    S_G0 = 10
    S_GI = 14
    S_RED = 20
    S_SUM2, S_MSQ2, S_BIAS2 = 24, 25, 26
    S_RED2 = 22
    S_FSQ, S_FLN, S_FINV = 28, 29, 30

    def row0(i):
        return 128 * i

    a1 = {}
    a1["wq"] = pa("sbuf_tensor", "a1_wq", [128, 8 * DKT], BF16)
    a1["wk"] = pa("sbuf_tensor", "a1_wk", [128, 8 * DKT], BF16)
    a1["wv"] = pa("sbuf_tensor", "a1_wv", [128, 8 * DVT], BF16)
    a1["wg"] = pa("sbuf_tensor", "a1_wg", [128, 8 * DVT], BF16)
    a1["wgk1"] = pa("sbuf_tensor", "a1_wgk1", [128, 8 * 16], BF16)
    a1["wgk2"] = pa("sbuf_tensor", "a1_wgk2", [16, DKT], BF16)
    a1["wo"] = pa("sbuf_tensor", "a1_wo", [128, 8 * D], BF16)
    a1["gat"] = pa("sbuf_tensor", "a1_gat", [128, D], F32)
    a1["bgk"] = pa("sbuf_tensor", "a1_bgk", [128, DKT], F32)
    a1["gag"] = pa("sbuf_tensor", "a1_gag", [128, DVT], F32)
    a1["xb"] = [pa("sbuf_tensor", f"a1_xb{j}", [128, D], F32) for j in range(2)]
    a1["xin"] = [pa("sbuf_tensor", f"a1_xin{j}", [128, D], F32) for j in range(2)]
    a1["x2b"] = [pa("sbuf_tensor", f"a1_x2b{j}", [128, D], F32) for j in range(2)]
    a1["hbf"] = pa("sbuf_tensor", "a1_hbf", [128, D], BF16)
    a1["hfm"] = pa("sbuf_tensor", "a1_hfm", [128, D], BF16)
    a1["q"] = pa("sbuf_tensor", "a1_q", [128, DKT], F32)
    a1["k"] = pa("sbuf_tensor", "a1_k", [128, DKT], F32)
    a1["vbf"] = pa("sbuf_tensor", "a1_vbf", [128, DVT], BF16)
    a1["g"] = pa("sbuf_tensor", "a1_g", [128, DVT], F32)
    a1["zg"] = pa("sbuf_tensor", "a1_zg", [128, DKT], F32)
    a1["gbf"] = pa("sbuf_tensor", "a1_gbf", [128, DKT], BF16)
    a1["expq"] = pa("sbuf_tensor", "a1_expq", [128, DKT], F32)
    a1["expm"] = pa("sbuf_tensor", "a1_expm", [128, DKT], F32)
    a1["qebf"] = pa("sbuf_tensor", "a1_qebf", [128, DKT], BF16)
    a1["kebf"] = pa("sbuf_tensor", "a1_kebf", [128, DKT], BF16)
    a1["qkfm"] = pa("sbuf_tensor", "a1_qkfm", [128, 2 * DKT], BF16)
    a1["atm"] = pa("sbuf_tensor", "a1_atm", [128, 256], BF16)
    a1["ebl"] = pa("sbuf_tensor", "a1_ebl", [128, 8], F32)
    a1["Sf"] = pa("sbuf_tensor", "a1_Sf", [128, DVT], F32)
    a1["Sb"] = pa("sbuf_tensor", "a1_Sb", [128, DVT], BF16)
    a1["on"] = pa("sbuf_tensor", "a1_on", [128, DVT], F32)
    a1["o2bf"] = pa("sbuf_tensor", "a1_o2bf", [128, DVT], BF16)
    a1["o2fm"] = pa("sbuf_tensor", "a1_o2fm", [128, DVT], BF16)
    a1["scrA"] = pa("sbuf_tensor", "a1_scrA", [128, D], F32)
    a1["scrB"] = pa("sbuf_tensor", "a1_scrB", [128, D], F32)
    a1["t1bf"] = pa("sbuf_tensor", "a1_t1bf", [128, 16], BF16)
    a1["t1fm"] = pa("sbuf_tensor", "a1_t1fm", [16, 128], BF16)
    a1["wbuf"] = [pa("sbuf_tensor", f"a1_wbuf{j}", [128, D], F32) for j in range(2)]
    a1["wqi"] = pa("sbuf_tensor", "a1_wqi", [128, D], I32)
    a1["wqo"] = pa("sbuf_tensor", "a1_wqo", [128, D], BF16)
    a1["psA"] = pa("psum_tensor", [128, 1024], F32)
    a1["psB"] = pa("psum_tensor", [128, 1024], F32)
    a1["psC"] = pa("psum_tensor", [128, 1024], F32)
    a1["psT"] = pa("psum_tensor", [128, 1024], BF16)
    a1["psX"] = pa("psum_tensor", [128, 512], F32)
    a1_opened = list(opened_phase)
    opened_phase.clear()
    _free(a1_opened)

    a2 = {}
    a2["wgate"] = pa("sbuf_tensor", "a2_wgate", [128, 8 * IA], BF16)
    a2["wup"] = pa("sbuf_tensor", "a2_wup", [128, 8 * IA], BF16)
    a2["wdown"] = pa("sbuf_tensor", "a2_wdown", [128, 22 * D], BF16)
    a2["gam"] = pa("sbuf_tensor", "a2_gam", [128, D], F32)
    a2["x2b"] = [pa("sbuf_tensor", f"a2_x2b{j}", [128, D], F32) for j in range(2)]
    a2["xb"] = [pa("sbuf_tensor", f"a2_xb{j}", [128, D], F32) for j in range(2)]
    a2["oab"] = [pa("sbuf_tensor", f"a2_oab{j}", [128, D], F32) for j in range(2)]
    a2["nbf"] = pa("sbuf_tensor", "a2_nbf", [128, D], BF16)
    a2["nfm"] = pa("sbuf_tensor", "a2_nfm", [128, D], BF16)
    a2["gate"] = [pa("sbuf_tensor", f"a2_gate{j}", [128, 1408], F32)
                  for j in range(2)]
    a2["scrA"] = pa("sbuf_tensor", "a2_scrA", [128, 1408], F32)
    a2["scrB"] = pa("sbuf_tensor", "a2_scrB", [128, 1408], F32)
    a2["scrC"] = pa("sbuf_tensor", "a2_scrC", [128, 1408], F32)
    a2["mlpb"] = [pa("sbuf_tensor", f"a2_mlpb{j}", [128, 1408], BF16)
                  for j in range(2)]
    a2["mlpfm"] = pa("sbuf_tensor", "a2_mlpfm", [128, 1408], BF16)
    a2["u"] = pa("sbuf_tensor", "a2_u", [128, D], F32)
    a2["ymaxr"] = pa("sbuf_tensor", "a2_ymaxr", [128, 1], F32)
    a2["yminr"] = pa("sbuf_tensor", "a2_yminr", [128, 1], F32)
    a2["psG"] = pa("psum_tensor", [128, 1536], F32)
    a2["psD"] = pa("psum_tensor", [128, 1024], F32)
    a2["psT"] = pa("psum_tensor", [128, 2048], BF16)
    a2["psX"] = pa("psum_tensor", [128, 512], F32)
    a2_opened = list(opened_phase)
    opened_phase.clear()
    _free(a2_opened)

    b1 = {}
    b1["wq1T"] = pa("sbuf_tensor", "b1_wq1T", [128, 8 * IF], BF16)
    b1["oab"] = [pa("sbuf_tensor", f"b1_oab{j}", [128, D], F32) for j in range(2)]
    b1["hbuf"] = [pa("sbuf_tensor", f"b1_hbuf{j}", [128, IF], F32) for j in range(2)]
    b1["y"] = pa("sbuf_tensor", "b1_y", [128, D], F32)
    b1["scr"] = pa("sbuf_tensor", "b1_scr", [128, D], F32)
    b1["qi"] = pa("sbuf_tensor", "b1_qi", [128, D], I32)
    b1["xqbf"] = pa("sbuf_tensor", "b1_xqbf", [128, D], BF16)
    b1["xqfm"] = pa("sbuf_tensor", "b1_xqfm", [128, D], BF16)
    b1["hmaxr"] = pa("sbuf_tensor", "b1_hmaxr", [128, 1], F32)
    b1["hminr"] = pa("sbuf_tensor", "b1_hminr", [128, 1], F32)
    b1["b1s"] = pa("sbuf_tensor", "b1_b1s", [1, IF], BF16)
    b1["psF"] = pa("psum_tensor", [128, 2048], F32)
    b1["psT"] = pa("psum_tensor", [128, 1024], BF16)
    b1["psX"] = pa("psum_tensor", [128, 512], F32)
    b1_opened = list(opened_phase)
    opened_phase.clear()
    _free(b1_opened)

    b2 = {}
    b2["wq2T"] = pa("sbuf_tensor", "b2_wq2T", [128, 32 * D], BF16)
    b2["hb"] = [pa("sbuf_tensor", f"b2_hb{j}", [128, IF], F32) for j in range(2)]
    b2["oab"] = [pa("sbuf_tensor", f"b2_oab{j}", [128, D], F32) for j in range(2)]
    b2["ob"] = [pa("sbuf_tensor", f"b2_ob{j}", [128, D], F32) for j in range(2)]
    b2["scr"] = pa("sbuf_tensor", "b2_scr", [128, IF], F32)
    b2["qi"] = pa("sbuf_tensor", "b2_qi", [128, IF], I32)
    b2["hqbf"] = pa("sbuf_tensor", "b2_hqbf", [128, IF], BF16)
    b2["hqfm"] = [pa("sbuf_tensor", f"b2_hqfm{j}", [128, D], BF16)
                  for j in range(2)]
    b2["fbuf"] = pa("sbuf_tensor", "b2_fbuf", [128, D], F32)
    b2["b2s"] = pa("sbuf_tensor", "b2_b2s", [1, D], BF16)
    b2["psD"] = pa("psum_tensor", [128, 1024], F32)
    b2["psT"] = pa("psum_tensor", [128, 1024], BF16)
    b2["psT2"] = pa("psum_tensor", [128, 1024], BF16)
    b2["psX"] = pa("psum_tensor", [128, 512], F32)
    b2_opened = list(opened_phase)
    opened_phase.clear()
    _free(b2_opened)

    def sched(env: Env):
        E = env
        E.on("v", lambda e: e.memset(cols[:, C_ONE:C_ONE + 1], 1.0))
        E.on("v", lambda e: e.memset(cols[:, C_EPSR:C_EPSR + 1], EPS_RMS))
        E.on("v", lambda e: e.memset(cols[:, C_LNSQ:C_LNSQ + 1], LN_SCALE_Q))

        for dst, src in ((identb, identb_d), (ident2, ident2b_d), (identf, identf_d),
                         (lmask, lmask_d), (maskt, maskt_d), (cones, cones_d),
                         (onesrf, onesrf_d), (onesrb, onesrb_d),
                         (a1["gat"], gam_attn), (a1["bgk"], bgk2b), (a1["gag"], gamg)):
            E.on("sy", lambda e, d=dst, s=src: e.dma_start(out=d[:, :], in_=s),
                 inc=[("dx", 16)])
        for wsb, wdr, ncol in ((a1["wq"], Wq, DKT), (a1["wk"], Wk, DKT),
                               (a1["wv"], Wv, DVT), (a1["wg"], Wg, DVT),
                               (a1["wgk1"], Wgk1, 16), (a1["wo"], Wo, D)):
            for c in range(8):
                E.on("g", lambda e, d=wsb, s=wdr, c=c, n=ncol:
                     e.dma_start(out=d[:, c * n:(c + 1) * n],
                                 in_=s[c * 128:(c + 1) * 128, :]),
                     inc=[("dw", 16)])
        E.on("g", lambda e: e.dma_start(out=a1["wgk2"][:, :], in_=Wgk2),
             inc=[("dw", 16)])
        dw_a1w = E.cnt["dw"]
        dx_setup = E.cnt["dx"]

        def wprep(slices, wqh_slices, r_sw, idx):
            nchunks = len(slices)
            wp_free, wp2_free, wq_st = {}, {}, {}
            ld = {}

            def emit_load(ldmap, c):
                E.on("sy", lambda e, c=c: e.dma_start(
                    out=a1["wbuf"][c % 2][:, :], in_=slices[c]), inc=[("dx", 16)])
                ldmap[c] = E.cnt["dx"]

            E.wait("sy", "v", E.cnt["v"])
            emit_load(ld, 0)
            emit_load(ld, 1)
            for c in range(nchunks):
                E.wait("v", "dx", ld[c])
                E.on("v", lambda e, c=c: e.reduce_max(
                    out=st[:, S_RED:S_RED + 1], in_=a1["wbuf"][c % 2][:, :],
                    axis=AX.X, apply_absolute_value=True))
                if c == 0:
                    v = E.chk("v", lambda e: e.tensor_copy(
                        out=st[:, S_RED + 1:S_RED + 2], in_=st[:, S_RED:S_RED + 1]))
                else:
                    v = E.chk("v", lambda e: e.tensor_max(
                        out=st[:, S_RED + 1:S_RED + 2], in0=st[:, S_RED + 1:S_RED + 2],
                        in1=st[:, S_RED:S_RED + 1]))
                wp_free[c] = v
                if c + 2 < nchunks:
                    E.wait("sy", "v", wp_free[c])
                    emit_load(ld, c + 2)
            E.wait("t", "v", wp_free[nchunks - 1])
            t = E.chk("t", lambda e: e.transpose(
                a1["psX"][0:1, idx * 128:(idx + 1) * 128],
                st[:, S_RED + 1:S_RED + 2], identf[:, :]))
            E.wait("v", "t", t)
            E.on("v", lambda e: e.reduce_max(
                out=rows[0:1, R_WMAX:R_WMAX + 1],
                in_=a1["psX"][0:1, idx * 128:(idx + 1) * 128], axis=AX.X))
            E.on("v", lambda e: e.tensor_scalar(
                out=rows[0:1, r_sw:r_sw + 1], in0=rows[0:1, R_WMAX:R_WMAX + 1],
                scalar1=1.0 / 127.0, scalar2=1e-8, op0=ALU.mult, op1=ALU.max))
            vsw = E.chk("v", lambda e: e.reciprocal(
                out=rows[0:1, R_T0 + idx:R_T0 + idx + 1],
                in_=rows[0:1, r_sw:r_sw + 1]))
            E.wait("t", "v", vsw)
            t = E.chk("t", lambda e: e.matmul(
                a1["psX"][:, idx:idx + 1], onesrf[0:1, :],
                rows[0:1, R_T0 + idx:R_T0 + idx + 1], start=True, stop=True))
            E.wait("v", "t", t)
            vbc = E.chk("v", lambda e: e.tensor_copy(out=cols[:, 8:9],
                                                     in_=a1["psX"][:, idx:idx + 1]))
            ld2 = {}
            E.wait("sy", "v", wp_free[nchunks - 1])
            emit_load(ld2, 0)
            emit_load(ld2, 1)
            for c in range(nchunks):
                E.wait("v", "dx", ld2[c])
                E.wait("v", "dw", wq_st.get(c - 1, 0))
                E.on("v", lambda e, c=c: e.tensor_scalar_mul(
                    out=a1["wbuf"][c % 2][:, :], in0=a1["wbuf"][c % 2][:, :],
                    scalar1=cols[:, 8:9]))
                E.on("v", lambda e, c=c: e.tensor_copy(
                    out=a1["wqi"][:, :], in_=a1["wbuf"][c % 2][:, :]))
                v = E.chk("v", lambda e, c=c: e.tensor_scalar(
                    out=a1["wqo"][:, :], in0=a1["wqi"][:, :],
                    scalar1=-127.0, scalar2=127.0, op0=ALU.max, op1=ALU.min))
                wp2_free[c] = v
                E.wait("g", "v", v)
                E.on("g", lambda e, c=c: e.dma_start(
                    out=wqh_slices[c], in_=a1["wqo"][:, :]), inc=[("dw", 16)])
                wq_st[c] = E.cnt["dw"]
                if c + 2 < nchunks:
                    E.wait("sy", "v", wp2_free[c])
                    emit_load(ld2, c + 2)

        w1_src = [W1[c * 128:(c + 1) * 128, :] for c in range(32)]
        w1_dst = [wq1h[c * 128:(c + 1) * 128, :] for c in range(32)]
        wprep(w1_src, w1_dst, R_SW1, 0)
        E.wait("v", "dw", E.cnt["dw"])
        w2_src = [W2[r * 128:(r + 1) * 128, cc * D:(cc + 1) * D]
                  for r in range(8) for cc in range(4)]
        w2_dst = [wq2h[r * 128:(r + 1) * 128, cc * D:(cc + 1) * D]
                  for r in range(8) for cc in range(4)]
        wprep(w2_src, w2_dst, R_SW2, 1)

        E.barrier()

        psA, psB, psC, psT, psX = (a1[k] for k in ("psA", "psB", "psC", "psT", "psX"))
        v_xin, s_sq, v_x2, st_x2, ld_x = {}, {}, {}, {}, {}
        v_hbf, v_o2fm, v_on, v_zb, v_atm1, v_o2 = {}, {}, {}, {}, {}, {}
        v_supd = 0
        vatm_prev = 0
        vqkfm_0 = 0
        sexp_last, supd_last, sebl_last, t_o_last = {}, {}, {}, {}

        E.on("sy", lambda e: e.dma_start(out=a1["xb"][0][:, :], in_=x[0:128, :]),
             inc=[("dx", 16)])
        ld_x[0] = E.cnt["dx"]

        def emit_wo_x2(i):
            E.wait("t", "v", v_o2fm[i])
            for kc in range(8):
                for half in range(2):
                    E.on("t", lambda e, kc=kc, half=half: e.matmul(
                        psB[:, half * 512:(half + 1) * 512],
                        a1["o2fm"][:, kc * 128:(kc + 1) * 128],
                        a1["wo"][:, kc * D + half * 512: kc * D + (half + 1) * 512],
                        start=(kc == 0), stop=(kc == 7)))
            two = E.tend()
            E.wait("v", "t", two)
            E.wait("v", "ds", st_x2.get(i - 2, 0))
            vx2 = E.chk("v", lambda e, i=i: e.tensor_add(
                out=a1["x2b"][i % 2][:, :], in0=psB[:, :], in1=a1["xin"][i % 2][:, :]))
            v_x2[i] = vx2
            E.wait("s", "v", vx2)
            E.on("s", lambda e, i=i: e.dma_start(
                out=x2h[row0(i):row0(i) + 128, :], in_=a1["x2b"][i % 2][:, :]),
                inc=[("ds", 16)])
            if phase_limit == "a1":
                E.on("s", lambda e, i=i: e.dma_start(
                    out=out[row0(i):row0(i) + 128, :], in_=a1["x2b"][i % 2][:, :]),
                    inc=[("ds", 16)])
            st_x2[i] = E.cnt["ds"]

        def a1_front(j):
            if j >= nt:
                return
            if j + 1 < nt:
                E.wait("sy", "v", v_xin.get(j - 1, 0))
                E.wait("sy", "s", s_sq.get(j - 1, 0))
                E.on("sy", lambda e, j=j: e.dma_start(
                    out=a1["xb"][(j + 1) % 2][:, :],
                    in_=x[row0(j + 1):row0(j + 1) + 128, :]), inc=[("dx", 16)])
                ld_x[j + 1] = E.cnt["dx"]
            if j % TPS == 0:
                E.wait("v", "t", t_o_last.get(j - 1, 0))
                E.on("v", lambda e: e.memset(a1["Sf"][:, :], 0.0))
                E.on("v", lambda e: e.memset(a1["Sb"][:, :], 0.0))
            E.wait("v", "dx", ld_x[j])
            E.on("v", lambda e, j=j: e.reduce_sum(
                out=st[:, S_SUM:S_SUM + 1], in_=a1["xb"][j % 2][:, :], axis=AX.X))
            E.on("v", lambda e: e.tensor_scalar_mul(
                out=st[:, S_M:S_M + 1], in0=st[:, S_SUM:S_SUM + 1], scalar1=1.0 / D))
            E.on("v", lambda e: e.tensor_mul(out=st[:, S_MSQ:S_MSQ + 1],
                                             in0=st[:, S_M:S_M + 1],
                                             in1=st[:, S_M:S_M + 1]))
            vb = E.chk("v", lambda e: e.tensor_scalar(
                out=st[:, S_BIAS:S_BIAS + 1], in0=st[:, S_MSQ:S_MSQ + 1],
                scalar1=-1.0, scalar2=EPS_LN, op0=ALU.mult, op1=ALU.add))
            E.wait("s", "dx", ld_x[j])
            E.wait("s", "v", v_o2.get(j - 1, 0))
            s_sq[j] = E.chk("s", lambda e, j=j: e.activation(
                out=a1["scrB"][:, :], in_=a1["xb"][j % 2][:, :], func=AF.Square,
                accum_out=st[:, S_SQ:S_SQ + 1]))
            E.wait("s", "v", vb)
            E.on("s", lambda e: e.activation(
                out=st[:, S_LNV:S_LNV + 1], in_=st[:, S_SQ:S_SQ + 1], func=AF.Ln,
                bias=st[:, S_BIAS:S_BIAS + 1], scale=1.0 / D))
            sinv = E.chk("s", lambda e: e.activation(
                out=st[:, S_INV1:S_INV1 + 1], in_=st[:, S_LNV:S_LNV + 1],
                func=AF.Exp, scale=-0.5))
            E.wait("v", "s", sinv)
            vxin = E.chk("v", lambda e, j=j: e.tensor_scalar(
                out=a1["xin"][j % 2][:, :], in0=a1["xb"][j % 2][:, :],
                scalar1=st[:, S_M:S_M + 1], scalar2=st[:, S_INV1:S_INV1 + 1],
                op0=ALU.subtract, op1=ALU.mult))
            v_xin[j] = vxin
            E.wait("s", "v", vxin)
            E.on("s", lambda e, j=j: e.activation(
                out=a1["scrB"][:, :], in_=a1["xin"][j % 2][:, :], func=AF.Square,
                accum_out=st[:, S_SQ2:S_SQ2 + 1]))
            E.on("s", lambda e: e.activation(
                out=st[:, S_LNV:S_LNV + 1], in_=st[:, S_SQ2:S_SQ2 + 1], func=AF.Ln,
                bias=cols[:, C_EPSR:C_EPSR + 1], scale=1.0 / D))
            sinv2 = E.chk("s", lambda e: e.activation(
                out=st[:, S_INV2:S_INV2 + 1], in_=st[:, S_LNV:S_LNV + 1],
                func=AF.Exp, scale=-0.5))
            E.wait("v", "s", sinv2)
            vhbf = E.chk("v", lambda e, j=j: e.scalar_tensor_tensor(
                out=a1["hbf"][:, :], in0=a1["xin"][j % 2][:, :],
                scalar=st[:, S_INV2:S_INV2 + 1], in1=a1["gat"][:, :],
                op0=ALU.mult, op1=ALU.mult))
            v_hbf[j] = vhbf
        a1_front(0)

        for i in range(nt):
            E.wait("t", "v", v_hbf[i])
            E.wait("t", "v", v_o2fm.get(i - 1, 0))
            for j in range(8):
                E.on("t", lambda e, j=j: e.transpose(
                    psT[:, j * 128:(j + 1) * 128], a1["hbf"][:, j * 128:(j + 1) * 128],
                    identb[:, :]))
            t_htr = E.tend()
            E.wait("v", "t", t_htr)
            vhfm = E.chk("v", lambda e: e.tensor_copy(out=a1["hfm"][:, :], in_=psT[:, :]))
            E.wait("t", "v", vhfm)
            E.wait("t", "v", v_on.get(i - 1, 0))
            E.wait("t", "v", v_x2.get(i - 1, 0))
            E.wait("t", "v", v_atm1.get(i - 1, 0))
            E.wait("t", "v", v_zb.get(i - 1, 0))
            for kc in range(8):
                E.on("t", lambda e, kc=kc: e.matmul(
                    psX[:, 0:16], a1["hfm"][:, kc * 128:(kc + 1) * 128],
                    a1["wgk1"][:, kc * 16:(kc + 1) * 16],
                    start=(kc == 0), stop=(kc == 7)))
            t_t1 = E.tend()
            E.wait("v", "t", t_t1)
            vt1 = E.chk("v", lambda e: e.tensor_copy(out=a1["t1bf"][:, :],
                                                     in_=psX[:, 0:16]))
            E.wait("t", "v", vt1)
            tt1 = E.chk("t", lambda e: e.transpose(
                psT[0:16, 0:128], a1["t1bf"][:, :], identb[:, :]))
            E.wait("v", "t", tt1)
            vt1f = E.chk("v", lambda e: e.tensor_copy(out=a1["t1fm"][:, :],
                                                      in_=psT[0:16, 0:128]))
            E.wait("t", "v", vt1f)
            tz = E.chk("t", lambda e: e.matmul(
                psX[:, 0:512], a1["t1fm"][:, :], a1["wgk2"][:, :],
                start=True, stop=True))
            E.wait("v", "t", tz)
            vzb = E.chk("v", lambda e: e.tensor_add(
                out=a1["zg"][:, :], in0=psX[:, 0:512], in1=a1["bgk"][:, :]))
            v_zb[i] = vzb
            E.wait("s", "v", vzb)
            E.on("s", lambda e: e.activation(out=a1["scrA"][:, 0:512],
                                             in_=a1["zg"][:, :], func=AF.Exp,
                                             scale=-1.0))
            for kc in range(8):
                lhs = a1["hfm"][:, kc * 128:(kc + 1) * 128]
                ss = (kc == 0, kc == 7)
                E.on("t", lambda e, l=lhs, kc=kc, ss=ss: e.matmul(
                    psA[:, 0:512], l, a1["wq"][:, kc * DKT:(kc + 1) * DKT],
                    start=ss[0], stop=ss[1]))
                E.on("t", lambda e, l=lhs, kc=kc, ss=ss: e.matmul(
                    psA[:, 512:1024], l, a1["wk"][:, kc * DKT:(kc + 1) * DKT],
                    start=ss[0], stop=ss[1]))
                for half in range(2):
                    E.on("t", lambda e, l=lhs, kc=kc, half=half, ss=ss: e.matmul(
                        psB[:, half * 512:(half + 1) * 512], l,
                        a1["wv"][:, kc * DVT + half * 512: kc * DVT + (half + 1) * 512],
                        start=ss[0], stop=ss[1]))
                    E.on("t", lambda e, l=lhs, kc=kc, half=half, ss=ss: e.matmul(
                        psC[:, half * 512:(half + 1) * 512], l,
                        a1["wg"][:, kc * DVT + half * 512: kc * DVT + (half + 1) * 512],
                        start=ss[0], stop=ss[1]))
            t_proj = E.tend()
            E.wait("v", "t", t_proj)
            E.on("v", lambda e: e.tensor_copy(out=a1["q"][:, :], in_=psA[:, 0:512]))
            vt1_unused = E.chk("v", lambda e: e.tensor_copy(out=a1["vbf"][:, :],
                                                            in_=psB[:, :]))
            E.wait("s", "t", t_proj)
            E.on("s", lambda e: e.copy(out=a1["k"][:, :], in_=psA[:, 512:1024]))
            skg = E.chk("s", lambda e: e.copy(out=a1["g"][:, :], in_=psC[:, :]))
            sgbf = E.chk("s", lambda e: e.activation(
                out=a1["gbf"][:, :], in_=a1["scrA"][:, 0:512], func=AF.Ln,
                bias=cols[:, C_ONE:C_ONE + 1]))

            E.wait("t", "s", sgbf)
            E.wait("t", "s", skg)
            E.wait("t", "s", sexp_last.get(i - 1, 0))
            tb0 = E.chk("t", lambda e: e.matmul(
                psC[0:64, 0:512], lmask[0:64, :], a1["gbf"][0:64, :],
                start=True, stop=True))
            tb1 = E.chk("t", lambda e: e.matmul(
                psC[64:128, 0:512], lmask[64:128, :], a1["gbf"][64:128, :],
                start=True, stop=True))
            E.wait("s", "t", tb1)
            E.on("s", lambda e: e.activation(
                out=a1["expq"][:, :], in_=psC[:, 0:512], func=AF.Exp,
                bias=cols[:, C_LNSQ:C_LNSQ + 1]))
            sexp = E.chk("s", lambda e: e.activation(
                out=a1["expm"][:, :], in_=psC[:, 0:512], func=AF.Exp, scale=-1.0))
            sexp_last[i] = sexp
            E.wait("t", "s", sexp)
            for ch in range(2):
                r0 = 64 * ch
                for h in range(NH):
                    E.on("t", lambda e, h=h, r0=r0, ch=ch: e.matmul(
                        psC[:, 1008 + ch * 4 + h:1009 + ch * 4 + h],
                        a1["gbf"][r0:r0 + 64, h * 128:(h + 1) * 128],
                        cones[r0:r0 + 64, :], start=True, stop=True))
            tbl = E.tend()
            E.wait("s", "t", tbl)
            E.wait("s", "v", supd_last.get(i - 1, 0))
            sebl = E.chk("s", lambda e: e.activation(
                out=a1["ebl"][:, 0:8], in_=psC[:, 1008:1016], func=AF.Exp))
            sebl_last[i] = sebl
            E.wait("v", "s", sexp)
            E.on("v", lambda e: e.tensor_mul(out=a1["qebf"][:, :], in0=a1["q"][:, :],
                                             in1=a1["expq"][:, :]))
            vqk = E.chk("v", lambda e: e.tensor_mul(
                out=a1["kebf"][:, :], in0=a1["k"][:, :], in1=a1["expm"][:, :]))
            E.wait("t", "v", vqk)
            E.wait("t", "v", vt1f)
            for ch in range(2):
                r0 = 64 * ch
                co = ch * 512
                if ch == 1:
                    E.wait("t", "v", vqkfm_0)
                for h in range(NH):
                    E.on("t", lambda e, h=h, r0=r0: e.transpose(
                        psT[:, h * 64:(h + 1) * 64],
                        a1["qebf"][r0:r0 + 64, h * 128:(h + 1) * 128],
                        ident2[r0:r0 + 64, :]))
                    E.on("t", lambda e, h=h, r0=r0: e.transpose(
                        psT[:, 256 + h * 64:256 + (h + 1) * 64],
                        a1["kebf"][r0:r0 + 64, h * 128:(h + 1) * 128],
                        ident2[r0:r0 + 64, :]))
                ttr = E.tend()
                E.wait("v", "t", ttr)
                vqkfm = E.chk("v", lambda e, co=co: e.tensor_copy(
                    out=a1["qkfm"][:, co:co + 512], in_=psT[:, 0:512]))
                if ch == 0:
                    vqkfm_0 = vqkfm
                E.wait("t", "v", vqkfm)
                E.wait("t", "s", sebl)
                if ch == 1:
                    E.wait("t", "v", vatm_prev)
                for h in range(NH):
                    E.on("t", lambda e, h=h, r0=r0, co=co: e.matmul(
                        psC[r0:r0 + 64, 512 + h * 64:512 + (h + 1) * 64],
                        a1["qkfm"][:, co + 256 + h * 64:co + 256 + (h + 1) * 64],
                        a1["qkfm"][:, co + h * 64:co + (h + 1) * 64],
                        start=True, stop=True))
                tat = E.tend()
                E.wait("v", "t", tat)
                vatm = E.chk("v", lambda e, r0=r0: e.tensor_mul(
                    out=a1["atm"][r0:r0 + 64, :], in0=psC[r0:r0 + 64, 512:768],
                    in1=maskt[r0:r0 + 64, :]))
                vatm_prev = vatm
                if ch == 1:
                    v_atm1[i] = vatm
                E.wait("t", "v", vatm)
                E.wait("t", "v", v_supd)
                for h in range(NH):
                    E.on("t", lambda e, h=h, r0=r0, co=co: e.matmul(
                        psA[r0:r0 + 64, h * 256:(h + 1) * 256],
                        a1["atm"][r0:r0 + 64, h * 64:(h + 1) * 64],
                        a1["vbf"][r0:r0 + 64, h * 256:(h + 1) * 256],
                        start=True, stop=False))
                    E.on("t", lambda e, h=h, r0=r0, co=co: e.matmul(
                        psA[r0:r0 + 64, h * 256:(h + 1) * 256],
                        a1["qkfm"][:, co + h * 64:co + (h + 1) * 64],
                        a1["Sb"][:, h * 256:(h + 1) * 256],
                        start=False, stop=True))
                t_o = E.tend()
                for h in range(NH):
                    E.on("t", lambda e, h=h, r0=r0: e.matmul(
                        psB[:, h * 256:(h + 1) * 256],
                        a1["kebf"][r0:r0 + 64, h * 128:(h + 1) * 128],
                        a1["vbf"][r0:r0 + 64, h * 256:(h + 1) * 256],
                        start=True, stop=True))
                t_p = E.tend()
                E.wait("v", "t", t_p)
                E.wait("v", "s", sebl)
                E.on("v", lambda e: e.tensor_add(out=a1["Sf"][:, :],
                                                 in0=a1["Sf"][:, :], in1=psB[:, :]))
                for h in range(NH):
                    E.on("v", lambda e, h=h, ch=ch: e.tensor_scalar_mul(
                        out=a1["Sf"][:, h * 256:(h + 1) * 256],
                        in0=a1["Sf"][:, h * 256:(h + 1) * 256],
                        scalar1=a1["ebl"][:, ch * 4 + h:ch * 4 + h + 1]))
                v_supd = E.chk("v", lambda e: e.tensor_copy(out=a1["Sb"][:, :],
                                                            in_=a1["Sf"][:, :]))
            supd_last[i] = v_supd
            t_o_last[i] = t_o

            E.wait("s", "t", t_o)
            for h in range(NH):
                E.on("s", lambda e, h=h: e.activation(
                    out=a1["scrB"][:, h * 256:(h + 1) * 256],
                    in_=psA[:, h * 256:(h + 1) * 256], func=AF.Square,
                    accum_out=st[:, S_G0 + h:S_G0 + h + 1]))
            E.on("s", lambda e: e.activation(
                out=st[:, S_GI:S_GI + 4], in_=st[:, S_G0:S_G0 + 4], func=AF.Ln,
                bias=cols[:, C_EPSR:C_EPSR + 1], scale=1.0 / 256.0))
            sgi = E.chk("s", lambda e: e.activation(
                out=st[:, S_GI:S_GI + 4], in_=st[:, S_GI:S_GI + 4], func=AF.Exp,
                scale=-0.5))
            E.on("s", lambda e: e.activation(out=a1["scrA"][:, :], in_=a1["g"][:, :],
                                             func=AF.Exp, scale=-1.0))
            E.on("s", lambda e: e.activation(out=a1["scrB"][:, :], in_=a1["scrA"][:, :],
                                             func=AF.Ln, bias=cols[:, C_ONE:C_ONE + 1]))
            ssig = E.chk("s", lambda e: e.activation(
                out=a1["scrA"][:, :], in_=a1["scrB"][:, :], func=AF.Exp, scale=-1.0))
            E.wait("v", "s", sgi)
            for h in range(NH):
                E.on("v", lambda e, h=h: e.scalar_tensor_tensor(
                    out=a1["on"][:, h * 256:(h + 1) * 256],
                    in0=psA[:, h * 256:(h + 1) * 256],
                    scalar=st[:, S_GI + h:S_GI + h + 1],
                    in1=a1["gag"][:, h * 256:(h + 1) * 256],
                    op0=ALU.mult, op1=ALU.mult))
            v_on[i] = E.chk("v", lambda e: e.engine_nop())
            E.wait("v", "s", ssig)
            E.on("v", lambda e: e.tensor_mul(out=a1["scrB"][:, :], in0=a1["g"][:, :],
                                             in1=a1["scrA"][:, :]))
            vo2 = E.chk("v", lambda e: e.tensor_mul(
                out=a1["o2bf"][:, :], in0=a1["on"][:, :], in1=a1["scrB"][:, :]))
            v_o2[i] = vo2
            E.wait("t", "v", vo2)
            for j in range(8):
                E.on("t", lambda e, j=j: e.transpose(
                    psT[:, j * 128:(j + 1) * 128],
                    a1["o2bf"][:, j * 128:(j + 1) * 128], identb[:, :]))
            to2t = E.tend()
            E.wait("v", "t", to2t)
            v_o2fm[i] = E.chk("v", lambda e: e.tensor_copy(out=a1["o2fm"][:, :],
                                                           in_=psT[:, :]))
            a1_front(i + 1)
            emit_wo_x2(i)

        E.barrier()
        if phase_limit == "a1":
            return

        psG, psD, psT, psX = a2["psG"], a2["psD"], a2["psT"], a2["psX"]
        for wsb, wdr, ncol, nch in ((a2["wgate"], Wgate, IA, 8),
                                    (a2["wup"], Wup, IA, 8),
                                    (a2["wdown"], Wdown, D, 22)):
            for c in range(nch):
                E.on("g", lambda e, d=wsb, s=wdr, c=c, n=ncol: e.dma_start(
                    out=d[:, c * n:(c + 1) * n], in_=s[c * 128:(c + 1) * 128, :]),
                    inc=[("dw", 16)])
        E.on("sy", lambda e: e.dma_start(out=a2["gam"][:, :], in_=gam_mlp),
             inc=[("dx", 16)])
        dw_a2w = E.cnt["dw"]
        dx_a2 = E.cnt["dx"]
        E.wait("t", "dw", dw_a2w)
        E.on("v", lambda e: e.memset(a2["ymaxr"][:, :], -1e38))
        E.on("v", lambda e: e.memset(a2["yminr"][:, :], 1e38))
        E.wait("v", "dx", dx_a2)

        ld2, v_cons, s_oast, v_nbf, v_mlpfm2, v_u = {}, {}, {}, {}, {}, {}
        t_ntr, v_mlp1 = {}, {}

        def a2_loads(j):
            if j >= nt:
                return
            E.wait("sy", "v", v_cons.get(j - 2, 0))
            E.wait("sy", "ds", s_oast.get(j - 2, 0))
            E.on("sy", lambda e, j=j: e.dma_start(
                out=a2["x2b"][j % 2][:, :],
                in_=x2h[row0(j):row0(j) + 128, :]), inc=[("dx", 16)])
            E.on("sy", lambda e, j=j: e.dma_start(
                out=a2["xb"][j % 2][:, :],
                in_=x[row0(j):row0(j) + 128, :]), inc=[("dx", 16)])
            ld2[j] = E.cnt["dx"]

        def a2_front(j):
            if j >= nt:
                return
            E.wait("s", "dx", ld2[j])
            E.wait("s", "v", v_mlp1.get(j - 1, 0))
            E.on("s", lambda e, j=j: e.activation(
                out=a2["scrB"][:, 0:D], in_=a2["x2b"][j % 2][:, :], func=AF.Square,
                accum_out=st[:, S_FSQ:S_FSQ + 1]))
            E.on("s", lambda e: e.activation(
                out=st[:, S_FLN:S_FLN + 1], in_=st[:, S_FSQ:S_FSQ + 1], func=AF.Ln,
                bias=cols[:, C_EPSR:C_EPSR + 1], scale=1.0 / D))
            sinv = E.chk("s", lambda e: e.activation(
                out=st[:, S_FINV:S_FINV + 1], in_=st[:, S_FLN:S_FLN + 1],
                func=AF.Exp, scale=-0.5))
            E.wait("v", "s", sinv)
            E.wait("v", "t", t_ntr.get(j - 1, 0))
            vnbf = E.chk("v", lambda e, j=j: e.scalar_tensor_tensor(
                out=a2["nbf"][:, :], in0=a2["x2b"][j % 2][:, :],
                scalar=st[:, S_FINV:S_FINV + 1], in1=a2["gam"][:, :],
                op0=ALU.mult, op1=ALU.mult))
            v_nbf[j] = vnbf

        def a2_premid(j, pv):
            if j >= nt:
                return 0
            E.wait("t", "v", v_nbf[j])
            E.wait("t", "v", pv)
            for jj in range(8):
                E.on("t", lambda e, jj=jj: e.transpose(
                    psT[:, jj * 128:(jj + 1) * 128],
                    a2["nbf"][:, jj * 128:(jj + 1) * 128], identb[:, :]))
            tntr = E.tend()
            t_ntr[j] = tntr
            E.wait("v", "t", tntr)
            return E.chk("v", lambda e: e.tensor_copy(out=a2["nfm"][:, :],
                                                      in_=psT[:, 0:1024]))

        a2_loads(0)
        a2_loads(1)
        a2_front(0)
        vnfm_next = a2_premid(0, 0)

        for i in range(nt):
            vnfm = vnfm_next
            snfm = 0
            segs = ((0, 512), (512, 1024), (1024, 1408))

            def a2_gate_or_up(wt, coff):
                for kc in range(8):
                    for (c0, c1) in segs:
                        E.on("t", lambda e, kc=kc, c0=c0, c1=c1, coff=coff, wt=wt:
                             e.matmul(psG[:, c0:c1],
                                      a2["nfm"][:, kc * 128:(kc + 1) * 128],
                                      wt[:, kc * IA + coff + c0: kc * IA + coff + c1],
                                      start=(kc == 0), stop=(kc == 7)))
                return E.tend()

            def a2_gate_evac(tg, half):
                E.wait("v", "t", tg)
                vg = E.chk("v", lambda e, half=half: e.tensor_copy(
                    out=a2["gate"][half][:, 0:704], in_=psG[:, 0:704]))
                E.wait("s", "t", tg)
                sg = E.chk("s", lambda e, half=half: e.copy(
                    out=a2["gate"][half][:, 704:1408], in_=psG[:, 704:1408]))
                return vg, sg

            def a2_silu(vg, half, o1, o2, wv=0):
                E.wait("s", "v", vg)
                E.on("s", lambda e, half=half, o1=o1: e.activation(
                    out=o1[:, :], in_=a2["gate"][half][:, :], func=AF.Exp,
                    scale=-1.0))
                E.wait("s", "v", wv)
                E.on("s", lambda e, o1=o1, o2=o2: e.activation(
                    out=o2[:, :], in_=o1[:, :], func=AF.Ln,
                    bias=cols[:, C_ONE:C_ONE + 1]))
                return E.chk("s", lambda e, o1=o1, o2=o2: e.activation(
                    out=o2[:, :], in_=o2[:, :], func=AF.Exp, scale=-1.0))

            def a2_muls(tu, ssig, sg, half, sig):
                E.wait("v", "t", tu)
                E.wait("v", "s", sg)
                vm1 = E.chk("v", lambda e, half=half: e.tensor_mul(
                    out=a2["scrC"][:, :], in0=psG[:, 0:1408],
                    in1=a2["gate"][half][:, :]))
                E.wait("v", "s", ssig)
                vmlp = E.chk("v", lambda e, half=half, sig=sig: e.tensor_mul(
                    out=a2["mlpb"][half][:, :], in0=a2["scrC"][:, :], in1=sig[:, :]))
                return vm1, vmlp

            def a2_tr(half):
                for j in range(11):
                    E.on("t", lambda e, j=j, half=half: e.transpose(
                        psT[:, j * 128:(j + 1) * 128],
                        a2["mlpb"][half][:, j * 128:(j + 1) * 128], identb[:, :]))
                tm1 = E.tend()
                E.wait("v", "t", tm1)
                vf = E.chk("v", lambda e: e.tensor_copy(out=a2["mlpfm"][:, 0:1408],
                                                        in_=psT[:, 0:1408]))
                return vf, 0

            def a2_down(half):
                for j in range(11):
                    kci = half * 11 + j
                    for hh2 in range(2):
                        E.on("t", lambda e, j=j, kci=kci, hh2=hh2: e.matmul(
                            psD[:, hh2 * 512:(hh2 + 1) * 512],
                            a2["mlpfm"][:, j * 128:(j + 1) * 128],
                            a2["wdown"][:, kci * D + hh2 * 512:
                                        kci * D + (hh2 + 1) * 512],
                            start=(kci == 0), stop=(kci == 21)))
                return E.tend()

            E.wait("t", "v", vnfm)
            E.wait("t", "s", snfm)
            tg0 = a2_gate_or_up(a2["wgate"], 0)
            vg0, sg0 = a2_gate_evac(tg0, 0)
            ssig0 = a2_silu(vg0, 0, a2["scrB"], a2["scrA"])
            E.wait("t", "v", vg0)
            E.wait("t", "s", sg0)
            tg1 = a2_gate_or_up(a2["wgate"], 1408)
            vg1, sg1 = a2_gate_evac(tg1, 1)
            ssig1 = None
            E.wait("t", "v", vg1)
            E.wait("t", "s", sg1)
            tu0 = a2_gate_or_up(a2["wup"], 0)
            vm1_0, vmlp0 = a2_muls(tu0, ssig0, sg0, 0, a2["scrA"])
            ssig1 = a2_silu(vg1, 1, a2["scrB"], a2["scrA"], wv=vmlp0)
            E.wait("t", "v", vm1_0)
            tu1 = a2_gate_or_up(a2["wup"], 1408)
            vm1_1, vmlp1 = a2_muls(tu1, ssig1, sg1, 1, a2["scrA"])
            v_mlp1[i] = vmlp1
            E.wait("t", "v", vmlp0)
            vf2_0, sf2_0 = a2_tr(0)
            E.wait("t", "v", vf2_0)
            E.wait("t", "s", sf2_0)
            E.wait("t", "v", v_u.get(i - 1, 0))
            a2_down(0)
            a2_front(i + 1)
            E.wait("t", "v", vmlp1)
            vf2_1, sf2_1 = a2_tr(1)
            E.wait("t", "v", vf2_1)
            tdown = a2_down(1)
            vnfm_next = a2_premid(i + 1, vf2_1)

            E.wait("v", "t", tdown)
            vu = E.chk("v", lambda e, i=i: e.tensor_add(
                out=a2["u"][:, :], in0=psD[:, :], in1=a2["x2b"][i % 2][:, :]))
            v_u[i] = vu
            E.on("v", lambda e: e.reduce_sum(out=st[:, S_SUM:S_SUM + 1],
                                             in_=a2["u"][:, :], axis=AX.X))
            E.on("v", lambda e: e.tensor_scalar_mul(
                out=st[:, S_M:S_M + 1], in0=st[:, S_SUM:S_SUM + 1], scalar1=1.0 / D))
            E.on("v", lambda e: e.tensor_mul(out=st[:, S_MSQ:S_MSQ + 1],
                                             in0=st[:, S_M:S_M + 1],
                                             in1=st[:, S_M:S_M + 1]))
            vb = E.chk("v", lambda e: e.tensor_scalar(
                out=st[:, S_BIAS:S_BIAS + 1], in0=st[:, S_MSQ:S_MSQ + 1],
                scalar1=-1.0, scalar2=EPS_LN, op0=ALU.mult, op1=ALU.add))
            E.wait("s", "v", vu)
            E.on("s", lambda e: e.activation(out=a2["scrB"][:, 0:D], in_=a2["u"][:, :],
                                             func=AF.Square,
                                             accum_out=st[:, S_SQ:S_SQ + 1]))
            E.wait("s", "v", vb)
            E.on("s", lambda e: e.activation(
                out=st[:, S_LNV:S_LNV + 1], in_=st[:, S_SQ:S_SQ + 1], func=AF.Ln,
                bias=st[:, S_BIAS:S_BIAS + 1], scale=1.0 / D))
            sinvu = E.chk("s", lambda e: e.activation(
                out=st[:, S_INV1:S_INV1 + 1], in_=st[:, S_LNV:S_LNV + 1],
                func=AF.Exp, scale=-0.5))
            E.wait("v", "s", sinvu)
            E.on("v", lambda e: e.tensor_scalar(
                out=a2["scrA"][:, 0:D], in0=a2["u"][:, :], scalar1=st[:, S_M:S_M + 1],
                scalar2=st[:, S_INV1:S_INV1 + 1], op0=ALU.subtract, op1=ALU.mult))
            E.wait("v", "ds", s_oast.get(i - 2, 0))
            voa = E.chk("v", lambda e, i=i: e.tensor_add(
                out=a2["oab"][i % 2][:, :], in0=a2["scrA"][:, 0:D],
                in1=a2["xb"][i % 2][:, :]))
            E.wait("s", "v", voa)
            E.on("s", lambda e, i=i: e.dma_start(
                out=oAh[row0(i):row0(i) + 128, :], in_=a2["oab"][i % 2][:, :]),
                inc=[("ds", 16)])
            if phase_limit == "a2":
                E.on("s", lambda e, i=i: e.dma_start(
                    out=out[row0(i):row0(i) + 128, :], in_=a2["oab"][i % 2][:, :]),
                    inc=[("ds", 16)])
            s_oast[i] = E.cnt["ds"]
            E.on("v", lambda e, i=i: e.reduce_sum(
                out=st[:, S_SUM2:S_SUM2 + 1], in_=a2["oab"][i % 2][:, :], axis=AX.X))
            E.on("v", lambda e, i=i: e.tensor_scalar_mul(
                out=ymt[:, i:i + 1], in0=st[:, S_SUM2:S_SUM2 + 1], scalar1=1.0 / D))
            E.on("v", lambda e, i=i: e.tensor_mul(out=st[:, S_MSQ2:S_MSQ2 + 1],
                                                  in0=ymt[:, i:i + 1],
                                                  in1=ymt[:, i:i + 1]))
            vb2 = E.chk("v", lambda e: e.tensor_scalar(
                out=st[:, S_BIAS2:S_BIAS2 + 1], in0=st[:, S_MSQ2:S_MSQ2 + 1],
                scalar1=-1.0, scalar2=EPS_LN, op0=ALU.mult, op1=ALU.add))
            E.on("s", lambda e, i=i: e.activation(
                out=a2["scrB"][:, 0:D], in_=a2["oab"][i % 2][:, :], func=AF.Square,
                accum_out=st[:, S_SQ:S_SQ + 1]))
            E.wait("s", "v", vb2)
            E.on("s", lambda e: e.activation(
                out=st[:, S_LNV:S_LNV + 1], in_=st[:, S_SQ:S_SQ + 1], func=AF.Ln,
                bias=st[:, S_BIAS2:S_BIAS2 + 1], scale=1.0 / D))
            sy2 = E.chk("s", lambda e, i=i: e.activation(
                out=yinvt[:, i:i + 1], in_=st[:, S_LNV:S_LNV + 1], func=AF.Exp,
                scale=-0.5))
            E.on("v", lambda e, i=i: e.reduce_max(out=st[:, S_RED:S_RED + 1],
                                                  in_=a2["oab"][i % 2][:, :],
                                                  axis=AX.X))
            E.on("v", lambda e, i=i: e.tensor_reduce(
                out=st[:, S_RED2:S_RED2 + 1], in_=a2["oab"][i % 2][:, :],
                axis=AX.X, op=ALU.min))
            E.wait("v", "s", sy2)
            E.on("v", lambda e, i=i: e.tensor_scalar(
                out=st[:, S_RED:S_RED + 1], in0=st[:, S_RED:S_RED + 1],
                scalar1=ymt[:, i:i + 1], scalar2=yinvt[:, i:i + 1],
                op0=ALU.subtract, op1=ALU.mult))
            E.on("v", lambda e: e.tensor_max(out=a2["ymaxr"][:, :],
                                             in0=a2["ymaxr"][:, :],
                                             in1=st[:, S_RED:S_RED + 1]))
            E.on("v", lambda e, i=i: e.tensor_scalar(
                out=st[:, S_RED2:S_RED2 + 1], in0=st[:, S_RED2:S_RED2 + 1],
                scalar1=ymt[:, i:i + 1], scalar2=yinvt[:, i:i + 1],
                op0=ALU.subtract, op1=ALU.mult))
            vmm = E.chk("v", lambda e: e.tensor_tensor(
                out=a2["yminr"][:, :], in0=a2["yminr"][:, :],
                in1=st[:, S_RED2:S_RED2 + 1], op=ALU.min))
            v_cons[i] = vmm
            a2_loads(i + 2)

        E.wait("t", "v", E.cnt["v"])
        tmx = E.chk("t", lambda e: e.transpose(psX[0:1, 0:128], a2["ymaxr"][:, :],
                                               identf[:, :]))
        E.wait("v", "t", tmx)
        vdone = E.chk("v", lambda e: e.reduce_max(out=ccsb[0:1, 0:1],
                                                  in_=psX[0:1, 0:128], axis=AX.X))
        E.wait("t", "v", vdone)
        tmn = E.chk("t", lambda e: e.transpose(psX[0:1, 0:128], a2["yminr"][:, :],
                                               identf[:, :]))
        E.wait("v", "t", tmn)
        E.on("v", lambda e: e.tensor_reduce(out=ccsb[0:1, 1:2], in_=psX[0:1, 0:128],
                                            axis=AX.X, op=ALU.min))
        E.on("v", lambda e: e.tensor_scalar_mul(out=ccsb[0:1, 1:2],
                                                in0=ccsb[0:1, 1:2], scalar1=-1.0))
        E.on("v", lambda e: e.memset(ccsb[0:1, 2:8], 0.0))

        E.barrier()
        if phase_limit == "a2":
            return

        def allreduce_and_scales(ps_x, r_sw, bsrow, brow_t, nb):
            vsnap = E.cnt["v"]
            E.wait("g", "v", vsnap)
            E.on("g", lambda e: e.dma_start(out=ccin[:, :], in_=ccsb[:, :]),
                 inc=[("dw", 16)])
            E.wait("g", "dw", E.cnt["dw"])
            E.on("g", lambda e: e.collective_compute(
                "AllReduce", ALU.max, replica_groups=[list(range(NCORES))],
                ins=[ccin.ap().opt()], outs=[ccout.ap().opt()]), inc=[("cc", 1)])
            E.wait("g", "cc", E.cnt["cc"])
            E.on("g", lambda e: e.dma_start(out=ccsb[:, :], in_=ccout[:, :]),
                 inc=[("dw", 16)])
            E.wait("v", "dw", E.cnt["dw"])
            E.on("v", lambda e: e.tensor_add(out=rows[0:1, R_D:R_D + 1],
                                             in0=ccsb[0:1, 0:1], in1=ccsb[0:1, 1:2]))
            E.on("v", lambda e: e.tensor_scalar(
                out=rows[0:1, R_SX:R_SX + 1], in0=rows[0:1, R_D:R_D + 1],
                scalar1=1.0 / 255.0, scalar2=1e-8, op0=ALU.mult, op1=ALU.max))
            E.on("v", lambda e: e.reciprocal(out=rows[0:1, R_ISX:R_ISX + 1],
                                             in_=rows[0:1, R_SX:R_SX + 1]))
            E.on("v", lambda e: e.tensor_mul(out=rows[0:1, R_ZT:R_ZT + 1],
                                             in0=ccsb[0:1, 1:2],
                                             in1=rows[0:1, R_ISX:R_ISX + 1]))
            E.on("v", lambda e: e.tensor_copy(out=rowi[0:1, 0:1],
                                              in_=rows[0:1, R_ZT:R_ZT + 1]))
            E.on("v", lambda e: e.tensor_copy(out=rows[0:1, R_ZT:R_ZT + 1],
                                              in_=rowi[0:1, 0:1]))
            E.on("v", lambda e: e.tensor_scalar_add(
                out=rows[0:1, R_ZP:R_ZP + 1], in0=rows[0:1, R_ZT:R_ZT + 1],
                scalar1=-128.0))
            E.on("v", lambda e: e.tensor_scalar(
                out=rows[0:1, R_ZP:R_ZP + 1], in0=rows[0:1, R_ZP:R_ZP + 1],
                scalar1=-128.0, scalar2=127.0, op0=ALU.max, op1=ALU.min))
            E.on("v", lambda e: e.tensor_scalar(
                out=rows[0:1, R_LO:R_LO + 1], in0=rows[0:1, R_ZP:R_ZP + 1],
                scalar1=-1.0, scalar2=-128.0, op0=ALU.mult, op1=ALU.add))
            E.on("v", lambda e: e.tensor_scalar(
                out=rows[0:1, R_HI:R_HI + 1], in0=rows[0:1, R_ZP:R_ZP + 1],
                scalar1=-1.0, scalar2=127.0, op0=ALU.mult, op1=ALU.add))
            E.on("v", lambda e: e.tensor_mul(out=rows[0:1, R_SXSW:R_SXSW + 1],
                                             in0=rows[0:1, R_SX:R_SX + 1],
                                             in1=rows[0:1, r_sw:r_sw + 1]))
            E.on("v", lambda e: e.reciprocal(out=rows[0:1, R_ISXSW:R_ISXSW + 1],
                                             in_=rows[0:1, R_SXSW:R_SXSW + 1]))
            E.on("v", lambda e, nb=nb: e.tensor_scalar_mul(
                out=bsrow[0:1, 0:nb], in0=brow_t,
                scalar1=rows[0:1, R_ISXSW:R_ISXSW + 1]))
            E.on("v", lambda e: e.tensor_copy(out=rows[0:1, 16:17],
                                              in_=rows[0:1, R_ISX:R_ISX + 1]))
            E.on("v", lambda e: e.tensor_copy(out=rows[0:1, 17:18],
                                              in_=rows[0:1, R_LO:R_LO + 1]))
            E.on("v", lambda e: e.tensor_copy(out=rows[0:1, 18:19],
                                              in_=rows[0:1, R_HI:R_HI + 1]))
            vpk = E.chk("v", lambda e: e.tensor_copy(
                out=rows[0:1, 19:20], in_=rows[0:1, R_SXSW:R_SXSW + 1]))
            E.wait("t", "v", vpk)
            tb = E.chk("t", lambda e, ps_x=ps_x: e.matmul(
                ps_x[:, 0:4], onesrf[0:1, :], rows[0:1, 16:20], start=True, stop=True))
            E.wait("v", "t", tb)
            return E.chk("v", lambda e, ps_x=ps_x: e.tensor_copy(
                out=cols[:, 0:4], in_=ps_x[:, 0:4]))

        E.on("sy", lambda e: e.dma_start(out=b1["hbuf"][0][0:1, :], in_=b1row),
             inc=[("dx", 16)])
        dx_b1r = E.cnt["dx"]
        E.wait("v", "dx", dx_b1r)
        allreduce_and_scales(b1["psX"], R_SW1, b1["b1s"], b1["hbuf"][0][0:1, 0:IF], IF)

        for c in range(8):
            E.on("sy", lambda e, c=c: e.dma_start(
                out=b1["wq1T"][:, c * IF:(c + 1) * IF],
                in_=wq1h[:, c * 128:(c + 1) * 128], transpose=True), inc=[("dx", 16)])
        dx_w1t = E.cnt["dx"]

        psF, psT, psX = b1["psF"], b1["psT"], b1["psX"]
        E.on("v", lambda e: e.memset(b1["hmaxr"][:, :], -1e38))
        E.on("v", lambda e: e.memset(b1["hminr"][:, :], 1e38))
        E.wait("t", "dx", dx_w1t)
        ldb, v_yq, st_h, v_xqfm, t_tr1 = {}, {}, {}, {}, {}

        def b1_loads(j):
            if j >= nt:
                return
            E.wait("sy", "v", v_yq.get(j - 2, 0))
            E.on("sy", lambda e, j=j: e.dma_start(
                out=b1["oab"][j % 2][:, :],
                in_=oAh[row0(j):row0(j) + 128, :]), inc=[("dx", 16)])
            ldb[j] = E.cnt["dx"]

        def b1_quant(j):
            if j >= nt:
                return
            E.wait("v", "dx", ldb[j])
            E.wait("v", "t", t_tr1.get(j - 1, 0))
            E.on("v", lambda e, j=j: e.tensor_scalar(
                out=b1["y"][:, :], in0=b1["oab"][j % 2][:, :],
                scalar1=ymt[:, j:j + 1], scalar2=yinvt[:, j:j + 1],
                op0=ALU.subtract, op1=ALU.mult))
            E.on("v", lambda e: e.tensor_scalar_mul(
                out=b1["scr"][:, :], in0=b1["y"][:, :],
                scalar1=cols[:, C_ISX:C_ISX + 1]))
            E.on("v", lambda e: e.tensor_copy(out=b1["qi"][:, :], in_=b1["scr"][:, :]))
            v_yq[j] = E.chk("v", lambda e: e.tensor_scalar(
                out=b1["xqbf"][:, :], in0=b1["qi"][:, :],
                scalar1=cols[:, C_LO:C_LO + 1], scalar2=cols[:, C_HI:C_HI + 1],
                op0=ALU.max, op1=ALU.min))

        b1_loads(0)
        b1_loads(1)
        b1_quant(0)
        sg_reg = [0, 0]
        for i in range(nt):
            E.wait("t", "v", v_yq[i])
            E.wait("t", "v", v_xqfm.get(i - 1, 0))
            for j in range(8):
                E.on("t", lambda e, j=j: e.transpose(
                    psT[:, j * 128:(j + 1) * 128], b1["xqbf"][:, j * 128:(j + 1) * 128],
                    identb[:, :]))
            ttr = E.tend()
            t_tr1[i] = ttr
            E.wait("v", "t", ttr)
            vfm = E.chk("v", lambda e: e.tensor_copy(out=b1["xqfm"][:, :], in_=psT[:, :]))
            v_xqfm[i] = vfm
            b1_loads(i + 2)
            b1_quant(i + 1)
            E.wait("t", "v", vfm)
            E.wait("s", "dw", st_h.get(i - 2, 0))
            for oc in range(4):
                r = oc % 2
                E.wait("t", "s", sg_reg[r])
                for kc in range(8):
                    for hh2 in range(2):
                        E.on("t", lambda e, oc=oc, r=r, kc=kc, hh2=hh2:
                             e.matmul(
                             psF[:, r * 1024 + hh2 * 512:
                                 r * 1024 + (hh2 + 1) * 512],
                             b1["xqfm"][:, kc * 128:(kc + 1) * 128],
                             b1["wq1T"][:, kc * IF + oc * 1024 + hh2 * 512:
                                        kc * IF + oc * 1024 + (hh2 + 1) * 512],
                             start=(kc == 0), stop=False))
                for hh2 in range(2):
                    E.on("t", lambda e, oc=oc, r=r, hh2=hh2: e.matmul(
                        psF[:, r * 1024 + hh2 * 512:
                            r * 1024 + (hh2 + 1) * 512],
                        onesrb[0:1, :],
                        b1["b1s"][0:1, oc * 1024 + hh2 * 512:
                                  oc * 1024 + (hh2 + 1) * 512],
                        start=False, stop=True))
                tmm = E.tend()
                E.wait("s", "t", tmm)
                sg = E.chk("s", lambda e, oc=oc, r=r, i=i: e.activation(
                    out=b1["hbuf"][i % 2][:, oc * 1024:(oc + 1) * 1024],
                    in_=psF[:, r * 1024:(r + 1) * 1024],
                    func=AF.Gelu, scale=cols[:, C_SXSW:C_SXSW + 1]))
                sg_reg[r] = sg
                E.wait("v", "s", sg)
                E.on("v", lambda e, oc=oc, i=i: e.reduce_max(
                    out=st[:, S_RED:S_RED + 1],
                    in_=b1["hbuf"][i % 2][:, oc * 1024:(oc + 1) * 1024], axis=AX.X))
                E.on("v", lambda e: e.tensor_max(out=b1["hmaxr"][:, :],
                                                 in0=b1["hmaxr"][:, :],
                                                 in1=st[:, S_RED:S_RED + 1]))
                E.on("v", lambda e, oc=oc, i=i: e.tensor_reduce(
                    out=st[:, S_RED:S_RED + 1],
                    in_=b1["hbuf"][i % 2][:, oc * 1024:(oc + 1) * 1024], axis=AX.X,
                    op=ALU.min))
                E.on("v", lambda e: e.tensor_tensor(
                    out=b1["hminr"][:, :], in0=b1["hminr"][:, :],
                    in1=st[:, S_RED:S_RED + 1], op=ALU.min))
            vred = E.chk("v", lambda e: e.engine_nop())
            E.wait("g", "v", vred)
            E.on("g", lambda e, i=i: e.dma_start(
                out=hh[row0(i):row0(i) + 128, :], in_=b1["hbuf"][i % 2][:, :]),
                inc=[("dw", 16)])
            if phase_limit == "b1":
                E.on("g", lambda e, i=i: e.dma_start(
                    out=out[row0(i):row0(i) + 128, :],
                    in_=b1["hbuf"][i % 2][:, (i % 4) * 1024:((i % 4) + 1) * 1024]),
                    inc=[("dw", 16)])
            st_h[i] = E.cnt["dw"]

        E.wait("t", "v", E.cnt["v"])
        tmx = E.chk("t", lambda e: e.transpose(psX[0:1, 0:128], b1["hmaxr"][:, :],
                                               identf[:, :]))
        E.wait("v", "t", tmx)
        vdone = E.chk("v", lambda e: e.reduce_max(out=ccsb[0:1, 0:1],
                                                  in_=psX[0:1, 0:128], axis=AX.X))
        E.wait("t", "v", vdone)
        tmn = E.chk("t", lambda e: e.transpose(psX[0:1, 0:128], b1["hminr"][:, :],
                                               identf[:, :]))
        E.wait("v", "t", tmn)
        E.on("v", lambda e: e.tensor_reduce(out=ccsb[0:1, 1:2], in_=psX[0:1, 0:128],
                                            axis=AX.X, op=ALU.min))
        E.on("v", lambda e: e.tensor_scalar_mul(out=ccsb[0:1, 1:2],
                                                in0=ccsb[0:1, 1:2], scalar1=-1.0))
        E.on("v", lambda e: e.memset(ccsb[0:1, 2:8], 0.0))

        E.barrier()
        if phase_limit == "b1":
            return

        E.on("sy", lambda e: e.dma_start(out=b2["hb"][0][0:1, 0:D], in_=b2row),
             inc=[("dx", 16)])
        dx_b2r = E.cnt["dx"]
        E.wait("v", "dx", dx_b2r)
        vsc2 = allreduce_and_scales(b2["psX"], R_SW2, b2["b2s"], b2["hb"][0][0:1, 0:D], D)
        for c in range(32):
            E.on("sy", lambda e, c=c: e.dma_start(
                out=b2["wq2T"][:, c * D:(c + 1) * D],
                in_=wq2h[:, c * 128:(c + 1) * 128], transpose=True), inc=[("dx", 16)])
        dx_w2t = E.cnt["dx"]
        psD, psX = b2["psD"], b2["psX"]
        psTs = [b2["psT"], b2["psT2"]]
        E.wait("t", "dx", dx_w2t)
        ldh, v_hq, st_o, v_ffn, v_out, t_trq = {}, {}, {}, {}, {}, {}
        E.wait("sy", "v", vsc2)

        def b2_loads(j):
            if j >= nt:
                return
            E.wait("sy", "v", v_out.get(j - 2, 0))
            E.on("sy", lambda e, j=j: e.dma_start(
                out=b2["hb"][j % 2][:, :],
                in_=hh[row0(j):row0(j) + 128, :]), inc=[("dx", 16)])
            E.on("sy", lambda e, j=j: e.dma_start(
                out=b2["oab"][j % 2][:, :],
                in_=oAh[row0(j):row0(j) + 128, :]), inc=[("dx", 16)])
            ldh[j] = E.cnt["dx"]

        def b2_quant(j):
            if j >= nt:
                return
            E.wait("v", "dx", ldh[j])
            E.wait("v", "t", t_trq.get(j - 1, 0))
            E.on("v", lambda e, j=j: e.tensor_scalar_mul(
                out=b2["scr"][:, :], in0=b2["hb"][j % 2][:, :],
                scalar1=cols[:, C_ISX:C_ISX + 1]))
            E.on("v", lambda e: e.tensor_copy(out=b2["qi"][:, :], in_=b2["scr"][:, :]))
            v_hq[j] = E.chk("v", lambda e: e.tensor_scalar(
                out=b2["hqbf"][:, :], in0=b2["qi"][:, :],
                scalar1=cols[:, C_LO:C_LO + 1], scalar2=cols[:, C_HI:C_HI + 1],
                op0=ALU.max, op1=ALU.min))

        b2_loads(0)
        b2_loads(1)
        b2_quant(0)
        for i in range(nt):
            E.wait("t", "v", v_hq[i])
            E.wait("t", "v", v_ffn.get(i - 1, 0))
            vfm_prev = {}
            ttr = 0

            def b2_tr(grp):
                for j in range(8):
                    E.on("t", lambda e, grp=grp, j=j: e.transpose(
                        psTs[grp % 2][:, j * 128:(j + 1) * 128],
                        b2["hqbf"][:, grp * 1024 + j * 128: grp * 1024 + (j + 1) * 128],
                        identb[:, :]))
                return E.tend()

            def b2_fm(grp, ttrg):
                E.wait("v", "t", ttrg)
                return E.chk("v", lambda e, grp=grp: e.tensor_copy(
                    out=b2["hqfm"][grp % 2][:, :], in_=psTs[grp % 2][:, :]))

            ttrs = {0: b2_tr(0)}
            vfms = {0: b2_fm(0, ttrs[0])}
            for grp in range(4):
                if grp + 1 < 4:
                    E.wait("t", "v", vfms.get(grp - 1, 0))
                    ttrs[grp + 1] = b2_tr(grp + 1)
                    vfms[grp + 1] = b2_fm(grp + 1, ttrs[grp + 1])
                E.wait("t", "v", vfms[grp])
                for j in range(8):
                    kc = grp * 8 + j
                    for hh2 in range(2):
                        E.on("t", lambda e, grp=grp, j=j, kc=kc, hh2=hh2: e.matmul(
                            psD[:, hh2 * 512:(hh2 + 1) * 512],
                            b2["hqfm"][grp % 2][:, j * 128:(j + 1) * 128],
                            b2["wq2T"][:, kc * D + hh2 * 512: kc * D + (hh2 + 1) * 512],
                            start=(kc == 0), stop=False))
            ttr = ttrs[3]
            t_trq[i] = ttr
            for hh2 in range(2):
                E.on("t", lambda e, hh2=hh2: e.matmul(
                    psD[:, hh2 * 512:(hh2 + 1) * 512], onesrb[0:1, :],
                    b2["b2s"][0:1, hh2 * 512:(hh2 + 1) * 512], start=False, stop=True))
            tmm = E.tend()
            b2_quant(i + 1)
            E.wait("s", "t", tmm)
            sffn = E.chk("s", lambda e: e.activation(
                out=b2["fbuf"][:, :], in_=psD[:, :], func=AF.Copy,
                scale=cols[:, C_SXSW:C_SXSW + 1]))
            E.wait("v", "s", sffn)
            v_ffn[i] = E.chk("v", lambda e: e.engine_nop())
            E.on("v", lambda e: e.reduce_sum(out=st[:, S_SUM:S_SUM + 1],
                                             in_=b2["fbuf"][:, :], axis=AX.X))
            E.on("v", lambda e: e.tensor_scalar_mul(
                out=st[:, S_M:S_M + 1], in0=st[:, S_SUM:S_SUM + 1], scalar1=1.0 / D))
            E.on("v", lambda e: e.tensor_mul(out=st[:, S_MSQ:S_MSQ + 1],
                                             in0=st[:, S_M:S_M + 1],
                                             in1=st[:, S_M:S_M + 1]))
            vb = E.chk("v", lambda e: e.tensor_scalar(
                out=st[:, S_BIAS:S_BIAS + 1], in0=st[:, S_MSQ:S_MSQ + 1],
                scalar1=-1.0, scalar2=EPS_LN, op0=ALU.mult, op1=ALU.add))
            E.wait("s", "v", v_hq.get(i + 1, 0))
            E.on("s", lambda e: e.activation(out=b2["scr"][:, 0:D], in_=b2["fbuf"][:, :],
                                             func=AF.Square,
                                             accum_out=st[:, S_SQ:S_SQ + 1]))
            E.wait("s", "v", vb)
            E.on("s", lambda e: e.activation(
                out=st[:, S_LNV:S_LNV + 1], in_=st[:, S_SQ:S_SQ + 1], func=AF.Ln,
                bias=st[:, S_BIAS:S_BIAS + 1], scale=1.0 / D))
            sinv = E.chk("s", lambda e: e.activation(
                out=st[:, S_INV1:S_INV1 + 1], in_=st[:, S_LNV:S_LNV + 1],
                func=AF.Exp, scale=-0.5))
            E.wait("v", "s", sinv)
            E.on("v", lambda e: e.tensor_scalar(
                out=b2["scr"][:, 0:D], in0=b2["fbuf"][:, :], scalar1=st[:, S_M:S_M + 1],
                scalar2=st[:, S_INV1:S_INV1 + 1], op0=ALU.subtract, op1=ALU.mult))
            E.wait("v", "ds", st_o.get(i - 2, 0))
            vo = E.chk("v", lambda e, i=i: e.tensor_add(
                out=b2["ob"][i % 2][:, :], in0=b2["scr"][:, 0:D],
                in1=b2["oab"][i % 2][:, :]))
            v_out[i] = vo
            E.wait("s", "v", vo)
            E.on("s", lambda e, i=i: e.dma_start(
                out=out[row0(i):row0(i) + 128, :], in_=b2["ob"][i % 2][:, :]),
                inc=[("ds", 16)])
            st_o[i] = E.cnt["ds"]
            b2_loads(i + 2)

        E.barrier()

    with nc.Block() as block:
        for cur in ENGS:
            def body(eng, cur=cur):
                sched(Env(cur, eng, sems))
            getattr(block, _BLOCK_NAME[cur])(body)

    _free(opened_global)
    return nc


def _consts():
    c = {}
    c["identb"] = np.eye(128, dtype=BF)
    id2 = np.zeros((128, 64), dtype=np.float32)
    for p in range(128):
        id2[p, p % 64] = 1.0
    c["ident2b"] = id2.astype(BF)
    c["identf"] = np.eye(128, dtype=np.float32)
    sc = np.arange(64)
    pc = np.arange(128) % 64
    lm = np.where(pc[:, None] <= sc[None, :], np.float32(-1.0 / 16.0), np.float32(0.0))
    c["lmask"] = lm.astype(BF)
    mt = np.where(pc[:, None] <= sc[None, :], np.float32(1.0), np.float32(0.0))
    c["maskt"] = np.tile(mt, (1, 4)).astype(np.float32)
    c["cones"] = np.full((128, 1), -1.0 / 16.0, dtype=BF)
    c["onesrf"] = np.ones((1, 128), dtype=np.float32)
    c["onesrb"] = np.ones((1, 128), dtype=BF)
    return c


def _shard(inputs):
    f32 = np.float32
    hs = np.ascontiguousarray(np.asarray(inputs["L0_attn_hidden_states"], dtype=f32))
    base = {
        "Wq": np.ascontiguousarray(inputs["L0_attn_Wq"], dtype=f32),
        "Wk": np.ascontiguousarray(inputs["L0_attn_Wk"], dtype=f32),
        "Wv": np.ascontiguousarray(inputs["L0_attn_Wv"], dtype=f32),
        "Wg": np.ascontiguousarray(inputs["L0_attn_Wg"], dtype=f32),
        "Wgk1": np.ascontiguousarray(inputs["L0_attn_Wgk1"], dtype=f32),
        "Wgk2": np.ascontiguousarray(inputs["L0_attn_Wgk2"], dtype=f32),
        "Wo": np.ascontiguousarray(inputs["L0_attn_Wo"], dtype=f32),
        "Wgate": np.ascontiguousarray(inputs["L0_attn_W_gate"], dtype=f32),
        "Wup": np.ascontiguousarray(inputs["L0_attn_W_up"], dtype=f32),
        "Wdown": np.ascontiguousarray(inputs["L0_attn_W_down"], dtype=f32),
        "W1": np.ascontiguousarray(inputs["L0_ffn_W1"], dtype=f32),
        "W2": np.ascontiguousarray(inputs["L0_ffn_W2"], dtype=f32),
        "gam_attn": np.ascontiguousarray(np.broadcast_to(
            np.asarray(inputs["L0_attn_attn_norm_w"], dtype=f32)[None, :], (128, D))),
        "bgk2b": np.ascontiguousarray(np.broadcast_to(
            np.asarray(inputs["L0_attn_bgk2"], dtype=f32)[None, :], (128, DKT))),
        "gamg": np.ascontiguousarray(np.broadcast_to(
            np.tile(np.asarray(inputs["L0_attn_gnorm_w"], dtype=f32), 4)[None, :],
            (128, DVT))),
        "gam_mlp": np.ascontiguousarray(np.broadcast_to(
            np.asarray(inputs["L0_attn_mlp_norm_w"], dtype=f32)[None, :], (128, D))),
        "b1row": np.ascontiguousarray(
            np.asarray(inputs["L0_ffn_b1"], dtype=f32)[None, :]),
        "b2row": np.ascontiguousarray(
            np.asarray(inputs["L0_ffn_b2"], dtype=f32)[None, :]),
        **_consts(),
    }
    in_maps = []
    for c in range(NCORES):
        m = dict(base)
        m["x"] = np.ascontiguousarray(hs[c * BC:(c + 1) * BC].reshape(BC * T, D))
        in_maps.append(m)
    return in_maps


def kernel(**inputs):
    in_maps = _shard(inputs)
    nc = _build("full")
    res = run_bass_kernel_spmd(nc, in_maps, core_ids=list(range(NCORES)))
    outs = [np.asarray(r["out"]).reshape(BC, T, D) for r in res.results]
    return np.concatenate(outs, axis=0)


# version 12 (final)
# speedup vs baseline: 1.1292x; 1.0124x over previous
import numpy as np
import ml_dtypes
import concourse.bass as bass
import concourse.mybir as mybir
from concourse.bass_utils import run_bass_kernel_spmd

F32 = mybir.dt.float32
BF16 = mybir.dt.bfloat16
I32 = mybir.dt.int32
AF = mybir.ActivationFunctionType
ALU = mybir.AluOpType
AX = mybir.AxisListType

NCORES = 8
BC = 4
T = 2048
D = 1024
NT = BC * T // 128
TPS = T // 128
NH = 4
DKT, DVT = 512, 1024
IA = 2816
IF = 4096
EPS_RMS = 1e-6
EPS_LN = 1e-5
LN_SCALE_Q = float(np.log(np.float32(128.0) ** np.float32(-0.5)))
BF = ml_dtypes.bfloat16

ENGS = ("sy", "t", "v", "s", "g")
_BLOCK_NAME = {"sy": "sync", "t": "tensor", "v": "vector", "s": "scalar", "g": "gpsimd"}
QUEUES = ("dx", "ds", "dw")
SEMS = ("sy", "t", "v", "s", "g", "cc") + tuple(
    q + p for q in QUEUES for p in ("0", "1"))


class Env:

    def __init__(self, cur, eng, sems):
        self.cur = cur
        self.eng = eng
        self.sems = sems
        self.cnt = {k: 0 for k in SEMS}
        for q in QUEUES:
            self.cnt[q] = 0
        self.qpar = {q: 0 for q in QUEUES}
        self.prev_op = {}
        self.last_instr = {}
        self.qat = {q: {0: {}} for q in QUEUES}

    def on(self, ename, fn, inc=None, is_drain=False):
        if ename in ("v", "s") and not is_drain and self.prev_op.get(ename):
            self.prev_op[ename] = False
            self.on(ename, lambda e: e.drain(), is_drain=True)
        real = []
        if inc is not None:
            for s_, a_ in inc:
                if s_ in QUEUES:
                    p = self.qpar[s_]
                    self.qpar[s_] ^= 1
                    sem = s_ + str(p)
                    self.wait(ename, sem, self.cnt[sem])
                    real.append((sem, a_))
                else:
                    real.append((s_, a_))
        if ename == self.cur:
            instr = fn(self.eng)
            for s_, a_ in real:
                instr.then_inc(self.sems[s_], a_)
            if not is_drain:
                self.last_instr[ename] = instr
        for s_, a_ in real:
            self.cnt[s_] += a_
        if inc is not None:
            for s_, a_ in inc:
                if s_ in QUEUES:
                    self.cnt[s_] += a_
                    self.qat[s_][self.cnt[s_]] = {
                        s_ + "0": self.cnt[s_ + "0"], s_ + "1": self.cnt[s_ + "1"]}
        if not is_drain:
            self.prev_op[ename] = True

    def wait(self, ename, sem, val):
        if val <= 0:
            return
        if sem in QUEUES:
            snap = self.qat[sem].get(val)
            assert snap is not None, f"wait on unrecorded {sem} count {val}"
            for s_, v_ in snap.items():
                self.wait(ename, s_, v_)
            return
        if ename == self.cur:
            self.eng.wait_ge(self.sems[sem], val)


    def chk(self, ename, fn, sem=None):
        sem = sem or ename
        self.on(ename, fn, inc=[(sem, 1)])
        return self.cnt[sem]

    def lastinc(self, ename):
        if ename == self.cur:
            self.last_instr[ename].then_inc(self.sems[ename], 1)
        self.cnt[ename] += 1
        return self.cnt[ename]

    def tend(self):
        self.on("t", lambda e: e.drain(), is_drain=True)
        return self.chk("t", lambda e: e.nop())

    def barrier(self):
        self.wait("sy", "dx", self.cnt["dx"])
        self.wait("s", "ds", self.cnt["ds"])
        self.wait("g", "dw", self.cnt["dw"])
        snaps = {}
        for e in ENGS:
            self.wait(e, e, self.cnt[e])
            snaps[e] = self.chk(e, lambda eng: eng.nop())
        for e in ENGS:
            for o in ENGS:
                if o != e:
                    self.wait(e, o, snaps[o])


def _alloc(nc, opened, kind, *args, **kw):
    cm = getattr(nc, kind)(*args, **kw)
    h = cm.__enter__()
    opened.append(cm)
    return h


def _free(opened):
    for cm in reversed(opened):
        cm.__exit__(None, None, None)
    opened.clear()


def _build(phase_limit="full", nt=NT):
    nc = bass.Bass("TRN2", target_bir_lowering=False, debug=False)

    def din(name, shape):
        return nc.dram_tensor(name, shape, F32, kind="ExternalInput").ap()

    def dinb(name, shape):
        return nc.dram_tensor(name, shape, BF16, kind="ExternalInput").ap()

    x = din("x", [BC * T, D])
    Wq, Wk = din("Wq", [D, DKT]), din("Wk", [D, DKT])
    Wv, Wg = din("Wv", [D, DVT]), din("Wg", [D, DVT])
    Wgk1, Wgk2 = din("Wgk1", [D, 16]), din("Wgk2", [16, DKT])
    Wo = din("Wo", [DVT, D])
    Wgate, Wup = din("Wgate", [D, IA]), din("Wup", [D, IA])
    Wdown = din("Wdown", [IA, D])
    W1, W2 = din("W1", [IF, D]), din("W2", [D, IF])
    gam_attn = din("gam_attn", [128, D])
    bgk2b = din("bgk2b", [128, DKT])
    gamg = din("gamg", [128, DVT])
    gam_mlp = din("gam_mlp", [128, D])
    b1row = din("b1row", [1, IF])
    b2row = din("b2row", [1, D])
    identb_d = dinb("identb", [128, 128])
    ident2b_d = dinb("ident2b", [128, 64])
    identf_d = din("identf", [128, 128])
    lmask_d = dinb("lmask", [128, 64])
    maskt_d = din("maskt", [128, 256])
    cones_d = dinb("cones", [128, 1])
    onesrf_d = din("onesrf", [1, 128])
    onesrb_d = dinb("onesrb", [1, 128])

    x2h = nc.dram_tensor("x2h", [BC * T, D], F32).ap()
    oAh = nc.dram_tensor("oAh", [BC * T, D], F32).ap()
    hh = nc.dram_tensor("hh", [BC * T, IF], F32).ap()
    wq1h = nc.dram_tensor("wq1h", [IF, D], BF16).ap()
    wq2h = nc.dram_tensor("wq2h", [D, IF], BF16).ap()
    ccin = nc.dram_tensor("ccin", [1, 8], F32)
    ccout = nc.dram_tensor("ccout", [1, 8], F32)
    out = nc.dram_tensor("out", [BC * T, D], F32, kind="ExternalOutput").ap()

    opened_global = []
    opened_phase = []

    def ga(kind, *a, **k):
        return _alloc(nc, opened_global, kind, *a, **k)

    def pa(kind, *a, **k):
        return _alloc(nc, opened_phase, kind, *a, **k)

    sems = {n: ga("semaphore", "sem_" + n) for n in SEMS}
    identb = ga("sbuf_tensor", "identb_t", [128, 128], BF16)
    ident2 = ga("sbuf_tensor", "ident2_t", [128, 64], BF16)
    identf = ga("sbuf_tensor", "identf_t", [128, 128], F32)
    lmask = ga("sbuf_tensor", "lmask_t", [128, 64], BF16)
    maskt = ga("sbuf_tensor", "maskt_t", [128, 256], F32)
    cones = ga("sbuf_tensor", "cones_t", [128, 1], BF16)
    onesrf = ga("sbuf_tensor", "onesrf_t", [1, 128], F32)
    onesrb = ga("sbuf_tensor", "onesrb_t", [1, 128], BF16)
    st = ga("sbuf_tensor", "st", [128, 32], F32)
    cols = ga("sbuf_tensor", "cols", [128, 16], F32)
    rows = ga("sbuf_tensor", "rows", [1, 32], F32)
    rowi = ga("sbuf_tensor", "rowi", [1, 4], I32)
    ymt = ga("sbuf_tensor", "ymt", [128, NT], F32)
    yinvt = ga("sbuf_tensor", "yinvt", [128, NT], F32)
    ccsb = ga("sbuf_tensor", "ccsb", [1, 8], F32)

    (R_AMAX, R_NAMIN, R_D, R_SX, R_ISX, R_ZT, R_ZP, R_LO, R_HI, R_SXSW, R_ISXSW,
     R_SW1, R_SW2, R_WMAX, R_T0, R_T1) = range(16)
    C_ISX, C_LO, C_HI, C_SXSW = 0, 1, 2, 3
    C_ONE, C_EPSR, C_LNSQ = 9, 10, 11
    (S_SUM, S_SQ, S_M, S_MSQ, S_BIAS, S_LNV, S_INV1, S_SQ2, S_INV2) = range(9)
    S_G0 = 10
    S_GI = 14
    S_RED = 20
    S_SUM2, S_MSQ2, S_BIAS2 = 24, 25, 26
    S_RED2 = 22
    S_FSQ, S_FLN, S_FINV = 28, 29, 30

    def row0(i):
        return 128 * i

    a1 = {}
    a1["wq"] = pa("sbuf_tensor", "a1_wq", [128, 8 * DKT], BF16)
    a1["wk"] = pa("sbuf_tensor", "a1_wk", [128, 8 * DKT], BF16)
    a1["wv"] = pa("sbuf_tensor", "a1_wv", [128, 8 * DVT], BF16)
    a1["wg"] = pa("sbuf_tensor", "a1_wg", [128, 8 * DVT], BF16)
    a1["wgk1"] = pa("sbuf_tensor", "a1_wgk1", [128, 8 * 16], BF16)
    a1["wgk2"] = pa("sbuf_tensor", "a1_wgk2", [16, DKT], BF16)
    a1["wo"] = pa("sbuf_tensor", "a1_wo", [128, 8 * D], BF16)
    a1["gat"] = pa("sbuf_tensor", "a1_gat", [128, D], F32)
    a1["bgk"] = pa("sbuf_tensor", "a1_bgk", [128, DKT], F32)
    a1["gag"] = pa("sbuf_tensor", "a1_gag", [128, DVT], F32)
    a1["xb"] = [pa("sbuf_tensor", f"a1_xb{j}", [128, D], F32) for j in range(2)]
    a1["xin"] = [pa("sbuf_tensor", f"a1_xin{j}", [128, D], F32) for j in range(2)]
    a1["x2b"] = [pa("sbuf_tensor", f"a1_x2b{j}", [128, D], F32) for j in range(2)]
    a1["hbf"] = pa("sbuf_tensor", "a1_hbf", [128, D], BF16)
    a1["hfm"] = pa("sbuf_tensor", "a1_hfm", [128, D], BF16)
    a1["q"] = pa("sbuf_tensor", "a1_q", [128, DKT], F32)
    a1["k"] = pa("sbuf_tensor", "a1_k", [128, DKT], F32)
    a1["vbf"] = pa("sbuf_tensor", "a1_vbf", [128, DVT], BF16)
    a1["g"] = pa("sbuf_tensor", "a1_g", [128, DVT], F32)
    a1["zg"] = pa("sbuf_tensor", "a1_zg", [128, DKT], F32)
    a1["gbf"] = pa("sbuf_tensor", "a1_gbf", [128, DKT], BF16)
    a1["expq"] = pa("sbuf_tensor", "a1_expq", [128, DKT], F32)
    a1["expm"] = pa("sbuf_tensor", "a1_expm", [128, DKT], F32)
    a1["qebf"] = pa("sbuf_tensor", "a1_qebf", [128, DKT], BF16)
    a1["kebf"] = pa("sbuf_tensor", "a1_kebf", [128, DKT], BF16)
    a1["qkfm"] = pa("sbuf_tensor", "a1_qkfm", [128, 2 * DKT], BF16)
    a1["atm"] = pa("sbuf_tensor", "a1_atm", [128, 256], BF16)
    a1["ebl"] = pa("sbuf_tensor", "a1_ebl", [128, 8], F32)
    a1["Sf"] = pa("sbuf_tensor", "a1_Sf", [128, DVT], F32)
    a1["Sb"] = pa("sbuf_tensor", "a1_Sb", [128, DVT], BF16)
    a1["on"] = pa("sbuf_tensor", "a1_on", [128, DVT], F32)
    a1["o2bf"] = pa("sbuf_tensor", "a1_o2bf", [128, DVT], BF16)
    a1["o2fm"] = pa("sbuf_tensor", "a1_o2fm", [128, DVT], BF16)
    a1["scrA"] = pa("sbuf_tensor", "a1_scrA", [128, D], F32)
    a1["scrB"] = pa("sbuf_tensor", "a1_scrB", [128, D], F32)
    a1["t1bf"] = pa("sbuf_tensor", "a1_t1bf", [128, 16], BF16)
    a1["t1fm"] = pa("sbuf_tensor", "a1_t1fm", [16, 128], BF16)
    a1["wbuf"] = [pa("sbuf_tensor", f"a1_wbuf{j}", [128, D], F32) for j in range(2)]
    a1["wqi"] = pa("sbuf_tensor", "a1_wqi", [128, D], I32)
    a1["wqo"] = pa("sbuf_tensor", "a1_wqo", [128, D], BF16)
    a1["psA"] = pa("psum_tensor", [128, 1024], F32)
    a1["psB"] = pa("psum_tensor", [128, 1024], F32)
    a1["psC"] = pa("psum_tensor", [128, 1024], F32)
    a1["psT"] = pa("psum_tensor", [128, 1024], BF16)
    a1["psX"] = pa("psum_tensor", [128, 512], F32)
    a1_opened = list(opened_phase)
    opened_phase.clear()
    _free(a1_opened)

    a2 = {}
    a2["wgate"] = pa("sbuf_tensor", "a2_wgate", [128, 8 * IA], BF16)
    a2["wup"] = pa("sbuf_tensor", "a2_wup", [128, 8 * IA], BF16)
    a2["wdown"] = pa("sbuf_tensor", "a2_wdown", [128, 22 * D], BF16)
    a2["gam"] = pa("sbuf_tensor", "a2_gam", [128, D], F32)
    a2["x2b"] = [pa("sbuf_tensor", f"a2_x2b{j}", [128, D], F32) for j in range(2)]
    a2["xb"] = [pa("sbuf_tensor", f"a2_xb{j}", [128, D], F32) for j in range(2)]
    a2["oab"] = [pa("sbuf_tensor", f"a2_oab{j}", [128, D], F32) for j in range(2)]
    a2["nbf"] = pa("sbuf_tensor", "a2_nbf", [128, D], BF16)
    a2["nfm"] = pa("sbuf_tensor", "a2_nfm", [128, D], BF16)
    a2["gate"] = [pa("sbuf_tensor", f"a2_gate{j}", [128, 1408], F32)
                  for j in range(2)]
    a2["scrA"] = pa("sbuf_tensor", "a2_scrA", [128, 1408], F32)
    a2["scrB"] = pa("sbuf_tensor", "a2_scrB", [128, 1408], F32)
    a2["scrC"] = pa("sbuf_tensor", "a2_scrC", [128, 1408], F32)
    a2["mlpb"] = [pa("sbuf_tensor", f"a2_mlpb{j}", [128, 1408], BF16)
                  for j in range(2)]
    a2["mlpfm"] = pa("sbuf_tensor", "a2_mlpfm", [128, 1408], BF16)
    a2["u"] = pa("sbuf_tensor", "a2_u", [128, D], F32)
    a2["ymaxr"] = pa("sbuf_tensor", "a2_ymaxr", [128, 1], F32)
    a2["yminr"] = pa("sbuf_tensor", "a2_yminr", [128, 1], F32)
    a2["psG"] = pa("psum_tensor", [128, 1536], F32)
    a2["psD"] = pa("psum_tensor", [128, 1024], F32)
    a2["psT"] = pa("psum_tensor", [128, 2048], BF16)
    a2["psX"] = pa("psum_tensor", [128, 512], F32)
    a2_opened = list(opened_phase)
    opened_phase.clear()
    _free(a2_opened)

    b1 = {}
    b1["wq1T"] = pa("sbuf_tensor", "b1_wq1T", [128, 8 * IF], BF16)
    b1["oab"] = [pa("sbuf_tensor", f"b1_oab{j}", [128, D], F32) for j in range(2)]
    b1["hbuf"] = [pa("sbuf_tensor", f"b1_hbuf{j}", [128, IF], F32) for j in range(2)]
    b1["y"] = pa("sbuf_tensor", "b1_y", [128, D], F32)
    b1["scr"] = pa("sbuf_tensor", "b1_scr", [128, D], F32)
    b1["qi"] = pa("sbuf_tensor", "b1_qi", [128, D], I32)
    b1["xqbf"] = pa("sbuf_tensor", "b1_xqbf", [128, D], BF16)
    b1["xqfm"] = pa("sbuf_tensor", "b1_xqfm", [128, D], BF16)
    b1["hmaxr"] = pa("sbuf_tensor", "b1_hmaxr", [128, 1], F32)
    b1["hminr"] = pa("sbuf_tensor", "b1_hminr", [128, 1], F32)
    b1["b1s"] = pa("sbuf_tensor", "b1_b1s", [1, IF], BF16)
    b1["psF"] = pa("psum_tensor", [128, 2048], F32)
    b1["psT"] = pa("psum_tensor", [128, 1024], BF16)
    b1["psX"] = pa("psum_tensor", [128, 512], F32)
    b1_opened = list(opened_phase)
    opened_phase.clear()
    _free(b1_opened)

    b2 = {}
    b2["wq2T"] = pa("sbuf_tensor", "b2_wq2T", [128, 32 * D], BF16)
    b2["hb"] = [pa("sbuf_tensor", f"b2_hb{j}", [128, IF], F32) for j in range(2)]
    b2["oab"] = [pa("sbuf_tensor", f"b2_oab{j}", [128, D], F32) for j in range(2)]
    b2["ob"] = [pa("sbuf_tensor", f"b2_ob{j}", [128, D], F32) for j in range(2)]
    b2["scr"] = pa("sbuf_tensor", "b2_scr", [128, IF], F32)
    b2["qi"] = pa("sbuf_tensor", "b2_qi", [128, IF], I32)
    b2["hqbf"] = pa("sbuf_tensor", "b2_hqbf", [128, IF], BF16)
    b2["hqfm"] = [pa("sbuf_tensor", f"b2_hqfm{j}", [128, D], BF16)
                  for j in range(2)]
    b2["fbuf"] = pa("sbuf_tensor", "b2_fbuf", [128, D], F32)
    b2["b2s"] = pa("sbuf_tensor", "b2_b2s", [1, D], BF16)
    b2["psD"] = pa("psum_tensor", [128, 1024], F32)
    b2["psT"] = pa("psum_tensor", [128, 1024], BF16)
    b2["psT2"] = pa("psum_tensor", [128, 1024], BF16)
    b2["psX"] = pa("psum_tensor", [128, 512], F32)
    b2_opened = list(opened_phase)
    opened_phase.clear()
    _free(b2_opened)

    def sched(env: Env):
        E = env
        E.on("v", lambda e: e.memset(cols[:, C_ONE:C_ONE + 1], 1.0))
        E.on("v", lambda e: e.memset(cols[:, C_EPSR:C_EPSR + 1], EPS_RMS))
        E.on("v", lambda e: e.memset(cols[:, C_LNSQ:C_LNSQ + 1], LN_SCALE_Q))

        for dst, src in ((identb, identb_d), (ident2, ident2b_d), (identf, identf_d),
                         (lmask, lmask_d), (maskt, maskt_d), (cones, cones_d),
                         (onesrf, onesrf_d), (onesrb, onesrb_d),
                         (a1["gat"], gam_attn), (a1["bgk"], bgk2b), (a1["gag"], gamg)):
            E.on("sy", lambda e, d=dst, s=src: e.dma_start(out=d[:, :], in_=s),
                 inc=[("dx", 16)])
        for wsb, wdr, ncol in ((a1["wq"], Wq, DKT), (a1["wk"], Wk, DKT),
                               (a1["wv"], Wv, DVT), (a1["wg"], Wg, DVT),
                               (a1["wgk1"], Wgk1, 16), (a1["wo"], Wo, D)):
            for c in range(8):
                E.on("g", lambda e, d=wsb, s=wdr, c=c, n=ncol:
                     e.dma_start(out=d[:, c * n:(c + 1) * n],
                                 in_=s[c * 128:(c + 1) * 128, :]),
                     inc=[("dw", 16)])
        E.on("g", lambda e: e.dma_start(out=a1["wgk2"][:, :], in_=Wgk2),
             inc=[("dw", 16)])
        dw_a1w = E.cnt["dw"]
        dx_setup = E.cnt["dx"]

        def wprep(slices, wqh_slices, r_sw, idx):
            nchunks = len(slices)
            wp_free, wp2_free, wq_st = {}, {}, {}
            ld = {}

            def emit_load(ldmap, c):
                E.on("sy", lambda e, c=c: e.dma_start(
                    out=a1["wbuf"][c % 2][:, :], in_=slices[c]), inc=[("dx", 16)])
                ldmap[c] = E.cnt["dx"]

            E.wait("sy", "v", E.cnt["v"])
            emit_load(ld, 0)
            emit_load(ld, 1)
            for c in range(nchunks):
                E.wait("v", "dx", ld[c])
                E.on("v", lambda e, c=c: e.reduce_max(
                    out=st[:, S_RED:S_RED + 1], in_=a1["wbuf"][c % 2][:, :],
                    axis=AX.X, apply_absolute_value=True))
                if c == 0:
                    v = E.chk("v", lambda e: e.tensor_copy(
                        out=st[:, S_RED + 1:S_RED + 2], in_=st[:, S_RED:S_RED + 1]))
                else:
                    v = E.chk("v", lambda e: e.tensor_max(
                        out=st[:, S_RED + 1:S_RED + 2], in0=st[:, S_RED + 1:S_RED + 2],
                        in1=st[:, S_RED:S_RED + 1]))
                wp_free[c] = v
                if c + 2 < nchunks:
                    E.wait("sy", "v", wp_free[c])
                    emit_load(ld, c + 2)
            E.wait("t", "v", wp_free[nchunks - 1])
            t = E.chk("t", lambda e: e.transpose(
                a1["psX"][0:1, idx * 128:(idx + 1) * 128],
                st[:, S_RED + 1:S_RED + 2], identf[:, :]))
            E.wait("v", "t", t)
            E.on("v", lambda e: e.reduce_max(
                out=rows[0:1, R_WMAX:R_WMAX + 1],
                in_=a1["psX"][0:1, idx * 128:(idx + 1) * 128], axis=AX.X))
            E.on("v", lambda e: e.tensor_scalar(
                out=rows[0:1, r_sw:r_sw + 1], in0=rows[0:1, R_WMAX:R_WMAX + 1],
                scalar1=1.0 / 127.0, scalar2=1e-8, op0=ALU.mult, op1=ALU.max))
            vsw = E.chk("v", lambda e: e.reciprocal(
                out=rows[0:1, R_T0 + idx:R_T0 + idx + 1],
                in_=rows[0:1, r_sw:r_sw + 1]))
            E.wait("t", "v", vsw)
            t = E.chk("t", lambda e: e.matmul(
                a1["psX"][:, idx:idx + 1], onesrf[0:1, :],
                rows[0:1, R_T0 + idx:R_T0 + idx + 1], start=True, stop=True))
            E.wait("v", "t", t)
            vbc = E.chk("v", lambda e: e.tensor_copy(out=cols[:, 8:9],
                                                     in_=a1["psX"][:, idx:idx + 1]))
            ld2 = {}
            E.wait("sy", "v", wp_free[nchunks - 1])
            emit_load(ld2, 0)
            emit_load(ld2, 1)
            for c in range(nchunks):
                E.wait("v", "dx", ld2[c])
                E.wait("v", "dw", wq_st.get(c - 1, 0))
                E.on("v", lambda e, c=c: e.tensor_scalar_mul(
                    out=a1["wbuf"][c % 2][:, :], in0=a1["wbuf"][c % 2][:, :],
                    scalar1=cols[:, 8:9]))
                E.on("v", lambda e, c=c: e.tensor_copy(
                    out=a1["wqi"][:, :], in_=a1["wbuf"][c % 2][:, :]))
                v = E.chk("v", lambda e, c=c: e.tensor_scalar(
                    out=a1["wqo"][:, :], in0=a1["wqi"][:, :],
                    scalar1=-127.0, scalar2=127.0, op0=ALU.max, op1=ALU.min))
                wp2_free[c] = v
                E.wait("g", "v", v)
                E.on("g", lambda e, c=c: e.dma_start(
                    out=wqh_slices[c], in_=a1["wqo"][:, :]), inc=[("dw", 16)])
                wq_st[c] = E.cnt["dw"]
                if c + 2 < nchunks:
                    E.wait("sy", "v", wp2_free[c])
                    emit_load(ld2, c + 2)

        w1_src = [W1[c * 128:(c + 1) * 128, :] for c in range(32)]
        w1_dst = [wq1h[c * 128:(c + 1) * 128, :] for c in range(32)]
        wprep(w1_src, w1_dst, R_SW1, 0)
        E.wait("v", "dw", E.cnt["dw"])
        w2_src = [W2[r * 128:(r + 1) * 128, cc * D:(cc + 1) * D]
                  for r in range(8) for cc in range(4)]
        w2_dst = [wq2h[r * 128:(r + 1) * 128, cc * D:(cc + 1) * D]
                  for r in range(8) for cc in range(4)]
        wprep(w2_src, w2_dst, R_SW2, 1)

        E.barrier()

        psA, psB, psC, psT, psX = (a1[k] for k in ("psA", "psB", "psC", "psT", "psX"))
        v_xin, s_sq, v_x2, st_x2, ld_x = {}, {}, {}, {}, {}
        v_hbf, v_o2fm, v_on, v_zb, v_atm1, v_o2 = {}, {}, {}, {}, {}, {}
        v_supd = 0
        vatm_prev = 0
        vqkfm_0 = 0
        sexp_last, supd_last, sebl_last, t_o_last = {}, {}, {}, {}

        E.on("sy", lambda e: e.dma_start(out=a1["xb"][0][:, :], in_=x[0:128, :]),
             inc=[("dx", 16)])
        ld_x[0] = E.cnt["dx"]

        def emit_wo_x2(i):
            E.wait("t", "v", v_o2fm[i])
            for kc in range(8):
                for half in range(2):
                    E.on("t", lambda e, kc=kc, half=half: e.matmul(
                        psB[:, half * 512:(half + 1) * 512],
                        a1["o2fm"][:, kc * 128:(kc + 1) * 128],
                        a1["wo"][:, kc * D + half * 512: kc * D + (half + 1) * 512],
                        start=(kc == 0), stop=(kc == 7)))
            two = E.tend()
            E.wait("v", "t", two)
            E.wait("v", "ds", st_x2.get(i - 2, 0))
            vx2 = E.chk("v", lambda e, i=i: e.tensor_add(
                out=a1["x2b"][i % 2][:, :], in0=psB[:, :], in1=a1["xin"][i % 2][:, :]))
            v_x2[i] = vx2
            E.wait("s", "v", vx2)
            E.on("s", lambda e, i=i: e.dma_start(
                out=x2h[row0(i):row0(i) + 128, :], in_=a1["x2b"][i % 2][:, :]),
                inc=[("ds", 16)])
            if phase_limit == "a1":
                E.on("s", lambda e, i=i: e.dma_start(
                    out=out[row0(i):row0(i) + 128, :], in_=a1["x2b"][i % 2][:, :]),
                    inc=[("ds", 16)])
            st_x2[i] = E.cnt["ds"]

        def a1_front(j):
            if j >= nt:
                return
            if j + 1 < nt:
                E.wait("sy", "v", v_xin.get(j - 1, 0))
                E.wait("sy", "s", s_sq.get(j - 1, 0))
                E.on("sy", lambda e, j=j: e.dma_start(
                    out=a1["xb"][(j + 1) % 2][:, :],
                    in_=x[row0(j + 1):row0(j + 1) + 128, :]), inc=[("dx", 16)])
                ld_x[j + 1] = E.cnt["dx"]
            if j % TPS == 0:
                E.wait("v", "t", t_o_last.get(j - 1, 0))
                E.on("v", lambda e: e.memset(a1["Sf"][:, :], 0.0))
                E.on("v", lambda e: e.memset(a1["Sb"][:, :], 0.0))
            E.wait("v", "dx", ld_x[j])
            E.on("v", lambda e, j=j: e.reduce_sum(
                out=st[:, S_SUM:S_SUM + 1], in_=a1["xb"][j % 2][:, :], axis=AX.X))
            E.on("v", lambda e: e.tensor_scalar_mul(
                out=st[:, S_M:S_M + 1], in0=st[:, S_SUM:S_SUM + 1], scalar1=1.0 / D))
            E.on("v", lambda e: e.tensor_mul(out=st[:, S_MSQ:S_MSQ + 1],
                                             in0=st[:, S_M:S_M + 1],
                                             in1=st[:, S_M:S_M + 1]))
            vb = E.chk("v", lambda e: e.tensor_scalar(
                out=st[:, S_BIAS:S_BIAS + 1], in0=st[:, S_MSQ:S_MSQ + 1],
                scalar1=-1.0, scalar2=EPS_LN, op0=ALU.mult, op1=ALU.add))
            E.wait("s", "dx", ld_x[j])
            E.wait("s", "v", v_o2.get(j - 1, 0))
            s_sq[j] = E.chk("s", lambda e, j=j: e.activation(
                out=a1["scrB"][:, :], in_=a1["xb"][j % 2][:, :], func=AF.Square,
                accum_out=st[:, S_SQ:S_SQ + 1]))
            E.wait("s", "v", vb)
            E.on("s", lambda e: e.activation(
                out=st[:, S_LNV:S_LNV + 1], in_=st[:, S_SQ:S_SQ + 1], func=AF.Ln,
                bias=st[:, S_BIAS:S_BIAS + 1], scale=1.0 / D))
            sinv = E.chk("s", lambda e: e.activation(
                out=st[:, S_INV1:S_INV1 + 1], in_=st[:, S_LNV:S_LNV + 1],
                func=AF.Exp, scale=-0.5))
            E.wait("v", "s", sinv)
            vxin = E.chk("v", lambda e, j=j: e.tensor_scalar(
                out=a1["xin"][j % 2][:, :], in0=a1["xb"][j % 2][:, :],
                scalar1=st[:, S_M:S_M + 1], scalar2=st[:, S_INV1:S_INV1 + 1],
                op0=ALU.subtract, op1=ALU.mult))
            v_xin[j] = vxin
            E.wait("s", "v", vxin)
            E.on("s", lambda e, j=j: e.activation(
                out=a1["scrB"][:, :], in_=a1["xin"][j % 2][:, :], func=AF.Square,
                accum_out=st[:, S_SQ2:S_SQ2 + 1]))
            E.on("s", lambda e: e.activation(
                out=st[:, S_LNV:S_LNV + 1], in_=st[:, S_SQ2:S_SQ2 + 1], func=AF.Ln,
                bias=cols[:, C_EPSR:C_EPSR + 1], scale=1.0 / D))
            sinv2 = E.chk("s", lambda e: e.activation(
                out=st[:, S_INV2:S_INV2 + 1], in_=st[:, S_LNV:S_LNV + 1],
                func=AF.Exp, scale=-0.5))
            E.wait("v", "s", sinv2)
            vhbf = E.chk("v", lambda e, j=j: e.scalar_tensor_tensor(
                out=a1["hbf"][:, :], in0=a1["xin"][j % 2][:, :],
                scalar=st[:, S_INV2:S_INV2 + 1], in1=a1["gat"][:, :],
                op0=ALU.mult, op1=ALU.mult))
            v_hbf[j] = vhbf
        a1_front(0)

        for i in range(nt):
            E.wait("t", "v", v_hbf[i])
            E.wait("t", "v", v_o2fm.get(i - 1, 0))
            for j in range(8):
                E.on("t", lambda e, j=j: e.transpose(
                    psT[:, j * 128:(j + 1) * 128], a1["hbf"][:, j * 128:(j + 1) * 128],
                    identb[:, :]))
            t_htr = E.tend()
            E.wait("v", "t", t_htr)
            vhfm = E.chk("v", lambda e: e.tensor_copy(out=a1["hfm"][:, :], in_=psT[:, :]))
            E.wait("t", "v", vhfm)
            E.wait("t", "v", v_on.get(i - 1, 0))
            E.wait("t", "v", v_x2.get(i - 1, 0))
            E.wait("t", "v", v_atm1.get(i - 1, 0))
            E.wait("t", "v", v_zb.get(i - 1, 0))
            for kc in range(8):
                E.on("t", lambda e, kc=kc: e.matmul(
                    psX[:, 0:16], a1["hfm"][:, kc * 128:(kc + 1) * 128],
                    a1["wgk1"][:, kc * 16:(kc + 1) * 16],
                    start=(kc == 0), stop=(kc == 7)))
            t_t1 = E.tend()
            E.wait("v", "t", t_t1)
            vt1 = E.chk("v", lambda e: e.tensor_copy(out=a1["t1bf"][:, :],
                                                     in_=psX[:, 0:16]))
            E.wait("t", "v", vt1)
            tt1 = E.chk("t", lambda e: e.transpose(
                psT[0:16, 0:128], a1["t1bf"][:, :], identb[:, :]))
            E.wait("v", "t", tt1)
            vt1f = E.chk("v", lambda e: e.tensor_copy(out=a1["t1fm"][:, :],
                                                      in_=psT[0:16, 0:128]))
            E.wait("t", "v", vt1f)
            tz = E.chk("t", lambda e: e.matmul(
                psX[:, 0:512], a1["t1fm"][:, :], a1["wgk2"][:, :],
                start=True, stop=True))
            E.wait("v", "t", tz)
            vzb = E.chk("v", lambda e: e.tensor_add(
                out=a1["zg"][:, :], in0=psX[:, 0:512], in1=a1["bgk"][:, :]))
            v_zb[i] = vzb
            E.wait("s", "v", vzb)
            E.on("s", lambda e: e.activation(out=a1["scrA"][:, 0:512],
                                             in_=a1["zg"][:, :], func=AF.Exp,
                                             scale=-1.0))
            for kc in range(8):
                lhs = a1["hfm"][:, kc * 128:(kc + 1) * 128]
                ss = (kc == 0, kc == 7)
                E.on("t", lambda e, l=lhs, kc=kc, ss=ss: e.matmul(
                    psA[:, 0:512], l, a1["wq"][:, kc * DKT:(kc + 1) * DKT],
                    start=ss[0], stop=ss[1]))
                E.on("t", lambda e, l=lhs, kc=kc, ss=ss: e.matmul(
                    psA[:, 512:1024], l, a1["wk"][:, kc * DKT:(kc + 1) * DKT],
                    start=ss[0], stop=ss[1]))
                for half in range(2):
                    E.on("t", lambda e, l=lhs, kc=kc, half=half, ss=ss: e.matmul(
                        psB[:, half * 512:(half + 1) * 512], l,
                        a1["wv"][:, kc * DVT + half * 512: kc * DVT + (half + 1) * 512],
                        start=ss[0], stop=ss[1]))
                    E.on("t", lambda e, l=lhs, kc=kc, half=half, ss=ss: e.matmul(
                        psC[:, half * 512:(half + 1) * 512], l,
                        a1["wg"][:, kc * DVT + half * 512: kc * DVT + (half + 1) * 512],
                        start=ss[0], stop=ss[1]))
            t_proj = E.tend()
            E.wait("v", "t", t_proj)
            E.on("v", lambda e: e.tensor_copy(out=a1["q"][:, :], in_=psA[:, 0:512]))
            vt1_unused = E.chk("v", lambda e: e.tensor_copy(out=a1["vbf"][:, :],
                                                            in_=psB[:, :]))
            E.wait("s", "t", t_proj)
            E.on("s", lambda e: e.copy(out=a1["k"][:, :], in_=psA[:, 512:1024]))
            skg = E.chk("s", lambda e: e.copy(out=a1["g"][:, :], in_=psC[:, :]))
            sgbf = E.chk("s", lambda e: e.activation(
                out=a1["gbf"][:, :], in_=a1["scrA"][:, 0:512], func=AF.Ln,
                bias=cols[:, C_ONE:C_ONE + 1]))

            E.wait("t", "s", sgbf)
            E.wait("t", "s", skg)
            E.wait("t", "s", sexp_last.get(i - 1, 0))
            tb0 = E.chk("t", lambda e: e.matmul(
                psC[0:64, 0:512], lmask[0:64, :], a1["gbf"][0:64, :],
                start=True, stop=True))
            tb1 = E.chk("t", lambda e: e.matmul(
                psC[64:128, 0:512], lmask[64:128, :], a1["gbf"][64:128, :],
                start=True, stop=True))
            E.wait("s", "t", tb1)
            E.on("s", lambda e: e.activation(
                out=a1["expq"][:, :], in_=psC[:, 0:512], func=AF.Exp,
                bias=cols[:, C_LNSQ:C_LNSQ + 1]))
            sexp = E.chk("s", lambda e: e.activation(
                out=a1["expm"][:, :], in_=psC[:, 0:512], func=AF.Exp, scale=-1.0))
            sexp_last[i] = sexp
            E.wait("t", "s", sexp)
            for ch in range(2):
                r0 = 64 * ch
                for h in range(NH):
                    E.on("t", lambda e, h=h, r0=r0, ch=ch: e.matmul(
                        psC[:, 1008 + ch * 4 + h:1009 + ch * 4 + h],
                        a1["gbf"][r0:r0 + 64, h * 128:(h + 1) * 128],
                        cones[r0:r0 + 64, :], start=True, stop=True))
            tbl = E.tend()
            E.wait("s", "t", tbl)
            E.wait("s", "v", supd_last.get(i - 1, 0))
            sebl = E.chk("s", lambda e: e.activation(
                out=a1["ebl"][:, 0:8], in_=psC[:, 1008:1016], func=AF.Exp))
            sebl_last[i] = sebl
            E.wait("v", "s", sexp)
            E.on("v", lambda e: e.tensor_mul(out=a1["qebf"][:, :], in0=a1["q"][:, :],
                                             in1=a1["expq"][:, :]))
            vqk = E.chk("v", lambda e: e.tensor_mul(
                out=a1["kebf"][:, :], in0=a1["k"][:, :], in1=a1["expm"][:, :]))
            E.wait("t", "v", vqk)
            E.wait("t", "v", vt1f)
            for ch in range(2):
                r0 = 64 * ch
                co = ch * 512
                if ch == 1:
                    E.wait("t", "v", vqkfm_0)
                for h in range(NH):
                    E.on("t", lambda e, h=h, r0=r0: e.transpose(
                        psT[:, h * 64:(h + 1) * 64],
                        a1["qebf"][r0:r0 + 64, h * 128:(h + 1) * 128],
                        ident2[r0:r0 + 64, :]))
                    E.on("t", lambda e, h=h, r0=r0: e.transpose(
                        psT[:, 256 + h * 64:256 + (h + 1) * 64],
                        a1["kebf"][r0:r0 + 64, h * 128:(h + 1) * 128],
                        ident2[r0:r0 + 64, :]))
                ttr = E.tend()
                E.wait("v", "t", ttr)
                vqkfm = E.chk("v", lambda e, co=co: e.tensor_copy(
                    out=a1["qkfm"][:, co:co + 512], in_=psT[:, 0:512]))
                if ch == 0:
                    vqkfm_0 = vqkfm
                E.wait("t", "v", vqkfm)
                E.wait("t", "s", sebl)
                if ch == 1:
                    E.wait("t", "v", vatm_prev)
                for h in range(NH):
                    E.on("t", lambda e, h=h, r0=r0, co=co: e.matmul(
                        psC[r0:r0 + 64, 512 + h * 64:512 + (h + 1) * 64],
                        a1["qkfm"][:, co + 256 + h * 64:co + 256 + (h + 1) * 64],
                        a1["qkfm"][:, co + h * 64:co + (h + 1) * 64],
                        start=True, stop=True))
                tat = E.tend()
                E.wait("v", "t", tat)
                vatm = E.chk("v", lambda e, r0=r0: e.tensor_mul(
                    out=a1["atm"][r0:r0 + 64, :], in0=psC[r0:r0 + 64, 512:768],
                    in1=maskt[r0:r0 + 64, :]))
                vatm_prev = vatm
                if ch == 1:
                    v_atm1[i] = vatm
                E.wait("t", "v", vatm)
                E.wait("t", "v", v_supd)
                for h in range(NH):
                    E.on("t", lambda e, h=h, r0=r0, co=co: e.matmul(
                        psA[r0:r0 + 64, h * 256:(h + 1) * 256],
                        a1["atm"][r0:r0 + 64, h * 64:(h + 1) * 64],
                        a1["vbf"][r0:r0 + 64, h * 256:(h + 1) * 256],
                        start=True, stop=False))
                    E.on("t", lambda e, h=h, r0=r0, co=co: e.matmul(
                        psA[r0:r0 + 64, h * 256:(h + 1) * 256],
                        a1["qkfm"][:, co + h * 64:co + (h + 1) * 64],
                        a1["Sb"][:, h * 256:(h + 1) * 256],
                        start=False, stop=True))
                t_o = E.tend()
                for h in range(NH):
                    E.on("t", lambda e, h=h, r0=r0: e.matmul(
                        psB[:, h * 256:(h + 1) * 256],
                        a1["kebf"][r0:r0 + 64, h * 128:(h + 1) * 128],
                        a1["vbf"][r0:r0 + 64, h * 256:(h + 1) * 256],
                        start=True, stop=True))
                t_p = E.tend()
                E.wait("v", "t", t_p)
                E.wait("v", "s", sebl)
                E.on("v", lambda e: e.tensor_add(out=a1["Sf"][:, :],
                                                 in0=a1["Sf"][:, :], in1=psB[:, :]))
                for h in range(NH):
                    E.on("v", lambda e, h=h, ch=ch: e.tensor_scalar_mul(
                        out=a1["Sf"][:, h * 256:(h + 1) * 256],
                        in0=a1["Sf"][:, h * 256:(h + 1) * 256],
                        scalar1=a1["ebl"][:, ch * 4 + h:ch * 4 + h + 1]))
                v_supd = E.chk("v", lambda e: e.tensor_copy(out=a1["Sb"][:, :],
                                                            in_=a1["Sf"][:, :]))
            supd_last[i] = v_supd
            t_o_last[i] = t_o

            E.wait("s", "t", t_o)
            for h in range(NH):
                E.on("s", lambda e, h=h: e.activation(
                    out=a1["scrB"][:, h * 256:(h + 1) * 256],
                    in_=psA[:, h * 256:(h + 1) * 256], func=AF.Square,
                    accum_out=st[:, S_G0 + h:S_G0 + h + 1]))
            E.on("s", lambda e: e.activation(
                out=st[:, S_GI:S_GI + 4], in_=st[:, S_G0:S_G0 + 4], func=AF.Ln,
                bias=cols[:, C_EPSR:C_EPSR + 1], scale=1.0 / 256.0))
            sgi = E.chk("s", lambda e: e.activation(
                out=st[:, S_GI:S_GI + 4], in_=st[:, S_GI:S_GI + 4], func=AF.Exp,
                scale=-0.5))
            E.on("s", lambda e: e.activation(out=a1["scrA"][:, :], in_=a1["g"][:, :],
                                             func=AF.Exp, scale=-1.0))
            E.on("s", lambda e: e.activation(out=a1["scrB"][:, :], in_=a1["scrA"][:, :],
                                             func=AF.Ln, bias=cols[:, C_ONE:C_ONE + 1]))
            ssig = E.chk("s", lambda e: e.activation(
                out=a1["scrA"][:, :], in_=a1["scrB"][:, :], func=AF.Exp, scale=-1.0))
            E.wait("v", "s", sgi)
            for h in range(NH):
                E.on("v", lambda e, h=h: e.scalar_tensor_tensor(
                    out=a1["on"][:, h * 256:(h + 1) * 256],
                    in0=psA[:, h * 256:(h + 1) * 256],
                    scalar=st[:, S_GI + h:S_GI + h + 1],
                    in1=a1["gag"][:, h * 256:(h + 1) * 256],
                    op0=ALU.mult, op1=ALU.mult))
            v_on[i] = E.chk("v", lambda e: e.engine_nop())
            E.wait("v", "s", ssig)
            E.on("v", lambda e: e.tensor_mul(out=a1["scrB"][:, 0:512],
                                             in0=a1["g"][:, 0:512],
                                             in1=a1["scrA"][:, 0:512]))
            vo2a = E.chk("v", lambda e: e.tensor_mul(
                out=a1["o2bf"][:, 0:512], in0=a1["on"][:, 0:512],
                in1=a1["scrB"][:, 0:512]))
            E.on("v", lambda e: e.tensor_mul(out=a1["scrB"][:, 512:1024],
                                             in0=a1["g"][:, 512:1024],
                                             in1=a1["scrA"][:, 512:1024]))
            vo2 = E.chk("v", lambda e: e.tensor_mul(
                out=a1["o2bf"][:, 512:1024], in0=a1["on"][:, 512:1024],
                in1=a1["scrB"][:, 512:1024]))
            v_o2[i] = vo2
            E.wait("t", "v", vo2a)
            for j in range(4):
                E.on("t", lambda e, j=j: e.transpose(
                    psT[:, j * 128:(j + 1) * 128],
                    a1["o2bf"][:, j * 128:(j + 1) * 128], identb[:, :]))
            E.wait("t", "v", vo2)
            for j in range(4, 8):
                E.on("t", lambda e, j=j: e.transpose(
                    psT[:, j * 128:(j + 1) * 128],
                    a1["o2bf"][:, j * 128:(j + 1) * 128], identb[:, :]))
            to2t = E.tend()
            E.wait("v", "t", to2t)
            v_o2fm[i] = E.chk("v", lambda e: e.tensor_copy(out=a1["o2fm"][:, :],
                                                           in_=psT[:, :]))
            a1_front(i + 1)
            emit_wo_x2(i)

        E.barrier()
        if phase_limit == "a1":
            return

        psG, psD, psT, psX = a2["psG"], a2["psD"], a2["psT"], a2["psX"]
        for wsb, wdr, ncol, nch in ((a2["wgate"], Wgate, IA, 8),
                                    (a2["wup"], Wup, IA, 8),
                                    (a2["wdown"], Wdown, D, 22)):
            for c in range(nch):
                E.on("g", lambda e, d=wsb, s=wdr, c=c, n=ncol: e.dma_start(
                    out=d[:, c * n:(c + 1) * n], in_=s[c * 128:(c + 1) * 128, :]),
                    inc=[("dw", 16)])
        E.on("sy", lambda e: e.dma_start(out=a2["gam"][:, :], in_=gam_mlp),
             inc=[("dx", 16)])
        dw_a2w = E.cnt["dw"]
        dx_a2 = E.cnt["dx"]
        E.wait("t", "dw", dw_a2w)
        E.on("v", lambda e: e.memset(a2["ymaxr"][:, :], -1e38))
        E.on("v", lambda e: e.memset(a2["yminr"][:, :], 1e38))
        E.wait("v", "dx", dx_a2)

        ld2, v_cons, s_oast, v_nbf, v_mlpfm2, v_u = {}, {}, {}, {}, {}, {}
        t_ntr, v_mlp1 = {}, {}

        def a2_loads(j):
            if j >= nt:
                return
            E.wait("sy", "v", v_cons.get(j - 2, 0))
            E.wait("sy", "ds", s_oast.get(j - 2, 0))
            E.on("sy", lambda e, j=j: e.dma_start(
                out=a2["x2b"][j % 2][:, :],
                in_=x2h[row0(j):row0(j) + 128, :]), inc=[("dx", 16)])
            E.on("sy", lambda e, j=j: e.dma_start(
                out=a2["xb"][j % 2][:, :],
                in_=x[row0(j):row0(j) + 128, :]), inc=[("dx", 16)])
            ld2[j] = E.cnt["dx"]

        def a2_front(j):
            if j >= nt:
                return
            E.wait("s", "dx", ld2[j])
            E.wait("s", "v", v_mlp1.get(j - 1, 0))
            E.on("s", lambda e, j=j: e.activation(
                out=a2["scrB"][:, 0:D], in_=a2["x2b"][j % 2][:, :], func=AF.Square,
                accum_out=st[:, S_FSQ:S_FSQ + 1]))
            E.on("s", lambda e: e.activation(
                out=st[:, S_FLN:S_FLN + 1], in_=st[:, S_FSQ:S_FSQ + 1], func=AF.Ln,
                bias=cols[:, C_EPSR:C_EPSR + 1], scale=1.0 / D))
            sinv = E.chk("s", lambda e: e.activation(
                out=st[:, S_FINV:S_FINV + 1], in_=st[:, S_FLN:S_FLN + 1],
                func=AF.Exp, scale=-0.5))
            E.wait("v", "s", sinv)
            E.wait("v", "t", t_ntr.get(j - 1, 0))
            vnbf = E.chk("v", lambda e, j=j: e.scalar_tensor_tensor(
                out=a2["nbf"][:, :], in0=a2["x2b"][j % 2][:, :],
                scalar=st[:, S_FINV:S_FINV + 1], in1=a2["gam"][:, :],
                op0=ALU.mult, op1=ALU.mult))
            v_nbf[j] = vnbf

        def a2_premid(j, pv):
            if j >= nt:
                return 0
            E.wait("t", "v", v_nbf[j])
            E.wait("t", "v", pv)
            for jj in range(8):
                E.on("t", lambda e, jj=jj: e.transpose(
                    psT[:, jj * 128:(jj + 1) * 128],
                    a2["nbf"][:, jj * 128:(jj + 1) * 128], identb[:, :]))
            tntr = E.tend()
            t_ntr[j] = tntr
            E.wait("v", "t", tntr)
            return E.chk("v", lambda e: e.tensor_copy(out=a2["nfm"][:, :],
                                                      in_=psT[:, 0:1024]))

        a2_loads(0)
        a2_loads(1)
        a2_front(0)
        vnfm_next = a2_premid(0, 0)

        for i in range(nt):
            vnfm = vnfm_next
            snfm = 0
            segs = ((0, 512), (512, 1024), (1024, 1408))

            def a2_gate_or_up(wt, coff):
                for kc in range(8):
                    for (c0, c1) in segs:
                        E.on("t", lambda e, kc=kc, c0=c0, c1=c1, coff=coff, wt=wt:
                             e.matmul(psG[:, c0:c1],
                                      a2["nfm"][:, kc * 128:(kc + 1) * 128],
                                      wt[:, kc * IA + coff + c0: kc * IA + coff + c1],
                                      start=(kc == 0), stop=(kc == 7)))
                return E.tend()

            def a2_gate_evac(tg, half):
                E.wait("v", "t", tg)
                vg = E.chk("v", lambda e, half=half: e.tensor_copy(
                    out=a2["gate"][half][:, 0:704], in_=psG[:, 0:704]))
                E.wait("s", "t", tg)
                sg = E.chk("s", lambda e, half=half: e.copy(
                    out=a2["gate"][half][:, 704:1408], in_=psG[:, 704:1408]))
                return vg, sg

            def a2_silu(vg, half, o1, o2, wv=0):
                E.wait("s", "v", vg)
                E.on("s", lambda e, half=half, o1=o1: e.activation(
                    out=o1[:, :], in_=a2["gate"][half][:, :], func=AF.Exp,
                    scale=-1.0))
                E.wait("s", "v", wv)
                E.on("s", lambda e, o1=o1, o2=o2: e.activation(
                    out=o2[:, :], in_=o1[:, :], func=AF.Ln,
                    bias=cols[:, C_ONE:C_ONE + 1]))
                return E.chk("s", lambda e, o1=o1, o2=o2: e.activation(
                    out=o2[:, :], in_=o2[:, :], func=AF.Exp, scale=-1.0))

            def a2_muls(tu, ssig, sg, half, sig):
                E.wait("v", "t", tu)
                E.wait("v", "s", sg)
                vm1 = E.chk("v", lambda e, half=half: e.tensor_mul(
                    out=a2["scrC"][:, :], in0=psG[:, 0:1408],
                    in1=a2["gate"][half][:, :]))
                E.wait("v", "s", ssig)
                vmlp = E.chk("v", lambda e, half=half, sig=sig: e.tensor_mul(
                    out=a2["mlpb"][half][:, :], in0=a2["scrC"][:, :], in1=sig[:, :]))
                return vm1, vmlp

            def a2_tr(half):
                for j in range(11):
                    E.on("t", lambda e, j=j, half=half: e.transpose(
                        psT[:, j * 128:(j + 1) * 128],
                        a2["mlpb"][half][:, j * 128:(j + 1) * 128], identb[:, :]))
                tm1 = E.tend()
                E.wait("v", "t", tm1)
                vfA = E.chk("v", lambda e: e.tensor_copy(out=a2["mlpfm"][:, 0:768],
                                                         in_=psT[:, 0:768]))
                vfB = E.chk("v", lambda e: e.tensor_copy(out=a2["mlpfm"][:, 768:1408],
                                                         in_=psT[:, 768:1408]))
                return vfA, vfB

            def a2_down(half, vfA, vfB):
                for j in range(11):
                    if j == 0:
                        E.wait("t", "v", vfA)
                    if j == 6:
                        E.wait("t", "v", vfB)
                    kci = half * 11 + j
                    for hh2 in range(2):
                        E.on("t", lambda e, j=j, kci=kci, hh2=hh2: e.matmul(
                            psD[:, hh2 * 512:(hh2 + 1) * 512],
                            a2["mlpfm"][:, j * 128:(j + 1) * 128],
                            a2["wdown"][:, kci * D + hh2 * 512:
                                        kci * D + (hh2 + 1) * 512],
                            start=(kci == 0), stop=(kci == 21)))
                return E.tend()

            E.wait("t", "v", vnfm)
            E.wait("t", "s", snfm)
            tg0 = a2_gate_or_up(a2["wgate"], 0)
            vg0, sg0 = a2_gate_evac(tg0, 0)
            ssig0 = a2_silu(vg0, 0, a2["scrB"], a2["scrA"])
            E.wait("t", "v", vg0)
            E.wait("t", "s", sg0)
            tg1 = a2_gate_or_up(a2["wgate"], 1408)
            vg1, sg1 = a2_gate_evac(tg1, 1)
            ssig1 = None
            E.wait("t", "v", vg1)
            E.wait("t", "s", sg1)
            tu0 = a2_gate_or_up(a2["wup"], 0)
            vm1_0, vmlp0 = a2_muls(tu0, ssig0, sg0, 0, a2["scrA"])
            ssig1 = a2_silu(vg1, 1, a2["scrB"], a2["scrA"], wv=vmlp0)
            E.wait("t", "v", vm1_0)
            tu1 = a2_gate_or_up(a2["wup"], 1408)
            vm1_1, vmlp1 = a2_muls(tu1, ssig1, sg1, 1, a2["scrA"])
            v_mlp1[i] = vmlp1
            E.wait("t", "v", vmlp0)
            vf2_0a, vf2_0b = a2_tr(0)
            E.wait("t", "v", v_u.get(i - 1, 0))
            a2_down(0, vf2_0a, vf2_0b)
            a2_front(i + 1)
            E.wait("t", "v", vmlp1)
            vf2_1a, vf2_1b = a2_tr(1)
            tdown = a2_down(1, vf2_1a, vf2_1b)
            vnfm_next = a2_premid(i + 1, vf2_1b)

            E.wait("v", "t", tdown)
            vu = E.chk("v", lambda e, i=i: e.tensor_add(
                out=a2["u"][:, :], in0=psD[:, :], in1=a2["x2b"][i % 2][:, :]))
            v_u[i] = vu
            E.on("v", lambda e: e.reduce_sum(out=st[:, S_SUM:S_SUM + 1],
                                             in_=a2["u"][:, :], axis=AX.X))
            E.on("v", lambda e: e.tensor_scalar_mul(
                out=st[:, S_M:S_M + 1], in0=st[:, S_SUM:S_SUM + 1], scalar1=1.0 / D))
            E.on("v", lambda e: e.tensor_mul(out=st[:, S_MSQ:S_MSQ + 1],
                                             in0=st[:, S_M:S_M + 1],
                                             in1=st[:, S_M:S_M + 1]))
            vb = E.chk("v", lambda e: e.tensor_scalar(
                out=st[:, S_BIAS:S_BIAS + 1], in0=st[:, S_MSQ:S_MSQ + 1],
                scalar1=-1.0, scalar2=EPS_LN, op0=ALU.mult, op1=ALU.add))
            E.wait("s", "v", vu)
            E.on("s", lambda e: e.activation(out=a2["scrB"][:, 0:D], in_=a2["u"][:, :],
                                             func=AF.Square,
                                             accum_out=st[:, S_SQ:S_SQ + 1]))
            E.wait("s", "v", vb)
            E.on("s", lambda e: e.activation(
                out=st[:, S_LNV:S_LNV + 1], in_=st[:, S_SQ:S_SQ + 1], func=AF.Ln,
                bias=st[:, S_BIAS:S_BIAS + 1], scale=1.0 / D))
            sinvu = E.chk("s", lambda e: e.activation(
                out=st[:, S_INV1:S_INV1 + 1], in_=st[:, S_LNV:S_LNV + 1],
                func=AF.Exp, scale=-0.5))
            E.wait("v", "s", sinvu)
            E.on("v", lambda e: e.tensor_scalar(
                out=a2["scrA"][:, 0:D], in0=a2["u"][:, :], scalar1=st[:, S_M:S_M + 1],
                scalar2=st[:, S_INV1:S_INV1 + 1], op0=ALU.subtract, op1=ALU.mult))
            E.wait("v", "ds", s_oast.get(i - 2, 0))
            voa = E.chk("v", lambda e, i=i: e.tensor_add(
                out=a2["oab"][i % 2][:, :], in0=a2["scrA"][:, 0:D],
                in1=a2["xb"][i % 2][:, :]))
            E.wait("s", "v", voa)
            E.on("s", lambda e, i=i: e.dma_start(
                out=oAh[row0(i):row0(i) + 128, :], in_=a2["oab"][i % 2][:, :]),
                inc=[("ds", 16)])
            if phase_limit == "a2":
                E.on("s", lambda e, i=i: e.dma_start(
                    out=out[row0(i):row0(i) + 128, :], in_=a2["oab"][i % 2][:, :]),
                    inc=[("ds", 16)])
            s_oast[i] = E.cnt["ds"]
            E.on("v", lambda e, i=i: e.reduce_sum(
                out=st[:, S_SUM2:S_SUM2 + 1], in_=a2["oab"][i % 2][:, :], axis=AX.X))
            E.on("v", lambda e, i=i: e.tensor_scalar_mul(
                out=ymt[:, i:i + 1], in0=st[:, S_SUM2:S_SUM2 + 1], scalar1=1.0 / D))
            E.on("v", lambda e, i=i: e.tensor_mul(out=st[:, S_MSQ2:S_MSQ2 + 1],
                                                  in0=ymt[:, i:i + 1],
                                                  in1=ymt[:, i:i + 1]))
            vb2 = E.chk("v", lambda e: e.tensor_scalar(
                out=st[:, S_BIAS2:S_BIAS2 + 1], in0=st[:, S_MSQ2:S_MSQ2 + 1],
                scalar1=-1.0, scalar2=EPS_LN, op0=ALU.mult, op1=ALU.add))
            E.on("s", lambda e, i=i: e.activation(
                out=a2["scrB"][:, 0:D], in_=a2["oab"][i % 2][:, :], func=AF.Square,
                accum_out=st[:, S_SQ:S_SQ + 1]))
            E.wait("s", "v", vb2)
            E.on("s", lambda e: e.activation(
                out=st[:, S_LNV:S_LNV + 1], in_=st[:, S_SQ:S_SQ + 1], func=AF.Ln,
                bias=st[:, S_BIAS2:S_BIAS2 + 1], scale=1.0 / D))
            sy2 = E.chk("s", lambda e, i=i: e.activation(
                out=yinvt[:, i:i + 1], in_=st[:, S_LNV:S_LNV + 1], func=AF.Exp,
                scale=-0.5))
            E.on("v", lambda e, i=i: e.reduce_max(out=st[:, S_RED:S_RED + 1],
                                                  in_=a2["oab"][i % 2][:, :],
                                                  axis=AX.X))
            E.on("v", lambda e, i=i: e.tensor_reduce(
                out=st[:, S_RED2:S_RED2 + 1], in_=a2["oab"][i % 2][:, :],
                axis=AX.X, op=ALU.min))
            E.wait("v", "s", sy2)
            E.on("v", lambda e, i=i: e.tensor_scalar(
                out=st[:, S_RED:S_RED + 1], in0=st[:, S_RED:S_RED + 1],
                scalar1=ymt[:, i:i + 1], scalar2=yinvt[:, i:i + 1],
                op0=ALU.subtract, op1=ALU.mult))
            E.on("v", lambda e: e.tensor_max(out=a2["ymaxr"][:, :],
                                             in0=a2["ymaxr"][:, :],
                                             in1=st[:, S_RED:S_RED + 1]))
            E.on("v", lambda e, i=i: e.tensor_scalar(
                out=st[:, S_RED2:S_RED2 + 1], in0=st[:, S_RED2:S_RED2 + 1],
                scalar1=ymt[:, i:i + 1], scalar2=yinvt[:, i:i + 1],
                op0=ALU.subtract, op1=ALU.mult))
            vmm = E.chk("v", lambda e: e.tensor_tensor(
                out=a2["yminr"][:, :], in0=a2["yminr"][:, :],
                in1=st[:, S_RED2:S_RED2 + 1], op=ALU.min))
            v_cons[i] = vmm
            a2_loads(i + 2)

        E.wait("t", "v", E.cnt["v"])
        tmx = E.chk("t", lambda e: e.transpose(psX[0:1, 0:128], a2["ymaxr"][:, :],
                                               identf[:, :]))
        E.wait("v", "t", tmx)
        vdone = E.chk("v", lambda e: e.reduce_max(out=ccsb[0:1, 0:1],
                                                  in_=psX[0:1, 0:128], axis=AX.X))
        E.wait("t", "v", vdone)
        tmn = E.chk("t", lambda e: e.transpose(psX[0:1, 0:128], a2["yminr"][:, :],
                                               identf[:, :]))
        E.wait("v", "t", tmn)
        E.on("v", lambda e: e.tensor_reduce(out=ccsb[0:1, 1:2], in_=psX[0:1, 0:128],
                                            axis=AX.X, op=ALU.min))
        E.on("v", lambda e: e.tensor_scalar_mul(out=ccsb[0:1, 1:2],
                                                in0=ccsb[0:1, 1:2], scalar1=-1.0))
        E.on("v", lambda e: e.memset(ccsb[0:1, 2:8], 0.0))

        E.barrier()
        if phase_limit == "a2":
            return

        def allreduce_and_scales(ps_x, r_sw, bsrow, brow_t, nb):
            vsnap = E.cnt["v"]
            E.wait("g", "v", vsnap)
            E.on("g", lambda e: e.dma_start(out=ccin[:, :], in_=ccsb[:, :]),
                 inc=[("dw", 16)])
            E.wait("g", "dw", E.cnt["dw"])
            E.on("g", lambda e: e.collective_compute(
                "AllReduce", ALU.max, replica_groups=[list(range(NCORES))],
                ins=[ccin.ap().opt()], outs=[ccout.ap().opt()]), inc=[("cc", 1)])
            E.wait("g", "cc", E.cnt["cc"])
            E.on("g", lambda e: e.dma_start(out=ccsb[:, :], in_=ccout[:, :]),
                 inc=[("dw", 16)])
            E.wait("v", "dw", E.cnt["dw"])
            E.on("v", lambda e: e.tensor_add(out=rows[0:1, R_D:R_D + 1],
                                             in0=ccsb[0:1, 0:1], in1=ccsb[0:1, 1:2]))
            E.on("v", lambda e: e.tensor_scalar(
                out=rows[0:1, R_SX:R_SX + 1], in0=rows[0:1, R_D:R_D + 1],
                scalar1=1.0 / 255.0, scalar2=1e-8, op0=ALU.mult, op1=ALU.max))
            E.on("v", lambda e: e.reciprocal(out=rows[0:1, R_ISX:R_ISX + 1],
                                             in_=rows[0:1, R_SX:R_SX + 1]))
            E.on("v", lambda e: e.tensor_mul(out=rows[0:1, R_ZT:R_ZT + 1],
                                             in0=ccsb[0:1, 1:2],
                                             in1=rows[0:1, R_ISX:R_ISX + 1]))
            E.on("v", lambda e: e.tensor_copy(out=rowi[0:1, 0:1],
                                              in_=rows[0:1, R_ZT:R_ZT + 1]))
            E.on("v", lambda e: e.tensor_copy(out=rows[0:1, R_ZT:R_ZT + 1],
                                              in_=rowi[0:1, 0:1]))
            E.on("v", lambda e: e.tensor_scalar_add(
                out=rows[0:1, R_ZP:R_ZP + 1], in0=rows[0:1, R_ZT:R_ZT + 1],
                scalar1=-128.0))
            E.on("v", lambda e: e.tensor_scalar(
                out=rows[0:1, R_ZP:R_ZP + 1], in0=rows[0:1, R_ZP:R_ZP + 1],
                scalar1=-128.0, scalar2=127.0, op0=ALU.max, op1=ALU.min))
            E.on("v", lambda e: e.tensor_scalar(
                out=rows[0:1, R_LO:R_LO + 1], in0=rows[0:1, R_ZP:R_ZP + 1],
                scalar1=-1.0, scalar2=-128.0, op0=ALU.mult, op1=ALU.add))
            E.on("v", lambda e: e.tensor_scalar(
                out=rows[0:1, R_HI:R_HI + 1], in0=rows[0:1, R_ZP:R_ZP + 1],
                scalar1=-1.0, scalar2=127.0, op0=ALU.mult, op1=ALU.add))
            E.on("v", lambda e: e.tensor_mul(out=rows[0:1, R_SXSW:R_SXSW + 1],
                                             in0=rows[0:1, R_SX:R_SX + 1],
                                             in1=rows[0:1, r_sw:r_sw + 1]))
            E.on("v", lambda e: e.reciprocal(out=rows[0:1, R_ISXSW:R_ISXSW + 1],
                                             in_=rows[0:1, R_SXSW:R_SXSW + 1]))
            E.on("v", lambda e, nb=nb: e.tensor_scalar_mul(
                out=bsrow[0:1, 0:nb], in0=brow_t,
                scalar1=rows[0:1, R_ISXSW:R_ISXSW + 1]))
            E.on("v", lambda e: e.tensor_copy(out=rows[0:1, 16:17],
                                              in_=rows[0:1, R_ISX:R_ISX + 1]))
            E.on("v", lambda e: e.tensor_copy(out=rows[0:1, 17:18],
                                              in_=rows[0:1, R_LO:R_LO + 1]))
            E.on("v", lambda e: e.tensor_copy(out=rows[0:1, 18:19],
                                              in_=rows[0:1, R_HI:R_HI + 1]))
            vpk = E.chk("v", lambda e: e.tensor_copy(
                out=rows[0:1, 19:20], in_=rows[0:1, R_SXSW:R_SXSW + 1]))
            E.wait("t", "v", vpk)
            tb = E.chk("t", lambda e, ps_x=ps_x: e.matmul(
                ps_x[:, 0:4], onesrf[0:1, :], rows[0:1, 16:20], start=True, stop=True))
            E.wait("v", "t", tb)
            return E.chk("v", lambda e, ps_x=ps_x: e.tensor_copy(
                out=cols[:, 0:4], in_=ps_x[:, 0:4]))

        E.on("sy", lambda e: e.dma_start(out=b1["hbuf"][0][0:1, :], in_=b1row),
             inc=[("dx", 16)])
        dx_b1r = E.cnt["dx"]
        E.wait("v", "dx", dx_b1r)
        allreduce_and_scales(b1["psX"], R_SW1, b1["b1s"], b1["hbuf"][0][0:1, 0:IF], IF)

        for c in range(8):
            E.on("sy", lambda e, c=c: e.dma_start(
                out=b1["wq1T"][:, c * IF:(c + 1) * IF],
                in_=wq1h[:, c * 128:(c + 1) * 128], transpose=True), inc=[("dx", 16)])
        dx_w1t = E.cnt["dx"]

        psF, psT, psX = b1["psF"], b1["psT"], b1["psX"]
        E.on("v", lambda e: e.memset(b1["hmaxr"][:, :], -1e38))
        E.on("v", lambda e: e.memset(b1["hminr"][:, :], 1e38))
        E.wait("t", "dx", dx_w1t)
        ldb, v_yq, st_h, v_xqfm, t_tr1 = {}, {}, {}, {}, {}

        def b1_loads(j):
            if j >= nt:
                return
            E.wait("sy", "v", v_yq.get(j - 2, 0))
            E.on("sy", lambda e, j=j: e.dma_start(
                out=b1["oab"][j % 2][:, :],
                in_=oAh[row0(j):row0(j) + 128, :]), inc=[("dx", 16)])
            ldb[j] = E.cnt["dx"]

        def b1_quant(j):
            if j >= nt:
                return
            E.wait("v", "dx", ldb[j])
            E.wait("v", "t", t_tr1.get(j - 1, 0))
            E.on("v", lambda e, j=j: e.tensor_scalar(
                out=b1["y"][:, :], in0=b1["oab"][j % 2][:, :],
                scalar1=ymt[:, j:j + 1], scalar2=yinvt[:, j:j + 1],
                op0=ALU.subtract, op1=ALU.mult))
            E.on("v", lambda e: e.tensor_scalar_mul(
                out=b1["scr"][:, :], in0=b1["y"][:, :],
                scalar1=cols[:, C_ISX:C_ISX + 1]))
            E.on("v", lambda e: e.tensor_copy(out=b1["qi"][:, :], in_=b1["scr"][:, :]))
            v_yq[j] = E.chk("v", lambda e: e.tensor_scalar(
                out=b1["xqbf"][:, :], in0=b1["qi"][:, :],
                scalar1=cols[:, C_LO:C_LO + 1], scalar2=cols[:, C_HI:C_HI + 1],
                op0=ALU.max, op1=ALU.min))

        b1_loads(0)
        b1_loads(1)
        b1_quant(0)
        sg_reg = [0, 0]
        for i in range(nt):
            E.wait("t", "v", v_yq[i])
            E.wait("t", "v", v_xqfm.get(i - 1, 0))
            for j in range(8):
                E.on("t", lambda e, j=j: e.transpose(
                    psT[:, j * 128:(j + 1) * 128], b1["xqbf"][:, j * 128:(j + 1) * 128],
                    identb[:, :]))
            ttr = E.tend()
            t_tr1[i] = ttr
            E.wait("v", "t", ttr)
            vfm = E.chk("v", lambda e: e.tensor_copy(out=b1["xqfm"][:, :], in_=psT[:, :]))
            v_xqfm[i] = vfm
            b1_loads(i + 2)
            b1_quant(i + 1)
            E.wait("t", "v", vfm)
            E.wait("s", "dw", st_h.get(i - 2, 0))
            for oc in range(4):
                r = oc % 2
                E.wait("t", "s", sg_reg[r])
                for kc in range(8):
                    for hh2 in range(2):
                        E.on("t", lambda e, oc=oc, r=r, kc=kc, hh2=hh2:
                             e.matmul(
                             psF[:, r * 1024 + hh2 * 512:
                                 r * 1024 + (hh2 + 1) * 512],
                             b1["xqfm"][:, kc * 128:(kc + 1) * 128],
                             b1["wq1T"][:, kc * IF + oc * 1024 + hh2 * 512:
                                        kc * IF + oc * 1024 + (hh2 + 1) * 512],
                             start=(kc == 0), stop=False))
                for hh2 in range(2):
                    E.on("t", lambda e, oc=oc, r=r, hh2=hh2: e.matmul(
                        psF[:, r * 1024 + hh2 * 512:
                            r * 1024 + (hh2 + 1) * 512],
                        onesrb[0:1, :],
                        b1["b1s"][0:1, oc * 1024 + hh2 * 512:
                                  oc * 1024 + (hh2 + 1) * 512],
                        start=False, stop=True))
                tmm = E.tend()
                E.wait("s", "t", tmm)
                sg = E.chk("s", lambda e, oc=oc, r=r, i=i: e.activation(
                    out=b1["hbuf"][i % 2][:, oc * 1024:(oc + 1) * 1024],
                    in_=psF[:, r * 1024:(r + 1) * 1024],
                    func=AF.Gelu, scale=cols[:, C_SXSW:C_SXSW + 1]))
                sg_reg[r] = sg
                E.wait("v", "s", sg)
                E.on("v", lambda e, oc=oc, i=i: e.reduce_max(
                    out=st[:, S_RED:S_RED + 1],
                    in_=b1["hbuf"][i % 2][:, oc * 1024:(oc + 1) * 1024], axis=AX.X))
                E.on("v", lambda e: e.tensor_max(out=b1["hmaxr"][:, :],
                                                 in0=b1["hmaxr"][:, :],
                                                 in1=st[:, S_RED:S_RED + 1]))
                E.on("v", lambda e, oc=oc, i=i: e.tensor_reduce(
                    out=st[:, S_RED:S_RED + 1],
                    in_=b1["hbuf"][i % 2][:, oc * 1024:(oc + 1) * 1024], axis=AX.X,
                    op=ALU.min))
                E.on("v", lambda e: e.tensor_tensor(
                    out=b1["hminr"][:, :], in0=b1["hminr"][:, :],
                    in1=st[:, S_RED:S_RED + 1], op=ALU.min))
            vred = E.chk("v", lambda e: e.engine_nop())
            E.wait("g", "v", vred)
            E.on("g", lambda e, i=i: e.dma_start(
                out=hh[row0(i):row0(i) + 128, :], in_=b1["hbuf"][i % 2][:, :]),
                inc=[("dw", 16)])
            if phase_limit == "b1":
                E.on("g", lambda e, i=i: e.dma_start(
                    out=out[row0(i):row0(i) + 128, :],
                    in_=b1["hbuf"][i % 2][:, (i % 4) * 1024:((i % 4) + 1) * 1024]),
                    inc=[("dw", 16)])
            st_h[i] = E.cnt["dw"]

        E.wait("t", "v", E.cnt["v"])
        tmx = E.chk("t", lambda e: e.transpose(psX[0:1, 0:128], b1["hmaxr"][:, :],
                                               identf[:, :]))
        E.wait("v", "t", tmx)
        vdone = E.chk("v", lambda e: e.reduce_max(out=ccsb[0:1, 0:1],
                                                  in_=psX[0:1, 0:128], axis=AX.X))
        E.wait("t", "v", vdone)
        tmn = E.chk("t", lambda e: e.transpose(psX[0:1, 0:128], b1["hminr"][:, :],
                                               identf[:, :]))
        E.wait("v", "t", tmn)
        E.on("v", lambda e: e.tensor_reduce(out=ccsb[0:1, 1:2], in_=psX[0:1, 0:128],
                                            axis=AX.X, op=ALU.min))
        E.on("v", lambda e: e.tensor_scalar_mul(out=ccsb[0:1, 1:2],
                                                in0=ccsb[0:1, 1:2], scalar1=-1.0))
        E.on("v", lambda e: e.memset(ccsb[0:1, 2:8], 0.0))

        E.barrier()
        if phase_limit == "b1":
            return

        E.on("sy", lambda e: e.dma_start(out=b2["hb"][0][0:1, 0:D], in_=b2row),
             inc=[("dx", 16)])
        dx_b2r = E.cnt["dx"]
        E.wait("v", "dx", dx_b2r)
        vsc2 = allreduce_and_scales(b2["psX"], R_SW2, b2["b2s"], b2["hb"][0][0:1, 0:D], D)
        for c in range(32):
            E.on("sy", lambda e, c=c: e.dma_start(
                out=b2["wq2T"][:, c * D:(c + 1) * D],
                in_=wq2h[:, c * 128:(c + 1) * 128], transpose=True), inc=[("dx", 16)])
        dx_w2t = E.cnt["dx"]
        psD, psX = b2["psD"], b2["psX"]
        psTs = [b2["psT"], b2["psT2"]]
        E.wait("t", "dx", dx_w2t)
        ldh, v_hq, st_o, v_ffn, v_out, t_trq = {}, {}, {}, {}, {}, {}
        E.wait("sy", "v", vsc2)

        def b2_loads(j):
            if j >= nt:
                return
            E.wait("sy", "v", v_out.get(j - 2, 0))
            E.on("sy", lambda e, j=j: e.dma_start(
                out=b2["hb"][j % 2][:, :],
                in_=hh[row0(j):row0(j) + 128, :]), inc=[("dx", 16)])
            E.on("sy", lambda e, j=j: e.dma_start(
                out=b2["oab"][j % 2][:, :],
                in_=oAh[row0(j):row0(j) + 128, :]), inc=[("dx", 16)])
            ldh[j] = E.cnt["dx"]

        def b2_quant(j):
            if j >= nt:
                return
            E.wait("v", "dx", ldh[j])
            E.wait("v", "t", t_trq.get(j - 1, 0))
            E.on("v", lambda e, j=j: e.tensor_scalar_mul(
                out=b2["scr"][:, :], in0=b2["hb"][j % 2][:, :],
                scalar1=cols[:, C_ISX:C_ISX + 1]))
            E.on("v", lambda e: e.tensor_copy(out=b2["qi"][:, :], in_=b2["scr"][:, :]))
            v_hq[j] = E.chk("v", lambda e: e.tensor_scalar(
                out=b2["hqbf"][:, :], in0=b2["qi"][:, :],
                scalar1=cols[:, C_LO:C_LO + 1], scalar2=cols[:, C_HI:C_HI + 1],
                op0=ALU.max, op1=ALU.min))

        b2_loads(0)
        b2_loads(1)
        b2_quant(0)
        for i in range(nt):
            E.wait("t", "v", v_hq[i])
            E.wait("t", "v", v_ffn.get(i - 1, 0))
            vfm_prev = {}
            ttr = 0

            def b2_tr(grp):
                for j in range(8):
                    E.on("t", lambda e, grp=grp, j=j: e.transpose(
                        psTs[grp % 2][:, j * 128:(j + 1) * 128],
                        b2["hqbf"][:, grp * 1024 + j * 128: grp * 1024 + (j + 1) * 128],
                        identb[:, :]))
                return E.tend()

            def b2_fm(grp, ttrg):
                E.wait("v", "t", ttrg)
                return E.chk("v", lambda e, grp=grp: e.tensor_copy(
                    out=b2["hqfm"][grp % 2][:, :], in_=psTs[grp % 2][:, :]))

            ttrs = {0: b2_tr(0)}
            vfms = {0: b2_fm(0, ttrs[0])}
            for grp in range(4):
                if grp + 1 < 4:
                    E.wait("t", "v", vfms.get(grp - 1, 0))
                    ttrs[grp + 1] = b2_tr(grp + 1)
                    vfms[grp + 1] = b2_fm(grp + 1, ttrs[grp + 1])
                E.wait("t", "v", vfms[grp])
                for j in range(8):
                    kc = grp * 8 + j
                    for hh2 in range(2):
                        E.on("t", lambda e, grp=grp, j=j, kc=kc, hh2=hh2: e.matmul(
                            psD[:, hh2 * 512:(hh2 + 1) * 512],
                            b2["hqfm"][grp % 2][:, j * 128:(j + 1) * 128],
                            b2["wq2T"][:, kc * D + hh2 * 512: kc * D + (hh2 + 1) * 512],
                            start=(kc == 0), stop=False))
            ttr = ttrs[3]
            t_trq[i] = ttr
            for hh2 in range(2):
                E.on("t", lambda e, hh2=hh2: e.matmul(
                    psD[:, hh2 * 512:(hh2 + 1) * 512], onesrb[0:1, :],
                    b2["b2s"][0:1, hh2 * 512:(hh2 + 1) * 512], start=False, stop=True))
            tmm = E.tend()
            b2_quant(i + 1)
            E.wait("s", "t", tmm)
            sffn = E.chk("s", lambda e: e.activation(
                out=b2["fbuf"][:, :], in_=psD[:, :], func=AF.Copy,
                scale=cols[:, C_SXSW:C_SXSW + 1]))
            E.wait("v", "s", sffn)
            v_ffn[i] = E.chk("v", lambda e: e.engine_nop())
            E.on("v", lambda e: e.reduce_sum(out=st[:, S_SUM:S_SUM + 1],
                                             in_=b2["fbuf"][:, :], axis=AX.X))
            E.on("v", lambda e: e.tensor_scalar_mul(
                out=st[:, S_M:S_M + 1], in0=st[:, S_SUM:S_SUM + 1], scalar1=1.0 / D))
            E.on("v", lambda e: e.tensor_mul(out=st[:, S_MSQ:S_MSQ + 1],
                                             in0=st[:, S_M:S_M + 1],
                                             in1=st[:, S_M:S_M + 1]))
            vb = E.chk("v", lambda e: e.tensor_scalar(
                out=st[:, S_BIAS:S_BIAS + 1], in0=st[:, S_MSQ:S_MSQ + 1],
                scalar1=-1.0, scalar2=EPS_LN, op0=ALU.mult, op1=ALU.add))
            E.wait("s", "v", v_hq.get(i + 1, 0))
            E.on("s", lambda e: e.activation(out=b2["scr"][:, 0:D], in_=b2["fbuf"][:, :],
                                             func=AF.Square,
                                             accum_out=st[:, S_SQ:S_SQ + 1]))
            E.wait("s", "v", vb)
            E.on("s", lambda e: e.activation(
                out=st[:, S_LNV:S_LNV + 1], in_=st[:, S_SQ:S_SQ + 1], func=AF.Ln,
                bias=st[:, S_BIAS:S_BIAS + 1], scale=1.0 / D))
            sinv = E.chk("s", lambda e: e.activation(
                out=st[:, S_INV1:S_INV1 + 1], in_=st[:, S_LNV:S_LNV + 1],
                func=AF.Exp, scale=-0.5))
            E.wait("v", "s", sinv)
            E.on("v", lambda e: e.tensor_scalar(
                out=b2["scr"][:, 0:D], in0=b2["fbuf"][:, :], scalar1=st[:, S_M:S_M + 1],
                scalar2=st[:, S_INV1:S_INV1 + 1], op0=ALU.subtract, op1=ALU.mult))
            E.wait("v", "ds", st_o.get(i - 2, 0))
            vo = E.chk("v", lambda e, i=i: e.tensor_add(
                out=b2["ob"][i % 2][:, :], in0=b2["scr"][:, 0:D],
                in1=b2["oab"][i % 2][:, :]))
            v_out[i] = vo
            E.wait("s", "v", vo)
            E.on("s", lambda e, i=i: e.dma_start(
                out=out[row0(i):row0(i) + 128, :], in_=b2["ob"][i % 2][:, :]),
                inc=[("ds", 16)])
            st_o[i] = E.cnt["ds"]
            b2_loads(i + 2)

        E.barrier()

    with nc.Block() as block:
        for cur in ENGS:
            def body(eng, cur=cur):
                sched(Env(cur, eng, sems))
            getattr(block, _BLOCK_NAME[cur])(body)

    _free(opened_global)
    return nc


def _consts():
    c = {}
    c["identb"] = np.eye(128, dtype=BF)
    id2 = np.zeros((128, 64), dtype=np.float32)
    for p in range(128):
        id2[p, p % 64] = 1.0
    c["ident2b"] = id2.astype(BF)
    c["identf"] = np.eye(128, dtype=np.float32)
    sc = np.arange(64)
    pc = np.arange(128) % 64
    lm = np.where(pc[:, None] <= sc[None, :], np.float32(-1.0 / 16.0), np.float32(0.0))
    c["lmask"] = lm.astype(BF)
    mt = np.where(pc[:, None] <= sc[None, :], np.float32(1.0), np.float32(0.0))
    c["maskt"] = np.tile(mt, (1, 4)).astype(np.float32)
    c["cones"] = np.full((128, 1), -1.0 / 16.0, dtype=BF)
    c["onesrf"] = np.ones((1, 128), dtype=np.float32)
    c["onesrb"] = np.ones((1, 128), dtype=BF)
    return c


def _shard(inputs):
    f32 = np.float32
    hs = np.ascontiguousarray(np.asarray(inputs["L0_attn_hidden_states"], dtype=f32))
    base = {
        "Wq": np.ascontiguousarray(inputs["L0_attn_Wq"], dtype=f32),
        "Wk": np.ascontiguousarray(inputs["L0_attn_Wk"], dtype=f32),
        "Wv": np.ascontiguousarray(inputs["L0_attn_Wv"], dtype=f32),
        "Wg": np.ascontiguousarray(inputs["L0_attn_Wg"], dtype=f32),
        "Wgk1": np.ascontiguousarray(inputs["L0_attn_Wgk1"], dtype=f32),
        "Wgk2": np.ascontiguousarray(inputs["L0_attn_Wgk2"], dtype=f32),
        "Wo": np.ascontiguousarray(inputs["L0_attn_Wo"], dtype=f32),
        "Wgate": np.ascontiguousarray(inputs["L0_attn_W_gate"], dtype=f32),
        "Wup": np.ascontiguousarray(inputs["L0_attn_W_up"], dtype=f32),
        "Wdown": np.ascontiguousarray(inputs["L0_attn_W_down"], dtype=f32),
        "W1": np.ascontiguousarray(inputs["L0_ffn_W1"], dtype=f32),
        "W2": np.ascontiguousarray(inputs["L0_ffn_W2"], dtype=f32),
        "gam_attn": np.ascontiguousarray(np.broadcast_to(
            np.asarray(inputs["L0_attn_attn_norm_w"], dtype=f32)[None, :], (128, D))),
        "bgk2b": np.ascontiguousarray(np.broadcast_to(
            np.asarray(inputs["L0_attn_bgk2"], dtype=f32)[None, :], (128, DKT))),
        "gamg": np.ascontiguousarray(np.broadcast_to(
            np.tile(np.asarray(inputs["L0_attn_gnorm_w"], dtype=f32), 4)[None, :],
            (128, DVT))),
        "gam_mlp": np.ascontiguousarray(np.broadcast_to(
            np.asarray(inputs["L0_attn_mlp_norm_w"], dtype=f32)[None, :], (128, D))),
        "b1row": np.ascontiguousarray(
            np.asarray(inputs["L0_ffn_b1"], dtype=f32)[None, :]),
        "b2row": np.ascontiguousarray(
            np.asarray(inputs["L0_ffn_b2"], dtype=f32)[None, :]),
        **_consts(),
    }
    in_maps = []
    for c in range(NCORES):
        m = dict(base)
        m["x"] = np.ascontiguousarray(hs[c * BC:(c + 1) * BC].reshape(BC * T, D))
        in_maps.append(m)
    return in_maps


def kernel(**inputs):
    in_maps = _shard(inputs)
    nc = _build("full")
    res = run_bass_kernel_spmd(nc, in_maps, core_ids=list(range(NCORES)))
    outs = [np.asarray(r["out"]).reshape(BC, T, D) for r in res.results]
    return np.concatenate(outs, axis=0)
